# version 23
# baseline (speedup 1.0000x reference)
import numpy as np
from contextlib import ExitStack

import concourse.bass as bass
import concourse.mybir as mybir
from concourse.bass_utils import run_bass_kernel_spmd
from concourse.alu_op_type import AluOpType

F32 = mybir.dt.float32
BF16 = mybir.dt.bfloat16
AF = mybir.ActivationFunctionType
AX = mybir.AxisListType

NCORES = 8
B, S, D = 2, 8192, 1024
TOK = B * S
T = TOK // NCORES
NT = T // 128
I_FFN = 2816
MAGIC = float(2 ** 23)
RHC = 1e-6
NIC = 6
ICH = [512] * 5 + [256]
ICO = [0, 512, 1024, 1536, 2048, 2560, 2816]
K2SC = 0.02 / 32.0


def _ts(eng, out, in0, s1, op0, s2=0.0, op1=AluOpType.add):
    return eng.tensor_scalar(out=out, in0=in0, scalar1=s1, scalar2=s2,
                             op0=op0, op1=op1)



def quant_scratch(stack, nc, tag, F):
    NB = F // 16
    def sb(name, shape, dt=F32):
        return stack.enter_context(nc.sbuf_tensor(f"{tag}_{name}", shape, dt))
    return {
        "bm": sb("bm", [128, NB]), "scl": sb("scl", [128, NB]),
        "rs": sb("rs", [128, NB]),
        "a": sb("a", [128, F]), "t": sb("t", [128, F]), "u": sb("u", [128, F]),
        "ql": sb("ql", [128, F], BF16), "u3": sb("u3", [128, F], BF16),
        "v7": sb("v7", [128, F], BF16), "q1": sb("q1", [128, F], BF16),
        "ssc": sb("ssc", [128, F], BF16), "sg": sb("sg", [128, F]),
        "mt": sb("mt", [128, F // 4]),
    }




_SER_OPS = frozenset([
    "tensor_scalar", "tensor_tensor", "scalar_tensor_tensor", "tensor_copy",
    "tensor_add", "tensor_sub", "tensor_mul", "tensor_max", "tensor_reduce",
    "memset", "memzero", "bn_stats", "bn_aggr", "reciprocal", "activation",
    "mul", "add", "copy", "sign", "sqrt", "square", "iota",
    "tensor_tensor_scan", "select", "max_index", "max_with_indices",
    "tensor_relu", "tensor_scalar_add", "tensor_scalar_mul",
    "tensor_scalar_max", "tensor_scalar_min", "tensor_scalar_sub",
    "copy_predicated", "stream_shuffle",
])


SER_FAST = False
_RISKY_OPS = frozenset(["tensor_reduce", "reciprocal", "bn_stats", "bn_aggr"])


class SerEngine:

    def __init__(self, eng, order_sem):
        self._e = eng
        self._os = order_sem
        self._pend = None
        self._pend_risky = False
        self._edge = None
        self._on = 0
        self._cum = {}

    def _pre_wait(self):
        if self._pend is not None:
            if (not SER_FAST) or self._pend_risky:
                self._on += 1
                self._pend.then_inc(self._os, 1)
                self._edge = (self._os, self._on)
            self._pend = None
        if self._edge is not None:
            self._e.wait_ge(self._edge[0], self._edge[1])
            self._edge = None

    def signal(self, sem, n=1):
        assert self._pend is not None, "signal() with no pending op"
        key = id(sem)
        self._cum[key] = self._cum.get(key, 0) + n
        self._pend.then_inc(sem, n)
        if (not SER_FAST) or self._pend_risky:
            self._edge = (sem, self._cum[key])
        self._pend = None
        return self._cum[key]

    def wait_ge(self, sem, v):
        self._e.wait_ge(sem, v)

    def __getattr__(self, name):
        target = getattr(self._e, name)
        if name in _SER_OPS:
            def wrapped(*a, **k):
                self._pre_wait()
                inst = target(*a, **k)
                self._pend = inst
                self._pend_risky = name in _RISKY_OPS
                return inst
            return wrapped
        return target



def emit_quant_pre_v(v, x, sc, F):
    NB = F // 16
    a, t = sc["a"][:, 0:F], sc["t"][:, 0:F]
    _ts(v, t, x, -1.0, AluOpType.mult)
    v.tensor_tensor(out=a, in0=x, in1=t, op=AluOpType.max)
    a3 = a.rearrange("p (nb k) -> p nb k", k=16)
    return v.tensor_reduce(out=sc["bm"][:, 0:NB], in_=a3,
                           axis=mybir.AxisListType.X, op=AluOpType.max)


def emit_quant_pre_g(g, x, sc, F):
    a, sg = sc["a"][:, 0:F], sc["sg"][:, 0:F]
    _ts(g, sg, x, 0.0, AluOpType.is_ge, 2.0, AluOpType.mult)
    _ts(g, sg, sg, 1.0, AluOpType.subtract)
    return g.tensor_tensor(out=a, in0=x, in1=sg, op=AluOpType.mult)


def emit_quant_tree_v(v, sc, F):
    NB = F // 16
    a3 = sc["a"][:, 0:F].rearrange("p (nb k) -> p nb k", k=16)
    u8 = sc["mt"][:, 0:F // 4]
    t8 = sc["t"][:, 0:F // 2].rearrange("p (nb k) -> p nb k", k=8)
    m4 = sc["mt"][:, 0:F // 4].rearrange("p (nb k) -> p nb k", k=4)
    m2 = sc["t"][:, 0:F // 8].rearrange("p (nb k) -> p nb k", k=2)
    v.tensor_tensor(out=t8, in0=a3[:, :, 0:8], in1=a3[:, :, 8:16], op=AluOpType.max)
    v.tensor_tensor(out=m4, in0=t8[:, :, 0:4], in1=t8[:, :, 4:8], op=AluOpType.max)
    v.tensor_tensor(out=m2, in0=m4[:, :, 0:2], in1=m4[:, :, 2:4], op=AluOpType.max)
    v.tensor_tensor(out=sc["bm"][:, 0:NB].unsqueeze(2),
                    in0=m2[:, :, 0:1], in1=m2[:, :, 1:2], op=AluOpType.max)
    return _ts(v, sc["scl"][:, 0:NB], sc["bm"][:, 0:NB], 1.0 / 6.0, AluOpType.mult,
               1e-12, AluOpType.max)


def emit_rs_scalar(s, sc, F):
    NB = F // 16
    s.activation(out=sc["scl"][:, 0:NB], in_=sc["bm"][:, 0:NB], func=AF.Copy,
                 scale=1.0 / 6.0, bias=1e-12)
    s.activation(out=sc["rs"][:, 0:NB], in_=sc["scl"][:, 0:NB], func=AF.Ln)
    return s.activation(out=sc["rs"][:, 0:NB], in_=sc["rs"][:, 0:NB], func=AF.Exp,
                        scale=-1.0)


def emit_quant_post(eng, x, xq, sc, F, is_pool=False):
    P = 128
    NB = F // 16
    a, t, u = sc["a"][:, 0:F], sc["t"][:, 0:F], sc["u"][:, 0:F]
    ql, u3, v7 = sc["ql"][:, 0:F], sc["u3"][:, 0:F], sc["v7"][:, 0:F]
    q1, ssc = sc["q1"][:, 0:F], sc["ssc"][:, 0:F]
    rs_b = sc["rs"][:, 0:NB].unsqueeze(2).broadcast_to((P, NB, 16))
    eng.tensor_tensor(out=a.rearrange("p (nb k) -> p nb k", k=16),
                      in0=a.rearrange("p (nb k) -> p nb k", k=16), in1=rs_b,
                      op=AluOpType.mult)
    _ts(eng, t, a, 2.0, AluOpType.mult, RHC, AluOpType.subtract)
    _ts(eng, t, t, MAGIC, AluOpType.add)
    _ts(eng, t, t, MAGIC, AluOpType.subtract, 0.0, AluOpType.max)
    _ts(eng, ql, t, 4.0, AluOpType.min, 0.5, AluOpType.mult)
    _ts(eng, u, a, 1e-6, AluOpType.subtract, MAGIC, AluOpType.add)
    _ts(eng, u, u, MAGIC, AluOpType.subtract, 2.0, AluOpType.subtract)
    _ts(eng, u3, u, 0.0, AluOpType.max, 2.0, AluOpType.min)
    _ts(eng, v7, a, 5.0, AluOpType.is_gt, 2.0, AluOpType.mult)
    if is_pool:
        sgv = sc["sg"][:, 0:F]
    else:
        _ts(eng, t, x, 0.0, AluOpType.is_ge, 2.0, AluOpType.mult)
        _ts(eng, t, t, 1.0, AluOpType.subtract)
        sgv = t
    scl_b = sc["scl"][:, 0:NB].unsqueeze(2).broadcast_to((P, NB, 16))
    eng.tensor_tensor(out=ssc.rearrange("p (nb k) -> p nb k", k=16),
                      in0=sgv.rearrange("p (nb k) -> p nb k", k=16), in1=scl_b,
                      op=AluOpType.mult)
    eng.tensor_tensor(out=q1, in0=ql, in1=u3, op=AluOpType.add)
    eng.tensor_tensor(out=q1, in0=q1, in1=v7, op=AluOpType.add)
    return eng.tensor_tensor(out=xq, in0=q1, in1=ssc, op=AluOpType.mult)



WGU_TPC = 6


def build_wprep():
    nc = bass.Bass("TRN2", target_bir_lowering=False)
    wgu = nc.dram_tensor("wgu", [WGU_TPC * 128, 1024], F32, kind="ExternalInput").ap()
    wd = nc.dram_tensor("wd", [128, 2816], F32, kind="ExternalInput").ap()
    wgu_q = nc.dram_tensor("wgu_q", [WGU_TPC * 128, 1024], BF16, kind="ExternalOutput").ap()
    wd_q = nc.dram_tensor("wd_q", [128, 2816], BF16, kind="ExternalOutput").ap()
    emit_wprep(nc, wgu, wd, wgu_q, wd_q)
    return nc


def emit_wprep(nc, wgu, wd, wgu_q, wd_q, tag=""):
    with ExitStack() as stack:
        def sb(name, shape, dt=F32):
            return stack.enter_context(nc.sbuf_tensor(tag + name, shape, dt))
        xg = [sb(f"xg{i}", [128, 1024]) for i in range(WGU_TPC)]
        xgq = [sb(f"xgq{i}", [128, 1024], BF16) for i in range(WGU_TPC)]
        xd = sb("xd", [128, 2816])
        xdq = sb("xdq", [128, 2816], BF16)
        scv = quant_scratch(stack, nc, tag + "scv", 2816)
        ld = nc.alloc_semaphore(name=tag + "ld")
        oov = nc.alloc_semaphore(name=tag + "oov")
        oosw = nc.alloc_semaphore(name=tag + "oosw")
        vscl = nc.alloc_semaphore(name=tag + "vscl")
        vrs = nc.alloc_semaphore(name=tag + "vrs")
        qd = nc.alloc_semaphore(name=tag + "qd")
        osem = nc.alloc_semaphore(name=tag + "osem")
        wgu3 = wgu.rearrange("(t p) d -> t p d", p=128)
        wgu_q3 = wgu_q.rearrange("(t p) d -> t p d", p=128)
        tiles = [(xg[i][:], xgq[i][:], 1024) for i in range(WGU_TPC)] + [(xd[:], xdq[:], 2816)]
        with nc.Block() as blk:
            @blk.sync
            def _(sync):
                for i in range(WGU_TPC):
                    sync.dma_start(out=xg[i][:], in_=wgu3[i]).then_inc(ld, 16)
                sync.dma_start(out=xd[:], in_=wd[:]).then_inc(ld, 16)
                for i in range(WGU_TPC + 1):
                    sync.wait_ge(qd, i + 1)
                    if i < WGU_TPC:
                        sync.dma_start(out=wgu_q3[i], in_=xgq[i][:]).then_inc(osem, 16)
                    else:
                        sync.dma_start(out=wd_q[:], in_=xdq[:]).then_inc(osem, 16)
                sync.wait_ge(osem, 16 * (WGU_TPC + 1))
            @blk.vector
            def _(v0):
                v = SerEngine(v0, oov)
                v.wait_ge(ld, 16 * (WGU_TPC + 1))
                for i, (xi, xo, F) in enumerate(tiles):
                    emit_quant_pre_v(v, xi, scv, F)
                    v.signal(vscl)
                    v.wait_ge(vrs, i + 1)
                    emit_quant_post(v, xi, xo, scv, F)
                    v.signal(qd)
            @blk.scalar
            def _(s0):
                s = SerEngine(s0, oosw)
                for i, (xi, xo, F) in enumerate(tiles):
                    s.wait_ge(vscl, i + 1)
                    emit_rs_scalar(s, scv, F)
                    s.signal(vrs)


def run_wprep(w_gate, w_up, w_down):
    wgu = np.concatenate([w_gate, w_up, np.zeros((512, 1024), np.float32)], axis=0)
    per = WGU_TPC * 128
    nc = build_wprep()
    in_maps = [{"wgu": np.ascontiguousarray(wgu[c * per:(c + 1) * per]),
                "wd": np.ascontiguousarray(w_down[c * 128:(c + 1) * 128])}
               for c in range(NCORES)]
    res = run_bass_kernel_spmd(nc, in_maps, core_ids=list(range(NCORES)))
    wgu_q = np.concatenate([np.asarray(res.results[c]["wgu_q"]) for c in range(NCORES)], axis=0)
    wd_q = np.concatenate([np.asarray(res.results[c]["wd_q"]) for c in range(NCORES)], axis=0)
    wgT = np.ascontiguousarray(wgu_q[0:2816].T)
    wuT = np.ascontiguousarray(wgu_q[2816:5632].T)
    wdT = np.ascontiguousarray(wd_q.T)
    return wgT, wuT, wdT



def emit_ln(s, x_tile, sums, sc2, xhat, junk):
    s.activation(out=junk, in_=x_tile, func=AF.Copy, accum_out=sums[:, 0:1])
    s.activation(out=junk, in_=x_tile, func=AF.Square, accum_out=sums[:, 1:2])
    s.mul(out=sc2[:, 0:1], in_=sums[:, 0:1], mul=-1.0 / D)
    s.activation(out=sc2[:, 1:2], in_=sums[:, 1:2], func=AF.Copy,
                 scale=1.0 / D, bias=1e-5)
    s.activation(out=sc2[:, 2:3], in_=sc2[:, 0:1], func=AF.Square)
    s.activation(out=sc2[:, 2:3], in_=sc2[:, 2:3], func=AF.Ln,
                 scale=-1.0, bias=sc2[:, 1:2])
    s.activation(out=sc2[:, 2:3], in_=sc2[:, 2:3], func=AF.Exp, scale=-0.5)
    s.activation(out=sc2[:, 3:4], in_=sc2[:, 0:1], func=AF.Identity,
                 scale=sc2[:, 2:3])
    return s.activation(out=xhat, in_=x_tile, func=AF.Identity,
                        scale=sc2[:, 2:3], bias=sc2[:, 3:4])



def build_ffn0(upto=5):
    nc = bass.Bass("TRN2", target_bir_lowering=False)
    x_in = nc.dram_tensor("x", [T, D], F32, kind="ExternalInput").ap()
    wgT_d = nc.dram_tensor("wgT", [D, I_FFN], BF16, kind="ExternalInput").ap()
    wuT_d = nc.dram_tensor("wuT", [D, I_FFN], BF16, kind="ExternalInput").ap()
    wdT_d = nc.dram_tensor("wdT", [I_FFN, D], BF16, kind="ExternalInput").ap()
    x_out = nc.dram_tensor("x_out", [T, D], F32, kind="ExternalOutput").ap()
    xqdbg = nc.dram_tensor("xqdbg", [T, D], BF16, kind="ExternalOutput").ap()
    hqdbg = nc.dram_tensor("hqdbg", [T, I_FFN], BF16, kind="ExternalOutput").ap()
    xq_dram = nc.dram_tensor("xq_dram", [T, D], BF16, kind="Internal").ap()
    hq_dram = nc.dram_tensor("hq_dram", [T, I_FFN], BF16, kind="Internal").ap()
    emit_ffn0(nc, x_in, x_out, wgT_d, wuT_d, wdT_d, xq_dram, hq_dram,
              xqdbg=xqdbg, hqdbg=hqdbg, upto=upto)
    return nc


def emit_ffn0(nc, x_in, x_out, wgT_d, wuT_d, wdT_d, xq_dram, hq_dram,
              xqdbg=None, hqdbg=None, upto=5, tag="", xh_out=None, xhg=None):
    x3 = x_in.rearrange("(t p) d -> t p d", p=128)
    xo3 = x_out.rearrange("(t p) d -> t p d", p=128)
    xq3 = xq_dram.rearrange("(t p) d -> t p d", p=128)
    hq3 = hq_dram.rearrange("(t p) d -> t p d", p=128)

    with ExitStack() as stack:
        def sb(name, shape, dt=F32):
            return stack.enter_context(nc.sbuf_tensor(tag + name, shape, dt))
        def psum(name, shape, dt=F32):
            return stack.enter_context(nc.psum_tensor(tag + name, shape, dt))
        xt = [sb(f"xt{i}", [128, D]) for i in range(NT)]
        lnsums = sb("lnsums", [128, 2])
        junk = sb("junk", [128, D], BF16)
        sc2 = sb("sc2", [128, 4])
        xhat = [sb(f"xhat{e}", [128, D]) for e in range(2)]
        xot2 = [sb(f"xot2{j}", [128, D]) for j in range(2)]
        xot = [xhat[0], xhat[1], xot2[0], xot2[1]]
        xqv = [sb(f"xqv{e}", [128, D], BF16) for e in range(2)]
        xhb2 = [sb(f"xhb2{e}", [128, D], BF16) for e in range(2)]
        scv = quant_scratch(stack, nc, tag + "scv", 512)
        scg = quant_scratch(stack, nc, tag + "scg", 512)
        xqf = [sb(f"xqf{k}", [128, T], BF16) for k in range(8)]
        hqs = [sb(f"hqs{b}", [128, 512], BF16) for b in range(2)]
        wg_s = [[sb(f"wg_s{b}_{k}", [128, 512], BF16) for k in range(8)] for b in range(2)]
        wu_s = [[sb(f"wu_s{b}_{k}", [128, 512], BF16) for k in range(8)] for b in range(2)]
        wd_s = [sb(f"wd_s{b}", [128, D], BF16) for b in range(2)]
        hg = [sb(f"hg{e}", [128, 512]) for e in range(4)]
        hv = [sb(f"hv{e}", [128, 512]) for e in range(2)]
        hqb = [sb(f"hqb{e}", [128, 512], BF16) for e in range(2)]
        pb = [psum(f"pb{j}", [128, 512]) for j in range(8)]

        sem_names = ["oos", "oov", "oog", "ld", "sxh", "qdv", "qdg", "stv", "stg", "pe", "seg",
                     "seu", "hqdv", "hqdg", "ev", "osem", "osc", "trx",
                     "vscl", "vrs", "gbm", "gscl", "grs", "sxh2", "sthx", "ccc2",
                     "wlA", "wlB", "sthA", "sthB", "trsA", "trsB", "wdlA", "wdlB"]
        sems = {n: nc.alloc_semaphore(name=tag + n) for n in sem_names}
        ld, sxh = sems["ld"], sems["sxh"]
        qdv, qdg = sems["qdv"], sems["qdg"]
        stv, stg = sems["stv"], sems["stg"]
        pe, seg, seu = sems["pe"], sems["seg"], sems["seu"]
        hqdv, hqdg, ev, osem, osc = (sems["hqdv"], sems["hqdg"], sems["ev"],
                                     sems["osem"], sems["osc"])
        trx = sems["trx"]
        vscl, vrs, gbm, gscl, grs = (sems["vscl"], sems["vrs"], sems["gbm"],
                                     sems["gscl"], sems["grs"])
        wl2 = [sems["wlA"], sems["wlB"]]
        SER = {"s": SerEngine(nc.scalar, sems["oos"]),
               "v": SerEngine(nc.vector, sems["oov"]),
               "g": SerEngine(nc.gpsimd, sems["oog"])}
        sth = [sems["sthA"], sems["sthB"]]
        trs2 = [sems["trsA"], sems["trsB"]]
        wdl2 = [sems["wdlA"], sems["wdlB"]]

        with nc.named_scope(tag + "P1"), nc.Block() as blk:
            @blk.sync
            def _(sync):
                for i in range(NT):
                    sync.dma_start(out=xt[i][:], in_=x3[i]).then_inc(ld, 16)
                for i in range(NT):
                    if i % 2 == 0:
                        sync.wait_ge(qdv, 2 * (i // 2 + 1))
                        sync.dma_start(out=xq3[i], in_=xqv[0][:]).then_inc(stv, 16)
                    else:
                        sync.wait_ge(qdg, 2 * ((i + 1) // 2))
                        sync.dma_start(out=xq3[i], in_=xqv[1][:]).then_inc(stg, 16)

            @blk.scalar
            def _(s0):
                s = SER["s"]
                for k in range(8):
                    s.dma_start(out=wg_s[0][k][:],
                                in_=wgT_d[128 * k:128 * (k + 1), ICO[0]:ICO[1]]).then_inc(wl2[0], 16)
                    s.dma_start(out=wu_s[0][k][:],
                                in_=wuT_d[128 * k:128 * (k + 1), ICO[0]:ICO[1]]).then_inc(wl2[0], 16)
                s.wait_ge(ld, 16 * NT)
                for i in range(NT):
                    if i >= 2:
                        if i % 2 == 0:
                            s.wait_ge(qdv, 2 * (i // 2))
                        else:
                            s.wait_ge(qdg, 2 * ((i - 1) // 2))
                    emit_ln(s, xt[i][:], lnsums, sc2, xhat[i % 2][:], junk[:])
                    s.signal(sxh)
                    for h in range(2):
                        s.wait_ge(vscl, 2 * i + h + 1)
                        emit_rs_scalar(s, scv, 512)
                        s.signal(vrs)

            @blk.vector
            def _(v0):
                v = SER["v"]
                for i in range(NT):
                    v.wait_ge(sxh, i + 1)
                    if i >= 2:
                        if i % 2 == 0:
                            v.wait_ge(stv, 16 * (i // 2))
                        else:
                            v.wait_ge(stg, 16 * ((i - 1) // 2))
                    for h in range(2):
                        emit_quant_pre_v(v, xhat[i % 2][:, 512 * h:512 * (h + 1)],
                                         scv, 512)
                        v.signal(vscl)
                        v.wait_ge(vrs, 2 * i + h + 1)
                        emit_quant_post(v, xhat[i % 2][:, 512 * h:512 * (h + 1)],
                                        xqv[i % 2][:, 512 * h:512 * (h + 1)],
                                        scv, 512)
                        v.signal(qdv if i % 2 == 0 else qdg)

        if upto == 1:
            with nc.Block() as blk:
                @blk.gpsimd
                def _(g):
                    g.wait_ge(stv, 16 * 8)
                    g.wait_ge(stg, 16 * 8)
                    g.dma_start(out=xqdbg[:], in_=xq_dram[:]).then_inc(osem, 16)
                    g.wait_ge(osem, 16)
            return

        with nc.named_scope(tag + "P2"), nc.Block() as blk:
            @blk.sync
            def _(sync):
                sync.wait_ge(stv, 16 * 8)
                sync.wait_ge(stg, 16 * 8)
                for k in range(8):
                    sync.dma_start(out=xqf[k][:], in_=xq_dram[:, 128 * k:128 * (k + 1)],
                                   transpose=True).then_inc(trx, 16)

        NS1 = 2 * NT
        with nc.named_scope(tag + "P3"), nc.Block() as blk:
            @blk.tensor
            def _(te):
                te.wait_ge(trx, 16 * 8)
                for i in range(NIC):
                    w = ICH[i]
                    te.wait_ge(wl2[i % 2], 256 * (i // 2 + 1))
                    for t in range(NT):
                        n = i * NT + t
                        bsel = 2 * (n % 4)
                        if n >= 4:
                            te.wait_ge(seg, n - 3)
                            te.wait_ge(seu, n - 3)
                        for k in range(8):
                            te.matmul(out=pb[bsel][:, 0:w],
                                      lhsT=xqf[k][:, 128 * t:128 * (t + 1)],
                                      rhs=wg_s[i % 2][k][:, 0:w],
                                      start=(k == 0), stop=(k == 7))
                        for k in range(8):
                            mm = te.matmul(out=pb[bsel + 1][:, 0:w],
                                           lhsT=xqf[k][:, 128 * t:128 * (t + 1)],
                                           rhs=wu_s[i % 2][k][:, 0:w],
                                           start=(k == 0), stop=(k == 7))
                        mm.then_inc(pe, 1)

            @blk.scalar
            def _(s0):
                s = SER["s"]
                nv = 0
                ng = 0
                for i in range(NIC):
                    if i + 1 < NIC:
                        wnext = ICH[i + 1]
                        if i >= 1:
                            s.wait_ge(pe, i * NT)
                        for k in range(8):
                            s.dma_start(out=wg_s[(i + 1) % 2][k][:, 0:wnext],
                                        in_=wgT_d[128 * k:128 * (k + 1), ICO[i + 1]:ICO[i + 2]]).then_inc(wl2[(i + 1) % 2], 16)
                            s.dma_start(out=wu_s[(i + 1) % 2][k][:, 0:wnext],
                                        in_=wuT_d[128 * k:128 * (k + 1), ICO[i + 1]:ICO[i + 2]]).then_inc(wl2[(i + 1) % 2], 16)
                    for t in range(NT):
                        n = i * NT + t
                        w = ICH[i]
                        bsel = 2 * (n % 4)
                        s.wait_ge(pe, n + 1)
                        if n >= 4:
                            s.wait_ge(seu, n - 3)
                        s.activation(out=hg[n % 4][:, 0:w], in_=pb[bsel][:, 0:w],
                                     func=AF.Silu)
                        s.signal(seg)
                        s.wait_ge(vscl, NS1 + n + 1)
                        emit_rs_scalar(s, scv, w)
                        s.signal(vrs)

            @blk.vector
            def _(v0):
                v = SER["v"]
                nv = 0
                ng = 0
                for i in range(NIC):
                    for t in range(NT):
                        n = i * NT + t
                        w = ICH[i]
                        p2 = t % 2
                        v.wait_ge(seg, n + 1)
                        if p2 == 0:
                            nv += 1
                            if nv >= 2:
                                v.wait_ge(sth[0], 16 * (nv - 1))
                        else:
                            ng += 1
                            if ng >= 2:
                                v.wait_ge(sth[1], 16 * (ng - 1))
                        v.tensor_tensor(out=hv[p2][:, 0:w], in0=hg[n % 4][:, 0:w],
                                        in1=pb[2 * (n % 4) + 1][:, 0:w],
                                        op=AluOpType.mult)
                        v.signal(seu)
                        emit_quant_pre_v(v, hv[p2][:, 0:w], scv, w)
                        v.signal(vscl)
                        v.wait_ge(vrs, NS1 + n + 1)
                        emit_quant_post(v, hv[p2][:, 0:w], hqb[p2][:, 0:w], scv, w)
                        v.signal(hqdv if p2 == 0 else hqdg)

            @blk.sync
            def _(sync):
                nev = 0
                nod = 0
                for i in range(NIC):
                    for t in range(NT):
                        w = ICH[i]
                        if t % 2 == 0:
                            nev += 1
                            sync.wait_ge(hqdv, nev)
                            sync.dma_start(out=hq3[t][:, ICO[i]:ICO[i + 1]],
                                           in_=hqb[0][:, 0:w]).then_inc(sth[0], 16)
                        else:
                            nod += 1
                            sync.wait_ge(hqdg, nod)
                            sync.dma_start(out=hq3[t][:, ICO[i]:ICO[i + 1]],
                                           in_=hqb[1][:, 0:w]).then_inc(sth[1], 16)

        if upto == 3:
            with nc.Block() as blk:
                @blk.gpsimd
                def _(g):
                    g.wait_ge(sth[0], 16 * 48)
                    g.wait_ge(sth[1], 16 * 48)
                    g.dma_start(out=hqdbg[:], in_=hq_dram[:]).then_inc(osem, 16)
                    g.wait_ge(osem, 16)
            return

        with nc.named_scope(tag + "P5"), nc.Block() as blk:
            @blk.sync
            def _(sync):
                sync.wait_ge(sth[0], 16 * 48)
                sync.wait_ge(sth[1], 16 * 48)
                for tg in range(4):
                    for i2 in range(22):
                        m = tg * 22 + i2
                        if m >= 2:
                            sync.wait_ge(pe, NIC * NT + (m - 1))
                        sync.dma_start(out=hqs[m % 2][:],
                                       in_=hq_dram[512 * tg:512 * (tg + 1),
                                                   128 * i2:128 * (i2 + 1)],
                                       transpose=True).then_inc(trs2[m % 2], 16)

            @blk.scalar
            def _(s0):
                s = SER["s"]
                for tg in range(4):
                    for i2 in range(22):
                        m = tg * 22 + i2
                        if m >= 2:
                            s.wait_ge(pe, NIC * NT + (m - 1))
                        s.dma_start(out=wd_s[m % 2][:],
                                    in_=wdT_d[128 * i2:128 * (i2 + 1), :]).then_inc(wdl2[m % 2], 16)
                    if xh_out is not None:
                        s.wait_ge(sems["ev"], tg + 1)
                        for j in range(4):
                            t4 = 4 * tg + j
                            if t4 >= 2:
                                s.wait_ge(sems["sthx"], 16 * (t4 - 1))
                            emit_ln(s, xot[j][:], lnsums, sc2,
                                    xhb2[t4 % 2][:], junk[:])
                            s.signal(sems["sxh2"])

            @blk.tensor
            def _(te):
                for tg in range(4):
                    if tg > 0:
                        te.wait_ge(ev, tg)
                    for i2 in range(22):
                        m = tg * 22 + i2
                        te.wait_ge(wdl2[m % 2], 16 * (m // 2 + 1))
                        te.wait_ge(trs2[m % 2], 16 * (m // 2 + 1))
                        mm = None
                        for j in range(4):
                            for nblk in range(2):
                                mm = te.matmul(
                                    out=pb[2 * j + nblk][:],
                                    lhsT=hqs[m % 2][:, 128 * j:128 * (j + 1)],
                                    rhs=wd_s[m % 2][:, 512 * nblk:512 * (nblk + 1)],
                                    start=(i2 == 0), stop=(i2 == 21))
                        mm.then_inc(pe, 1)

            @blk.vector
            def _(v0):
                v = SER["v"]
                for tg in range(4):
                    v.wait_ge(pe, NIC * NT + 22 * (tg + 1))
                    if tg >= 1:
                        v.wait_ge(osc, tg)
                    for j in range(4):
                        t = tg * 4 + j
                        for nblk in range(2):
                            v.scalar_tensor_tensor(
                                out=xot[j][:, 512 * nblk:512 * (nblk + 1)],
                                in0=pb[2 * j + nblk][:], scalar=0.5,
                                in1=xt[t][:, 512 * nblk:512 * (nblk + 1)],
                                op0=AluOpType.mult, op1=AluOpType.add)
                    v.signal(ev)

            @blk.gpsimd
            def _(g):
                xh3o = (xh_out.rearrange("(t p) d -> t p d", p=128)
                        if xh_out is not None else None)
                for tg in range(4):
                    g.wait_ge(ev, tg + 1)
                    for j in range(4):
                        g.dma_start(out=xo3[tg * 4 + j], in_=xot[j][:]).then_inc(osem, 16)
                    if xh_out is not None:
                        for j in range(4):
                            t4 = 4 * tg + j
                            g.wait_ge(sems["sxh2"], t4 + 1)
                            g.dma_start(out=xh3o[t4], in_=xhb2[t4 % 2][:]) \
                                .then_inc(sems["sthx"], 16)
                    g.wait_ge(osem, 16 * 4 * (tg + 1))
                    if xh_out is not None:
                        g.wait_ge(sems["sthx"], 16 * 4 * (tg + 1))
                    g.sem_inc(osc, 1)
        if xh_out is not None:
            with nc.Block() as blk:
                @blk.gpsimd
                def _(g0):
                    g0.collective_compute(
                        "AllGather", mybir.AluOpType.bypass,
                        replica_groups=[list(range(NCORES))],
                        ins=[xh_out.opt()], outs=[xhg.opt()],
                    ).then_inc(sems["ccc2"])
                    g0.wait_ge(sems["ccc2"], 1)


def run_ffn0(nc_cache, x_flat, wgT, wuT, wdT):
    if "ffn0" not in nc_cache:
        nc_cache["ffn0"] = build_ffn0()
    nc = nc_cache["ffn0"]
    in_maps = [{"x": np.ascontiguousarray(x_flat[c * T:(c + 1) * T]),
                "wgT": wgT, "wuT": wuT, "wdT": wdT} for c in range(NCORES)]
    res = run_bass_kernel_spmd(nc, in_maps, core_ids=list(range(NCORES)))
    return np.concatenate([np.asarray(res.results[c]["x_out"]) for c in range(NCORES)], axis=0)

def build_moe(rg_cores=NCORES):
    nc = bass.Bass("TRN2", target_bir_lowering=False)
    x_in = nc.dram_tensor("x", [T, D], F32, kind="ExternalInput").ap()
    wg_d = nc.dram_tensor("wgate", [D, 8], F32, kind="ExternalInput").ap()
    we_d = nc.dram_tensor("wexp", [D, D], F32, kind="ExternalInput").ap()
    be_d = nc.dram_tensor("bexp", [1, D], F32, kind="ExternalInput").ap()
    mask_d = nc.dram_tensor("coremask", [8, 8], F32, kind="ExternalInput").ap()
    x_out = nc.dram_tensor("x_out", [T, D], F32, kind="ExternalOutput").ap()
    xh_dram = nc.dram_tensor("xh_dram", [T, D], BF16, kind="Internal").ap()
    cin = nc.dram_tensor("cin", [8, 1], F32).ap()
    cout = nc.dram_tensor("cout", [64, 1], F32).ap()
    emit_moe(nc, x_in, x_out, wg_d, we_d, be_d, mask_d, xh_dram, cin, cout,
             rg_cores=rg_cores)
    return nc


def emit_moe(nc, x_in, x_out, wg_d, we_d, be_d, mask_d, xh_dram, cin, cout,
             rg_cores=NCORES, tag="", xh_out=None, xhg=None):
    x3 = x_in.rearrange("(t p) d -> t p d", p=128)
    xo3 = x_out.rearrange("(t p) d -> t p d", p=128)
    xh3 = xh_dram.rearrange("(t p) d -> t p d", p=128)

    with ExitStack() as stack:
        def sb(name, shape, dt=F32):
            return stack.enter_context(nc.sbuf_tensor(tag + name, shape, dt))
        def psum(name, shape, dt=F32):
            return stack.enter_context(nc.psum_tensor(tag + name, shape, dt))
        xt = [sb(f"xt{i}", [128, D]) for i in range(NT)]
        lnsums = sb("lnsums", [128, 2]); sc2 = sb("sc2", [128, 4])
        junk = sb("junk", [128, D], BF16)
        xhb = [sb(f"xhb{e}", [128, D], BF16) for e in range(2)]
        xhf = [sb(f"xhf{k}", [128, T], BF16) for k in range(8)]
        wg8 = [sb(f"wg8{k}", [128, 8], BF16) for k in range(8)]
        wexp = [sb(f"wexp{k}", [128, D], BF16) for k in range(8)]
        bb = sb("bb", [128, D])
        bsb = sb("bsb", [1, D])
        colidx = sb("colidx", [128, 128]); rowidx = sb("rowidx", [128, 128])
        identb = sb("identb", [128, 128], BF16)
        onesr = sb("onesr", [1, 128])
        ones8 = sb("ones8", [8, 1])
        lt = sb("lt", [128, 8]); mtree = sb("mtree", [128, 8])
        nm = sb("nm", [128, 1]); et = sb("et", [128, 8]); se = sb("se", [128, 1])
        ohr = sb("ohr", [128, 8]); p1 = sb("p1", [128, 8]); p2 = sb("p2", [128, 8])
        p4 = sb("p4", [128, 8]); msk = sb("msk", [128, 8])
        ohb16 = sb("ohb16", [128, 8], BF16)
        gv = sb("gv", [128, NT])
        ohm = sb("ohm", [8, T]); counts = sb("counts", [8, T])
        zed = sb("zed", [8, T])
        tot = sb("tot", [8, 1]); cm = sb("cm", [8, 8]); cmm = sb("cmm", [8, 8])
        car1 = sb("car1", [8, 4]); car2 = sb("car2", [8, 2]); carry = sb("carry", [8, 1])
        tmpc = sb("tmpc", [8, T]); km = sb("km", [8, T])
        chalf = sb("chalf", [128, NT])
        uev = [sb(f"uev{e}", [128, 512]) for e in range(2)]
        xot = [sb(f"xot{j}", [128, D]) for j in range(2)]
        xhb2m = [sb(f"xhb2m{e}", [128, D], BF16) for e in range(2)]
        masksb = sb("masksb", [8, 8])
        p_l = psum("p_l", [128, 8])
        p_t = psum("p_t", [8, 128], BF16)
        p_k = psum("p_k", [128, NT])
        p_y = [psum(f"p_y{j}", [128, 512]) for j in range(4)]

        sem_names = ["oos", "oov", "oog", "ld", "wld", "sxh", "stv", "stg",
                     "pe", "pet", "pek", "peb", "pey", "vnm", "sgin", "voh",
                     "vohm", "vscan", "cc", "gkm", "vc", "ev", "osem", "osc",
                     "trx", "lnb", "wldc", "ccc", "sxh2", "sthx", "ccc2"]
        sems = {n: nc.alloc_semaphore(name=tag + n) for n in sem_names}
        S = sems
        SER = {"s": SerEngine(nc.scalar, S["oos"]),
               "v": SerEngine(nc.vector, S["oov"]),
               "g": SerEngine(nc.gpsimd, S["oog"])}

        with nc.Block() as blk:
            @blk.sync
            def _(sync):
                for i in range(NT):
                    sync.dma_start(out=xt[i][:], in_=x3[i]).then_inc(sems["ld"], 16)
                sync.dma_start(out=bsb[:], in_=be_d[:]).then_inc(S["wldc"], 16)
                sync.dma_start(out=masksb[:], in_=mask_d[:]).then_inc(S["wldc"], 16)
            @blk.gpsimd
            def _(g0):
                g = SER["g"]
                for k in range(8):
                    g.dma_start(out=wg8[k][:], in_=wg_d[128 * k:128 * (k + 1), :]) \
                        .then_inc(sems["wld"], 16)
                    g.dma_start(out=wexp[k][:], in_=we_d[128 * k:128 * (k + 1), :]) \
                        .then_inc(sems["wld"], 16)
                g.iota(colidx[:], [[1, 128]], base=0, channel_multiplier=0,
                       allow_small_or_imprecise_dtypes=True)
                g.iota(rowidx[:], [[0, 128]], base=0, channel_multiplier=1,
                       allow_small_or_imprecise_dtypes=True)
                g.signal(S["lnb"])
            @blk.vector
            def _(v0):
                v = SER["v"]
                v.memset(onesr[:], 1.0)
                v.memset(ones8[:], 1.0)
                v.memset(zed[:], 0.0)
                v.wait_ge(S["lnb"], 1)
                v.tensor_tensor(out=identb[:], in0=colidx[:], in1=rowidx[:],
                                op=AluOpType.is_equal)
                v.signal(S["vnm"])
                v.wait_ge(S["peb"], 1)
                v.tensor_copy(out=bb[:, 0:512], in_=p_y[0][:])
                v.tensor_copy(out=bb[:, 512:1024], in_=p_y[1][:])
                v.signal(S["peb"])
            @blk.tensor
            def _(te):
                te.wait_ge(S["wldc"], 32)
                te.wait_ge(S["vnm"], 1)
                te.matmul(out=p_y[0][:], lhsT=onesr[:], rhs=bsb[:, 0:512],
                          start=True, stop=True)
                te.matmul(out=p_y[1][:], lhsT=onesr[:], rhs=bsb[:, 512:1024],
                          start=True, stop=True).then_inc(S["peb"], 1)

        with nc.Block() as blk:
            @blk.scalar
            def _(s0):
                s = SER["s"]
                s.wait_ge(sems["ld"], 16 * NT)
                for i in range(NT):
                    if i >= 2:
                        if i % 2 == 0:
                            s.wait_ge(S["stv"], 16 * (i // 2))
                        else:
                            s.wait_ge(S["stg"], 16 * ((i - 1) // 2))
                    emit_ln(s, xt[i][:], lnsums, sc2, xhb[i % 2][:], junk[:])
                    s.signal(S["sxh"])
            @blk.sync
            def _(sync):
                for i in range(NT):
                    sync.wait_ge(S["sxh"], i + 1)
                    if i % 2 == 0:
                        sync.dma_start(out=xh3[i], in_=xhb[0][:]).then_inc(S["stv"], 16)
                    else:
                        sync.dma_start(out=xh3[i], in_=xhb[1][:]).then_inc(S["stg"], 16)

        with nc.Block() as blk:
            @blk.sync
            def _(sync):
                sync.wait_ge(S["stv"], 16 * 8)
                sync.wait_ge(S["stg"], 16 * 8)
                for k in range(8):
                    sync.dma_start(out=xhf[k][:], in_=xh_dram[:, 128 * k:128 * (k + 1)],
                                   transpose=True).then_inc(S["trx"], 16)

        with nc.Block() as blk:
            @blk.tensor
            def _(te):
                te.wait_ge(S["trx"], 16 * 8)
                te.wait_ge(sems["wld"], 16 * NT)
                for t in range(NT):
                    if t >= 1:
                        te.wait_ge(S["vnm"], 1 + 2 * t)
                        te.wait_ge(S["vohm"], t)
                    for k in range(8):
                        mm = te.matmul(out=p_l[:], lhsT=xhf[k][:, 128 * t:128 * (t + 1)],
                                       rhs=wg8[k][:], start=(k == 0), stop=(k == 7))
                    mm.then_inc(S["pe"], 1)
                    te.wait_ge(S["voh"], t + 1)
                    te.transpose(out=p_t[:], in_=ohb16[:], identity=identb[:]) \
                        .then_inc(S["pet"], 1)

            @blk.vector
            def _(v0):
                v = SER["v"]
                for t in range(NT):
                    v.wait_ge(S["pe"], t + 1)
                    if t >= 1:
                        v.wait_ge(S["sgin"], t)
                        v.wait_ge(S["pet"], t)
                    v.tensor_copy(out=lt[:], in_=p_l[:])
                    v.signal(S["vnm"])
                    v.tensor_tensor(out=mtree[:, 0:4], in0=lt[:, 0:4], in1=lt[:, 4:8],
                                    op=AluOpType.max)
                    v.tensor_tensor(out=mtree[:, 4:6], in0=mtree[:, 0:2],
                                    in1=mtree[:, 2:4], op=AluOpType.max)
                    v.tensor_tensor(out=mtree[:, 6:7], in0=mtree[:, 4:5],
                                    in1=mtree[:, 5:6], op=AluOpType.max)
                    _ts(v, nm[:], mtree[:, 6:7], -1.0, AluOpType.mult)
                    v.signal(S["vnm"])
                    mb = mtree[:, 6:7].broadcast_to((128, 8))
                    v.tensor_tensor(out=ohr[:], in0=lt[:], in1=mb, op=AluOpType.is_equal)
                    v.tensor_copy(out=p1[:], in_=ohr[:])
                    v.tensor_tensor(out=p1[:, 1:8], in0=ohr[:, 1:8], in1=ohr[:, 0:7],
                                    op=AluOpType.add)
                    v.tensor_copy(out=p2[:], in_=p1[:])
                    v.tensor_tensor(out=p2[:, 2:8], in0=p1[:, 2:8], in1=p1[:, 0:6],
                                    op=AluOpType.add)
                    v.tensor_copy(out=p4[:], in_=p2[:])
                    v.tensor_tensor(out=p4[:, 4:8], in0=p2[:, 4:8], in1=p2[:, 0:4],
                                    op=AluOpType.add)
                    _ts(v, msk[:], p4[:], 1.0, AluOpType.is_equal)
                    v.tensor_tensor(out=ohb16[:], in0=ohr[:], in1=msk[:],
                                    op=AluOpType.mult)
                    v.signal(S["voh"])
                    v.wait_ge(S["pet"], t + 1)
                    v.tensor_copy(out=ohm[:, 128 * t:128 * (t + 1)], in_=p_t[:])
                    v.signal(S["vohm"])

            @blk.scalar
            def _(s0):
                s = SER["s"]
                for t in range(NT):
                    s.wait_ge(S["vnm"], 2 * t + 3)
                    s.activation(out=et[:], in_=lt[:], func=AF.Exp, bias=nm[:],
                                 accum_out=se[:])
                    s.activation(out=se[:], in_=se[:], func=AF.Ln)
                    s.activation(out=gv[:, t:t + 1], in_=se[:], func=AF.Exp, scale=-1.0)
                    s.signal(S["sgin"])

        with nc.Block() as blk:
            @blk.vector
            def _(v0):
                v = SER["v"]
                v.wait_ge(S["vohm"], NT)
                v.tensor_tensor_scan(out=counts[:], data0=ohm[:], data1=zed[:],
                                     initial=0.0, op0=AluOpType.add, op1=AluOpType.add)
                v.signal(S["vscan"])
                v.tensor_copy(out=tot[:], in_=counts[:, T - 1:T])
                v.signal(S["vscan"])
            @blk.gpsimd
            def _(g0):
                g = SER["g"]
                g.wait_ge(S["vscan"], 2)
                g.dma_start(out=cin[:], in_=tot[:]).then_inc(S["cc"], 16)
                g.wait_ge(S["cc"], 16)
                g.collective_compute(
                    "AllGather", mybir.AluOpType.bypass,
                    replica_groups=[list(range(rg_cores))],
                    ins=[cin.opt()], outs=[cout[0:8 * rg_cores, :].opt()],
                ).then_inc(S["ccc"], 1)
                g.wait_ge(S["ccc"], 1)
                g.memset(cm[:], 0.0)
                g.signal(S["lnb"])
                g.wait_ge(S["lnb"], 2)
                g.dma_start(out=cm[:, 0:rg_cores],
                            in_=cout[0:8 * rg_cores, :].rearrange("(c e) one -> e (c one)", e=8),
                            allow_slow_non_contiguous=True).then_inc(S["cc"], 16)
                g.wait_ge(S["cc"], 32)
                g.wait_ge(S["wldc"], 32)
                g.tensor_tensor(out=cmm[:], in0=cm[:], in1=masksb[:], op=AluOpType.mult)
                g.tensor_tensor(out=car1[:], in0=cmm[:, 0:4], in1=cmm[:, 4:8],
                                op=AluOpType.add)
                g.tensor_tensor(out=car2[:], in0=car1[:, 0:2], in1=car1[:, 2:4],
                                op=AluOpType.add)
                g.tensor_tensor(out=carry[:], in0=car2[:, 0:1], in1=car2[:, 1:2],
                                op=AluOpType.add)
                _ts(g, tmpc[:], counts[:], carry[:], AluOpType.add, 100.0,
                    AluOpType.is_le)
                g.tensor_tensor(out=km[:], in0=tmpc[:], in1=ohm[:], op=AluOpType.mult)
                g.signal(S["gkm"])
            @blk.tensor
            def _(te):
                te.wait_ge(S["gkm"], 1)
                mm = None
                for t in range(NT):
                    mm = te.matmul(out=p_k[:, t:t + 1], lhsT=km[:, 128 * t:128 * (t + 1)],
                                   rhs=ones8[:], start=True, stop=True)
                mm.then_inc(S["pek"], 1)
            @blk.vector
            def _(v0):
                v = SER["v"]
                v.wait_ge(S["pek"], 1)
                v.wait_ge(S["sgin"], NT)
                v.scalar_tensor_tensor(out=chalf[:], in0=p_k[:], scalar=0.5,
                                       in1=gv[:], op0=AluOpType.mult,
                                       op1=AluOpType.mult)
                v.signal(S["vc"])

        with nc.Block() as blk:
            @blk.tensor
            def _(te):
                te.wait_ge(S["vc"], 1)
                te.wait_ge(S["peb"], 2)
                for t in range(NT):
                    if t >= 2:
                        te.wait_ge(S["ev"], 2 * (t - 1))
                    for nblk in range(2):
                        j = 2 * (t % 2) + nblk
                        for k in range(8):
                            mm = te.matmul(out=p_y[j][:],
                                           lhsT=xhf[k][:, 128 * t:128 * (t + 1)],
                                           rhs=wexp[k][:, 512 * nblk:512 * (nblk + 1)],
                                           start=(k == 0), stop=(k == 7))
                        mm.then_inc(S["pey"], 1)
            @blk.vector
            def _(v0):
                v = SER["v"]
                for t in range(NT):
                    e = t % 2
                    if t >= 2:
                        v.wait_ge(S["osc"], t - 1)
                    for nblk in range(2):
                        j = 2 * e + nblk
                        v.wait_ge(S["pey"], 2 * t + nblk + 1)
                        v.tensor_tensor(out=uev[e][:], in0=p_y[j][:],
                                        in1=bb[:, 512 * nblk:512 * (nblk + 1)],
                                        op=AluOpType.add)
                        v.scalar_tensor_tensor(
                            out=xot[e][:, 512 * nblk:512 * (nblk + 1)],
                            in0=uev[e][:], scalar=chalf[:, t:t + 1],
                            in1=xt[t][:, 512 * nblk:512 * (nblk + 1)],
                            op0=AluOpType.mult, op1=AluOpType.add)
                        v.signal(S["ev"])
            @blk.scalar
            def _(s0):
                if xh_out is None:
                    return
                s = SER["s"]
                for t in range(NT):
                    s.wait_ge(S["ev"], 2 * t + 2)
                    if t >= 2:
                        s.wait_ge(S["sthx"], 16 * (t - 1))
                    emit_ln(s, xot[t % 2][:], lnsums, sc2, xhb2m[t % 2][:],
                            junk[:])
                    s.signal(S["sxh2"])
            @blk.gpsimd
            def _(g0):
                g = SER["g"]
                xh3o = (xh_out.rearrange("(t p) d -> t p d", p=128)
                        if xh_out is not None else None)
                for t in range(NT):
                    g.wait_ge(S["ev"], 2 * t + 2)
                    g.dma_start(out=xo3[t], in_=xot[t % 2][:]).then_inc(S["osem"], 16)
                    if xh_out is not None:
                        g.wait_ge(S["sxh2"], t + 1)
                        g.dma_start(out=xh3o[t], in_=xhb2m[t % 2][:]) \
                            .then_inc(S["sthx"], 16)
                        g.wait_ge(S["sthx"], 16 * (t + 1))
                    g.wait_ge(S["osem"], 16 * (t + 1))
                    g.sem_inc(S["osc"], 1)
        if xh_out is not None:
            with nc.Block() as blk:
                @blk.gpsimd
                def _(g0):
                    g0.collective_compute(
                        "AllGather", mybir.AluOpType.bypass,
                        replica_groups=[list(range(NCORES))],
                        ins=[xh_out.opt()], outs=[xhg.opt()],
                    ).then_inc(S["ccc2"])
                    g0.wait_ge(S["ccc2"], 1)


def run_moe(nc_cache, x_flat, w_gate, w_expert, b_expert):
    if "moe" not in nc_cache:
        nc_cache["moe"] = build_moe()
    nc = nc_cache["moe"]
    in_maps = []
    for c in range(NCORES):
        mask = np.tile((np.arange(8) < c).astype(np.float32)[None, :], (8, 1))
        in_maps.append({"x": np.ascontiguousarray(x_flat[c * T:(c + 1) * T]),
                        "wgate": w_gate, "wexp": w_expert,
                        "bexp": b_expert.reshape(1, D), "coremask": mask})
    res = run_bass_kernel_spmd(nc, in_maps, core_ids=list(range(NCORES)))
    return np.concatenate([np.asarray(res.results[c]["x_out"]) for c in range(NCORES)], axis=0)

def _emit_ln_stream(nc, stack, S, SER, x3, xh3, NTILES=NT, tag=""):
    lnsums = stack.enter_context(nc.sbuf_tensor(tag + "lnsums2", [128, 2], F32))
    sc2 = stack.enter_context(nc.sbuf_tensor(tag + "sc22", [128, 4], F32))
    junk = stack.enter_context(nc.sbuf_tensor(tag + "junk2", [128, D], BF16))
    xin = [stack.enter_context(nc.sbuf_tensor(f"{tag}xin{e}", [128, D], F32))
           for e in range(2)]
    xhb = [stack.enter_context(nc.sbuf_tensor(f"{tag}xhb{e}", [128, D], BF16))
           for e in range(2)]
    with nc.Block() as blk:
        @blk.sync
        def _(sync):
            for i in range(NTILES):
                if i >= 2:
                    sync.wait_ge(S["sxh"], i - 1)
                sync.dma_start(out=xin[i % 2][:], in_=x3[i]).then_inc(
                    S["ldv" if i % 2 == 0 else "ldg"], 16)
        @blk.gpsimd
        def _(g0):
            g = SER["g"]
            for i in range(NTILES):
                g.wait_ge(S["sxh"], i + 1)
                if i % 2 == 0:
                    g.dma_start(out=xh3[i], in_=xhb[0][:]).then_inc(S["stv"], 16)
                else:
                    g.dma_start(out=xh3[i], in_=xhb[1][:]).then_inc(S["stg"], 16)
        @blk.scalar
        def _(s0):
            s = SER["s"]
            for i in range(NTILES):
                s.wait_ge(S["ldv" if i % 2 == 0 else "ldg"], 16 * (i // 2 + 1))
                if i >= 2:
                    if i % 2 == 0:
                        s.wait_ge(S["stv"], 16 * (i // 2))
                    else:
                        s.wait_ge(S["stg"], 16 * ((i - 1) // 2))
                emit_ln(s, xin[i % 2][:], lnsums, sc2, xhb[i % 2][:], junk[:])
                s.signal(S["sxh"])


def _emit_ln_to_dram(nc, stack, S, SER, xt, x3, xh3, NTILES=NT):
    lnsums = stack.enter_context(nc.sbuf_tensor("lnsums", [128, 2], F32))
    sc2 = stack.enter_context(nc.sbuf_tensor("sc2", [128, 4], F32))
    junk = stack.enter_context(nc.sbuf_tensor("junk", [128, D], BF16))
    xhb = [stack.enter_context(nc.sbuf_tensor(f"xhb{e}", [128, D], BF16))
           for e in range(2)]
    with nc.Block() as blk:
        @blk.scalar
        def _(s0):
            s = SER["s"]
            s.wait_ge(S["ld"], 16 * NTILES)
            for i in range(NTILES):
                if i >= 2:
                    if i % 2 == 0:
                        s.wait_ge(S["stv"], 16 * (i // 2))
                    else:
                        s.wait_ge(S["stg"], 16 * ((i - 1) // 2))
                emit_ln(s, xt[i][:], lnsums, sc2, xhb[i % 2][:], junk[:])
                s.signal(S["sxh"])
        @blk.sync
        def _(sync):
            for i in range(NTILES):
                sync.wait_ge(S["sxh"], i + 1)
                if i % 2 == 0:
                    sync.dma_start(out=xh3[i], in_=xhb[0][:]).then_inc(S["stv"], 16)
                else:
                    sync.dma_start(out=xh3[i], in_=xhb[1][:]).then_inc(S["stg"], 16)


def build_att1_a():
    nc = bass.Bass("TRN2", target_bir_lowering=False)
    x_in = nc.dram_tensor("x", [T, D], F32, kind="ExternalInput").ap()
    wqT_d = nc.dram_tensor("wqT", [D, D], F32, kind="ExternalInput").ap()
    wkT_d = nc.dram_tensor("wkT", [D, D], F32, kind="ExternalInput").ap()
    wvT_d = nc.dram_tensor("wvT", [D, D], F32, kind="ExternalInput").ap()
    bq_d = nc.dram_tensor("bq", [1, D], F32, kind="ExternalInput").ap()
    bk_d = nc.dram_tensor("bk", [1, D], F32, kind="ExternalInput").ap()
    bv_d = nc.dram_tensor("bv", [1, D], F32, kind="ExternalInput").ap()
    poff_d = nc.dram_tensor("poff", [1, 1], F32, kind="ExternalInput").ap()
    q_out = nc.dram_tensor("q_out", [T, D], BF16, kind="ExternalOutput").ap()
    k_out = nc.dram_tensor("k_out", [T, D], BF16, kind="ExternalOutput").ap()
    v_out = nc.dram_tensor("v_out", [T, D], BF16, kind="ExternalOutput").ap()
    xh_dram = nc.dram_tensor("xh_dram", [T, D], BF16, kind="Internal").ap()

    x3 = x_in.rearrange("(t p) d -> t p d", p=128)
    xh3 = xh_dram.rearrange("(t p) d -> t p d", p=128)
    q3 = q_out.rearrange("(t p) d -> t p d", p=128)
    k3 = k_out.rearrange("(t p) d -> t p d", p=128)
    v3 = v_out.rearrange("(t p) d -> t p d", p=128)

    with ExitStack() as stack:
        def sb(name, shape, dt=F32):
            return stack.enter_context(nc.sbuf_tensor(name, shape, dt))
        def psum(name, shape, dt=F32):
            return stack.enter_context(nc.psum_tensor(name, shape, dt))
        xhf = [sb(f"xhf{k}", [128, T], BF16) for k in range(8)]
        wq_s = [sb(f"wq_s{k}", [128, D], BF16) for k in range(8)]
        wk_s = [sb(f"wk_s{k}", [128, D], BF16) for k in range(8)]
        wv_s = [sb(f"wv_s{k}", [128, D], BF16) for k in range(8)]
        bqs = sb("bqs", [1, D]); bks = sb("bks", [1, D]); bvs = sb("bvs", [1, D])
        bb = {0: sb("bbq", [128, D]), 1: sb("bbk", [128, D]), 2: sb("bbv", [128, D])}
        onesr = sb("onesr", [1, 128])
        poffs = sb("poffs", [1, 1])
        iotar = sb("iotar", [1, 128])
        iotaj = sb("iotaj", [1, 32])
        invf = sb("invf", [1, 32]); invf64 = sb("invf64", [1, 64])
        posr = sb("posr", [1, 128])
        angs = sb("angs", [128, 64])
        rr = [sb(f"rr{e}", [128, 64]) for e in range(2)]
        sint = sb("sint", [128, 64 * NT]); cost = sb("cost", [128, 64 * NT])
        prj = sb("prj", [128, D])
        rot = sb("rot", [128, D]); tmpr = sb("tmpr", [128, D])
        outq = [sb(f"outq{e}", [128, D], BF16) for e in range(2)]
        outk = [sb(f"outk{e}", [128, D], BF16) for e in range(2)]
        outv = [sb(f"outv{e}", [128, D], BF16) for e in range(2)]
        p_q = [psum(f"p_q{j}", [128, 512]) for j in range(2)]
        p_k = [psum(f"p_k{j}", [128, 512]) for j in range(2)]
        p_v = [psum(f"p_v{j}", [128, 512]) for j in range(2)]
        p_ang = psum("p_ang", [128, 64])

        sem_names = ["oos", "oov", "oog", "ldv", "ldg", "wld", "cld", "sxh",
                     "stv", "stg", "trx", "peq", "pek", "pev", "pang", "pbias",
                     "vcb", "vq", "vk", "vv", "ostq0", "ostq1", "ostk0", "ostk1",
                     "ostv0", "ostv1", "vps", "ssin", "gcon"]
        S = {n: stack.enter_context(nc.semaphore(n)) for n in sem_names}
        SER = {"s": SerEngine(nc.scalar, S["oos"]),
               "v": SerEngine(nc.vector, S["oov"]),
               "g": SerEngine(nc.gpsimd, S["oog"])}

        with nc.Block() as blk:
            @blk.sync
            def _(sync):
                sync.dma_start(out=bqs[:], in_=bq_d[:]).then_inc(S["cld"], 16)
                sync.dma_start(out=bks[:], in_=bk_d[:]).then_inc(S["cld"], 16)
                sync.dma_start(out=bvs[:], in_=bv_d[:]).then_inc(S["cld"], 16)
                sync.dma_start(out=poffs[:], in_=poff_d[:]).then_inc(S["cld"], 16)
            @blk.gpsimd
            def _(g0):
                g = SER["g"]
                for k in range(8):
                    g.dma_start(out=wq_s[k][:], in_=wqT_d[128 * k:128 * (k + 1), :]) \
                        .then_inc(S["wld"], 16)
                    g.dma_start(out=wk_s[k][:], in_=wkT_d[128 * k:128 * (k + 1), :]) \
                        .then_inc(S["wld"], 16)
                    g.dma_start(out=wv_s[k][:], in_=wvT_d[128 * k:128 * (k + 1), :]) \
                        .then_inc(S["wld"], 16)
                g.iota(iotar[:], [[1, 128]], base=0, channel_multiplier=0,
                       allow_small_or_imprecise_dtypes=True)
                g.iota(iotaj[:], [[1, 32]], base=0, channel_multiplier=0,
                       allow_small_or_imprecise_dtypes=True)
                g.signal(S["gcon"])
            @blk.vector
            def _(v0):
                v = SER["v"]
                v.memset(onesr[:], 1.0)
                v.signal(S["vcb"])
            @blk.scalar
            def _(s0):
                s = SER["s"]
                s.wait_ge(S["gcon"], 1)
                s.activation(out=invf[:], in_=iotaj[:], func=AF.Exp,
                             scale=-float(np.log(10000.0)) / 32.0)
                iv2 = invf64[:].rearrange("one (k two) -> one k two", two=2)
                s.activation(out=iv2[:, :, 0:1], in_=invf[:].unsqueeze(2), func=AF.Copy)
                s.activation(out=iv2[:, :, 1:2], in_=invf[:].unsqueeze(2), func=AF.Copy)
                s.signal(S["ssin"])

        with nc.Block() as blk:
            @blk.tensor
            def _(te):
                te.wait_ge(S["cld"], 16 * 4)
                te.wait_ge(S["vcb"], 1)
                for nm, src in enumerate((bqs, bks, bvs)):
                    for half in range(2):
                        te.matmul(out=p_v[half][:], lhsT=onesr[:],
                                  rhs=src[:, 512 * half:512 * (half + 1)],
                                  start=True, stop=True).then_inc(S["pbias"], 1)
                    te.wait_ge(S["vcb"], 1 + nm + 1)
                for t in range(NT):
                    te.wait_ge(S["vps"], 4 * t + 1)
                    te.matmul(out=p_ang[:], lhsT=posr[:], rhs=invf64[:],
                              start=True, stop=True).then_inc(S["pang"], 1)
                    te.wait_ge(S["vps"], 4 * t + 2)
            @blk.vector
            def _(v0):
                v = SER["v"]
                for nm in range(3):
                    v.wait_ge(S["pbias"], 2 * nm + 2)
                    v.tensor_copy(out=bb[nm][:, 0:512], in_=p_v[0][:])
                    v.tensor_copy(out=bb[nm][:, 512:1024], in_=p_v[1][:])
                    v.signal(S["vcb"])
                for t in range(NT):
                    _ts(v, posr[:], iotar[:], poffs[0:1, 0:1], AluOpType.add,
                        float(128 * t), AluOpType.add)
                    v.signal(S["vps"])
                    v.wait_ge(S["pang"], t + 1)
                    v.tensor_copy(out=angs[:], in_=p_ang[:])
                    v.signal(S["vps"])
                    _ts(v, rr[0][:], angs[:], 1.0 / (2 * np.pi), AluOpType.mult)
                    _ts(v, rr[0][:], rr[0][:], MAGIC, AluOpType.add)
                    _ts(v, rr[0][:], rr[0][:], MAGIC, AluOpType.subtract)
                    _ts(v, rr[0][:], rr[0][:], -2 * np.pi, AluOpType.mult)
                    v.tensor_tensor(out=rr[0][:], in0=angs[:], in1=rr[0][:],
                                    op=AluOpType.add)
                    _ts(v, rr[0][:], rr[0][:], -float(np.pi), AluOpType.max,
                        float(np.pi), AluOpType.min)
                    v.signal(S["vps"])
                    _ts(v, rr[1][:], angs[:], np.pi / 2, AluOpType.add)
                    _ts(v, tmpr[:, 0:64], rr[1][:], 1.0 / (2 * np.pi), AluOpType.mult)
                    _ts(v, tmpr[:, 0:64], tmpr[:, 0:64], MAGIC, AluOpType.add)
                    _ts(v, tmpr[:, 0:64], tmpr[:, 0:64], MAGIC, AluOpType.subtract)
                    _ts(v, tmpr[:, 0:64], tmpr[:, 0:64], -2 * np.pi, AluOpType.mult)
                    v.tensor_tensor(out=rr[1][:], in0=rr[1][:], in1=tmpr[:, 0:64],
                                    op=AluOpType.add)
                    _ts(v, rr[1][:], rr[1][:], -float(np.pi), AluOpType.max,
                        float(np.pi), AluOpType.min)
                    v.signal(S["vps"])
                    v.wait_ge(S["ssin"], 2 * t + 3)
            @blk.scalar
            def _(s0):
                s = SER["s"]
                for t in range(NT):
                    s.wait_ge(S["vps"], 4 * t + 3)
                    s.activation(out=sint[:, 64 * t:64 * (t + 1)], in_=rr[0][:],
                                 func=AF.Sin)
                    s.signal(S["ssin"])
                    s.wait_ge(S["vps"], 4 * t + 4)
                    s.activation(out=cost[:, 64 * t:64 * (t + 1)], in_=rr[1][:],
                                 func=AF.Sin)
                    s.signal(S["ssin"])

        _emit_ln_stream(nc, stack, S, SER, x3, xh3)

        with nc.Block() as blk:
            @blk.sync
            def _(sync):
                sync.wait_ge(S["stv"], 16 * 8)
                sync.wait_ge(S["stg"], 16 * 8)
                for k in range(8):
                    sync.dma_start(out=xhf[k][:], in_=xh_dram[:, 128 * k:128 * (k + 1)],
                                   transpose=True).then_inc(S["trx"], 16)

        with nc.Block() as blk:
            @blk.tensor
            def _(te):
                te.wait_ge(S["trx"], 16 * 8)
                te.wait_ge(S["wld"], 16 * 24)
                for t in range(NT):
                    for (ps, w_s, sem, vsem) in ((p_q, wq_s, "peq", "vq"),
                                                 (p_k, wk_s, "pek", "vk"),
                                                 (p_v, wv_s, "pev", "vv")):
                        if t >= 1:
                            te.wait_ge(S[vsem], t)
                        for half in range(2):
                            for k in range(8):
                                mm = te.matmul(out=ps[half][:],
                                               lhsT=xhf[k][:, 128 * t:128 * (t + 1)],
                                               rhs=w_s[k][:, 512 * half:512 * (half + 1)],
                                               start=(k == 0), stop=(k == 7))
                        mm.then_inc(S[sem], 1)
            @blk.vector
            def _(v0):
                v = SER["v"]
                v.wait_ge(S["ssin"], 1 + 2 * NT)
                for t in range(NT):
                    e = t % 2
                    cr = cost[:, 64 * t:64 * (t + 1)].unsqueeze(1) \
                        .broadcast_to((128, 16, 64))
                    sr = sint[:, 64 * t:64 * (t + 1)].unsqueeze(1) \
                        .broadcast_to((128, 16, 64))
                    for (ps, pesem, stsem, outx, rope) in (
                            (p_q, "peq", "ostq", outq, True),
                            (p_k, "pek", "ostk", outk, True),
                            (p_v, "pev", "ostv", outv, False)):
                        v.wait_ge(S[pesem], t + 1)
                        if t >= 2:
                            v.wait_ge(S[stsem + str(e)], 16 * (t // 2))
                        bbx = bb[0] if ps is p_q else (bb[1] if ps is p_k else bb[2])
                        for half in range(2):
                            v.tensor_tensor(out=prj[:, 512 * half:512 * (half + 1)],
                                            in0=ps[half][:],
                                            in1=bbx[:, 512 * half:512 * (half + 1)],
                                            op=AluOpType.add)
                        if rope:
                            pr2 = prj[:].rearrange("p (h k two) -> p h k two",
                                                   h=16, two=2)
                            ro2 = rot[:].rearrange("p (h k two) -> p h k two",
                                                   h=16, two=2)
                            _ts(v, ro2[:, :, :, 0:1], pr2[:, :, :, 1:2], -1.0,
                                AluOpType.mult)
                            v.tensor_copy(out=ro2[:, :, :, 1:2], in_=pr2[:, :, :, 0:1])
                            v.tensor_tensor(
                                out=tmpr[:].rearrange("p (h k) -> p h k", h=16),
                                in0=prj[:].rearrange("p (h k) -> p h k", h=16),
                                in1=cr, op=AluOpType.mult)
                            v.tensor_tensor(
                                out=rot[:].rearrange("p (h k) -> p h k", h=16),
                                in0=rot[:].rearrange("p (h k) -> p h k", h=16),
                                in1=sr, op=AluOpType.mult)
                            v.tensor_tensor(out=outx[e][:], in0=tmpr[:], in1=rot[:],
                                            op=AluOpType.add)
                        else:
                            v.tensor_copy(out=outx[e][:], in_=prj[:])
                        vs = "vq" if ps is p_q else ("vk" if ps is p_k else "vv")
                        v.signal(S[vs])
            @blk.sync
            def _(sync):
                for t in range(NT):
                    e = str(t % 2)
                    sync.wait_ge(S["vq"], t + 1)
                    sync.dma_start(out=q3[t], in_=outq[t % 2][:]) \
                        .then_inc(S["ostq" + e], 16)
                    sync.wait_ge(S["vk"], t + 1)
                    sync.dma_start(out=k3[t], in_=outk[t % 2][:]) \
                        .then_inc(S["ostk" + e], 16)
                    sync.wait_ge(S["vv"], t + 1)
                    sync.dma_start(out=v3[t], in_=outv[t % 2][:]) \
                        .then_inc(S["ostv" + e], 16)
                for e in "01":
                    sync.wait_ge(S["ostq" + e], 16 * (NT // 2))
                    sync.wait_ge(S["ostk" + e], 16 * (NT // 2))
                    sync.wait_ge(S["ostv" + e], 16 * (NT // 2))
    return nc



SQ = 8192

def _mk_sched():
    sched = []
    for qb in range(SQ // 512):
        nkt = 4 * (qb + 1)
        for kt in range(nkt):
            sched.append((qb, kt, kt == 0, kt == nkt - 1, kt >= 4 * qb))
    return sched

_SCHED_B = _mk_sched()
NMASK_UPTO = []
_nm = 0
for _u in range(2):
    for (_qb, _kt, _f, _l, _d) in _SCHED_B:
        if _d:
            _nm += 1
        NMASK_UPTO.append(_nm)

def build_att1_b():
    nc = bass.Bass("TRN2", target_bir_lowering=False)
    TEN = {}
    for n in ("qsl", "ksl", "vsl"):
        TEN[n] = nc.dram_tensor(n, [2, SQ, 128], BF16, kind="ExternalInput").ap()
    TEN["oT_out"] = nc.dram_tensor("oT_out", [2, 2, 65, SQ], BF16,
                                   kind="ExternalOutput").ap()
    emit_att1b(nc, TEN, fm=False)
    return nc


def emit_att1b(nc, TEN, tag="", fm=False):
    if fm:
        qfm_loc = TEN["qfm_loc"]; kfm_loc = TEN["kfm_loc"]
        vfm_loc = TEN["vfm_loc"]
    else:
        qsl = TEN["qsl"]; ksl = TEN["ksl"]; vsl = TEN["vsl"]
    oT_out = TEN["oT_out"]

    NQB = SQ // 512
    sched = []
    for qb in range(NQB):
        nkt = 4 * (qb + 1)
        for kt in range(nkt):
            sched.append((qb, kt, kt == 0, kt == nkt - 1, kt >= 4 * qb))
    NSTEP = len(sched)

    with ExitStack() as stack:
        def sb(name, shape, dt=F32):
            return stack.enter_context(nc.sbuf_tensor(tag + name, shape, dt))
        def psum(name, shape, dt=F32):
            return stack.enter_context(nc.psum_tensor(tag + name, shape, dt))
        qfm = sb("qfm", [128, SQ], BF16)
        kfm = sb("kfm", [128, SQ], BF16)
        VS = 128 if fm else 65
        vv = [sb(f"vv{e}", [128, 64 * VS], BF16) for e in range(2)]
        a2 = [sb(f"a2_{c}", [128, 1024], BF16) for c in range(3)]
        masks = [sb(f"mask{j}", [128, 512], BF16) for j in range(4)]
        coli = sb("coli", [128, 512]); rowi = sb("rowi", [128, 512])
        dmi = sb("dmi", [128, 512])
        b15 = sb("b15", [128, 1])
        oev = [sb(f"oev{e}", [128, 512], BF16) for e in range(2)]
        p_s = [psum(f"p_s{c}", [128, 1024]) for c in range(3)]
        p_o = [psum(f"p_o{e}", [128, 512]) for e in range(2)]

        sem_names = ["oos", "oov", "oog", "qld", "kld", "vld", "pes", "sexp",
                     "vmsk", "pav", "vev", "ostA", "ostB", "gcon", "vcon"]
        S = {n: nc.alloc_semaphore(name=tag + n) for n in sem_names}
        SER = {"s": SerEngine(nc.scalar, S["oos"]),
               "v": SerEngine(nc.vector, S["oov"]),
               "g": SerEngine(nc.gpsimd, S["oog"])}
        ost = [S["ostA"], S["ostB"]]

        with nc.Block() as blk:
            @blk.gpsimd
            def _(g0):
                g = SER["g"]
                g.iota(coli[:], [[1, 512]], base=0, channel_multiplier=0,
                       allow_small_or_imprecise_dtypes=True)
                g.iota(rowi[:], [[0, 512]], base=0, channel_multiplier=1,
                       allow_small_or_imprecise_dtypes=True)
                g.signal(S["gcon"])
            @blk.vector
            def _(v0):
                v = SER["v"]
                v.memset(b15[:], -15.0)
                for e in range(2):
                    v.memset(vv[e][:].rearrange("p (n c) -> p n c", c=VS)[:, :, 64:65],
                             1.0)
                v.wait_ge(S["gcon"], 1)
                v.tensor_tensor(out=dmi[:], in0=coli[:], in1=rowi[:],
                                op=AluOpType.subtract)
                for j in range(4):
                    _ts(v, masks[j][:], dmi[:], float(128 * j), AluOpType.is_ge)
                v.signal(S["vcon"])

        cnt = {"pes": 0, "sexp": 0, "vmsk": 0, "pav": 0, "vev": 0,
               "ost": [0, 0], "qld": 0, "kld": 0, "vld": 0}
        nmask_upto = []
        for u in range(2):
            with nc.Block() as blk:
                @blk.sync
                def _(sync, u=u):
                    if u == 1:
                        sync.wait_ge(S["pav"], NSTEP)
                    if fm:
                        sync.dma_start(out=qfm[:], in_=qfm_loc[u]) \
                            .then_inc(S["qld"], 16)
                        sync.dma_start(out=kfm[:], in_=kfm_loc[u]) \
                            .then_inc(S["kld"], 16)
                        for e in range(2):
                            for kt in range(SQ // 128):
                                sync.dma_start(
                                    out=vv[e][:, VS * kt:VS * kt + 64],
                                    in_=vfm_loc[u][64 * e:64 * (e + 1),
                                                   128 * kt:128 * (kt + 1)],
                                    transpose=True).then_inc(S["vld"], 16)
                    else:
                        sync.dma_start(out=qfm[:], in_=qsl[u], transpose=True) \
                            .then_inc(S["qld"], 16)
                        sync.dma_start(out=kfm[:], in_=ksl[u], transpose=True) \
                            .then_inc(S["kld"], 16)
                @blk.gpsimd
                def _(g0, u=u):
                    if fm:
                        return
                    g = SER["g"]
                    if u == 1:
                        g.wait_ge(S["pav"], NSTEP)
                    vin = vsl[u].rearrange("(n p) h2 -> p n h2", p=128)
                    for e in range(2):
                        g.dma_start(
                            out=vv[e][:].rearrange("p (n c) -> p n c", c=VS)[:, :, 0:64],
                            in_=vin[:, :, 64 * e:64 * (e + 1)]).then_inc(S["vld"], 16)

            with nc.Block() as blk:
                base = u * NSTEP
                qb_base = u * NQB

                @blk.tensor
                def _(te, u=u):
                    te.wait_ge(S["qld"], 16 * (u + 1))
                    te.wait_ge(S["kld"], 16 * (u + 1))
                    te.wait_ge(S["vld"], (16 * 128 if fm else 32) * (u + 1))
                    te.wait_ge(S["vcon"], 1)

                    def issue_score(i):
                        qb, kt, first, last, diag = sched[i]
                        gi = base + i
                        cur = gi % 3
                        if gi >= 3:
                            te.wait_ge(S["sexp"], gi - 2)
                        mm = None
                        for e in range(2):
                            mm = te.matmul(
                                out=p_s[cur][:, 512 * e:512 * (e + 1)],
                                lhsT=kfm[64 * e:64 * (e + 1), 128 * kt:128 * (kt + 1)],
                                rhs=qfm[64 * e:64 * (e + 1), 512 * qb:512 * (qb + 1)],
                                start=True, stop=True)
                        mm.then_inc(S["pes"], 1)

                    issue_score(0)
                    for i, (qb, kt, first, last, diag) in enumerate(sched):
                        gi = base + i
                        cur = gi % 3
                        if i + 1 < NSTEP:
                            issue_score(i + 1)
                        if diag:
                            te.wait_ge(S["vmsk"], NMASK_UPTO[gi])
                        else:
                            te.wait_ge(S["sexp"], gi + 1)
                        if first:
                            te.wait_ge(S["vev"], 2 * (qb_base + qb))
                        mm = None
                        for e in range(2):
                            mm = te.matmul(out=p_o[e][0:65, :],
                                           lhsT=vv[e][:, VS * kt:VS * kt + 65],
                                           rhs=a2[cur][:, 512 * e:512 * (e + 1)],
                                           start=first, stop=last)
                        mm.then_inc(S["pav"], 1)

                @blk.scalar
                def _(s0, u=u):
                    s = SER["s"]
                    for i, (qb, kt, first, last, diag) in enumerate(sched):
                        gi = base + i
                        cur = gi % 3
                        s.wait_ge(S["pes"], gi + 1)
                        if gi >= 3:
                            s.wait_ge(S["pav"], gi - 2)
                        s.activation(out=a2[cur][:], in_=p_s[cur][:], func=AF.Exp,
                                     scale=0.125, bias=b15[:])
                        s.signal(S["sexp"])

                @blk.vector
                def _(v0, u=u):
                    v = SER["v"]
                    ev_i = 0
                    for i, (qb, kt, first, last, diag) in enumerate(sched):
                        gi = base + i
                        cur = gi % 3
                        if diag:
                            v.wait_ge(S["sexp"], gi + 1)
                            j = kt - 4 * qb
                            mrep = masks[j][:].unsqueeze(1).broadcast_to((128, 2, 512))
                            v.tensor_tensor(
                                out=a2[cur][:].rearrange("p (e q) -> p e q", e=2),
                                in0=a2[cur][:].rearrange("p (e q) -> p e q", e=2),
                                in1=mrep, op=AluOpType.mult)
                            v.signal(S["vmsk"])
                        if last:
                            v.wait_ge(S["pav"], gi + 1)
                            for e in range(2):
                                gqb = qb_base + qb
                                if gqb >= 1:
                                    v.wait_ge(ost[e], 16 * gqb)
                                v.tensor_copy(out=oev[e][0:65, :], in_=p_o[e][0:65, :])
                                v.signal(S["vev"])

                @blk.sync
                def _(sync, u=u):
                    for qb in range(NQB):
                        gqb = qb_base + qb
                        for e in range(2):
                            sync.wait_ge(S["vev"], 2 * gqb + e + 1)
                            sync.dma_start(out=oT_out[u, e, :, 512 * qb:512 * (qb + 1)],
                                           in_=oev[e][0:65, :]).then_inc(ost[e], 16)

        with nc.Block() as blk:
            @blk.sync
            def _(sync):
                sync.wait_ge(S["ostA"], 16 * 2 * NQB)
                sync.wait_ge(S["ostB"], 16 * 2 * NQB)
    return nc

def build_att1_c(normalize=True):
    nc = bass.Bass("TRN2", target_bir_lowering=False)
    x_in = nc.dram_tensor("x", [T, D], F32, kind="ExternalInput").ap()
    onum_d = nc.dram_tensor("onum", [T, D], BF16, kind="ExternalInput").ap()
    oden_d = nc.dram_tensor("oden", [T, 16], F32, kind="ExternalInput").ap()
    woT_d = nc.dram_tensor("woT", [D, D], F32, kind="ExternalInput").ap()
    bo_d = nc.dram_tensor("bo", [1, D], F32, kind="ExternalInput").ap()
    x_out = nc.dram_tensor("x_out", [T, D], F32, kind="ExternalOutput").ap()
    on_dram = nc.dram_tensor("on_dram", [T, D], BF16, kind="Internal").ap()

    x3 = x_in.rearrange("(t p) d -> t p d", p=128)
    xo3 = x_out.rearrange("(t p) d -> t p d", p=128)
    onum3 = onum_d.rearrange("(t p) d -> t p d", p=128)
    oden3 = oden_d.rearrange("(t p) d -> t p d", p=128)
    on3 = on_dram.rearrange("(t p) d -> t p d", p=128)

    with ExitStack() as stack:
        def sb(name, shape, dt=F32):
            return stack.enter_context(nc.sbuf_tensor(name, shape, dt))
        def psum(name, shape, dt=F32):
            return stack.enter_context(nc.psum_tensor(name, shape, dt))
        xt = [sb(f"xt{i}", [128, D]) for i in range(NT)]
        onb = [sb(f"onb{e}", [128, D], BF16) for e in range(2)]
        odn = sb("odn", [128, 16])
        rdn = sb("rdn", [128, 16])
        onn = [sb(f"onn{e}", [128, D], BF16) for e in range(2)]
        onf = [sb(f"onf{k}", [128, T], BF16) for k in range(8)]
        wo_s = [sb(f"wo_s{k}", [128, D], BF16) for k in range(8)]
        bos = sb("bos", [1, D]); bbo = sb("bbo", [128, D])
        onesr = sb("onesr", [1, 128])
        uev = sb("uev", [128, 512])
        xot = [sb(f"xot{e}", [128, D]) for e in range(2)]
        p_y = [psum(f"p_y{j}", [128, 512]) for j in range(4)]

        sem_names = ["oos", "oov", "oog", "ld", "old", "dld", "wld", "cld",
                     "snrm", "vnrm", "stv", "stg", "trx", "pbias", "vcb",
                     "pey", "vev", "osem", "osc"]
        S = {n: stack.enter_context(nc.semaphore(n)) for n in sem_names}
        SER = {"s": SerEngine(nc.scalar, S["oos"]),
               "v": SerEngine(nc.vector, S["oov"]),
               "g": SerEngine(nc.gpsimd, S["oog"])}

        with nc.Block() as blk:
            @blk.sync
            def _(sync):
                for i in range(NT):
                    sync.dma_start(out=xt[i][:], in_=x3[i]).then_inc(S["ld"], 16)
                sync.dma_start(out=bos[:], in_=bo_d[:]).then_inc(S["cld"], 16)
            @blk.gpsimd
            def _(g0):
                g = SER["g"]
                for k in range(8):
                    g.dma_start(out=wo_s[k][:], in_=woT_d[128 * k:128 * (k + 1), :]) \
                        .then_inc(S["wld"], 16)
            @blk.vector
            def _(v0):
                v = SER["v"]
                v.memset(onesr[:], 1.0)
                v.signal(S["vcb"])
            @blk.tensor
            def _(te):
                te.wait_ge(S["cld"], 16)
                te.wait_ge(S["vcb"], 1)
                for half in range(2):
                    te.matmul(out=p_y[half][:], lhsT=onesr[:],
                              rhs=bos[:, 512 * half:512 * (half + 1)],
                              start=True, stop=True).then_inc(S["pbias"], 1)
        with nc.Block() as blk:
            @blk.vector
            def _(v0):
                v = SER["v"]
                v.wait_ge(S["pbias"], 2)
                v.tensor_copy(out=bbo[:, 0:512], in_=p_y[0][:])
                v.tensor_copy(out=bbo[:, 512:1024], in_=p_y[1][:])
                v.signal(S["vcb"])

        with nc.Block() as blk:
            @blk.sync
            def _(sync):
                for i in range(NT):
                    if i >= 2:
                        sync.wait_ge(S["vnrm"], i - 1)
                    sync.dma_start(out=onb[i % 2][:], in_=onum3[i]).then_inc(
                        S["old" if i % 2 == 0 else "dld"], 16)
            @blk.scalar
            def _(s0):
                s = SER["s"]
                s.dma_start(out=odn[:], in_=oden3[0]).then_inc(S["cld"], 16)
                for i in range(NT):
                    s.wait_ge(S["cld"], 16 * (i + 2))
                    if i >= 1:
                        s.wait_ge(S["vnrm"], i)
                    s.activation(out=rdn[:], in_=odn[:], func=AF.Ln)
                    s.activation(out=rdn[:], in_=rdn[:], func=AF.Exp, scale=-1.0)
                    s.signal(S["snrm"])
                    if i + 1 < NT:
                        s.wait_ge(S["snrm"], i + 1)
                        s.dma_start(out=odn[:], in_=oden3[i + 1]).then_inc(S["cld"], 16)
            @blk.vector
            def _(v0):
                v = SER["v"]
                for i in range(NT):
                    e = i % 2
                    v.wait_ge(S["old" if e == 0 else "dld"], 16 * (i // 2 + 1))
                    v.wait_ge(S["snrm"], i + 1)
                    if i >= 2:
                        v.wait_ge(S["stv" if e == 0 else "stg"], 16 * (i // 2))
                    if normalize:
                        rrep = rdn[:].unsqueeze(2).broadcast_to((128, 16, 64))
                        v.tensor_tensor(out=onn[e][:].rearrange("p (h k) -> p h k", h=16),
                                        in0=onb[e][:].rearrange("p (h k) -> p h k", h=16),
                                        in1=rrep, op=AluOpType.mult)
                    else:
                        v.tensor_copy(out=onn[e][:], in_=onb[e][:])
                    v.signal(S["vnrm"])
            @blk.gpsimd
            def _(g0):
                g = SER["g"]
                for i in range(NT):
                    g.wait_ge(S["vnrm"], i + 1)
                    g.dma_start(out=on3[i], in_=onn[i % 2][:]).then_inc(
                        S["stv" if i % 2 == 0 else "stg"], 16)

        with nc.Block() as blk:
            @blk.sync
            def _(sync):
                sync.wait_ge(S["stv"], 16 * 8)
                sync.wait_ge(S["stg"], 16 * 8)
                for k in range(8):
                    sync.dma_start(out=onf[k][:], in_=on_dram[:, 128 * k:128 * (k + 1)],
                                   transpose=True).then_inc(S["trx"], 16)

        with nc.Block() as blk:
            @blk.tensor
            def _(te):
                te.wait_ge(S["trx"], 16 * 8)
                te.wait_ge(S["wld"], 16 * 8)
                te.wait_ge(S["vcb"], 2)
                for t in range(NT):
                    for half in range(2):
                        j = 2 * (t % 2) + half
                        if t >= 2:
                            te.wait_ge(S["vev"], 2 * (t - 1))
                        mm = None
                        for k in range(8):
                            mm = te.matmul(out=p_y[j][:],
                                           lhsT=onf[k][:, 128 * t:128 * (t + 1)],
                                           rhs=wo_s[k][:, 512 * half:512 * (half + 1)],
                                           start=(k == 0), stop=(k == 7))
                        mm.then_inc(S["pey"], 1)
            @blk.vector
            def _(v0):
                v = SER["v"]
                v.wait_ge(S["ld"], 16 * NT)
                for t in range(NT):
                    e = t % 2
                    if t >= 2:
                        v.wait_ge(S["osc"], t - 1)
                    for half in range(2):
                        j = 2 * e + half
                        v.wait_ge(S["pey"], 2 * t + half + 1)
                        v.tensor_tensor(out=uev[:], in0=p_y[j][:],
                                        in1=bbo[:, 512 * half:512 * (half + 1)],
                                        op=AluOpType.add)
                        v.tensor_tensor(out=xot[e][:, 512 * half:512 * (half + 1)],
                                        in0=uev[:],
                                        in1=xt[t][:, 512 * half:512 * (half + 1)],
                                        op=AluOpType.add)
                        v.signal(S["vev"])
            @blk.gpsimd
            def _(g0):
                g = SER["g"]
                for t in range(NT):
                    g.wait_ge(S["vev"], 2 * t + 2)
                    g.dma_start(out=xo3[t], in_=xot[t % 2][:]).then_inc(S["osem"], 16)
                    g.wait_ge(S["osem"], 16 * (t + 1))
                    g.sem_inc(S["osc"], 1)
    return nc


def run_att1(nc_cache, x_flat, Wq, bq, Wk, bk, Wv, bv, Wo, bo):
    if "att1a" not in nc_cache:
        nc_cache["att1a"] = build_att1_a()
        nc_cache["att1b"] = build_att1_b()
        nc_cache["att1c"] = build_att1_c()
    wqT = np.ascontiguousarray(Wq.T)
    wkT = np.ascontiguousarray(Wk.T)
    wvT = np.ascontiguousarray(Wv.T)
    woT = np.ascontiguousarray(Wo.T)
    in_maps = []
    for c in range(NCORES):
        poff = np.array([[float((c % 4) * T)]], np.float32)
        in_maps.append({"x": np.ascontiguousarray(x_flat[c * T:(c + 1) * T]),
                        "wqT": wqT, "wkT": wkT, "wvT": wvT,
                        "bq": bq.reshape(1, D), "bk": bk.reshape(1, D),
                        "bv": bv.reshape(1, D), "poff": poff})
    res = run_bass_kernel_spmd(nc_cache["att1a"], in_maps, core_ids=list(range(NCORES)))
    qf = np.concatenate([np.asarray(res.results[c]["q_out"]) for c in range(NCORES)], 0)
    kf = np.concatenate([np.asarray(res.results[c]["k_out"]) for c in range(NCORES)], 0)
    vf = np.concatenate([np.asarray(res.results[c]["v_out"]) for c in range(NCORES)], 0)
    in_maps = []
    for c in range(NCORES):
        cols = slice(128 * c, 128 * (c + 1))
        in_maps.append({
            "qsl": np.ascontiguousarray(qf.reshape(2, SQ, D)[:, :, cols]),
            "ksl": np.ascontiguousarray(kf.reshape(2, SQ, D)[:, :, cols]),
            "vsl": np.ascontiguousarray(vf.reshape(2, SQ, D)[:, :, cols])})
    res = run_bass_kernel_spmd(nc_cache["att1b"], in_maps, core_ids=list(range(NCORES)))
    onum = np.zeros((2, SQ, D), qf.dtype)
    oden = np.zeros((2, SQ, 16), np.float32)
    for c in range(NCORES):
        oT = np.asarray(res.results[c]["oT_out"])
        for u in range(2):
            for e in range(2):
                h = 2 * c + e
                onum[u, :, 64 * h:64 * (h + 1)] = oT[u, e, 0:64, :].T
                oden[u, :, h] = oT[u, e, 64, :].astype(np.float32)
    onum = onum.reshape(TOK, D)
    oden = oden.reshape(TOK, 16)
    in_maps = []
    for c in range(NCORES):
        rows = slice(c * T, (c + 1) * T)
        in_maps.append({"x": np.ascontiguousarray(x_flat[rows]),
                        "onum": np.ascontiguousarray(onum[rows]),
                        "oden": np.ascontiguousarray(oden[rows]),
                        "woT": woT, "bo": bo.reshape(1, D)})
    res = run_bass_kernel_spmd(nc_cache["att1c"], in_maps, core_ids=list(range(NCORES)))
    return np.concatenate([np.asarray(res.results[c]["x_out"]) for c in range(NCORES)], 0)

def build_gla_a():
    nc = bass.Bass("TRN2", target_bir_lowering=False)
    x_in = nc.dram_tensor("x", [T, D], F32, kind="ExternalInput").ap()
    wq_dd = nc.dram_tensor("wq", [D, D], F32, kind="ExternalInput").ap()
    wk1_dd = nc.dram_tensor("wk1", [D, D], F32, kind="ExternalInput").ap()
    wk2_dd = nc.dram_tensor("wk2", [D, D], F32, kind="ExternalInput").ap()
    bq_d = nc.dram_tensor("bq", [1, D], F32, kind="ExternalInput").ap()
    bk1_d = nc.dram_tensor("bk1", [1, D], F32, kind="ExternalInput").ap()
    bk2_d = nc.dram_tensor("bk2", [1, D], F32, kind="ExternalInput").ap()
    wgh_d = nc.dram_tensor("wgh", [D, 16], F32, kind="ExternalInput").ap()
    bg_d = nc.dram_tensor("bg", [1, 1], F32, kind="ExternalInput").ap()
    xh_out = nc.dram_tensor("xh_out", [T, D], BF16, kind="ExternalOutput").ap()
    q_out = nc.dram_tensor("q_out", [T, D], BF16, kind="ExternalOutput").ap()
    k1_out = nc.dram_tensor("k1_out", [T, D], BF16, kind="ExternalOutput").ap()
    k2_out = nc.dram_tensor("k2_out", [T, D], BF16, kind="ExternalOutput").ap()
    q2_out = nc.dram_tensor("q2_out", [T, D], BF16, kind="ExternalOutput").ap()
    lg_out = nc.dram_tensor("lg_out", [16, T], F32, kind="ExternalOutput").ap()

    x3 = x_in.rearrange("(t p) d -> t p d", p=128)
    xh3 = xh_out.rearrange("(t p) d -> t p d", p=128)
    q3 = q_out.rearrange("(t p) d -> t p d", p=128)
    k13 = k1_out.rearrange("(t p) d -> t p d", p=128)
    k23 = k2_out.rearrange("(t p) d -> t p d", p=128)
    q23 = q2_out.rearrange("(t p) d -> t p d", p=128)

    K2SC = 0.02 / 32.0

    with ExitStack() as stack:
        def sb(name, shape, dt=F32):
            return stack.enter_context(nc.sbuf_tensor(name, shape, dt))
        def psum(name, shape, dt=F32):
            return stack.enter_context(nc.psum_tensor(name, shape, dt))
        xhf = [sb(f"xhf{k}", [128, T], BF16) for k in range(8)]
        wq_s = [sb(f"wq_s{k}", [128, D], BF16) for k in range(8)]
        wk1_s = [sb(f"wk1_s{k}", [128, D], BF16) for k in range(8)]
        wk2_s = [sb(f"wk2_s{k}", [128, D], BF16) for k in range(8)]
        wgh_s = [sb(f"wgh_s{k}", [128, 16], BF16) for k in range(8)]
        bqb = sb("bqb", [1, D], BF16); bk1b = sb("bk1b", [1, D], BF16)
        bk2b = sb("bk2b", [1, D], BF16)
        bgc = sb("bgc", [16, 1])
        ones1 = sb("ones1", [1, 128])
        ones16c = sb("ones16c", [1, 16])
        bgs = sb("bgs", [1, 1])
        prj = sb("prj", [128, D])
        qn1 = sb("qn1", [128, D]); qab = sb("qab", [128, D])
        ones1b = sb("ones1b", [1, 128], BF16)
        outq = [sb(f"outq{e}", [128, D], BF16) for e in range(2)]
        outk1 = [sb(f"outk1{e}", [128, D], BF16) for e in range(2)]
        outk2 = [sb(f"outk2{e}", [128, D], BF16) for e in range(2)]
        outq2 = [sb(f"outq2{e}", [128, D], BF16) for e in range(2)]
        glg = sb("glg", [16, T])
        p_q = [psum(f"p_q{j}", [128, 512]) for j in range(2)]
        p_k1 = [psum(f"p_k1{j}", [128, 512]) for j in range(2)]
        p_k2 = [psum(f"p_k2{j}", [128, 512]) for j in range(2)]
        p_g = psum("p_g", [16, T // 4])
        p_bg = psum("p_bg", [16, 1])

        sem_names = ["oos", "oov", "oog", "ldv", "ldg", "wld", "cld", "bld",
                     "sxh",
                     "stv", "stg", "trx", "peq", "pek1", "pek2", "peg", "pbg",
                     "vq", "vk1", "sk2", "vq2", "sg", "sga", "sgb", "vcb",
                     "ostq0", "ostq1", "ostk10", "ostk11", "ostk20", "ostk21",
                     "ostq20", "ostq21", "ostlg"]
        S = {n: stack.enter_context(nc.semaphore(n)) for n in sem_names}
        SER = {"s": SerEngine(nc.scalar, S["oos"]),
               "v": SerEngine(nc.vector, S["oov"]),
               "g": SerEngine(nc.gpsimd, S["oog"])}

        with nc.Block() as blk:
            @blk.sync
            def _(sync):
                sync.dma_start(out=bgs[:], in_=bg_d[:]).then_inc(S["cld"], 16)
            @blk.gpsimd
            def _(g0):
                g = SER["g"]
                g.dma_start(out=bqb[:], in_=bq_d[:]).then_inc(S["bld"], 16)
                g.dma_start(out=bk1b[:], in_=bk1_d[:]).then_inc(S["bld"], 16)
                g.dma_start(out=bk2b[:], in_=bk2_d[:]).then_inc(S["bld"], 16)
                for k in range(8):
                    g.dma_start(out=wq_s[k][:], in_=wq_dd[128 * k:128 * (k + 1), :]) \
                        .then_inc(S["wld"], 16)
                    g.dma_start(out=wk1_s[k][:], in_=wk1_dd[128 * k:128 * (k + 1), :]) \
                        .then_inc(S["wld"], 16)
                    g.dma_start(out=wk2_s[k][:], in_=wk2_dd[128 * k:128 * (k + 1), :]) \
                        .then_inc(S["wld"], 16)
                    g.dma_start(out=wgh_s[k][:], in_=wgh_d[128 * k:128 * (k + 1), :]) \
                        .then_inc(S["wld"], 16)
            @blk.vector
            def _(v0):
                v = SER["v"]
                v.memset(ones1[:], 1.0)
                v.memset(ones1b[:], 1.0)
                v.memset(ones16c[:], 1.0)
                v.signal(S["vcb"])
            @blk.tensor
            def _(te):
                te.wait_ge(S["cld"], 16)
                te.wait_ge(S["bld"], 16 * 3)
                te.wait_ge(S["vcb"], 1)
                te.matmul(out=p_bg[:], lhsT=ones16c[:], rhs=bgs[:],
                          start=True, stop=True).then_inc(S["pbg"], 1)
        with nc.Block() as blk:
            @blk.vector
            def _(v0):
                v = SER["v"]
                v.wait_ge(S["pbg"], 1)
                v.tensor_copy(out=bgc[:], in_=p_bg[:])
                v.signal(S["vcb"])

        _emit_ln_stream(nc, stack, S, SER, x3, xh3)
        with nc.Block() as blk:
            @blk.sync
            def _(sync):
                sync.wait_ge(S["stv"], 16 * 8)
                sync.wait_ge(S["stg"], 16 * 8)
                for k in range(8):
                    sync.dma_start(out=xhf[k][:], in_=xh_out[:, 128 * k:128 * (k + 1)],
                                   transpose=True).then_inc(S["trx"], 16)

        with nc.Block() as blk:
            @blk.tensor
            def _(te):
                te.wait_ge(S["trx"], 16 * 8)
                te.wait_ge(S["wld"], 16 * 32)
                for cb in range(4):
                    for k in range(8):
                        mm = te.matmul(out=p_g[:, 0:512],
                                       lhsT=wgh_s[k][:],
                                       rhs=xhf[k][:, 512 * cb:512 * (cb + 1)],
                                       start=(k == 0), stop=(k == 7))
                    mm.then_inc(S["peg"], 1)
                    te.wait_ge(S["sga"], cb + 1)
            @blk.scalar
            def _(s0):
                s = SER["s"]
                s.wait_ge(S["vcb"], 2)
                for cb in range(4):
                    s.wait_ge(S["peg"], cb + 1)
                    s.activation(out=glg[:, 512 * cb:512 * (cb + 1)],
                                 in_=p_g[:, 0:512], func=AF.Sigmoid, bias=bgc[:])
                    s.signal(S["sga"])
            @blk.vector
            def _(v0):
                v = SER["v"]
                for cb in range(4):
                    v.wait_ge(S["sga"], cb + 1)
                    _ts(v, glg[:, 512 * cb:512 * (cb + 1)],
                        glg[:, 512 * cb:512 * (cb + 1)], 1e-6, AluOpType.max)
                    v.signal(S["sgb"])
        with nc.Block() as blk:
            @blk.scalar
            def _(s0):
                s = SER["s"]
                s.wait_ge(S["sgb"], 4)
                s.activation(out=glg[:], in_=glg[:], func=AF.Ln)
                s.signal(S["sga"])
            @blk.gpsimd
            def _(g0):
                g = SER["g"]
                g.wait_ge(S["sga"], 5)
                g.dma_start(out=lg_out[:], in_=glg[:]).then_inc(S["ostlg"], 16)

        with nc.Block() as blk:
            @blk.tensor
            def _(te):
                for t in range(NT):
                    for (ps, w_s, bsrc, pesem, vsem) in (
                            (p_q, wq_s, bqb, "peq", "vq"),
                            (p_k1, wk1_s, bk1b, "pek1", "vk1"),
                            (p_k2, wk2_s, bk2b, "pek2", "sk2")):
                        if t >= 1:
                            te.wait_ge(S[vsem], t)
                        mm = None
                        for half in range(2):
                            for k in range(8):
                                te.matmul(out=ps[half][:],
                                          lhsT=xhf[k][:, 128 * t:128 * (t + 1)],
                                          rhs=w_s[k][:, 512 * half:512 * (half + 1)],
                                          start=(k == 0), stop=False)
                            mm = te.matmul(out=ps[half][:], lhsT=ones1b[:],
                                           rhs=bsrc[:, 512 * half:512 * (half + 1)],
                                           start=False, stop=True)
                        mm.then_inc(S[pesem], 1)

            @blk.vector
            def _(v0):
                v = SER["v"]
                for t in range(NT):
                    e = str(t % 2)
                    ei = t % 2
                    v.wait_ge(S["peq"], t + 1)
                    if t >= 2:
                        v.wait_ge(S["ostq" + e], 16 * (t // 2))
                    for half in range(2):
                        v.tensor_copy(out=prj[:, 512 * half:512 * (half + 1)],
                                      in_=p_q[half][:])
                        v.tensor_copy(out=outq[ei][:, 512 * half:512 * (half + 1)],
                                      in_=p_q[half][:])
                    v.signal(S["vq"])
                    if t >= 2:
                        v.wait_ge(S["ostq2" + e], 16 * (t // 2))
                    _ts(v, qn1[:], prj[:], -1.0, AluOpType.mult)
                    v.tensor_tensor(out=qab[:], in0=prj[:], in1=qn1[:], op=AluOpType.max)
                    _ts(v, qn1[:], prj[:], 0.06375, AluOpType.mult)
                    v.scalar_tensor_tensor(out=outq2[ei][:], in0=qab[:],
                                           scalar=-0.06125, in1=qn1[:],
                                           op0=AluOpType.mult, op1=AluOpType.add)
                    v.signal(S["vq2"])
                    v.wait_ge(S["pek1"], t + 1)
                    if t >= 2:
                        v.wait_ge(S["ostk1" + e], 16 * (t // 2))
                    for half in range(2):
                        v.tensor_copy(out=outk1[ei][:, 512 * half:512 * (half + 1)],
                                      in_=p_k1[half][:])
                    v.signal(S["vk1"])

            @blk.scalar
            def _(s0):
                s = SER["s"]
                for t in range(NT):
                    e = str(t % 2)
                    s.wait_ge(S["pek2"], t + 1)
                    if t >= 2:
                        s.wait_ge(S["ostk2" + e], 16 * (t // 2))
                    for half in range(2):
                        s.activation(out=outk2[t % 2][:, 512 * half:512 * (half + 1)],
                                     in_=p_k2[half][:], func=AF.Sigmoid, scale=K2SC)
                    s.signal(S["sk2"])

            @blk.sync
            def _(sync):
                for t in range(NT):
                    e = str(t % 2)
                    sync.wait_ge(S["vq"], t + 1)
                    sync.dma_start(out=q3[t], in_=outq[t % 2][:]) \
                        .then_inc(S["ostq" + e], 16)
                    sync.wait_ge(S["vq2"], t + 1)
                    sync.dma_start(out=q23[t], in_=outq2[t % 2][:]) \
                        .then_inc(S["ostq2" + e], 16)
                    sync.wait_ge(S["vk1"], t + 1)
                    sync.dma_start(out=k13[t], in_=outk1[t % 2][:]) \
                        .then_inc(S["ostk1" + e], 16)
                    sync.wait_ge(S["sk2"], t + 1)
                    sync.dma_start(out=k23[t], in_=outk2[t % 2][:]) \
                        .then_inc(S["ostk2" + e], 16)
                for e in "01":
                    for nm in ("ostq", "ostq2", "ostk1", "ostk2"):
                        sync.wait_ge(S[nm + e], 16 * (NT // 2))
                sync.wait_ge(S["ostlg"], 16)
    return nc

def build_gla_b(upto=4, ld_lvl=4, p1_lvl=4):
    nc = bass.Bass("TRN2", target_bir_lowering=False)
    TEN = {}
    for n in ("qsl", "k1sl", "k2sl", "q2sl", "xhsl", "k2ssl"):
        TEN[n] = nc.dram_tensor(n, [2, SQ, 128], BF16, kind="ExternalInput").ap()
    TEN["lgsl"] = nc.dram_tensor("lgsl", [2, 2, SQ], F32, kind="ExternalInput").ap()
    TEN["ws"] = nc.dram_tensor("ws", [64, 1], F32, kind="ExternalInput").ap()
    TEN["bs"] = nc.dram_tensor("bs", [1, 1], F32, kind="ExternalInput").ap()
    TEN["o_slab"] = nc.dram_tensor("o_slab", [2, SQ, 128], BF16,
                                   kind="ExternalOutput").ap()
    TEN["rowd"] = nc.dram_tensor("rowd", [SQ], F32, kind="Internal").ap()
    emit_glab(nc, TEN, upto=upto, ld_lvl=ld_lvl, p1_lvl=p1_lvl, fm=False)
    return nc


def emit_glab(nc, TEN, upto=4, ld_lvl=4, p1_lvl=4, tag="", fm=False):
    if fm:
        qfm_loc = TEN["qfm_loc"]; k1fm_loc = TEN["k1fm_loc"]
        k2fm_loc = TEN["k2fm_loc"]; q2fm_loc = TEN["q2fm_loc"]
        xhfm_loc = TEN["xhfm_loc"]; k2sfm_loc = TEN["k2sfm_loc"]
        rrow_d = TEN["rrow_d"]
    else:
        qsl = TEN["qsl"]; k1sl = TEN["k1sl"]; k2sl = TEN["k2sl"]
        q2sl = TEN["q2sl"]; xhsl = TEN["xhsl"]; k2ssl = TEN["k2ssl"]
    lgsl = TEN["lgsl"]; ws_d = TEN["ws"]; bs_d = TEN["bs"]
    o_slab = TEN["o_slab"]; rowd = TEN["rowd"]
    rowd4 = nc.dram_tensor(tag + "rowd4", [4, SQ], F32, kind="Internal").ap()

    NCH = SQ // 128
    HSQ = SQ // 2

    with ExitStack() as stack:
        def sb(name, shape, dt=F32):
            return stack.enter_context(nc.sbuf_tensor(tag + name, shape, dt))
        def psum(name, shape, dt=F32):
            return stack.enter_context(nc.psum_tensor(tag + name, shape, dt))
        qfm = sb("qfm", [128, SQ], BF16)
        k1fm = sb("k1fm", [128, SQ], BF16)
        k2fm = sb("k2fm", [128, SQ], BF16)
        q2fm = sb("q2fm", [128, SQ], BF16)
        vtok = sb("vtok", [128, SQ], BF16)
        k1tok = sb("k1tok", [128, SQ], BF16)
        k2stok = sb("k2stok", [128, SQ], BF16)
        k2tok = sb("k2tok", [128, SQ], BF16)
        k2s0 = sb("k2s0", [128, 128], BF16)
        q2s0 = sb("q2s0", [128, 128], BF16)
        o1b = [sb(f"o1b{e}", [128, 64 * NCH], BF16) for e in range(2)]
        if not fm:
            rowA = sb("rowA", [1, HSQ])
        carry4 = sb("carry4", [1, 1]); zero1 = sb("zero1", [1, 1])
        row4A = sb("row4A", [4, HSQ]); row4B = sb("row4B", [4, HSQ])
        carry4b = sb("carry4b", [4, 1]); zero4 = sb("zero4", [4, 1])
        ws_e = [sb(f"ws_e{e}", [128, 1], BF16) for e in range(2)]
        bs_s = sb("bs_s", [1, 1])
        aT = [[sb(f"aT{u}{e}", [128, NCH]) for e in range(2)] for u in range(2)]
        bT = [[sb(f"bT{u}{e}", [128, NCH]) for e in range(2)] for u in range(2)]
        tsc = sb("tsc", [128, NCH]); tsc2 = sb("tsc2", [128, NCH])
        fold = sb("fold", [64, 128])
        colidx = sb("colidx", [128, 128]); rowix = sb("rowix", [128, 128])
        identf = sb("identf", [128, 128])
        triuI = sb("triuI", [128, 128], BF16)
        triuSN = sb("triuSN", [128, 128], BF16)
        mb = [sb(f"mb{e}", [128, 128], BF16) for e in range(2)]
        at_ = [sb(f"at_{e}", [128, 128], BF16) for e in range(2)]
        a2v = [sb(f"a2v{e}", [128, 128], BF16) for e in range(2)]
        a2o = [sb(f"a2o{e}", [128, 128], BF16) for e in range(2)]
        kt_ = [sb(f"kt_{e}", [128, 64], BF16) for e in range(2)]
        k2n = [sb(f"k2n{e}", [128, 64], BF16) for e in range(2)]
        sbE = [sb(f"sbE{e}", [128, 64], BF16) for e in range(2)]
        s2bE = [sb(f"s2bE{e}", [128, 64], BF16) for e in range(2)]
        sacc = [sb(f"sacc{e}", [64, 64]) for e in range(2)]
        s2acc = [sb(f"s2acc{e}", [64, 64]) for e in range(2)]
        obuf = [sb(f"obuf{e2}", [128, 128], BF16) for e2 in range(2)]
        p_eh = [psum(f"p_eh{e}", [128, 512]) for e in range(2)]
        p_p = [[p_eh[e][:, 0:128], p_eh[e][:, 256:384]] for e in range(2)]
        p_p2 = [[p_eh[e][:, 128:256], p_eh[e][:, 384:512]] for e in range(2)]
        p_o = psum("p_o", [128, 128])
        p_sB = psum("p_sB", [64, 128])
        p_s = [p_sB[0:64, 64 * e:64 * (e + 1)] for e in range(2)]
        p_tr = psum("p_tr", [128, 128])
        p_r = psum("p_r", [1, 512])

        sem_names = ["oos", "oov", "oog", "seqc", "dA", "dB", "wsld", "tld", "pld",
                     "pldg", "pes", "peo", "pesd", "vS", "vA", "vK", "vO1", "vmm",
                     "ost0", "ost1", "vcon", "gcon"]
        S = {n: nc.alloc_semaphore(name=tag + n) for n in sem_names}
        SER = {"s": SerEngine(nc.scalar, S["oos"]),
               "v": SerEngine(nc.vector, S["oov"]),
               "g": SerEngine(nc.gpsimd, S["oog"])}

        with nc.Block() as blk:
            @blk.gpsimd
            def _(g0):
                g = SER["g"]
                g.iota(colidx[:], [[1, 128]], base=0, channel_multiplier=0,
                       allow_small_or_imprecise_dtypes=True)
                g.iota(rowix[:], [[0, 128]], base=0, channel_multiplier=1,
                       allow_small_or_imprecise_dtypes=True)
                g.memset(ws_e[0][64:128, :], 0.0)
                g.memset(ws_e[1][0:64, :], 0.0)
                g.dma_start(out=ws_e[0][0:64, :], in_=ws_d[:]).then_inc(S["wsld"], 16)
                g.dma_start(out=ws_e[1][64:128, :], in_=ws_d[:]).then_inc(S["wsld"], 16)
                g.signal(S["gcon"])
            @blk.sync
            def _(sync):
                sync.dma_start(out=bs_s[:], in_=bs_d[:]).then_inc(S["dA"], 16)
            @blk.vector
            def _(v0):
                v = SER["v"]
                v.memset(zero1[:], 0.0)
                v.memset(zero4[:], 0.0)
                for e in range(2):
                    v.memset(sbE[e][:], 0.0)
                    v.memset(s2bE[e][:], 0.0)
                v.wait_ge(S["gcon"], 1)
                v.tensor_tensor(out=identf[:], in0=colidx[:], in1=rowix[:],
                                op=AluOpType.is_equal)
                v.tensor_tensor(out=triuI[:], in0=colidx[:], in1=rowix[:],
                                op=AluOpType.is_ge)
                v.tensor_tensor(out=triuSN[:], in0=rowix[:], in1=colidx[:],
                                op=AluOpType.is_lt)
                _ts(v, triuSN[:], triuSN[:], -1.0, AluOpType.mult)
                v.signal(S["vcon"])

        seq_n = [0]
        dcnt = {"dA": 16, "dB": 0, "tld": 0}

        def step_wait(eng):
            if seq_n[0] > 0:
                eng.wait_ge(S["seqc"], seq_n[0])

        def step_done(eng_raw):
            seq_n[0] += 1
            eng_raw.sem_inc(S["seqc"], 1)

        _pro_scope = nc.enter_named_scope(tag + "pro", False)
        with nc.Block() as blk:
            steps = []

            def emit_prologue():
                lgflat = lgsl.rearrange("u e q -> (u e) q")
                for hf in range(2):
                    def ld_lg4(sync, hf=hf):
                        dcnt["dA"] += 16
                        sync.dma_start(out=row4A[:],
                                       in_=lgflat[:, HSQ * hf:HSQ * (hf + 1)]) \
                            .then_inc(S["dA"], 16)
                        sync.wait_ge(S["dA"], dcnt["dA"])
                    steps.append(("sync", ld_lg4))

                    def scan4(v, hf=hf):
                        ini = zero4[0:4, 0:1] if hf == 0 else carry4b[0:4, 0:1]
                        v.tensor_tensor_scan(
                            out=row4B[:], data0=row4A[:],
                            data1=zero4[:].broadcast_to((4, HSQ)),
                            initial=ini, op0=AluOpType.add, op1=AluOpType.add)
                    steps.append(("v", scan4))
                    if hf == 0:
                        def save4(g):
                            g.tensor_copy(out=carry4b[:], in_=row4B[:, HSQ - 1:HSQ])
                        steps.append(("g", save4))

                    def clip4(g):
                        _ts(g, row4B[:], row4B[:], -30.0, AluOpType.max,
                            30.0, AluOpType.min)
                    steps.append(("g", clip4))

                    def st4(sync, hf=hf):
                        dcnt["dB"] += 16
                        sync.dma_start(out=rowd4[:, HSQ * hf:HSQ * (hf + 1)],
                                       in_=row4B[:]).then_inc(S["dB"], 16)
                        sync.wait_ge(S["dB"], dcnt["dB"])
                    steps.append(("sync", st4))
                for u in range(2):
                    def ld_k1(sync, u=u):
                        if u == 0:
                            sync.wait_ge(S["wsld"], 32)
                        dcnt["tld"] += 16
                        if fm:
                            sync.dma_start(out=k1fm[:], in_=k1fm_loc[u]) \
                                .then_inc(S["tld"], 16)
                        else:
                            sync.dma_start(out=k1fm[:], in_=k1sl[u], transpose=True) \
                                .then_inc(S["tld"], 16)
                        sync.wait_ge(S["tld"], dcnt["tld"])
                    steps.append(("sync", ld_k1))
                    for e in range(2):
                        def ld_fold(sync, u=u, e=e):
                            dcnt["dA"] += 16
                            sync.dma_start(out=fold[:],
                                           in_=rowd4[2 * u + e, :].rearrange(
                                               "(c p) -> c p", p=128)) \
                                .then_inc(S["dA"], 16)
                            sync.wait_ge(S["dA"], dcnt["dA"])
                        steps.append(("sync", ld_fold))

                        def tr_fold(te):
                            return te.transpose(out=p_tr[:, 0:64], in_=fold[:],
                                                identity=identf[0:64, 0:64])
                        steps.append(("te", tr_fold))

                        def mk_aT(s, u=u, e=e):
                            s.activation(out=aT[u][e][:], in_=p_tr[0:128, 0:64],
                                         func=AF.Exp)
                        steps.append(("s", mk_aT))

                        def add_eps(v, u=u, e=e):
                            _ts(v, aT[u][e][:], aT[u][e][:], 1e-6, AluOpType.add)
                        steps.append(("v", add_eps))

                        def mk_tinv(s, u=u, e=e):
                            s.activation(out=tsc[:], in_=aT[u][e][:], func=AF.Ln)
                            s.activation(out=tsc[:], in_=tsc[:], func=AF.Exp,
                                         scale=-1.0)
                        steps.append(("s", mk_tinv))
                        if not fm:
                            for cq in range(16):
                                def mm_r(te, e=e, cq=cq):
                                    return te.matmul(out=p_r[:],
                                                     lhsT=ws_e[e][:],
                                                     rhs=k1fm[:, 512 * cq:512 * (cq + 1)],
                                                     start=True, stop=True)
                                steps.append(("te", mm_r))

                                def silu_r(s, cq=cq):
                                    off = (512 * cq) % HSQ
                                    s.activation(out=rowA[:, off:off + 512],
                                                 in_=p_r[:], func=AF.Silu,
                                                 bias=bs_s[0:1, 0:1])
                                steps.append(("s", silu_r))
                                if cq == 7 or cq == 15:
                                    def st_r(sync, cq=cq):
                                        hf = 0 if cq == 7 else 1
                                        dcnt["dB"] += 16
                                        sync.dma_start(out=rowd[HSQ * hf:HSQ * (hf + 1)]
                                                       .unsqueeze(0), in_=rowA[:]) \
                                            .then_inc(S["dB"], 16)
                                        sync.wait_ge(S["dB"], dcnt["dB"])
                                    steps.append(("sync", st_r))

                        def ld_fold2(sync, u=u, e=e):
                            dcnt["dA"] += 16
                            if fm:
                                sync.dma_start(
                                    out=fold[:],
                                    in_=rrow_d[u, e, :].rearrange(
                                        "(c p) -> c p", p=128)) \
                                    .then_inc(S["dA"], 16)
                            else:
                                sync.dma_start(out=fold[:],
                                               in_=rowd.rearrange("(c p) -> c p", p=128)) \
                                    .then_inc(S["dA"], 16)
                            sync.wait_ge(S["dA"], dcnt["dA"])
                        steps.append(("sync", ld_fold2))

                        def tr_fold2(te):
                            return te.transpose(out=p_tr[:, 0:64], in_=fold[:],
                                                identity=identf[0:64, 0:64])
                        steps.append(("te", tr_fold2))

                        def mk_bT(v, u=u, e=e):
                            v.tensor_copy(out=tsc2[:], in_=p_tr[0:128, 0:64])
                            v.tensor_tensor(out=bT[u][e][:], in0=tsc2[:], in1=tsc[:],
                                            op=AluOpType.mult)
                        steps.append(("v", mk_bT))

            emit_prologue()

            @blk.sync
            def _(sync):
                for i, (eng, fn) in enumerate(steps):
                    if eng == "sync":
                        if i > 0:
                            sync.wait_ge(S["seqc"], i)
                        fn(sync)
                        sync.sem_inc(S["seqc"], 1)
            @blk.vector
            def _(v0):
                v = SER["v"]
                for i, (eng, fn) in enumerate(steps):
                    if eng == "v":
                        if i > 0:
                            v.wait_ge(S["seqc"], i)
                        fn(v)
                        v.signal(S["seqc"])
            @blk.gpsimd
            def _(g0):
                g = SER["g"]
                for i, (eng, fn) in enumerate(steps):
                    if eng == "g":
                        if i > 0:
                            g.wait_ge(S["seqc"], i)
                        fn(g)
                        g.signal(S["seqc"])
            @blk.scalar
            def _(s0):
                s = SER["s"]
                for i, (eng, fn) in enumerate(steps):
                    if eng == "s":
                        if i > 0:
                            s.wait_ge(S["seqc"], i)
                        fn(s)
                        s.signal(S["seqc"])
            @blk.tensor
            def _(te):
                te.wait_ge(S["vcon"], 1)
                for i, (eng, fn) in enumerate(steps):
                    if eng == "te":
                        if i > 0:
                            te.wait_ge(S["seqc"], i)
                        inst = fn(te)
                        inst.then_inc(S["seqc"], 1)

        nc.leave_named_scope(tag + "pro", _pro_scope[0], False)
        _mn_scope = nc.enter_named_scope(tag + "mn", False)
        cnt = {"pes": 0, "peo": 0, "pesd": 0, "vS": 0, "vA": 0, "vK": 0,
               "vO1": 0, "vmm": 0, "ost": [0, 0], "tld": dcnt["tld"], "pld": 0,
               "pldg": 0}
        n_units = 2 if upto >= 4 else (1 if upto >= 1 else 0)
        for u in range(n_units):
            _u_scope = nc.enter_named_scope(tag + f"u{u}", False)
            with nc.Block() as blk:
                @blk.sync
                def _(sync, u=u):
                    if u == 1:
                        sync.wait_ge(S["ost0"], 16 * (NCH // 2))
                        sync.wait_ge(S["ost1"], 16 * (NCH // 2))
                    if ld_lvl >= 1:
                        if fm:
                            for srct, dst in ((qfm_loc, qfm), (k1fm_loc, k1fm),
                                              (k2fm_loc, k2fm), (q2fm_loc, q2fm)):
                                cnt["tld"] += 16
                                sync.dma_start(out=dst[:], in_=srct[u]) \
                                    .then_inc(S["tld"], 16)
                        else:
                            for srct, dst in ((qsl, qfm), (k1sl, k1fm),
                                              (k2sl, k2fm), (q2sl, q2fm)):
                                cnt["tld"] += 16
                                sync.dma_start(out=dst[:], in_=srct[u],
                                               transpose=True) \
                                    .then_inc(S["tld"], 16)
                    if fm and ld_lvl >= 2:
                        for j in range(NCH):
                            for srct, dst in ((xhfm_loc, vtok), (k1fm_loc, k1tok)):
                                cnt["pld"] += 16
                                sync.dma_start(out=dst[:, 128 * j:128 * (j + 1)],
                                               in_=srct[u][:, 128 * j:128 * (j + 1)],
                                               transpose=True) \
                                    .then_inc(S["pld"], 16)
                        for j in range(NCH):
                            for srct, dst in ((k2fm_loc, k2tok),
                                              (k2sfm_loc, k2stok)):
                                cnt["pldg"] += 16
                                sync.dma_start(out=dst[:, 128 * j:128 * (j + 1)],
                                               in_=srct[u][:, 128 * j:128 * (j + 1)],
                                               transpose=True) \
                                    .then_inc(S["pldg"], 16)
                @blk.gpsimd
                def _(g0, u=u):
                    g = SER["g"]
                    if u == 1:
                        g.wait_ge(S["ost0"], 16 * (NCH // 2))
                        g.wait_ge(S["ost1"], 16 * (NCH // 2))
                    if ld_lvl >= 2 and not fm:
                        for srct, dst in ((xhsl, vtok), (k1sl, k1tok), (k2sl, k2tok),
                                         (k2ssl, k2stok)):
                            cnt["pld"] += 16
                            g.dma_start(out=dst[:].rearrange("p (n h) -> p n h", h=128),
                                        in_=srct[u].rearrange("(n p) h -> p n h", p=128)) \
                                .then_inc(S["pld"], 16)
                    if ld_lvl >= 2:
                        g0.sem_inc(S["gcon"], 1)

            with nc.Block() as blk:
                @blk.vector
                def _(v0, u=u):
                    v = SER["v"]
                    v.wait_ge(S["tld"], 16 * cnt["tld"] // 16)
                    v.memset(k2s0[:, 0:1], 0.0)
                    v.memset(q2s0[:, 0:1], 0.0)
                    v.tensor_copy(out=k2s0[:, 1:128], in_=k2fm[:, 0:127])
                    v.tensor_copy(out=q2s0[:, 1:128], in_=q2fm[:, 0:127])
                    for e in range(2):
                        v.memset(sbE[e][:], 0.0)
                        v.memset(s2bE[e][:], 0.0)
                        v.memset(sacc[e][:], 0.0)
                        v.memset(s2acc[e][:], 0.0)
                    v.signal(S["vcon"])

            _p1_scope = nc.enter_named_scope(tag + f"p1u{u}", False)
            with nc.Block() as blk:
                @blk.tensor
                def _(te, u=u):
                    if upto < 2:
                        return
                    te.wait_ge(S["vcon"], 2 + u)
                    pld_base = cnt["pld"] - 16 * 2 * NCH

                    def p1_score(j):
                        if fm:
                            te.wait_ge(S["pld"], pld_base + 16 * 2 * (j + 1))
                        elif j == 0:
                            te.wait_ge(S["pld"], cnt["pld"])
                        if j >= 2:
                            te.wait_ge(S["vmm"], cnt["vmm"] + 2 * (j - 1))
                        mm = None
                        for e in range(2):
                            mm = te.matmul(out=p_p[e][j % 2],
                                           lhsT=k1fm[64 * e:64 * (e + 1),
                                                     128 * j:128 * (j + 1)],
                                           rhs=qfm[64 * e:64 * (e + 1),
                                                   128 * j:128 * (j + 1)],
                                           start=True, stop=True)
                        mm.then_inc(S["pes"], 1)

                    p1_score(0)
                    for j in range(NCH):
                        if j + 1 < NCH:
                            p1_score(j + 1)
                        if p1_lvl >= 3:
                            te.wait_ge(S["vA"], cnt["vA"] + 2 * (j + 1))
                            if j >= 1 and p1_lvl >= 4:
                                te.wait_ge(S["vS"], cnt["vS"] + j)
                            if j >= 1:
                                te.wait_ge(S["vO1"], cnt["vO1"] + 2 * j)
                            mm = None
                            for e in range(2):
                                te.matmul(out=p_o[:, 64 * e:64 * (e + 1)],
                                          lhsT=at_[e][:],
                                          rhs=vtok[:, 128 * j + 64 * e:
                                                   128 * j + 64 * (e + 1)],
                                          start=True, stop=False)
                                mm = te.matmul(out=p_o[:, 64 * e:64 * (e + 1)],
                                               lhsT=qfm[:, 128 * j:128 * (j + 1)],
                                               rhs=sbE[e][:],
                                               start=False, stop=True)
                            mm.then_inc(S["peo"], 1)
                        if p1_lvl >= 4:
                            te.wait_ge(S["vK"], cnt["vK"] + 2 * (j + 1))
                            mm = None
                            for e in range(2):
                                mm = te.matmul(out=p_s[e], lhsT=kt_[e][:],
                                               rhs=vtok[:, 128 * j + 64 * e:
                                                        128 * j + 64 * (e + 1)],
                                               start=True, stop=True)
                            mm.then_inc(S["pesd"], 1)

                @blk.vector
                def _(v0, u=u):
                    if upto < 2:
                        return
                    v = SER["v"]
                    for j in range(NCH):
                        if j >= 1 and p1_lvl >= 4:
                            v.wait_ge(S["pesd"], cnt["pesd"] + j)
                            for e in range(2):
                                v.tensor_tensor(out=sacc[e][:], in0=sacc[e][:],
                                                in1=p_s[e], op=AluOpType.add)
                                v.tensor_copy(out=sbE[e][64 * e:64 * (e + 1), :],
                                              in_=sacc[e][:])
                            v.signal(S["vS"])
                        v.wait_ge(S["pes"], cnt["pes"] + j + 1)
                        if j >= 1:
                            v.wait_ge(S["vA"], cnt["vA"] + 2 * j)
                        for e in range(2):
                            v.tensor_tensor(out=mb[e][:], in0=p_p[e][j % 2],
                                            in1=triuI[:], op=AluOpType.mult)
                            v.signal(S["vmm"])

                @blk.scalar
                def _(s0, u=u):
                    if upto < 2 or p1_lvl < 2:
                        return
                    s = SER["s"]
                    pld_base_s = cnt["pld"] - 16 * 2 * NCH
                    for j in range(NCH):
                        if fm:
                            s.wait_ge(S["pld"], pld_base_s + 16 * 2 * (j + 1))
                        elif j == 0:
                            s.wait_ge(S["pld"], cnt["pld"])
                        for e in range(2):
                            s.wait_ge(S["vmm"], cnt["vmm"] + 2 * j + e + 1)
                            s.activation(out=at_[e][:], in_=mb[e][:],
                                         func=AF.Identity,
                                         scale=bT[u][e][:, j:j + 1])
                            s.signal(S["vA"])
                        for e in range(2):
                            if j >= 1 and e == 0 and p1_lvl >= 4:
                                s.wait_ge(S["pesd"], cnt["pesd"] + j)
                            s.activation(
                                out=kt_[e][:],
                                in_=k1tok[:, 128 * j + 64 * e:128 * j + 64 * (e + 1)],
                                func=AF.Identity, scale=bT[u][e][:, j:j + 1])
                            s.signal(S["vK"])
                        if p1_lvl >= 3:
                            s.wait_ge(S["peo"], cnt["peo"] + j + 1)
                            for e in range(2):
                                s.activation(out=o1b[e][:, 64 * j:64 * (j + 1)],
                                             in_=p_o[:, 64 * e:64 * (e + 1)],
                                             func=AF.Identity,
                                             scale=aT[u][e][:, j:j + 1])
                                s.signal(S["vO1"])
                cnt["pes"] += NCH
                cnt["peo"] += NCH
                cnt["pesd"] += NCH
                cnt["vS"] += NCH - 1
                cnt["vA"] += 2 * NCH
                cnt["vK"] += 2 * NCH
                cnt["vO1"] += 2 * NCH
                cnt["vmm"] = cnt.get("vmm", 0) + 2 * NCH

            nc.leave_named_scope(tag + f"p1u{u}", _p1_scope[0], False)
            _p2_scope = nc.enter_named_scope(tag + f"p2u{u}", False)
            with nc.Block() as blk:
                @blk.tensor
                def _(te, u=u):
                    if upto < 3:
                        return
                    te.wait_ge(S["gcon"], 2 + u)

                    def p2_score(j):
                        k2sf = k2s0 if j == 0 else k2fm[:, 128 * j - 1:128 * (j + 1) - 1]
                        q2sf = q2s0 if j == 0 else q2fm[:, 128 * j - 1:128 * (j + 1) - 1]
                        if j >= 2:
                            te.wait_ge(S["vA"], cnt["vA"] + 4 * (j - 1))
                        mm = None
                        for e in range(2):
                            te.matmul(out=p_p[e][j % 2],
                                      lhsT=k2sf[64 * e:64 * (e + 1), :],
                                      rhs=q2sf[64 * e:64 * (e + 1), :],
                                      start=True, stop=True)
                            mm = te.matmul(out=p_p2[e][j % 2],
                                           lhsT=k2fm[64 * e:64 * (e + 1),
                                                     128 * j:128 * (j + 1)],
                                           rhs=q2sf[64 * e:64 * (e + 1), :],
                                           start=True, stop=True)
                        mm.then_inc(S["pes"], 1)

                    p2_score(0)
                    for j in range(NCH):
                        q2sf = q2s0 if j == 0 else q2fm[:, 128 * j - 1:128 * (j + 1) - 1]
                        if j + 1 < NCH:
                            p2_score(j + 1)
                        te.wait_ge(S["vA"], cnt["vA"] + 4 * (j + 1))
                        if j >= 1:
                            te.wait_ge(S["vS"], cnt["vS"] + j)
                            te.wait_ge(S["vO1"], cnt["vO1"] + j)
                        mm = None
                        for e in range(2):
                            te.matmul(out=p_o[:, 64 * e:64 * (e + 1)], lhsT=a2v[e][:],
                                      rhs=vtok[:, 128 * j + 64 * e:
                                               128 * j + 64 * (e + 1)],
                                      start=True, stop=False)
                            te.matmul(out=p_o[:, 64 * e:64 * (e + 1)], lhsT=a2o[e][:],
                                      rhs=o1b[e][:, 64 * j:64 * (j + 1)],
                                      start=False, stop=False)
                            mm = te.matmul(out=p_o[:, 64 * e:64 * (e + 1)],
                                           lhsT=q2sf[:],
                                           rhs=s2bE[e][:],
                                           start=False, stop=True)
                        mm.then_inc(S["peo"], 1)
                        te.wait_ge(S["vK"], cnt["vK"] + 2 * (j + 1))
                        mm = None
                        for e in range(2):
                            te.matmul(out=p_s[e],
                                      lhsT=k2stok[:, 128 * j + 64 * e:
                                                  128 * j + 64 * (e + 1)],
                                      rhs=vtok[:, 128 * j + 64 * e:
                                               128 * j + 64 * (e + 1)],
                                      start=True, stop=False)
                            mm = te.matmul(out=p_s[e], lhsT=k2n[e][:],
                                           rhs=o1b[e][:, 64 * j:64 * (j + 1)],
                                           start=False, stop=True)
                        mm.then_inc(S["pesd"], 1)

                @blk.vector
                def _(v0, u=u):
                    if upto < 3:
                        return
                    v = SER["v"]
                    for j in range(NCH):
                        if j >= 1:
                            v.wait_ge(S["pesd"], cnt["pesd"] + j)
                            for e in range(2):
                                v.tensor_tensor(out=s2acc[e][:], in0=s2acc[e][:],
                                                in1=p_s[e], op=AluOpType.add)
                                v.tensor_copy(out=s2bE[e][64 * e:64 * (e + 1), :],
                                              in_=s2acc[e][:])
                            v.signal(S["vS"])
                        v.wait_ge(S["pes"], cnt["pes"] + j + 1)
                        for e in range(2):
                            v.tensor_tensor(out=a2v[e][:],
                                            in0=p_p[e][j % 2],
                                            in1=triuI[:], op=AluOpType.mult)
                            v.signal(S["vA"])
                            v.tensor_tensor(out=a2o[e][:],
                                            in0=p_p2[e][j % 2],
                                            in1=triuSN[:], op=AluOpType.mult)
                            v.signal(S["vA"])
                        v.wait_ge(S["peo"], cnt["peo"] + j + 1)
                        e2 = j % 2
                        if j >= 2:
                            v.wait_ge(S["ost" + str(e2)], 16 * cnt["ost"][e2] + 16 * (j // 2))
                        for e in range(2):
                            v.tensor_tensor(out=obuf[e2][:, 64 * e:64 * (e + 1)],
                                            in0=p_o[:, 64 * e:64 * (e + 1)],
                                            in1=o1b[e][:, 64 * j:64 * (j + 1)],
                                            op=AluOpType.add)
                        v.signal(S["vO1"])

                @blk.scalar
                def _(s0, u=u):
                    if upto < 3:
                        return
                    s = SER["s"]
                    s.wait_ge(S["pld"], cnt["pld"])
                    if fm:
                        s.wait_ge(S["pldg"], cnt["pldg"])
                    for j in range(NCH):
                        for e in range(2):
                            if j >= 1 and e == 0:
                                s.wait_ge(S["pesd"], cnt["pesd"] + j)
                            s.mul(out=k2n[e][:],
                                  in_=k2tok[:, 128 * j + 64 * e:
                                            128 * j + 64 * (e + 1)],
                                  mul=-1.0)
                            s.signal(S["vK"])

                @blk.sync
                def _(sync, u=u):
                    if upto < 3:
                        return
                    for j in range(NCH):
                        e2 = j % 2
                        sync.wait_ge(S["vO1"], cnt["vO1"] + j + 1)
                        sync.dma_start(out=o_slab[u, 128 * j:128 * (j + 1), :],
                                       in_=obuf[e2][:]).then_inc(S["ost" + str(e2)], 16)
                cnt["pes"] += NCH
                cnt["peo"] += NCH
                cnt["pesd"] += NCH
                cnt["vS"] += NCH - 1
                cnt["vA"] += 4 * NCH
                cnt["vK"] += 2 * NCH
                cnt["vO1"] += NCH
                cnt["ost"][0] += NCH // 2
                cnt["ost"][1] += NCH // 2
            nc.leave_named_scope(tag + f"p2u{u}", _p2_scope[0], False)
            nc.leave_named_scope(tag + f"u{u}", _u_scope[0], False)

        nc.leave_named_scope(tag + "mn", _mn_scope[0], False)
        with nc.Block() as blk:
            @blk.sync
            def _(sync):
                if upto >= 3:
                    sync.wait_ge(S["ost0"], 16 * (NCH // 2) * n_units)
                    sync.wait_ge(S["ost1"], 16 * (NCH // 2) * n_units)
                elif upto >= 1:
                    sync.wait_ge(S["tld"], cnt["tld"])
                    sync.wait_ge(S["pld"], cnt["pld"])
                    if cnt["pldg"]:
                        sync.wait_ge(S["pldg"], cnt["pldg"])
    return nc

def run_gla(nc_cache, x_flat, Wq, bq, Wk1, bk1, Wk2, bk2, Wg, bg, Ws, bs, Wp, bp):
    if "glaa" not in nc_cache:
        nc_cache["glaa"] = build_gla_a()
        nc_cache["glab"] = build_gla_b()
        nc_cache["glac"] = build_att1_c(normalize=False)
    wgh = np.zeros((D, 16), np.float32)
    for h in range(16):
        wgh[64 * h:64 * (h + 1), h] = Wg[:, 0]
    in_maps = []
    for c in range(NCORES):
        in_maps.append({"x": np.ascontiguousarray(x_flat[c * T:(c + 1) * T]),
                        "wq": Wq, "wk1": Wk1, "wk2": Wk2,
                        "bq": bq.reshape(1, D), "bk1": bk1.reshape(1, D),
                        "bk2": bk2.reshape(1, D), "wgh": wgh,
                        "bg": bg.reshape(1, 1)})
    res = run_bass_kernel_spmd(nc_cache["glaa"], in_maps, core_ids=list(range(NCORES)))
    xh = np.concatenate([np.asarray(res.results[c]["xh_out"]) for c in range(NCORES)], 0)
    qf = np.concatenate([np.asarray(res.results[c]["q_out"]) for c in range(NCORES)], 0)
    k1 = np.concatenate([np.asarray(res.results[c]["k1_out"]) for c in range(NCORES)], 0)
    k2 = np.concatenate([np.asarray(res.results[c]["k2_out"]) for c in range(NCORES)], 0)
    q2 = np.concatenate([np.asarray(res.results[c]["q2_out"]) for c in range(NCORES)], 0)
    lg = np.concatenate([np.asarray(res.results[c]["lg_out"]) for c in range(NCORES)], 1)
    k2r = k2.reshape(2, SQ, D)
    k2s = np.zeros_like(k2r)
    k2s[:, 1:] = k2r[:, :-1]
    in_maps = []
    for c in range(NCORES):
        cols = slice(128 * c, 128 * (c + 1))
        hh = slice(2 * c, 2 * c + 2)
        in_maps.append({
            "qsl": np.ascontiguousarray(qf.reshape(2, SQ, D)[:, :, cols]),
            "k1sl": np.ascontiguousarray(k1.reshape(2, SQ, D)[:, :, cols]),
            "k2sl": np.ascontiguousarray(k2r[:, :, cols]),
            "q2sl": np.ascontiguousarray(q2.reshape(2, SQ, D)[:, :, cols]),
            "xhsl": np.ascontiguousarray(xh.reshape(2, SQ, D)[:, :, cols]),
            "k2ssl": np.ascontiguousarray(k2s[:, :, cols]),
            "lgsl": np.ascontiguousarray(
                lg.reshape(16, 2, SQ).transpose(1, 0, 2)[:, hh, :]),
            "ws": Ws, "bs": bs.reshape(1, 1)})
    res = run_bass_kernel_spmd(nc_cache["glab"], in_maps, core_ids=list(range(NCORES)))
    o_full = np.zeros((2, SQ, D), qf.dtype)
    for c in range(NCORES):
        o_full[:, :, 128 * c:128 * (c + 1)] = np.asarray(res.results[c]["o_slab"])
    o_full = o_full.reshape(TOK, D)
    oden_dummy = np.ones((T, 16), np.float32)
    in_maps = []
    for c in range(NCORES):
        rows = slice(c * T, (c + 1) * T)
        in_maps.append({"x": np.ascontiguousarray(x_flat[rows]),
                        "onum": np.ascontiguousarray(o_full[rows]),
                        "oden": oden_dummy,
                        "woT": Wp, "bo": bp.reshape(1, D)})
    res = run_bass_kernel_spmd(nc_cache["glac"], in_maps, core_ids=list(range(NCORES)))
    return np.concatenate([np.asarray(res.results[c]["x_out"]) for c in range(NCORES)], 0)



def kernel_multi(**inputs):
    g = lambda n: np.ascontiguousarray(np.asarray(inputs[n], np.float32))
    x = g("L0_attn_x").reshape(TOK, D)
    nc_cache = {}
    wgT, wuT, wdT = run_wprep(g("L0_ffn_w_gate"), g("L0_ffn_w_up"),
                              g("L0_ffn_w_down"))
    x = run_ffn0(nc_cache, x, wgT, wuT, wdT)
    x = run_gla(nc_cache, x, g("L0_attn_Wq"), g("L0_attn_bq"),
                g("L0_attn_Wk1"), g("L0_attn_bk1"),
                g("L0_attn_Wk2"), g("L0_attn_bk2"),
                g("L0_attn_Wg"), g("L0_attn_bg"),
                g("L0_attn_Ws"), g("L0_attn_bs"),
                g("L0_attn_Wp"), g("L0_attn_bp"))
    x = run_ffn0(nc_cache, x, wgT, wuT, wdT)
    x = run_moe(nc_cache, x, g("L1_ffn_w_gate"), g("L1_ffn_w_expert"),
                g("L1_ffn_b_expert"))
    x = run_att1(nc_cache, x, g("L1_attn_Wq"), g("L1_attn_bq"),
                 g("L1_attn_Wk"), g("L1_attn_bk"),
                 g("L1_attn_Wv"), g("L1_attn_bv"),
                 g("L1_attn_Wo"), g("L1_attn_bo"))
    x = run_moe(nc_cache, x, g("L1_ffn_w_gate"), g("L1_ffn_w_expert"),
                g("L1_ffn_b_expert"))
    return np.ascontiguousarray(x.reshape(B, S, D).astype(np.float32))



def emit_lnag(nc, x_in, xh_loc, xhg, tag=""):
    with ExitStack() as stack:
        names = ("oos", "oov", "oog", "ldv", "ldg", "sxh", "stv", "stg", "ccc")
        S = {n: nc.alloc_semaphore(name=tag + n) for n in names}
        SER = {"s": SerEngine(nc.scalar, S["oos"]),
               "v": SerEngine(nc.vector, S["oov"]),
               "g": SerEngine(nc.gpsimd, S["oog"])}
        x3 = x_in.rearrange("(t p) d -> t p d", p=128)
        xh3 = xh_loc.rearrange("(t p) d -> t p d", p=128)
        _emit_ln_stream(nc, stack, S, SER, x3, xh3, tag=tag)
        with nc.Block() as blk:
            @blk.gpsimd
            def _(g0):
                g0.wait_ge(S["stv"], 16 * (NT // 2))
                g0.wait_ge(S["stg"], 16 * (NT // 2))
                g0.collective_compute(
                    "AllGather", mybir.AluOpType.bypass,
                    replica_groups=[list(range(NCORES))],
                    ins=[xh_loc.opt()], outs=[xhg.opt()],
                ).then_inc(S["ccc"])
                g0.wait_ge(S["ccc"], 1)


def _emit_rope_args(v, angs, rr, tmpr):
    _ts(v, rr[0][:], angs[:], 1.0 / (2 * np.pi), AluOpType.mult)
    _ts(v, rr[0][:], rr[0][:], MAGIC, AluOpType.add)
    _ts(v, rr[0][:], rr[0][:], MAGIC, AluOpType.subtract)
    _ts(v, rr[0][:], rr[0][:], -2 * np.pi, AluOpType.mult)
    v.tensor_tensor(out=rr[0][:], in0=angs[:], in1=rr[0][:], op=AluOpType.add)
    _ts(v, rr[0][:], rr[0][:], -float(np.pi), AluOpType.max,
        float(np.pi), AluOpType.min)
    _ts(v, rr[1][:], angs[:], np.pi / 2, AluOpType.add)
    _ts(v, tmpr[:], rr[1][:], 1.0 / (2 * np.pi), AluOpType.mult)
    _ts(v, tmpr[:], tmpr[:], MAGIC, AluOpType.add)
    _ts(v, tmpr[:], tmpr[:], MAGIC, AluOpType.subtract)
    _ts(v, tmpr[:], tmpr[:], -2 * np.pi, AluOpType.mult)
    v.tensor_tensor(out=rr[1][:], in0=rr[1][:], in1=tmpr[:], op=AluOpType.add)
    _ts(v, rr[1][:], rr[1][:], -float(np.pi), AluOpType.max,
        float(np.pi), AluOpType.min)


def emit_att1proj(nc, xhg, wq_c, wk_c, wv_c, bq_cc, bk_cc, bv_cc, prot_d,
                  invf2_d, qfm_loc, kfm_loc, vfm_loc, tag=""):
    NTB = TOK // 512
    with ExitStack() as stack:
        def sb(name, shape, dt=F32):
            return stack.enter_context(nc.sbuf_tensor(tag + name, shape, dt))
        def psum(name, shape, dt=F32):
            return stack.enter_context(nc.psum_tensor(tag + name, shape, dt))
        names = ("oos", "oov", "oog", "wld", "cld", "xld0", "xld1", "pe",
                 "per", "sev", "vro", "ost", "vcb", "scs", "gcon", "spos")
        S = {n: nc.alloc_semaphore(name=tag + n) for n in names}
        SER = {"s": SerEngine(nc.scalar, S["oos"]),
               "v": SerEngine(nc.vector, S["oov"]),
               "g": SerEngine(nc.gpsimd, S["oog"])}
        w_s = {nm: [sb(f"w{nm}{k}", [128, 128], BF16) for k in range(8)]
               for nm in ("q", "k", "v")}
        prot_b = sb("prot_b", [128, 128], BF16)
        invf2 = sb("invf2", [128, 1])
        bcol = {nm: sb(f"bcol{nm}", [128, 1]) for nm in ("q", "k", "v")}
        xck = [sb(f"xck{e}", [128, 512], BF16) for e in range(2)]
        qsb = sb("qsb", [128, 512], BF16)
        ksb = sb("ksb", [128, 512], BF16)
        vsb = sb("vsb", [128, 512], BF16)
        qro = sb("qro", [128, 512], BF16)
        kro = sb("kro", [128, 512], BF16)
        angs = sb("angs", [128, 512])
        rr = [sb(f"rr{e}", [128, 512]) for e in range(2)]
        tmpr = sb("tmpr", [128, 512])
        sint = sb("sint", [128, 512])
        cost = sb("cost", [128, 512])
        tq = sb("tq", [128, 512])
        tq2 = sb("tq2", [128, 512])
        iotaf = sb("iotaf", [128, 512])
        ang0 = sb("ang0", [128, 512])
        poscol = sb("poscol", [128, 1])
        p_q = psum("p_q", [128, 512])
        p_k = psum("p_k", [128, 512])
        p_v = psum("p_v", [128, 512])
        p_rot = psum("p_rot", [128, 512])

        with nc.Block() as blk:
            @blk.gpsimd
            def _(g0):
                g = SER["g"]
                g.iota(iotaf[:], [[1, 512]], base=0, channel_multiplier=0,
                       allow_small_or_imprecise_dtypes=True)
                g.signal(S["gcon"])
                for nm, wd in (("q", wq_c), ("k", wk_c), ("v", wv_c)):
                    for k in range(8):
                        g.dma_start(out=w_s[nm][k][:],
                                    in_=wd[128 * k:128 * (k + 1), :]) \
                            .then_inc(S["wld"], 16)
                g.dma_start(out=prot_b[:], in_=prot_d[:]).then_inc(S["wld"], 16)
            @blk.sync
            def _(sync):
                sync.dma_start(out=invf2[:], in_=invf2_d[:]).then_inc(S["cld"], 16)
                for nm, bd in (("q", bq_cc), ("k", bk_cc), ("v", bv_cc)):
                    sync.dma_start(out=bcol[nm][:], in_=bd[:]).then_inc(S["cld"], 16)

        with nc.Block() as blk:
            @blk.sync
            def _(sync):
                for tb in range(NTB):
                    for k in range(8):
                        m = tb * 8 + k
                        if m >= 2:
                            sync.wait_ge(S["pe"], 3 * (m - 1))
                        sync.dma_start(
                            out=xck[m % 2][:],
                            in_=xhg[512 * tb:512 * (tb + 1), 128 * k:128 * (k + 1)],
                            transpose=True).then_inc(S["xld" + str(m % 2)], 16)
            @blk.tensor
            def _(te):
                te.wait_ge(S["wld"], 16 * 25)
                for tb in range(NTB):
                    for k in range(8):
                        m = tb * 8 + k
                        te.wait_ge(S["xld" + str(m % 2)], 16 * (m // 2 + 1))
                        if k == 0 and tb >= 1:
                            te.wait_ge(S["sev"], 3 * tb)
                        mm = None
                        for nm, ps in (("q", p_q), ("k", p_k), ("v", p_v)):
                            mm = te.matmul(out=ps[:], lhsT=w_s[nm][k][:],
                                           rhs=xck[m % 2][:],
                                           start=(k == 0), stop=(k == 7))
                            mm.then_inc(S["pe"], 1)
                    te.wait_ge(S["sev"], 3 * tb + 2)
                    te.matmul(out=p_rot[:], lhsT=prot_b[:], rhs=qsb[:],
                              start=True, stop=True).then_inc(S["per"], 1)
                    te.wait_ge(S["vro"], 2 * tb + 1)
                    te.matmul(out=p_rot[:], lhsT=prot_b[:], rhs=ksb[:],
                              start=True, stop=True).then_inc(S["per"], 1)
            @blk.scalar
            def _(s0):
                s = SER["s"]
                s.wait_ge(S["cld"], 16 * 4)
                for tb in range(NTB):
                    if tb >= 1:
                        s.wait_ge(S["vcb"], tb)
                    s.mul(out=poscol[:], in_=invf2[:],
                          mul=float((512 * tb) % SQ))
                    s.signal(S["spos"])
                    if tb >= 1:
                        s.wait_ge(S["vro"], 2 * tb)
                    s.wait_ge(S["pe"], 24 * tb + 22)
                    s.activation(out=qsb[:], in_=p_q[:], func=AF.Identity,
                                 bias=bcol["q"][:])
                    s.signal(S["sev"])
                    s.wait_ge(S["pe"], 24 * tb + 23)
                    s.activation(out=ksb[:], in_=p_k[:], func=AF.Identity,
                                 bias=bcol["k"][:])
                    s.signal(S["sev"])
                    s.wait_ge(S["pe"], 24 * tb + 24)
                    if tb >= 1:
                        s.wait_ge(S["ost"], 16 * 3 * tb)
                    s.activation(out=vsb[:], in_=p_v[:], func=AF.Identity,
                                 bias=bcol["v"][:])
                    s.signal(S["sev"])
                    s.wait_ge(S["vcb"], tb + 1)
                    s.activation(out=sint[:], in_=rr[0][:], func=AF.Sin)
                    s.signal(S["scs"])
                    s.activation(out=cost[:], in_=rr[1][:], func=AF.Sin)
                    s.signal(S["scs"])
            @blk.vector
            def _(v0):
                v = SER["v"]
                v.wait_ge(S["cld"], 16)
                v.wait_ge(S["gcon"], 1)
                _ts(v, ang0[:], iotaf[:], invf2[:, 0:1], AluOpType.mult)
                for tb in range(NTB):
                    if tb >= 1:
                        v.wait_ge(S["scs"], 2 * tb)
                    v.wait_ge(S["spos"], tb + 1)
                    _ts(v, angs[:], ang0[:], poscol[:, 0:1], AluOpType.add)
                    _emit_rope_args(v, angs, rr, tmpr)
                    v.signal(S["vcb"])
                    v.wait_ge(S["per"], 2 * tb + 1)
                    v.wait_ge(S["scs"], 2 * tb + 2)
                    if tb >= 1:
                        v.wait_ge(S["ost"], 16 * 3 * tb)
                    v.tensor_tensor(out=tq[:], in0=qsb[:], in1=cost[:],
                                    op=AluOpType.mult)
                    v.tensor_tensor(out=tq2[:], in0=p_rot[:], in1=sint[:],
                                    op=AluOpType.mult)
                    v.tensor_tensor(out=qro[:], in0=tq[:], in1=tq2[:],
                                    op=AluOpType.add)
                    v.signal(S["vro"])
                    v.wait_ge(S["per"], 2 * tb + 2)
                    v.tensor_tensor(out=tq[:], in0=ksb[:], in1=cost[:],
                                    op=AluOpType.mult)
                    v.tensor_tensor(out=tq2[:], in0=p_rot[:], in1=sint[:],
                                    op=AluOpType.mult)
                    v.tensor_tensor(out=kro[:], in0=tq[:], in1=tq2[:],
                                    op=AluOpType.add)
                    v.signal(S["vro"])
            @blk.gpsimd
            def _(g0):
                g = SER["g"]
                for tb in range(NTB):
                    u = (512 * tb) // SQ
                    c0 = (512 * tb) % SQ
                    g.wait_ge(S["vro"], 2 * tb + 1)
                    g.dma_start(out=qfm_loc[u][:, c0:c0 + 512], in_=qro[:]) \
                        .then_inc(S["ost"], 16)
                    g.wait_ge(S["vro"], 2 * tb + 2)
                    g.dma_start(out=kfm_loc[u][:, c0:c0 + 512], in_=kro[:]) \
                        .then_inc(S["ost"], 16)
                    g.wait_ge(S["sev"], 3 * tb + 3)
                    g.dma_start(out=vfm_loc[u][:, c0:c0 + 512], in_=vsb[:]) \
                        .then_inc(S["ost"], 16)
        with nc.Block() as blk:
            @blk.sync
            def _(sync):
                sync.wait_ge(S["ost"], 16 * 3 * NTB)


def emit_dproj(nc, kind, o_src, wo_c, bo_d, x_in, part_loc, rs_loc, x_out,
               rdn_d=None, tag=""):
    with ExitStack() as stack:
        def sb(name, shape, dt=F32):
            return stack.enter_context(nc.sbuf_tensor(tag + name, shape, dt))
        def psum(name, shape, dt=F32):
            return stack.enter_context(nc.psum_tensor(tag + name, shape, dt))
        names = ("oos", "oov", "oog", "wld", "old", "dld", "snm", "rst", "rtl",
                 "pe", "vev", "pst", "ccc", "rld", "xld", "vres", "ost",
                 "pbb", "vbb", "sev2")
        S = {n: nc.alloc_semaphore(name=tag + n) for n in names}
        SER = {"s": SerEngine(nc.scalar, S["oos"]),
               "v": SerEngine(nc.vector, S["oov"]),
               "g": SerEngine(nc.gpsimd, S["oog"])}
        wo_s = [sb(f"wo_s{nb}", [128, 512], BF16) for nb in range(2)]
        o2h = sb("o2h", [128, SQ], BF16)
        att = kind == "att1"
        if att:
            denT = sb("denT", [128, SQ // 128], BF16)
            rdnT = [sb(f"rdnT{e}", [128, SQ // 128]) for e in range(2)]
            ta0s = [[sb(f"ta0s{s2}_{nb}", [128, 512]) for nb in range(2)]
                    for s2 in range(2)]
            ta1 = sb("ta1", [128, 512])
        pout = [sb(f"pout{e}", [128, D], BF16) for e in range(2)]
        p_pt = [psum(f"p_pt{j}", [128, 512]) for j in range(8 if att else 4)]
        part3 = part_loc.rearrange("(t p) d -> t p d", p=128)
        NCK = SQ // 128

        with nc.Block() as blk:
            @blk.gpsimd
            def _(g0):
                g = SER["g"]
                for nb in range(2):
                    g.dma_start(out=wo_s[nb][:],
                                in_=wo_c[:, 512 * nb:512 * (nb + 1)]) \
                        .then_inc(S["wld"], 16)

        for u in range(2):
            if att:
                with nc.Block() as blk:
                    @blk.sync
                    def _(sync, u=u):
                        if u == 1:
                            sync.wait_ge(S["pst"], 16 * NCK)
                            sync.wait_ge(S["vev"], NCK)
                        for e in range(2):
                            sync.dma_start(out=o2h[64 * e:64 * (e + 1), :],
                                           in_=o_src[u, e, 0:64, :]) \
                                .then_inc(S["old"], 16)
                        for e in range(2):
                            n = 2 * u + e
                            if n >= 1:
                                sync.wait_ge(S["snm"], 2 * n)
                            sync.dma_start(
                                out=denT[:],
                                in_=o_src[u, e, 64, :].rearrange(
                                    "(c p) -> c p", p=128),
                                transpose=True).then_inc(S["dld"], 16)
                            sync.wait_ge(S["dld"], 16 * (n + 1))
                    @blk.scalar
                    def _(s0, u=u):
                        s = SER["s"]
                        for e in range(2):
                            n = 2 * u + e
                            s.wait_ge(S["dld"], 16 * (n + 1))
                            s.activation(out=rdnT[e][:], in_=denT[:], func=AF.Ln)
                            s.signal(S["snm"])
                            s.activation(out=rdnT[e][:], in_=rdnT[e][:],
                                         func=AF.Exp, scale=-1.0)
                            s.signal(S["snm"])
            else:
                with nc.Block() as blk:
                    @blk.sync
                    def _(sync, u=u):
                        if u == 1:
                            sync.wait_ge(S["pst"], 16 * NCK)
                        for j in range(NCK):
                            sync.dma_start(out=o2h[:, 128 * j:128 * (j + 1)],
                                           in_=o_src[u, 128 * j:128 * (j + 1), :],
                                           transpose=True).then_inc(S["old"], 16)

            with nc.Block() as blk:
                @blk.tensor
                def _(te, u=u):
                    te.wait_ge(S["wld"], 16 * 2)
                    te.wait_ge(S["old"], 16 * (2 if att else NCK) * (u + 1))
                    for tc in range(NCK):
                        mm = None
                        if att:
                            if tc >= 2:
                                te.wait_ge(S["vev"], u * NCK + tc - 1)
                            for nb in range(2):
                                for e in range(2):
                                    mm = te.matmul(
                                        out=p_pt[4 * (tc % 2) + 2 * nb + e][:],
                                        lhsT=o2h[64 * e:64 * (e + 1),
                                                 128 * tc:128 * (tc + 1)],
                                        rhs=wo_s[nb][64 * e:64 * (e + 1), :],
                                        start=True, stop=True)
                        else:
                            if tc >= 2:
                                te.wait_ge(S["vev"], u * NCK + tc - 1)
                            for nb in range(2):
                                mm = te.matmul(out=p_pt[2 * (tc % 2) + nb][:],
                                               lhsT=o2h[:, 128 * tc:128 * (tc + 1)],
                                               rhs=wo_s[nb][:],
                                               start=True, stop=True)
                        mm.then_inc(S["pe"], 1)
                @blk.vector
                def _(v0, u=u):
                    v = SER["v"]
                    if att:
                        v.wait_ge(S["snm"], 4 * u + 4)
                    for tc in range(NCK):
                        v.wait_ge(S["pe"], u * NCK + tc + 1)
                        if tc >= 2:
                            v.wait_ge(S["pst"], 16 * (u * NCK + tc - 1))
                        for nb in range(2):
                            if att:
                                _ts(v, ta0s[0][0][:],
                                    p_pt[4 * (tc % 2) + 2 * nb][:],
                                    rdnT[0][:, tc:tc + 1], AluOpType.mult)
                                _ts(v, ta1[:], p_pt[4 * (tc % 2) + 2 * nb + 1][:],
                                    rdnT[1][:, tc:tc + 1], AluOpType.mult)
                                v.tensor_tensor(
                                    out=pout[tc % 2][:, 512 * nb:512 * (nb + 1)],
                                    in0=ta0s[0][0][:], in1=ta1[:], op=AluOpType.add)
                            else:
                                v.tensor_copy(
                                    out=pout[tc % 2][:, 512 * nb:512 * (nb + 1)],
                                    in_=p_pt[2 * (tc % 2) + nb][:])
                        v.signal(S["vev"])
                @blk.gpsimd
                def _(g0, u=u):
                    g = SER["g"]
                    for tc in range(NCK):
                        g.wait_ge(S["vev"], u * NCK + tc + 1)
                        g.dma_start(out=part3[u * NCK + tc], in_=pout[tc % 2][:]) \
                            .then_inc(S["pst"], 16)

        with nc.Block() as blk:
            @blk.gpsimd
            def _(g0):
                g0.wait_ge(S["pst"], 16 * 2 * NCK)
                g0.collective_compute(
                    "ReduceScatter", mybir.AluOpType.add,
                    replica_groups=[list(range(NCORES))],
                    ins=[part_loc.opt()], outs=[rs_loc.opt()],
                ).then_inc(S["ccc"])
                g0.wait_ge(S["ccc"], 1)
        with ExitStack() as stk2:
            def sb2(name, shape, dt=F32):
                return stk2.enter_context(nc.sbuf_tensor(tag + name, shape, dt))
            xt2 = [sb2(f"xt2{e}", [128, D]) for e in range(2)]
            rt2 = [sb2(f"rt2{e}", [128, D], BF16) for e in range(2)]
            ot2 = [sb2(f"ot2{e}", [128, D]) for e in range(2)]
            bosb = sb2("bosb", [1, D], BF16)
            onescol = sb2("onescol", [1, 128], BF16)
            bb = sb2("bb", [128, D])
            p_bb = p_pt[0]
            x3 = x_in.rearrange("(t p) d -> t p d", p=128)
            rs3 = rs_loc.rearrange("(t p) d -> t p d", p=128)
            xo3 = x_out.rearrange("(t p) d -> t p d", p=128)
            with nc.Block() as blk:
                @blk.gpsimd
                def _(g0):
                    g = SER["g"]
                    g.memset(onescol[:], 1.0)
                    g.dma_start(out=bosb[:], in_=bo_d[:]).then_inc(S["rld"], 16)
                @blk.sync
                def _(sync):
                    for i in range(NT):
                        if i >= 2:
                            sync.wait_ge(S["vres"], i - 1)
                        sync.dma_start(out=xt2[i % 2][:], in_=x3[i]) \
                            .then_inc(S["xld"], 16)
                        sync.dma_start(out=rt2[i % 2][:], in_=rs3[i]) \
                            .then_inc(S["xld"], 16)
                @blk.tensor
                def _(te):
                    te.wait_ge(S["rld"], 16)
                    for nb in range(2):
                        te.matmul(out=p_bb[:], lhsT=onescol[:],
                                  rhs=bosb[:, 512 * nb:512 * (nb + 1)],
                                  start=True, stop=True).then_inc(S["pbb"], 1)
                        te.wait_ge(S["vbb"], nb + 1)
                @blk.vector
                def _(v0):
                    v = SER["v"]
                    for nb in range(2):
                        v.wait_ge(S["pbb"], nb + 1)
                        v.tensor_copy(out=bb[:, 512 * nb:512 * (nb + 1)],
                                      in_=p_bb[:])
                        v.signal(S["vbb"])
                    for i in range(NT):
                        v.wait_ge(S["xld"], 16 * 2 * (i + 1))
                        if i >= 2:
                            v.wait_ge(S["ost"], 16 * (i - 1))
                        v.tensor_tensor(out=ot2[i % 2][:], in0=xt2[i % 2][:],
                                        in1=rt2[i % 2][:], op=AluOpType.add)
                        v.tensor_tensor(out=ot2[i % 2][:], in0=ot2[i % 2][:],
                                        in1=bb[:], op=AluOpType.add)
                        v.signal(S["vres"])
                @blk.gpsimd
                def _(g0):
                    g = SER["g"]
                    for i in range(NT):
                        g.wait_ge(S["vres"], i + 1)
                        g.dma_start(out=xo3[i], in_=ot2[i % 2][:]) \
                            .then_inc(S["ost"], 16)
                    g.wait_ge(S["ost"], 16 * NT)


def emit_glaproj(nc, xhg, wq_c, wk1_c, wk2_c, sel_c, wgh2_c, bq_cc, bk1_cc,
                 bk2_cc, bg2_d, qfm_loc, k1fm_loc, k2fm_loc, k2sfm_loc,
                 q2fm_loc, xhfm_loc, lgsl_loc, ws2_d=None, bs2_d=None,
                 rrow_d=None, tag=""):
    NTB = TOK // 512
    with ExitStack() as stack:
        def sb(name, shape, dt=F32):
            return stack.enter_context(nc.sbuf_tensor(tag + name, shape, dt))
        def psum(name, shape, dt=F32):
            return stack.enter_context(nc.psum_tensor(tag + name, shape, dt))
        names = ("oos", "oov", "oog", "wld", "cld", "xld0", "xld1", "pe",
                 "sev", "vq2", "vxh", "vsg", "slg", "stq", "stk1", "stk2",
                 "stq2", "stxh", "stlg", "zst", "vkb", "per2", "srr", "strr")
        S = {n: nc.alloc_semaphore(name=tag + n) for n in names}
        SER = {"s": SerEngine(nc.scalar, S["oos"]),
               "v": SerEngine(nc.vector, S["oov"]),
               "g": SerEngine(nc.gpsimd, S["oog"])}
        w_s = {nm: [sb(f"w{nm}{k}", [128, 128], BF16) for k in range(8)]
               for nm in ("q", "k1", "k2", "sl")}
        wg_s = [sb(f"wgs{k}", [128, 2], BF16) for k in range(8)]
        bqc = sb("bqc", [128, 1]); bk1c = sb("bk1c", [128, 1])
        bk2c = sb("bk2c", [128, 1]); bk2sc = sb("bk2sc", [128, 1])
        bg2c = sb("bg2c", [2, 1])
        zcol = sb("zcol", [128, 1], BF16)
        xck = [sb(f"xck{e}", [128, 512], BF16) for e in range(2)]
        qf = sb("qf", [128, 512])
        k1sb = sb("k1sb", [128, 512])
        k1b = sb("k1b", [128, 512], BF16)
        ws2_b = sb("ws2_b", [128, 2], BF16)
        bs2c = sb("bs2c", [2, 1])
        rrowb = sb("rrowb", [2, 512])
        k2sb = sb("k2sb", [128, 512], BF16)
        q2b = sb("q2b", [128, 512], BF16)
        xsb = sb("xsb", [128, 512], BF16)
        qn1 = sb("qn1", [128, 512]); qab = sb("qab", [128, 512])
        glg = sb("glg", [2, 512])
        p_q = psum("p_q", [128, 512])
        p_k1 = psum("p_k1", [128, 512])
        p_k2 = psum("p_k2", [128, 512])
        p_xh = psum("p_xh", [128, 512])
        p_g = psum("p_g", [2, 512])
        p_r2 = psum("p_r2", [2, 512])

        with nc.Block() as blk:
            @blk.gpsimd
            def _(g0):
                g = SER["g"]
                g.memset(zcol[:], 0.0)
                g.signal(S["zst"])
                for nm, wd in (("q", wq_c), ("k1", wk1_c), ("k2", wk2_c),
                               ("sl", sel_c)):
                    for k in range(8):
                        g.dma_start(out=w_s[nm][k][:],
                                    in_=wd[128 * k:128 * (k + 1), :]) \
                            .then_inc(S["wld"], 16)
                for k in range(8):
                    g.dma_start(out=wg_s[k][:],
                                in_=wgh2_c[128 * k:128 * (k + 1), :]) \
                        .then_inc(S["wld"], 16)
                if rrow_d is not None:
                    g.dma_start(out=ws2_b[:], in_=ws2_d[:]).then_inc(S["wld"], 16)
            @blk.sync
            def _(sync):
                pairs = [(bqc, bq_cc), (bk1c, bk1_cc), (bk2c, bk2_cc),
                         (bg2c, bg2_d)]
                if rrow_d is not None:
                    pairs.append((bs2c, bs2_d))
                for dst, srcd in pairs:
                    sync.dma_start(out=dst[:], in_=srcd[:]).then_inc(S["cld"], 16)
            @blk.scalar
            def _(s0):
                s = SER["s"]
                s.wait_ge(S["cld"], 16 * 4)
                s.mul(out=bk2sc[:], in_=bk2c[:], mul=K2SC)
                s.signal(S["slg"])
        with nc.Block() as blk:
            @blk.gpsimd
            def _(g0):
                g = SER["g"]
                g.wait_ge(S["zst"], 1)
                for u in range(2):
                    g.dma_start(out=k2sfm_loc[u][:, 0:1], in_=zcol[:],
                                allow_slow_non_contiguous=True) \
                        .then_inc(S["stk2"], 16)

        with nc.Block() as blk:
            @blk.sync
            def _(sync):
                for tb in range(NTB):
                    u = (512 * tb) // SQ
                    c0 = (512 * tb) % SQ
                    for k in range(8):
                        m = tb * 8 + k
                        if m >= 2:
                            sync.wait_ge(S["pe"], m - 1)
                        sync.dma_start(
                            out=xck[m % 2][:],
                            in_=xhg[512 * tb:512 * (tb + 1), 128 * k:128 * (k + 1)],
                            transpose=True).then_inc(S["xld" + str(m % 2)], 16)
                    sync.wait_ge(S["slg"], tb + 2)
                    sync.dma_start(out=lgsl_loc[u, :, c0:c0 + 512], in_=glg[:]) \
                        .then_inc(S["stlg"], 16)
                    if rrow_d is not None:
                        sync.wait_ge(S["srr"], tb + 1)
                        sync.dma_start(out=rrow_d[u, :, c0:c0 + 512], in_=rrowb[:]) \
                            .then_inc(S["strr"], 16)
            @blk.tensor
            def _(te):
                te.wait_ge(S["wld"], 16 * (41 if rrow_d is not None else 40))
                for tb in range(NTB):
                    for k in range(8):
                        m = tb * 8 + k
                        te.wait_ge(S["xld" + str(m % 2)], 16 * (m // 2 + 1))
                        if k == 0 and tb >= 1:
                            te.wait_ge(S["sev"], 4 * tb)
                            te.wait_ge(S["vxh"], tb)
                        mm = None
                        for nm, ps in (("q", p_q), ("k1", p_k1), ("k2", p_k2),
                                       ("sl", p_xh)):
                            te.matmul(out=ps[:], lhsT=w_s[nm][k][:],
                                      rhs=xck[m % 2][:],
                                      start=(k == 0), stop=(k == 7))
                        mm = te.matmul(out=p_g[:], lhsT=wg_s[k][:],
                                       rhs=xck[m % 2][:],
                                       start=(k == 0), stop=(k == 7))
                        mm.then_inc(S["pe"], 1)
                    if rrow_d is not None:
                        te.wait_ge(S["vkb"], tb + 1)
                        if tb >= 1:
                            te.wait_ge(S["srr"], tb)
                        te.matmul(out=p_r2[:], lhsT=ws2_b[:], rhs=k1b[:],
                                  start=True, stop=True).then_inc(S["per2"], 1)
            @blk.scalar
            def _(s0):
                s = SER["s"]
                for tb in range(NTB):
                    s.wait_ge(S["pe"], 8 * tb + 8)
                    if tb >= 1:
                        s.wait_ge(S["vq2"], tb)
                        s.wait_ge(S["stq"], 16 * tb)
                    s.activation(out=qf[:], in_=p_q[:], func=AF.Identity,
                                 bias=bqc[:])
                    s.signal(S["sev"])
                    if tb >= 1:
                        s.wait_ge(S["stk1"], 16 * tb)
                    s.activation(out=k1sb[:], in_=p_k1[:], func=AF.Identity,
                                 bias=bk1c[:])
                    s.signal(S["sev"])
                    if tb >= 1:
                        s.wait_ge(S["stk2"], 32 + 32 * tb)
                    s.activation(out=k2sb[:], in_=p_k2[:], func=AF.Sigmoid,
                                 scale=K2SC, bias=bk2sc[:])
                    s.signal(S["sev"])
                    if tb >= 1:
                        s.wait_ge(S["stlg"], 16 * tb)
                    s.activation(out=glg[:], in_=p_g[:], func=AF.Sigmoid,
                                 bias=bg2c[:])
                    s.signal(S["sev"])
                    s.wait_ge(S["vsg"], tb + 1)
                    s.activation(out=glg[:], in_=glg[:], func=AF.Ln)
                    s.signal(S["slg"])
                    if rrow_d is not None:
                        s.wait_ge(S["per2"], tb + 1)
                        if tb >= 1:
                            s.wait_ge(S["strr"], 16 * tb)
                        s.activation(out=rrowb[:], in_=p_r2[:], func=AF.Silu,
                                     bias=bs2c[:])
                        s.signal(S["srr"])
            @blk.vector
            def _(v0):
                v = SER["v"]
                for tb in range(NTB):
                    v.wait_ge(S["sev"], 4 * tb + 1)
                    if tb >= 1:
                        v.wait_ge(S["stq2"], 16 * tb)
                    _ts(v, qn1[:], qf[:], -1.0, AluOpType.mult)
                    v.tensor_tensor(out=qab[:], in0=qf[:], in1=qn1[:],
                                    op=AluOpType.max)
                    _ts(v, qn1[:], qf[:], 0.06375, AluOpType.mult)
                    v.scalar_tensor_tensor(out=q2b[:], in0=qab[:],
                                           scalar=-0.06125, in1=qn1[:],
                                           op0=AluOpType.mult, op1=AluOpType.add)
                    v.signal(S["vq2"])
                    v.wait_ge(S["pe"], 8 * tb + 8)
                    if tb >= 1:
                        v.wait_ge(S["stxh"], 16 * tb)
                    v.tensor_copy(out=xsb[:], in_=p_xh[:])
                    v.signal(S["vxh"])
                    if rrow_d is not None:
                        v.wait_ge(S["sev"], 4 * tb + 2)
                        if tb >= 1:
                            v.wait_ge(S["per2"], tb)
                        v.tensor_copy(out=k1b[:], in_=k1sb[:])
                        v.signal(S["vkb"])
                    v.wait_ge(S["sev"], 4 * tb + 4)
                    _ts(v, glg[:], glg[:], 1e-6, AluOpType.max)
                    v.signal(S["vsg"])
            @blk.gpsimd
            def _(g0):
                g = SER["g"]
                for tb in range(NTB):
                    u = (512 * tb) // SQ
                    c0 = (512 * tb) % SQ
                    g.wait_ge(S["sev"], 4 * tb + 1)
                    g.dma_start(out=qfm_loc[u][:, c0:c0 + 512], in_=qf[:]) \
                        .then_inc(S["stq"], 16)
                    g.wait_ge(S["sev"], 4 * tb + 2)
                    g.dma_start(out=k1fm_loc[u][:, c0:c0 + 512], in_=k1sb[:]) \
                        .then_inc(S["stk1"], 16)
                    g.wait_ge(S["sev"], 4 * tb + 3)
                    g.dma_start(out=k2fm_loc[u][:, c0:c0 + 512], in_=k2sb[:]) \
                        .then_inc(S["stk2"], 16)
                    if c0 + 512 < SQ:
                        g.dma_start(out=k2sfm_loc[u][:, c0 + 1:c0 + 513],
                                    in_=k2sb[:]).then_inc(S["stk2"], 16)
                    else:
                        g.dma_start(out=k2sfm_loc[u][:, c0 + 1:SQ],
                                    in_=k2sb[:, 0:511]).then_inc(S["stk2"], 16)
                    g.wait_ge(S["vq2"], tb + 1)
                    g.dma_start(out=q2fm_loc[u][:, c0:c0 + 512], in_=q2b[:]) \
                        .then_inc(S["stq2"], 16)
                    g.wait_ge(S["vxh"], tb + 1)
                    g.dma_start(out=xhfm_loc[u][:, c0:c0 + 512], in_=xsb[:]) \
                        .then_inc(S["stxh"], 16)
        with nc.Block() as blk:
            @blk.sync
            def _(sync):
                sync.wait_ge(S["stq"], 16 * NTB)
                sync.wait_ge(S["stk1"], 16 * NTB)
                sync.wait_ge(S["stk2"], 32 + 32 * NTB)
                sync.wait_ge(S["stq2"], 16 * NTB)
                sync.wait_ge(S["stxh"], 16 * NTB)
                sync.wait_ge(S["stlg"], 16 * NTB)
                if rrow_d is not None:
                    sync.wait_ge(S["strr"], 16 * NTB)


def _fused_att1_tensors(nc):
    TEN = {}
    TEN["x"] = nc.dram_tensor("x", [T, D], F32, kind="ExternalInput").ap()
    for n in ("waq_c", "wak_c", "wav_c"):
        TEN[n] = nc.dram_tensor(n, [D, 128], F32, kind="ExternalInput").ap()
    for n in ("baq_cc", "bak_cc", "bav_cc"):
        TEN[n] = nc.dram_tensor(n, [128, 1], F32, kind="ExternalInput").ap()
    TEN["prot"] = nc.dram_tensor("prot", [128, 128], F32, kind="ExternalInput").ap()
    TEN["invf2"] = nc.dram_tensor("invf2", [128, 1], F32, kind="ExternalInput").ap()
    TEN["wo_c"] = nc.dram_tensor("wo_c", [128, D], F32, kind="ExternalInput").ap()
    TEN["bo"] = nc.dram_tensor("bo", [1, D], F32, kind="ExternalInput").ap()
    TEN["x_out"] = nc.dram_tensor("x_out", [T, D], F32, kind="ExternalOutput").ap()
    return TEN


def _fused_att1_scratch(nc, TEN, pfx=""):
    TEN["xh_loc"] = nc.dram_tensor(pfx + "xh_loc", [T, D], BF16).ap()
    TEN["xhg"] = nc.dram_tensor(pfx + "xhg", [TOK, D], BF16,
                                addr_space="Shared").ap()
    for n in ("qfm_loc", "kfm_loc", "vfm_loc"):
        TEN[n] = nc.dram_tensor(pfx + n, [2, 128, SQ], BF16).ap()
    TEN["oT_loc"] = nc.dram_tensor(pfx + "oT_loc", [2, 2, 65, SQ], BF16).ap()
    TEN["part_loc"] = nc.dram_tensor(pfx + "part_loc", [TOK, D], BF16).ap()
    TEN["rs_loc"] = nc.dram_tensor(pfx + "rs_loc", [T, D], BF16).ap()
    TEN["rdn_d"] = nc.dram_tensor(pfx + "rdn_d", [2, SQ], F32).ap()


def emit_att1_fused(nc, TEN, x_in, x_out, pfx="", pre_ln=False):
    with nc.cleanup_on_exit():
        if not pre_ln:
            with nc.named_scope(pfx + "a_lnag"):
                emit_lnag(nc, x_in, TEN["xh_loc"], TEN["xhg"], tag=pfx + "la_")
        with nc.named_scope(pfx + "a_proj"):
            emit_att1proj(nc, TEN["xhg"], TEN["waq_c"], TEN["wak_c"],
                          TEN["wav_c"], TEN["baq_cc"], TEN["bak_cc"],
                          TEN["bav_cc"], TEN["prot"], TEN["invf2"],
                          TEN["qfm_loc"], TEN["kfm_loc"], TEN["vfm_loc"],
                          tag=pfx + "pj_")
        with nc.named_scope(pfx + "a_flash"):
            emit_att1b(nc, {"qfm_loc": TEN["qfm_loc"],
                            "kfm_loc": TEN["kfm_loc"],
                            "vfm_loc": TEN["vfm_loc"],
                            "oT_out": TEN["oT_loc"]},
                       tag=pfx + "fa_", fm=True)
        with nc.named_scope(pfx + "a_dproj"):
            emit_dproj(nc, "att1", TEN["oT_loc"], TEN["wo_c"], TEN["bo"], x_in,
                       TEN["part_loc"], TEN["rs_loc"], x_out,
                       rdn_d=TEN["rdn_d"], tag=pfx + "dp_")


def build_att1_fused():
    nc = bass.Bass("TRN2", target_bir_lowering=False)
    TEN = _fused_att1_tensors(nc)
    _fused_att1_scratch(nc, TEN)
    emit_att1_fused(nc, TEN, TEN["x"], TEN["x_out"])
    return nc


def _att1_fused_inmaps(x_flat, Wq, bq, Wk, bk, Wv, bv, Wo, bo):
    invf2 = (1.0 / 10000.0 ** ((np.arange(128) % 64 // 2).astype(np.float32)
                               / 32.0)).astype(np.float32).reshape(128, 1)
    prot = np.zeros((128, 128), np.float32)
    for i in range(64):
        prot[2 * i + 1, 2 * i] = -1.0
        prot[2 * i, 2 * i + 1] = 1.0
    wqT = Wq.T; wkT = Wk.T; wvT = Wv.T; woT = Wo.T
    in_maps = []
    for c in range(NCORES):
        cols = slice(128 * c, 128 * (c + 1))
        in_maps.append({
            "x": np.ascontiguousarray(x_flat[c * T:(c + 1) * T]),
            "waq_c": np.ascontiguousarray(wqT[:, cols]),
            "wak_c": np.ascontiguousarray(wkT[:, cols]),
            "wav_c": np.ascontiguousarray(wvT[:, cols]),
            "baq_cc": np.ascontiguousarray(bq[cols].reshape(128, 1)),
            "bak_cc": np.ascontiguousarray(bk[cols].reshape(128, 1)),
            "bav_cc": np.ascontiguousarray(bv[cols].reshape(128, 1)),
            "prot": prot, "invf2": invf2,
            "wo_c": np.ascontiguousarray(woT[cols, :]),
            "bo": bo.reshape(1, D)})
    return in_maps


def run_att1_fused(nc_cache, x_flat, Wq, bq, Wk, bk, Wv, bv, Wo, bo):
    if "att1f" not in nc_cache:
        nc_cache["att1f"] = build_att1_fused()
    in_maps = _att1_fused_inmaps(x_flat, Wq, bq, Wk, bk, Wv, bv, Wo, bo)
    res = run_bass_kernel_spmd(nc_cache["att1f"], in_maps,
                               core_ids=list(range(NCORES)))
    return np.concatenate([np.asarray(res.results[c]["x_out"])
                           for c in range(NCORES)], 0)


def build_att1_fused_dbg(upto=4):
    nc = bass.Bass("TRN2", target_bir_lowering=False)
    TEN = _fused_att1_tensors(nc)
    _fused_att1_scratch(nc, TEN)
    dumps = []
    if upto >= 1:
        with nc.cleanup_on_exit():
            emit_lnag(nc, TEN["x"], TEN["xh_loc"], TEN["xhg"], tag="la_")
        dumps.append(("xhg_dbg", TEN["xhg"], [TOK, D], BF16))
    if upto >= 2:
        with nc.cleanup_on_exit():
            emit_att1proj(nc, TEN["xhg"], TEN["waq_c"], TEN["wak_c"],
                          TEN["wav_c"], TEN["baq_cc"], TEN["bak_cc"],
                          TEN["bav_cc"], TEN["prot"], TEN["invf2"],
                          TEN["qfm_loc"], TEN["kfm_loc"], TEN["vfm_loc"],
                          tag="pj_")
        for n in ("qfm_loc", "kfm_loc", "vfm_loc"):
            dumps.append((n + "_dbg", TEN[n], [2, 128, SQ], BF16))
    if upto >= 3:
        with nc.cleanup_on_exit():
            emit_att1b(nc, {"qfm_loc": TEN["qfm_loc"], "kfm_loc": TEN["kfm_loc"],
                            "vfm_loc": TEN["vfm_loc"], "oT_out": TEN["oT_loc"]},
                       tag="fa_", fm=True)
        dumps.append(("oT_dbg", TEN["oT_loc"], [2, 2, 65, SQ], BF16))
    if upto >= 4:
        with nc.cleanup_on_exit():
            emit_dproj(nc, "att1", TEN["oT_loc"], TEN["wo_c"], TEN["bo"],
                       TEN["x"], TEN["part_loc"], TEN["rs_loc"], TEN["x_out"],
                       rdn_d=TEN["rdn_d"], tag="dp_")
        dumps.append(("part_dbg", TEN["part_loc"], [TOK, D], F32))
        dumps.append(("rs_dbg", TEN["rs_loc"], [T, D], F32))
        dumps.append(("rdn_dbg", TEN["rdn_d"], [2, SQ], F32))
    with ExitStack() as stack:
        dsem = nc.alloc_semaphore(name="dbg_sem")
        with nc.Block() as blk:
            @blk.gpsimd
            def _(g0):
                ninc = 0
                for nm, src, shape, dt in dumps:
                    dst = nc.dram_tensor(nm, shape, dt, kind="ExternalOutput").ap()
                    g0.dma_start(out=dst, in_=src).then_inc(dsem, 16)
                    ninc += 16
                if ninc:
                    g0.wait_ge(dsem, ninc)
    return nc


def _fused_gla_tensors(nc):
    TEN = {}
    TEN["x"] = nc.dram_tensor("x", [T, D], F32, kind="ExternalInput").ap()
    for n in ("wq_c", "wk1_c", "wk2_c", "sel_c"):
        TEN[n] = nc.dram_tensor(n, [D, 128], F32, kind="ExternalInput").ap()
    TEN["wgh2_c"] = nc.dram_tensor("wgh2_c", [D, 2], F32, kind="ExternalInput").ap()
    for n in ("bq_cc", "bk1_cc", "bk2_cc"):
        TEN[n] = nc.dram_tensor(n, [128, 1], F32, kind="ExternalInput").ap()
    TEN["bg2"] = nc.dram_tensor("bg2", [2, 1], F32, kind="ExternalInput").ap()
    TEN["ws"] = nc.dram_tensor("ws", [64, 1], F32, kind="ExternalInput").ap()
    TEN["bs"] = nc.dram_tensor("bs", [1, 1], F32, kind="ExternalInput").ap()
    TEN["ws2_c"] = nc.dram_tensor("ws2_c", [128, 2], F32,
                                  kind="ExternalInput").ap()
    TEN["bs2"] = nc.dram_tensor("bs2", [2, 1], F32, kind="ExternalInput").ap()
    TEN["wp_c"] = nc.dram_tensor("wp_c", [128, D], F32, kind="ExternalInput").ap()
    TEN["bp"] = nc.dram_tensor("bp", [1, D], F32, kind="ExternalInput").ap()
    TEN["x_out"] = nc.dram_tensor("x_out", [T, D], F32, kind="ExternalOutput").ap()
    return TEN


def _fused_gla_scratch(nc, TEN, pfx=""):
    TEN["xh_loc"] = nc.dram_tensor(pfx + "gxh_loc", [T, D], BF16).ap()
    TEN["xhg"] = nc.dram_tensor(pfx + "gxhg", [TOK, D], BF16,
                                addr_space="Shared").ap()
    for n in ("qfm_loc", "k1fm_loc", "k2fm_loc", "k2sfm_loc", "q2fm_loc",
              "xhfm_loc"):
        TEN[n] = nc.dram_tensor(pfx + "g" + n, [2, 128, SQ], BF16).ap()
    TEN["lgsl"] = nc.dram_tensor(pfx + "glgsl", [2, 2, SQ], F32).ap()
    TEN["rrow_d"] = nc.dram_tensor(pfx + "grrow", [2, 2, SQ], F32).ap()
    TEN["o_slab"] = nc.dram_tensor(pfx + "go_slab", [2, SQ, 128], BF16).ap()
    TEN["rowd"] = nc.dram_tensor(pfx + "growd", [SQ], F32, kind="Internal").ap()
    TEN["part_loc"] = nc.dram_tensor(pfx + "gpart_loc", [TOK, D], BF16).ap()
    TEN["rs_loc"] = nc.dram_tensor(pfx + "grs_loc", [T, D], BF16).ap()


def emit_gla_fused(nc, TEN, x_in, x_out, pfx="", pre_ln=False):
    if not pre_ln:
        with nc.named_scope(pfx + "g_lnag"), nc.cleanup_on_exit():
            emit_lnag(nc, x_in, TEN["xh_loc"], TEN["xhg"], tag=pfx + "gla_")
    with nc.cleanup_on_exit():
      with nc.named_scope(pfx + "g_proj"):
        emit_glaproj(nc, TEN["xhg"], TEN["wq_c"], TEN["wk1_c"], TEN["wk2_c"],
                     TEN["sel_c"], TEN["wgh2_c"], TEN["bq_cc"], TEN["bk1_cc"],
                     TEN["bk2_cc"], TEN["bg2"], TEN["qfm_loc"], TEN["k1fm_loc"],
                     TEN["k2fm_loc"], TEN["k2sfm_loc"], TEN["q2fm_loc"],
                     TEN["xhfm_loc"], TEN["lgsl"], ws2_d=TEN["ws2_c"],
                     bs2_d=TEN["bs2"], rrow_d=TEN["rrow_d"], tag=pfx + "gpj_")
      with nc.named_scope(pfx + "g_scan"):
        emit_glab(nc, {"qfm_loc": TEN["qfm_loc"], "k1fm_loc": TEN["k1fm_loc"],
                       "k2fm_loc": TEN["k2fm_loc"],
                       "k2sfm_loc": TEN["k2sfm_loc"],
                       "q2fm_loc": TEN["q2fm_loc"],
                       "xhfm_loc": TEN["xhfm_loc"], "lgsl": TEN["lgsl"],
                       "ws": TEN["ws"], "bs": TEN["bs"],
                       "rrow_d": TEN["rrow_d"],
                       "o_slab": TEN["o_slab"], "rowd": TEN["rowd"]},
                  tag=pfx + "gsc_", fm=True)
      with nc.named_scope(pfx + "g_dproj"):
        emit_dproj(nc, "gla", TEN["o_slab"], TEN["wp_c"], TEN["bp"], x_in,
                   TEN["part_loc"], TEN["rs_loc"], x_out, tag=pfx + "gdp_")


def build_gla_fused():
    nc = bass.Bass("TRN2", target_bir_lowering=False)
    TEN = _fused_gla_tensors(nc)
    _fused_gla_scratch(nc, TEN)
    emit_gla_fused(nc, TEN, TEN["x"], TEN["x_out"])
    return nc


def _gla_fused_inmaps(x_flat, Wq, bq, Wk1, bk1, Wk2, bk2, Wg, bg, Ws, bs,
                      Wp, bp):
    in_maps = []
    for c in range(NCORES):
        cols = slice(128 * c, 128 * (c + 1))
        sel = np.zeros((D, 128), np.float32)
        sel[np.arange(128 * c, 128 * (c + 1)), np.arange(128)] = 1.0
        wgh2 = np.zeros((D, 2), np.float32)
        ws2 = np.zeros((128, 2), np.float32)
        for e in range(2):
            h = 2 * c + e
            wgh2[64 * h:64 * (h + 1), e] = Wg[:, 0]
            ws2[64 * e:64 * (e + 1), e] = Ws[:, 0]
        in_maps.append({
            "x": np.ascontiguousarray(x_flat[c * T:(c + 1) * T]),
            "wq_c": np.ascontiguousarray(Wq[:, cols]),
            "wk1_c": np.ascontiguousarray(Wk1[:, cols]),
            "wk2_c": np.ascontiguousarray(Wk2[:, cols]),
            "sel_c": sel, "wgh2_c": wgh2,
            "bq_cc": np.ascontiguousarray(bq[cols].reshape(128, 1)),
            "bk1_cc": np.ascontiguousarray(bk1[cols].reshape(128, 1)),
            "bk2_cc": np.ascontiguousarray(bk2[cols].reshape(128, 1)),
            "bg2": np.full((2, 1), bg[0], np.float32),
            "ws": Ws, "bs": bs.reshape(1, 1),
            "ws2_c": ws2, "bs2": np.full((2, 1), bs[0], np.float32),
            "wp_c": np.ascontiguousarray(Wp[cols, :]),
            "bp": bp.reshape(1, D)})
    return in_maps


def run_gla_fused(nc_cache, x_flat, Wq, bq, Wk1, bk1, Wk2, bk2, Wg, bg, Ws,
                  bs, Wp, bp):
    if "glaf" not in nc_cache:
        nc_cache["glaf"] = build_gla_fused()
    in_maps = _gla_fused_inmaps(x_flat, Wq, bq, Wk1, bk1, Wk2, bk2, Wg, bg,
                                Ws, bs, Wp, bp)
    res = run_bass_kernel_spmd(nc_cache["glaf"], in_maps,
                               core_ids=list(range(NCORES)))
    return np.concatenate([np.asarray(res.results[c]["x_out"])
                           for c in range(NCORES)], 0)


def emit_wb(nc, wgu_q, wd_q, wgu_qg, wd_qg, wgT_dram, wuT_dram, wdT_dram,
            tag=""):
    with ExitStack() as stack:
        def sb(name, shape, dt=F32):
            return stack.enter_context(nc.sbuf_tensor(tag + name, shape, dt))
        names = ("ccc", "tld", "tst")
        S = {n: nc.alloc_semaphore(name=tag + n) for n in names}
        stg = [sb(f"stg{e}", [128, 2816], BF16) for e in range(2)]
        with nc.Block() as blk:
            @blk.gpsimd
            def _(g0):
                g0.collective_compute(
                    "AllGather", mybir.AluOpType.bypass,
                    replica_groups=[list(range(NCORES))],
                    ins=[wgu_q.opt()], outs=[wgu_qg.opt()],
                ).then_inc(S["ccc"])
                g0.collective_compute(
                    "AllGather", mybir.AluOpType.bypass,
                    replica_groups=[list(range(NCORES))],
                    ins=[wd_q.opt()], outs=[wd_qg.opt()],
                ).then_inc(S["ccc"])
                g0.wait_ge(S["ccc"], 2)
        jobs = []
        for k in range(8):
            jobs.append((wgu_qg[0:2816, 128 * k:128 * (k + 1)],
                         wgT_dram[128 * k:128 * (k + 1), :], 2816))
            jobs.append((wgu_qg[2816:5632, 128 * k:128 * (k + 1)],
                         wuT_dram[128 * k:128 * (k + 1), :], 2816))
        for m in range(22):
            jobs.append((wd_qg[0:1024, 128 * m:128 * (m + 1)],
                         wdT_dram[128 * m:128 * (m + 1), :], 1024))
        with nc.Block() as blk:
            @blk.sync
            def _(sync):
                for i, (src, dst, w) in enumerate(jobs):
                    if i >= 2:
                        sync.wait_ge(S["tst"], 16 * (i - 1))
                    sync.dma_start(out=stg[i % 2][:, 0:w], in_=src,
                                   transpose=True).then_inc(S["tld"], 16)
            @blk.gpsimd
            def _(g0):
                for i, (src, dst, w) in enumerate(jobs):
                    g0.wait_ge(S["tld"], 16 * (i + 1))
                    g0.dma_start(out=dst, in_=stg[i % 2][:, 0:w]) \
                        .then_inc(S["tst"], 16)
                g0.wait_ge(S["tst"], 16 * len(jobs))


def build_mega():
    nc = bass.Bass("TRN2", target_bir_lowering=False)
    TEN = {}
    TEN["x"] = nc.dram_tensor("x", [T, D], F32, kind="ExternalInput").ap()
    TEN["wgu"] = nc.dram_tensor("wgu", [WGU_TPC * 128, 1024], F32,
                                kind="ExternalInput").ap()
    TEN["wd"] = nc.dram_tensor("wd", [128, 2816], F32,
                               kind="ExternalInput").ap()
    for n in ("wq_c", "wk1_c", "wk2_c", "sel_c"):
        TEN[n] = nc.dram_tensor(n, [D, 128], F32, kind="ExternalInput").ap()
    TEN["wgh2_c"] = nc.dram_tensor("wgh2_c", [D, 2], F32,
                                   kind="ExternalInput").ap()
    for n in ("bq_cc", "bk1_cc", "bk2_cc"):
        TEN[n] = nc.dram_tensor(n, [128, 1], F32, kind="ExternalInput").ap()
    TEN["bg2"] = nc.dram_tensor("bg2", [2, 1], F32, kind="ExternalInput").ap()
    TEN["ws"] = nc.dram_tensor("ws", [64, 1], F32, kind="ExternalInput").ap()
    TEN["bs"] = nc.dram_tensor("bs", [1, 1], F32, kind="ExternalInput").ap()
    TEN["ws2_c"] = nc.dram_tensor("ws2_c", [128, 2], F32,
                                  kind="ExternalInput").ap()
    TEN["bs2"] = nc.dram_tensor("bs2", [2, 1], F32, kind="ExternalInput").ap()
    TEN["wp_c"] = nc.dram_tensor("wp_c", [128, D], F32,
                                 kind="ExternalInput").ap()
    TEN["bp"] = nc.dram_tensor("bp", [1, D], F32, kind="ExternalInput").ap()
    for n in ("waq_c", "wak_c", "wav_c"):
        TEN[n] = nc.dram_tensor(n, [D, 128], F32, kind="ExternalInput").ap()
    for n in ("baq_cc", "bak_cc", "bav_cc"):
        TEN[n] = nc.dram_tensor(n, [128, 1], F32, kind="ExternalInput").ap()
    TEN["prot"] = nc.dram_tensor("prot", [128, 128], F32,
                                 kind="ExternalInput").ap()
    TEN["invf2"] = nc.dram_tensor("invf2", [128, 1], F32,
                                  kind="ExternalInput").ap()
    TEN["wo_c"] = nc.dram_tensor("wo_c", [128, D], F32,
                                 kind="ExternalInput").ap()
    TEN["bo"] = nc.dram_tensor("bo", [1, D], F32, kind="ExternalInput").ap()
    TEN["wgate"] = nc.dram_tensor("wgate", [D, 8], F32,
                                  kind="ExternalInput").ap()
    TEN["wexp"] = nc.dram_tensor("wexp", [D, D], F32,
                                 kind="ExternalInput").ap()
    TEN["bexp"] = nc.dram_tensor("bexp", [1, D], F32,
                                 kind="ExternalInput").ap()
    TEN["coremask"] = nc.dram_tensor("coremask", [8, 8], F32,
                                     kind="ExternalInput").ap()
    TEN["x_out"] = nc.dram_tensor("x_out", [T, D], F32,
                                  kind="ExternalOutput").ap()
    wgu_q = nc.dram_tensor("wgu_q", [WGU_TPC * 128, 1024], BF16).ap()
    wd_q = nc.dram_tensor("wd_q", [128, 2816], BF16).ap()
    wgu_qg = nc.dram_tensor("wgu_qg", [WGU_TPC * 128 * 8, 1024], BF16,
                            addr_space="Shared").ap()
    wd_qg = nc.dram_tensor("wd_qg", [1024, 2816], BF16,
                           addr_space="Shared").ap()
    wgT_dram = nc.dram_tensor("wgT_dram", [D, I_FFN], BF16, kind="Internal").ap()
    wuT_dram = nc.dram_tensor("wuT_dram", [D, I_FFN], BF16, kind="Internal").ap()
    wdT_dram = nc.dram_tensor("wdT_dram", [I_FFN, D], BF16, kind="Internal").ap()
    xq_dram = nc.dram_tensor("xq_dram", [T, D], BF16, kind="Internal").ap()
    hq_dram = nc.dram_tensor("hq_dram", [T, I_FFN], BF16, kind="Internal").ap()
    xh_moe = nc.dram_tensor("xh_moe", [T, D], BF16, kind="Internal").ap()
    cin = nc.dram_tensor("cin", [8, 1], F32).ap()
    cout = nc.dram_tensor("cout", [64, 1], F32, addr_space="Shared").ap()
    xs = [nc.dram_tensor(f"xstage{i}", [T, D], F32, kind="Internal").ap()
          for i in range(5)]
    _fused_gla_scratch(nc, TEN)
    _fused_att1_scratch(nc, TEN)

    with nc.named_scope("p_wprep"), nc.cleanup_on_exit():
        emit_wprep(nc, TEN["wgu"], TEN["wd"], wgu_q, wd_q, tag="wp_")
    with nc.named_scope("p_wb"), nc.cleanup_on_exit():
        emit_wb(nc, wgu_q, wd_q, wgu_qg, wd_qg, wgT_dram, wuT_dram, wdT_dram,
                tag="wb_")
    with nc.named_scope("p_ffn1"), nc.cleanup_on_exit():
        emit_ffn0(nc, TEN["x"], xs[0], wgT_dram, wuT_dram, wdT_dram,
                  xq_dram, hq_dram, tag="f1_", xh_out=TEN["xh_loc"],
                  xhg=TEN["xhg"])
    emit_gla_fused(nc, TEN, xs[0], xs[1], pfx="", pre_ln=True)
    with nc.named_scope("p_ffn2"), nc.cleanup_on_exit():
        emit_ffn0(nc, xs[1], xs[2], wgT_dram, wuT_dram, wdT_dram,
                  xq_dram, hq_dram, tag="f2_")
    with nc.named_scope("p_moe1"), nc.cleanup_on_exit():
        emit_moe(nc, xs[2], xs[3], TEN["wgate"], TEN["wexp"], TEN["bexp"],
                 TEN["coremask"], xh_moe, cin, cout, tag="m1_",
                 xh_out=TEN["xh_loc"], xhg=TEN["xhg"])
    emit_att1_fused(nc, TEN, xs[3], xs[4], pfx="", pre_ln=True)
    with nc.named_scope("p_moe2"), nc.cleanup_on_exit():
        emit_moe(nc, xs[4], TEN["x_out"], TEN["wgate"], TEN["wexp"],
                 TEN["bexp"], TEN["coremask"], xh_moe, cin, cout, tag="m2_")
    return nc


def kernel(**inputs):
    g = lambda n: np.ascontiguousarray(np.asarray(inputs[n], np.float32))
    x = g("L0_attn_x").reshape(TOK, D)
    wgu = np.concatenate([g("L0_ffn_w_gate"), g("L0_ffn_w_up"),
                          np.zeros((512, 1024), np.float32)], axis=0)
    per = WGU_TPC * 128
    w_down = g("L0_ffn_w_down")
    gla_maps = _gla_fused_inmaps(
        x, g("L0_attn_Wq"), g("L0_attn_bq"), g("L0_attn_Wk1"), g("L0_attn_bk1"),
        g("L0_attn_Wk2"), g("L0_attn_bk2"), g("L0_attn_Wg"), g("L0_attn_bg"),
        g("L0_attn_Ws"), g("L0_attn_bs"), g("L0_attn_Wp"), g("L0_attn_bp"))
    att_maps = _att1_fused_inmaps(
        x, g("L1_attn_Wq"), g("L1_attn_bq"), g("L1_attn_Wk"), g("L1_attn_bk"),
        g("L1_attn_Wv"), g("L1_attn_bv"), g("L1_attn_Wo"), g("L1_attn_bo"))
    nc = build_mega()
    in_maps = []
    for c in range(NCORES):
        m = {"x": np.ascontiguousarray(x[c * T:(c + 1) * T]),
             "wgu": np.ascontiguousarray(wgu[c * per:(c + 1) * per]),
             "wd": np.ascontiguousarray(w_down[c * 128:(c + 1) * 128]),
             "wgate": g("L1_ffn_w_gate"),
             "wexp": g("L1_ffn_w_expert"),
             "bexp": g("L1_ffn_b_expert").reshape(1, D),
             "coremask": np.tile((np.arange(8) < c).astype(np.float32)[None, :],
                                 (8, 1))}
        for k, v in gla_maps[c].items():
            if k != "x":
                m[k] = v
        for k, v in att_maps[c].items():
            if k != "x":
                m[k] = v
        in_maps.append(m)
    res = run_bass_kernel_spmd(nc, in_maps, core_ids=list(range(NCORES)))
    out = np.concatenate([np.asarray(res.results[c]["x_out"])
                          for c in range(NCORES)], 0)
    return np.ascontiguousarray(out.reshape(B, S, D).astype(np.float32))


# version 25
# speedup vs baseline: 1.1422x; 1.1422x over previous
import numpy as np
from contextlib import ExitStack

import concourse.bass as bass
import concourse.mybir as mybir
from concourse.bass_utils import run_bass_kernel_spmd
from concourse.alu_op_type import AluOpType

F32 = mybir.dt.float32
BF16 = mybir.dt.bfloat16
AF = mybir.ActivationFunctionType
AX = mybir.AxisListType

NCORES = 8
B, S, D = 2, 8192, 1024
TOK = B * S
T = TOK // NCORES
NT = T // 128
I_FFN = 2816
MAGIC = float(2 ** 23)
RHC = 1e-6
NIC = 6
ICH = [512] * 5 + [256]
ICO = [0, 512, 1024, 1536, 2048, 2560, 2816]
K2SC = 0.02 / 32.0


def _ts(eng, out, in0, s1, op0, s2=0.0, op1=AluOpType.add):
    return eng.tensor_scalar(out=out, in0=in0, scalar1=s1, scalar2=s2,
                             op0=op0, op1=op1)



def quant_scratch(stack, nc, tag, F):
    NB = F // 16
    def sb(name, shape, dt=F32):
        return stack.enter_context(nc.sbuf_tensor(f"{tag}_{name}", shape, dt))
    return {
        "bm": sb("bm", [128, NB]), "scl": sb("scl", [128, NB]),
        "rs": sb("rs", [128, NB]),
        "a": sb("a", [128, F]), "t": sb("t", [128, F]), "u": sb("u", [128, F]),
        "ql": sb("ql", [128, F], BF16), "u3": sb("u3", [128, F], BF16),
        "v7": sb("v7", [128, F], BF16), "q1": sb("q1", [128, F], BF16),
        "ssc": sb("ssc", [128, F], BF16), "sg": sb("sg", [128, F]),
        "mt": sb("mt", [128, F // 4]),
    }




_SER_OPS = frozenset([
    "tensor_scalar", "tensor_tensor", "scalar_tensor_tensor", "tensor_copy",
    "tensor_add", "tensor_sub", "tensor_mul", "tensor_max", "tensor_reduce",
    "memset", "memzero", "bn_stats", "bn_aggr", "reciprocal", "activation",
    "mul", "add", "copy", "sign", "sqrt", "square", "iota",
    "tensor_tensor_scan", "select", "max_index", "max_with_indices",
    "tensor_relu", "tensor_scalar_add", "tensor_scalar_mul",
    "tensor_scalar_max", "tensor_scalar_min", "tensor_scalar_sub",
    "copy_predicated", "stream_shuffle",
])


SER_FAST = False
_RISKY_OPS = frozenset(["tensor_reduce", "reciprocal", "bn_stats", "bn_aggr"])


class SerEngine:

    def __init__(self, eng, order_sem):
        self._e = eng
        self._os = order_sem
        self._pend = None
        self._pend_risky = False
        self._edge = None
        self._on = 0
        self._cum = {}

    def _pre_wait(self):
        if self._pend is not None:
            if (not SER_FAST) or self._pend_risky:
                self._on += 1
                self._pend.then_inc(self._os, 1)
                self._edge = (self._os, self._on)
            self._pend = None
        if self._edge is not None:
            self._e.wait_ge(self._edge[0], self._edge[1])
            self._edge = None

    def signal(self, sem, n=1):
        assert self._pend is not None, "signal() with no pending op"
        key = id(sem)
        self._cum[key] = self._cum.get(key, 0) + n
        self._pend.then_inc(sem, n)
        if (not SER_FAST) or self._pend_risky:
            self._edge = (sem, self._cum[key])
        self._pend = None
        return self._cum[key]

    def wait_ge(self, sem, v):
        self._e.wait_ge(sem, v)

    def __getattr__(self, name):
        target = getattr(self._e, name)
        if name in _SER_OPS:
            def wrapped(*a, **k):
                self._pre_wait()
                inst = target(*a, **k)
                self._pend = inst
                self._pend_risky = name in _RISKY_OPS
                return inst
            return wrapped
        return target



def emit_quant_pre_v(v, x, sc, F):
    NB = F // 16
    a, t = sc["a"][:, 0:F], sc["t"][:, 0:F]
    _ts(v, t, x, -1.0, AluOpType.mult)
    v.tensor_tensor(out=a, in0=x, in1=t, op=AluOpType.max)
    a3 = a.rearrange("p (nb k) -> p nb k", k=16)
    return v.tensor_reduce(out=sc["bm"][:, 0:NB], in_=a3,
                           axis=mybir.AxisListType.X, op=AluOpType.max)


def emit_quant_pre_g(g, x, sc, F):
    a, sg = sc["a"][:, 0:F], sc["sg"][:, 0:F]
    _ts(g, sg, x, 0.0, AluOpType.is_ge, 2.0, AluOpType.mult)
    _ts(g, sg, sg, 1.0, AluOpType.subtract)
    return g.tensor_tensor(out=a, in0=x, in1=sg, op=AluOpType.mult)


def emit_quant_tree_v(v, sc, F):
    NB = F // 16
    a3 = sc["a"][:, 0:F].rearrange("p (nb k) -> p nb k", k=16)
    u8 = sc["mt"][:, 0:F // 4]
    t8 = sc["t"][:, 0:F // 2].rearrange("p (nb k) -> p nb k", k=8)
    m4 = sc["mt"][:, 0:F // 4].rearrange("p (nb k) -> p nb k", k=4)
    m2 = sc["t"][:, 0:F // 8].rearrange("p (nb k) -> p nb k", k=2)
    v.tensor_tensor(out=t8, in0=a3[:, :, 0:8], in1=a3[:, :, 8:16], op=AluOpType.max)
    v.tensor_tensor(out=m4, in0=t8[:, :, 0:4], in1=t8[:, :, 4:8], op=AluOpType.max)
    v.tensor_tensor(out=m2, in0=m4[:, :, 0:2], in1=m4[:, :, 2:4], op=AluOpType.max)
    v.tensor_tensor(out=sc["bm"][:, 0:NB].unsqueeze(2),
                    in0=m2[:, :, 0:1], in1=m2[:, :, 1:2], op=AluOpType.max)
    return _ts(v, sc["scl"][:, 0:NB], sc["bm"][:, 0:NB], 1.0 / 6.0, AluOpType.mult,
               1e-12, AluOpType.max)


def emit_rs_scalar(s, sc, F):
    NB = F // 16
    s.activation(out=sc["scl"][:, 0:NB], in_=sc["bm"][:, 0:NB], func=AF.Copy,
                 scale=1.0 / 6.0, bias=1e-12)
    s.activation(out=sc["rs"][:, 0:NB], in_=sc["scl"][:, 0:NB], func=AF.Ln)
    return s.activation(out=sc["rs"][:, 0:NB], in_=sc["rs"][:, 0:NB], func=AF.Exp,
                        scale=-1.0)


def emit_quant_post(eng, x, xq, sc, F, is_pool=False):
    P = 128
    NB = F // 16
    a, t, u = sc["a"][:, 0:F], sc["t"][:, 0:F], sc["u"][:, 0:F]
    ql, u3, v7 = sc["ql"][:, 0:F], sc["u3"][:, 0:F], sc["v7"][:, 0:F]
    q1, ssc = sc["q1"][:, 0:F], sc["ssc"][:, 0:F]
    rs_b = sc["rs"][:, 0:NB].unsqueeze(2).broadcast_to((P, NB, 16))
    eng.tensor_tensor(out=a.rearrange("p (nb k) -> p nb k", k=16),
                      in0=a.rearrange("p (nb k) -> p nb k", k=16), in1=rs_b,
                      op=AluOpType.mult)
    _ts(eng, t, a, 2.0, AluOpType.mult, RHC, AluOpType.subtract)
    _ts(eng, t, t, MAGIC, AluOpType.add)
    _ts(eng, t, t, MAGIC, AluOpType.subtract, 0.0, AluOpType.max)
    _ts(eng, ql, t, 4.0, AluOpType.min, 0.5, AluOpType.mult)
    _ts(eng, u, a, 1e-6, AluOpType.subtract, MAGIC, AluOpType.add)
    _ts(eng, u, u, MAGIC, AluOpType.subtract, 2.0, AluOpType.subtract)
    _ts(eng, u3, u, 0.0, AluOpType.max, 2.0, AluOpType.min)
    _ts(eng, v7, a, 5.0, AluOpType.is_gt, 2.0, AluOpType.mult)
    if is_pool:
        sgv = sc["sg"][:, 0:F]
    else:
        _ts(eng, t, x, 0.0, AluOpType.is_ge, 2.0, AluOpType.mult)
        _ts(eng, t, t, 1.0, AluOpType.subtract)
        sgv = t
    scl_b = sc["scl"][:, 0:NB].unsqueeze(2).broadcast_to((P, NB, 16))
    eng.tensor_tensor(out=ssc.rearrange("p (nb k) -> p nb k", k=16),
                      in0=sgv.rearrange("p (nb k) -> p nb k", k=16), in1=scl_b,
                      op=AluOpType.mult)
    eng.tensor_tensor(out=q1, in0=ql, in1=u3, op=AluOpType.add)
    eng.tensor_tensor(out=q1, in0=q1, in1=v7, op=AluOpType.add)
    return eng.tensor_tensor(out=xq, in0=q1, in1=ssc, op=AluOpType.mult)



WGU_TPC = 6


def build_wprep():
    nc = bass.Bass("TRN2", target_bir_lowering=False)
    wgu = nc.dram_tensor("wgu", [WGU_TPC * 128, 1024], F32, kind="ExternalInput").ap()
    wd = nc.dram_tensor("wd", [128, 2816], F32, kind="ExternalInput").ap()
    wgu_q = nc.dram_tensor("wgu_q", [WGU_TPC * 128, 1024], BF16, kind="ExternalOutput").ap()
    wd_q = nc.dram_tensor("wd_q", [128, 2816], BF16, kind="ExternalOutput").ap()
    emit_wprep(nc, wgu, wd, wgu_q, wd_q)
    return nc


def emit_wprep(nc, wgu, wd, wgu_q, wd_q, tag=""):
    with ExitStack() as stack:
        def sb(name, shape, dt=F32):
            return stack.enter_context(nc.sbuf_tensor(tag + name, shape, dt))
        xg = [sb(f"xg{i}", [128, 1024]) for i in range(WGU_TPC)]
        xgq = [sb(f"xgq{i}", [128, 1024], BF16) for i in range(WGU_TPC)]
        xd = sb("xd", [128, 2816])
        xdq = sb("xdq", [128, 2816], BF16)
        scv = quant_scratch(stack, nc, tag + "scv", 2816)
        ld = nc.alloc_semaphore(name=tag + "ld")
        oov = nc.alloc_semaphore(name=tag + "oov")
        oosw = nc.alloc_semaphore(name=tag + "oosw")
        vscl = nc.alloc_semaphore(name=tag + "vscl")
        vrs = nc.alloc_semaphore(name=tag + "vrs")
        qd = nc.alloc_semaphore(name=tag + "qd")
        osem = nc.alloc_semaphore(name=tag + "osem")
        wgu3 = wgu.rearrange("(t p) d -> t p d", p=128)
        wgu_q3 = wgu_q.rearrange("(t p) d -> t p d", p=128)
        tiles = [(xg[i][:], xgq[i][:], 1024) for i in range(WGU_TPC)] + [(xd[:], xdq[:], 2816)]
        with nc.Block() as blk:
            @blk.sync
            def _(sync):
                for i in range(WGU_TPC):
                    sync.dma_start(out=xg[i][:], in_=wgu3[i]).then_inc(ld, 16)
                sync.dma_start(out=xd[:], in_=wd[:]).then_inc(ld, 16)
                for i in range(WGU_TPC + 1):
                    sync.wait_ge(qd, i + 1)
                    if i < WGU_TPC:
                        sync.dma_start(out=wgu_q3[i], in_=xgq[i][:]).then_inc(osem, 16)
                    else:
                        sync.dma_start(out=wd_q[:], in_=xdq[:]).then_inc(osem, 16)
                sync.wait_ge(osem, 16 * (WGU_TPC + 1))
            @blk.vector
            def _(v0):
                v = SerEngine(v0, oov)
                v.wait_ge(ld, 16 * (WGU_TPC + 1))
                for i, (xi, xo, F) in enumerate(tiles):
                    emit_quant_pre_v(v, xi, scv, F)
                    v.signal(vscl)
                    v.wait_ge(vrs, i + 1)
                    emit_quant_post(v, xi, xo, scv, F)
                    v.signal(qd)
            @blk.scalar
            def _(s0):
                s = SerEngine(s0, oosw)
                for i, (xi, xo, F) in enumerate(tiles):
                    s.wait_ge(vscl, i + 1)
                    emit_rs_scalar(s, scv, F)
                    s.signal(vrs)


def run_wprep(w_gate, w_up, w_down):
    wgu = np.concatenate([w_gate, w_up, np.zeros((512, 1024), np.float32)], axis=0)
    per = WGU_TPC * 128
    nc = build_wprep()
    in_maps = [{"wgu": np.ascontiguousarray(wgu[c * per:(c + 1) * per]),
                "wd": np.ascontiguousarray(w_down[c * 128:(c + 1) * 128])}
               for c in range(NCORES)]
    res = run_bass_kernel_spmd(nc, in_maps, core_ids=list(range(NCORES)))
    wgu_q = np.concatenate([np.asarray(res.results[c]["wgu_q"]) for c in range(NCORES)], axis=0)
    wd_q = np.concatenate([np.asarray(res.results[c]["wd_q"]) for c in range(NCORES)], axis=0)
    wgT = np.ascontiguousarray(wgu_q[0:2816].T)
    wuT = np.ascontiguousarray(wgu_q[2816:5632].T)
    wdT = np.ascontiguousarray(wd_q.T)
    return wgT, wuT, wdT



def emit_ln(s, x_tile, sums, sc2, xhat, junk):
    s.activation(out=junk, in_=x_tile, func=AF.Copy, accum_out=sums[:, 0:1])
    s.activation(out=junk, in_=x_tile, func=AF.Square, accum_out=sums[:, 1:2])
    s.mul(out=sc2[:, 0:1], in_=sums[:, 0:1], mul=-1.0 / D)
    s.activation(out=sc2[:, 1:2], in_=sums[:, 1:2], func=AF.Copy,
                 scale=1.0 / D, bias=1e-5)
    s.activation(out=sc2[:, 2:3], in_=sc2[:, 0:1], func=AF.Square)
    s.activation(out=sc2[:, 2:3], in_=sc2[:, 2:3], func=AF.Ln,
                 scale=-1.0, bias=sc2[:, 1:2])
    s.activation(out=sc2[:, 2:3], in_=sc2[:, 2:3], func=AF.Exp, scale=-0.5)
    s.activation(out=sc2[:, 3:4], in_=sc2[:, 0:1], func=AF.Identity,
                 scale=sc2[:, 2:3])
    return s.activation(out=xhat, in_=x_tile, func=AF.Identity,
                        scale=sc2[:, 2:3], bias=sc2[:, 3:4])



def build_ffn0(upto=5):
    nc = bass.Bass("TRN2", target_bir_lowering=False)
    x_in = nc.dram_tensor("x", [T, D], F32, kind="ExternalInput").ap()
    wgT_d = nc.dram_tensor("wgT", [D, I_FFN], BF16, kind="ExternalInput").ap()
    wuT_d = nc.dram_tensor("wuT", [D, I_FFN], BF16, kind="ExternalInput").ap()
    wdT_d = nc.dram_tensor("wdT", [I_FFN, D], BF16, kind="ExternalInput").ap()
    x_out = nc.dram_tensor("x_out", [T, D], F32, kind="ExternalOutput").ap()
    xqdbg = nc.dram_tensor("xqdbg", [T, D], BF16, kind="ExternalOutput").ap()
    hqdbg = nc.dram_tensor("hqdbg", [T, I_FFN], BF16, kind="ExternalOutput").ap()
    xq_dram = nc.dram_tensor("xq_dram", [T, D], BF16, kind="Internal").ap()
    hq_dram = nc.dram_tensor("hq_dram", [T, I_FFN], BF16, kind="Internal").ap()
    emit_ffn0(nc, x_in, x_out, wgT_d, wuT_d, wdT_d, xq_dram, hq_dram,
              xqdbg=xqdbg, hqdbg=hqdbg, upto=upto)
    return nc


def emit_ffn0(nc, x_in, x_out, wgT_d, wuT_d, wdT_d, xq_dram, hq_dram,
              xqdbg=None, hqdbg=None, upto=5, tag="", xh_out=None, xhg=None):
    x3 = x_in.rearrange("(t p) d -> t p d", p=128)
    xo3 = x_out.rearrange("(t p) d -> t p d", p=128)
    xq3 = xq_dram.rearrange("(t p) d -> t p d", p=128)
    hq3 = hq_dram.rearrange("(t p) d -> t p d", p=128)

    with ExitStack() as stack:
        def sb(name, shape, dt=F32):
            return stack.enter_context(nc.sbuf_tensor(tag + name, shape, dt))
        def psum(name, shape, dt=F32):
            return stack.enter_context(nc.psum_tensor(tag + name, shape, dt))
        xt = [sb(f"xt{i}", [128, D]) for i in range(NT)]
        lnsums = sb("lnsums", [128, 2])
        junk = sb("junk", [128, D], BF16)
        sc2 = sb("sc2", [128, 4])
        xhat = [sb(f"xhat{e}", [128, D]) for e in range(2)]
        xot2 = [sb(f"xot2{j}", [128, D]) for j in range(2)]
        xot = [xhat[0], xhat[1], xot2[0], xot2[1]]
        xqv = [sb(f"xqv{e}", [128, D], BF16) for e in range(2)]
        xhb2 = [sb(f"xhb2{e}", [128, D], BF16) for e in range(2)]
        scv = quant_scratch(stack, nc, tag + "scv", 512)
        scg = quant_scratch(stack, nc, tag + "scg", 512)
        xqf = [sb(f"xqf{k}", [128, T], BF16) for k in range(8)]
        hqs = [sb(f"hqs{b}", [128, 512], BF16) for b in range(2)]
        wg_s = [[sb(f"wg_s{b}_{k}", [128, 512], BF16) for k in range(8)] for b in range(2)]
        wu_s = [[sb(f"wu_s{b}_{k}", [128, 512], BF16) for k in range(8)] for b in range(2)]
        wd_s = [sb(f"wd_s{b}", [128, D], BF16) for b in range(2)]
        hg = [sb(f"hg{e}", [128, 512]) for e in range(4)]
        hv = [sb(f"hv{e}", [128, 512]) for e in range(2)]
        hqb = [sb(f"hqb{e}", [128, 512], BF16) for e in range(2)]
        pb = [psum(f"pb{j}", [128, 512]) for j in range(8)]

        sem_names = ["oos", "oov", "oog", "ld", "sxh", "qdv", "qdg", "stv", "stg", "pe", "seg",
                     "seu", "hqdv", "hqdg", "ev", "osem", "osc", "trx",
                     "vscl", "vrs", "gbm", "gscl", "grs", "sxh2", "sthx", "ccc2",
                     "wlA", "wlB", "sthA", "sthB", "trsA", "trsB", "wdlA", "wdlB"]
        sems = {n: nc.alloc_semaphore(name=tag + n) for n in sem_names}
        ld, sxh = sems["ld"], sems["sxh"]
        qdv, qdg = sems["qdv"], sems["qdg"]
        stv, stg = sems["stv"], sems["stg"]
        pe, seg, seu = sems["pe"], sems["seg"], sems["seu"]
        hqdv, hqdg, ev, osem, osc = (sems["hqdv"], sems["hqdg"], sems["ev"],
                                     sems["osem"], sems["osc"])
        trx = sems["trx"]
        vscl, vrs, gbm, gscl, grs = (sems["vscl"], sems["vrs"], sems["gbm"],
                                     sems["gscl"], sems["grs"])
        wl2 = [sems["wlA"], sems["wlB"]]
        SER = {"s": SerEngine(nc.scalar, sems["oos"]),
               "v": SerEngine(nc.vector, sems["oov"]),
               "g": SerEngine(nc.gpsimd, sems["oog"])}
        sth = [sems["sthA"], sems["sthB"]]
        trs2 = [sems["trsA"], sems["trsB"]]
        wdl2 = [sems["wdlA"], sems["wdlB"]]

        with nc.named_scope(tag + "P1"), nc.Block() as blk:
            @blk.sync
            def _(sync):
                for i in range(NT):
                    sync.dma_start(out=xt[i][:], in_=x3[i]).then_inc(ld, 16)
                for i in range(NT):
                    if i % 2 == 0:
                        sync.wait_ge(qdv, 2 * (i // 2 + 1))
                        sync.dma_start(out=xq3[i], in_=xqv[0][:]).then_inc(stv, 16)
                    else:
                        sync.wait_ge(qdg, 2 * ((i + 1) // 2))
                        sync.dma_start(out=xq3[i], in_=xqv[1][:]).then_inc(stg, 16)

            @blk.scalar
            def _(s0):
                s = SER["s"]
                for k in range(8):
                    s.dma_start(out=wg_s[0][k][:],
                                in_=wgT_d[128 * k:128 * (k + 1), ICO[0]:ICO[1]]).then_inc(wl2[0], 16)
                    s.dma_start(out=wu_s[0][k][:],
                                in_=wuT_d[128 * k:128 * (k + 1), ICO[0]:ICO[1]]).then_inc(wl2[0], 16)
                s.wait_ge(ld, 16 * NT)
                for i in range(NT):
                    if i >= 2:
                        if i % 2 == 0:
                            s.wait_ge(qdv, 2 * (i // 2))
                        else:
                            s.wait_ge(qdg, 2 * ((i - 1) // 2))
                    emit_ln(s, xt[i][:], lnsums, sc2, xhat[i % 2][:], junk[:])
                    s.signal(sxh)
                    for h in range(2):
                        s.wait_ge(vscl, 2 * i + h + 1)
                        emit_rs_scalar(s, scv, 512)
                        s.signal(vrs)

            @blk.vector
            def _(v0):
                v = SER["v"]
                for i in range(NT):
                    v.wait_ge(sxh, i + 1)
                    if i >= 2:
                        if i % 2 == 0:
                            v.wait_ge(stv, 16 * (i // 2))
                        else:
                            v.wait_ge(stg, 16 * ((i - 1) // 2))
                    for h in range(2):
                        emit_quant_pre_v(v, xhat[i % 2][:, 512 * h:512 * (h + 1)],
                                         scv, 512)
                        v.signal(vscl)
                        v.wait_ge(vrs, 2 * i + h + 1)
                        emit_quant_post(v, xhat[i % 2][:, 512 * h:512 * (h + 1)],
                                        xqv[i % 2][:, 512 * h:512 * (h + 1)],
                                        scv, 512)
                        v.signal(qdv if i % 2 == 0 else qdg)

        if upto == 1:
            with nc.Block() as blk:
                @blk.gpsimd
                def _(g):
                    g.wait_ge(stv, 16 * 8)
                    g.wait_ge(stg, 16 * 8)
                    g.dma_start(out=xqdbg[:], in_=xq_dram[:]).then_inc(osem, 16)
                    g.wait_ge(osem, 16)
            return

        with nc.named_scope(tag + "P2"), nc.Block() as blk:
            @blk.sync
            def _(sync):
                sync.wait_ge(stv, 16 * 8)
                sync.wait_ge(stg, 16 * 8)
                for k in range(8):
                    sync.dma_start(out=xqf[k][:], in_=xq_dram[:, 128 * k:128 * (k + 1)],
                                   transpose=True).then_inc(trx, 16)

        NS1 = 2 * NT
        with nc.named_scope(tag + "P3"), nc.Block() as blk:
            @blk.tensor
            def _(te):
                te.wait_ge(trx, 16 * 8)
                for i in range(NIC):
                    w = ICH[i]
                    te.wait_ge(wl2[i % 2], 256 * (i // 2 + 1))
                    for t in range(NT):
                        n = i * NT + t
                        bsel = 2 * (n % 4)
                        if n >= 4:
                            te.wait_ge(seg, n - 3)
                            te.wait_ge(seu, n - 3)
                        for k in range(8):
                            te.matmul(out=pb[bsel][:, 0:w],
                                      lhsT=xqf[k][:, 128 * t:128 * (t + 1)],
                                      rhs=wg_s[i % 2][k][:, 0:w],
                                      start=(k == 0), stop=(k == 7))
                        for k in range(8):
                            mm = te.matmul(out=pb[bsel + 1][:, 0:w],
                                           lhsT=xqf[k][:, 128 * t:128 * (t + 1)],
                                           rhs=wu_s[i % 2][k][:, 0:w],
                                           start=(k == 0), stop=(k == 7))
                        mm.then_inc(pe, 1)

            @blk.scalar
            def _(s0):
                s = SER["s"]
                nv = 0
                ng = 0
                for i in range(NIC):
                    if i + 1 < NIC:
                        wnext = ICH[i + 1]
                        if i >= 1:
                            s.wait_ge(pe, i * NT)
                        for k in range(8):
                            s.dma_start(out=wg_s[(i + 1) % 2][k][:, 0:wnext],
                                        in_=wgT_d[128 * k:128 * (k + 1), ICO[i + 1]:ICO[i + 2]]).then_inc(wl2[(i + 1) % 2], 16)
                            s.dma_start(out=wu_s[(i + 1) % 2][k][:, 0:wnext],
                                        in_=wuT_d[128 * k:128 * (k + 1), ICO[i + 1]:ICO[i + 2]]).then_inc(wl2[(i + 1) % 2], 16)
                    for t in range(NT):
                        n = i * NT + t
                        w = ICH[i]
                        bsel = 2 * (n % 4)
                        s.wait_ge(pe, n + 1)
                        if n >= 4:
                            s.wait_ge(seu, n - 3)
                        s.activation(out=hg[n % 4][:, 0:w], in_=pb[bsel][:, 0:w],
                                     func=AF.Silu)
                        s.signal(seg)
                        s.wait_ge(vscl, NS1 + n + 1)
                        emit_rs_scalar(s, scv, w)
                        s.signal(vrs)

            @blk.vector
            def _(v0):
                v = SER["v"]
                nv = 0
                ng = 0
                for i in range(NIC):
                    for t in range(NT):
                        n = i * NT + t
                        w = ICH[i]
                        p2 = t % 2
                        v.wait_ge(seg, n + 1)
                        if p2 == 0:
                            nv += 1
                            if nv >= 2:
                                v.wait_ge(sth[0], 16 * (nv - 1))
                        else:
                            ng += 1
                            if ng >= 2:
                                v.wait_ge(sth[1], 16 * (ng - 1))
                        v.tensor_tensor(out=hv[p2][:, 0:w], in0=hg[n % 4][:, 0:w],
                                        in1=pb[2 * (n % 4) + 1][:, 0:w],
                                        op=AluOpType.mult)
                        v.signal(seu)
                        emit_quant_pre_v(v, hv[p2][:, 0:w], scv, w)
                        v.signal(vscl)
                        v.wait_ge(vrs, NS1 + n + 1)
                        emit_quant_post(v, hv[p2][:, 0:w], hqb[p2][:, 0:w], scv, w)
                        v.signal(hqdv if p2 == 0 else hqdg)

            @blk.sync
            def _(sync):
                nev = 0
                nod = 0
                for i in range(NIC):
                    for t in range(NT):
                        w = ICH[i]
                        if t % 2 == 0:
                            nev += 1
                            sync.wait_ge(hqdv, nev)
                            sync.dma_start(out=hq3[t][:, ICO[i]:ICO[i + 1]],
                                           in_=hqb[0][:, 0:w]).then_inc(sth[0], 16)
                        else:
                            nod += 1
                            sync.wait_ge(hqdg, nod)
                            sync.dma_start(out=hq3[t][:, ICO[i]:ICO[i + 1]],
                                           in_=hqb[1][:, 0:w]).then_inc(sth[1], 16)

        if upto == 3:
            with nc.Block() as blk:
                @blk.gpsimd
                def _(g):
                    g.wait_ge(sth[0], 16 * 48)
                    g.wait_ge(sth[1], 16 * 48)
                    g.dma_start(out=hqdbg[:], in_=hq_dram[:]).then_inc(osem, 16)
                    g.wait_ge(osem, 16)
            return

        with nc.named_scope(tag + "P5"), nc.Block() as blk:
            @blk.sync
            def _(sync):
                sync.wait_ge(sth[0], 16 * 48)
                sync.wait_ge(sth[1], 16 * 48)
                for tg in range(4):
                    for i2 in range(22):
                        m = tg * 22 + i2
                        if m >= 2:
                            sync.wait_ge(pe, NIC * NT + (m - 1))
                        sync.dma_start(out=hqs[m % 2][:],
                                       in_=hq_dram[512 * tg:512 * (tg + 1),
                                                   128 * i2:128 * (i2 + 1)],
                                       transpose=True).then_inc(trs2[m % 2], 16)

            @blk.scalar
            def _(s0):
                s = SER["s"]
                for tg in range(4):
                    for i2 in range(22):
                        m = tg * 22 + i2
                        if m >= 2:
                            s.wait_ge(pe, NIC * NT + (m - 1))
                        s.dma_start(out=wd_s[m % 2][:],
                                    in_=wdT_d[128 * i2:128 * (i2 + 1), :]).then_inc(wdl2[m % 2], 16)
                    if xh_out is not None:
                        s.wait_ge(sems["ev"], tg + 1)
                        for j in range(4):
                            t4 = 4 * tg + j
                            if t4 >= 2:
                                s.wait_ge(sems["sthx"], 16 * (t4 - 1))
                            emit_ln(s, xot[j][:], lnsums, sc2,
                                    xhb2[t4 % 2][:], junk[:])
                            s.signal(sems["sxh2"])

            @blk.tensor
            def _(te):
                for tg in range(4):
                    if tg > 0:
                        te.wait_ge(ev, tg)
                    for i2 in range(22):
                        m = tg * 22 + i2
                        te.wait_ge(wdl2[m % 2], 16 * (m // 2 + 1))
                        te.wait_ge(trs2[m % 2], 16 * (m // 2 + 1))
                        mm = None
                        for j in range(4):
                            for nblk in range(2):
                                mm = te.matmul(
                                    out=pb[2 * j + nblk][:],
                                    lhsT=hqs[m % 2][:, 128 * j:128 * (j + 1)],
                                    rhs=wd_s[m % 2][:, 512 * nblk:512 * (nblk + 1)],
                                    start=(i2 == 0), stop=(i2 == 21))
                        mm.then_inc(pe, 1)

            @blk.vector
            def _(v0):
                v = SER["v"]
                for tg in range(4):
                    v.wait_ge(pe, NIC * NT + 22 * (tg + 1))
                    if tg >= 1:
                        v.wait_ge(osc, tg)
                    for j in range(4):
                        t = tg * 4 + j
                        for nblk in range(2):
                            v.scalar_tensor_tensor(
                                out=xot[j][:, 512 * nblk:512 * (nblk + 1)],
                                in0=pb[2 * j + nblk][:], scalar=0.5,
                                in1=xt[t][:, 512 * nblk:512 * (nblk + 1)],
                                op0=AluOpType.mult, op1=AluOpType.add)
                    v.signal(ev)

            @blk.gpsimd
            def _(g):
                xh3o = (xh_out.rearrange("(t p) d -> t p d", p=128)
                        if xh_out is not None else None)
                for tg in range(4):
                    g.wait_ge(ev, tg + 1)
                    for j in range(4):
                        g.dma_start(out=xo3[tg * 4 + j], in_=xot[j][:]).then_inc(osem, 16)
                    if xh_out is not None:
                        for j in range(4):
                            t4 = 4 * tg + j
                            g.wait_ge(sems["sxh2"], t4 + 1)
                            g.dma_start(out=xh3o[t4], in_=xhb2[t4 % 2][:]) \
                                .then_inc(sems["sthx"], 16)
                    g.wait_ge(osem, 16 * 4 * (tg + 1))
                    if xh_out is not None:
                        g.wait_ge(sems["sthx"], 16 * 4 * (tg + 1))
                    g.sem_inc(osc, 1)
        if xh_out is not None:
            with nc.Block() as blk:
                @blk.gpsimd
                def _(g0):
                    g0.collective_compute(
                        "AllGather", mybir.AluOpType.bypass,
                        replica_groups=[list(range(NCORES))],
                        ins=[xh_out.opt()], outs=[xhg.opt()],
                    ).then_inc(sems["ccc2"])
                    g0.wait_ge(sems["ccc2"], 1)


def run_ffn0(nc_cache, x_flat, wgT, wuT, wdT):
    if "ffn0" not in nc_cache:
        nc_cache["ffn0"] = build_ffn0()
    nc = nc_cache["ffn0"]
    in_maps = [{"x": np.ascontiguousarray(x_flat[c * T:(c + 1) * T]),
                "wgT": wgT, "wuT": wuT, "wdT": wdT} for c in range(NCORES)]
    res = run_bass_kernel_spmd(nc, in_maps, core_ids=list(range(NCORES)))
    return np.concatenate([np.asarray(res.results[c]["x_out"]) for c in range(NCORES)], axis=0)

def build_moe(rg_cores=NCORES):
    nc = bass.Bass("TRN2", target_bir_lowering=False)
    x_in = nc.dram_tensor("x", [T, D], F32, kind="ExternalInput").ap()
    wg_d = nc.dram_tensor("wgate", [D, 8], F32, kind="ExternalInput").ap()
    we_d = nc.dram_tensor("wexp", [D, D], F32, kind="ExternalInput").ap()
    be_d = nc.dram_tensor("bexp", [1, D], F32, kind="ExternalInput").ap()
    mask_d = nc.dram_tensor("coremask", [8, 8], F32, kind="ExternalInput").ap()
    x_out = nc.dram_tensor("x_out", [T, D], F32, kind="ExternalOutput").ap()
    xh_dram = nc.dram_tensor("xh_dram", [T, D], BF16, kind="Internal").ap()
    cin = nc.dram_tensor("cin", [8, 1], F32).ap()
    cout = nc.dram_tensor("cout", [64, 1], F32).ap()
    emit_moe(nc, x_in, x_out, wg_d, we_d, be_d, mask_d, xh_dram, cin, cout,
             rg_cores=rg_cores)
    return nc


def emit_moe(nc, x_in, x_out, wg_d, we_d, be_d, mask_d, xh_dram, cin, cout,
             rg_cores=NCORES, tag="", xh_out=None, xhg=None):
    x3 = x_in.rearrange("(t p) d -> t p d", p=128)
    xo3 = x_out.rearrange("(t p) d -> t p d", p=128)
    xh3 = xh_dram.rearrange("(t p) d -> t p d", p=128)

    with ExitStack() as stack:
        def sb(name, shape, dt=F32):
            return stack.enter_context(nc.sbuf_tensor(tag + name, shape, dt))
        def psum(name, shape, dt=F32):
            return stack.enter_context(nc.psum_tensor(tag + name, shape, dt))
        xt = [sb(f"xt{i}", [128, D]) for i in range(NT)]
        lnsums = sb("lnsums", [128, 2]); sc2 = sb("sc2", [128, 4])
        junk = sb("junk", [128, D], BF16)
        xhb = [sb(f"xhb{e}", [128, D], BF16) for e in range(2)]
        xhf = [sb(f"xhf{k}", [128, T], BF16) for k in range(8)]
        wg8 = [sb(f"wg8{k}", [128, 8], BF16) for k in range(8)]
        wexp = [sb(f"wexp{k}", [128, D], BF16) for k in range(8)]
        bb = sb("bb", [128, D])
        bsb = sb("bsb", [1, D])
        colidx = sb("colidx", [128, 128]); rowidx = sb("rowidx", [128, 128])
        identb = sb("identb", [128, 128], BF16)
        onesr = sb("onesr", [1, 128])
        ones8 = sb("ones8", [8, 1])
        lt = sb("lt", [128, 8]); mtree = sb("mtree", [128, 8])
        nm = sb("nm", [128, 1]); et = sb("et", [128, 8]); se = sb("se", [128, 1])
        ohr = sb("ohr", [128, 8]); p1 = sb("p1", [128, 8]); p2 = sb("p2", [128, 8])
        p4 = sb("p4", [128, 8]); msk = sb("msk", [128, 8])
        ohb16 = sb("ohb16", [128, 8], BF16)
        gv = sb("gv", [128, NT])
        ohm = sb("ohm", [8, T]); counts = sb("counts", [8, T])
        zed = sb("zed", [8, T])
        tot = sb("tot", [8, 1]); cm = sb("cm", [8, 8]); cmm = sb("cmm", [8, 8])
        car1 = sb("car1", [8, 4]); car2 = sb("car2", [8, 2]); carry = sb("carry", [8, 1])
        tmpc = sb("tmpc", [8, T]); km = sb("km", [8, T])
        chalf = sb("chalf", [128, NT])
        uev = [sb(f"uev{e}", [128, 512]) for e in range(2)]
        xot = [sb(f"xot{j}", [128, D]) for j in range(2)]
        xhb2m = [sb(f"xhb2m{e}", [128, D], BF16) for e in range(2)]
        masksb = sb("masksb", [8, 8])
        p_l = psum("p_l", [128, 8])
        p_t = psum("p_t", [8, 128], BF16)
        p_k = psum("p_k", [128, NT])
        p_y = [psum(f"p_y{j}", [128, 512]) for j in range(4)]

        sem_names = ["oos", "oov", "oog", "ld", "wld", "sxh", "stv", "stg",
                     "pe", "pet", "pek", "peb", "pey", "vnm", "sgin", "voh",
                     "vohm", "vscan", "cc", "gkm", "vc", "ev", "osem", "osc",
                     "trx", "lnb", "wldc", "ccc", "sxh2", "sthx", "ccc2"]
        sems = {n: nc.alloc_semaphore(name=tag + n) for n in sem_names}
        S = sems
        SER = {"s": SerEngine(nc.scalar, S["oos"]),
               "v": SerEngine(nc.vector, S["oov"]),
               "g": SerEngine(nc.gpsimd, S["oog"])}

        with nc.Block() as blk:
            @blk.sync
            def _(sync):
                for i in range(NT):
                    sync.dma_start(out=xt[i][:], in_=x3[i]).then_inc(sems["ld"], 16)
                sync.dma_start(out=bsb[:], in_=be_d[:]).then_inc(S["wldc"], 16)
                sync.dma_start(out=masksb[:], in_=mask_d[:]).then_inc(S["wldc"], 16)
            @blk.gpsimd
            def _(g0):
                g = SER["g"]
                for k in range(8):
                    g.dma_start(out=wg8[k][:], in_=wg_d[128 * k:128 * (k + 1), :]) \
                        .then_inc(sems["wld"], 16)
                    g.dma_start(out=wexp[k][:], in_=we_d[128 * k:128 * (k + 1), :]) \
                        .then_inc(sems["wld"], 16)
                g.iota(colidx[:], [[1, 128]], base=0, channel_multiplier=0,
                       allow_small_or_imprecise_dtypes=True)
                g.iota(rowidx[:], [[0, 128]], base=0, channel_multiplier=1,
                       allow_small_or_imprecise_dtypes=True)
                g.signal(S["lnb"])
            @blk.vector
            def _(v0):
                v = SER["v"]
                v.memset(onesr[:], 1.0)
                v.memset(ones8[:], 1.0)
                v.memset(zed[:], 0.0)
                v.wait_ge(S["lnb"], 1)
                v.tensor_tensor(out=identb[:], in0=colidx[:], in1=rowidx[:],
                                op=AluOpType.is_equal)
                v.signal(S["vnm"])
                v.wait_ge(S["peb"], 1)
                v.tensor_copy(out=bb[:, 0:512], in_=p_y[0][:])
                v.tensor_copy(out=bb[:, 512:1024], in_=p_y[1][:])
                v.signal(S["peb"])
            @blk.tensor
            def _(te):
                te.wait_ge(S["wldc"], 32)
                te.wait_ge(S["vnm"], 1)
                te.matmul(out=p_y[0][:], lhsT=onesr[:], rhs=bsb[:, 0:512],
                          start=True, stop=True)
                te.matmul(out=p_y[1][:], lhsT=onesr[:], rhs=bsb[:, 512:1024],
                          start=True, stop=True).then_inc(S["peb"], 1)

        with nc.Block() as blk:
            @blk.scalar
            def _(s0):
                s = SER["s"]
                s.wait_ge(sems["ld"], 16 * NT)
                for i in range(NT):
                    if i >= 2:
                        if i % 2 == 0:
                            s.wait_ge(S["stv"], 16 * (i // 2))
                        else:
                            s.wait_ge(S["stg"], 16 * ((i - 1) // 2))
                    emit_ln(s, xt[i][:], lnsums, sc2, xhb[i % 2][:], junk[:])
                    s.signal(S["sxh"])
            @blk.sync
            def _(sync):
                for i in range(NT):
                    sync.wait_ge(S["sxh"], i + 1)
                    if i % 2 == 0:
                        sync.dma_start(out=xh3[i], in_=xhb[0][:]).then_inc(S["stv"], 16)
                    else:
                        sync.dma_start(out=xh3[i], in_=xhb[1][:]).then_inc(S["stg"], 16)

        with nc.Block() as blk:
            @blk.sync
            def _(sync):
                sync.wait_ge(S["stv"], 16 * 8)
                sync.wait_ge(S["stg"], 16 * 8)
                for k in range(8):
                    sync.dma_start(out=xhf[k][:], in_=xh_dram[:, 128 * k:128 * (k + 1)],
                                   transpose=True).then_inc(S["trx"], 16)

        with nc.Block() as blk:
            @blk.tensor
            def _(te):
                te.wait_ge(S["trx"], 16 * 8)
                te.wait_ge(sems["wld"], 16 * NT)
                for t in range(NT):
                    if t >= 1:
                        te.wait_ge(S["vnm"], 1 + 2 * t)
                        te.wait_ge(S["vohm"], t)
                    for k in range(8):
                        mm = te.matmul(out=p_l[:], lhsT=xhf[k][:, 128 * t:128 * (t + 1)],
                                       rhs=wg8[k][:], start=(k == 0), stop=(k == 7))
                    mm.then_inc(S["pe"], 1)
                    te.wait_ge(S["voh"], t + 1)
                    te.transpose(out=p_t[:], in_=ohb16[:], identity=identb[:]) \
                        .then_inc(S["pet"], 1)

            @blk.vector
            def _(v0):
                v = SER["v"]
                for t in range(NT):
                    v.wait_ge(S["pe"], t + 1)
                    if t >= 1:
                        v.wait_ge(S["sgin"], t)
                        v.wait_ge(S["pet"], t)
                    v.tensor_copy(out=lt[:], in_=p_l[:])
                    v.signal(S["vnm"])
                    v.tensor_tensor(out=mtree[:, 0:4], in0=lt[:, 0:4], in1=lt[:, 4:8],
                                    op=AluOpType.max)
                    v.tensor_tensor(out=mtree[:, 4:6], in0=mtree[:, 0:2],
                                    in1=mtree[:, 2:4], op=AluOpType.max)
                    v.tensor_tensor(out=mtree[:, 6:7], in0=mtree[:, 4:5],
                                    in1=mtree[:, 5:6], op=AluOpType.max)
                    _ts(v, nm[:], mtree[:, 6:7], -1.0, AluOpType.mult)
                    v.signal(S["vnm"])
                    mb = mtree[:, 6:7].broadcast_to((128, 8))
                    v.tensor_tensor(out=ohr[:], in0=lt[:], in1=mb, op=AluOpType.is_equal)
                    v.tensor_copy(out=p1[:], in_=ohr[:])
                    v.tensor_tensor(out=p1[:, 1:8], in0=ohr[:, 1:8], in1=ohr[:, 0:7],
                                    op=AluOpType.add)
                    v.tensor_copy(out=p2[:], in_=p1[:])
                    v.tensor_tensor(out=p2[:, 2:8], in0=p1[:, 2:8], in1=p1[:, 0:6],
                                    op=AluOpType.add)
                    v.tensor_copy(out=p4[:], in_=p2[:])
                    v.tensor_tensor(out=p4[:, 4:8], in0=p2[:, 4:8], in1=p2[:, 0:4],
                                    op=AluOpType.add)
                    _ts(v, msk[:], p4[:], 1.0, AluOpType.is_equal)
                    v.tensor_tensor(out=ohb16[:], in0=ohr[:], in1=msk[:],
                                    op=AluOpType.mult)
                    v.signal(S["voh"])
                    v.wait_ge(S["pet"], t + 1)
                    v.tensor_copy(out=ohm[:, 128 * t:128 * (t + 1)], in_=p_t[:])
                    v.signal(S["vohm"])

            @blk.scalar
            def _(s0):
                s = SER["s"]
                for t in range(NT):
                    s.wait_ge(S["vnm"], 2 * t + 3)
                    s.activation(out=et[:], in_=lt[:], func=AF.Exp, bias=nm[:],
                                 accum_out=se[:])
                    s.activation(out=se[:], in_=se[:], func=AF.Ln)
                    s.activation(out=gv[:, t:t + 1], in_=se[:], func=AF.Exp, scale=-1.0)
                    s.signal(S["sgin"])

        with nc.Block() as blk:
            @blk.vector
            def _(v0):
                v = SER["v"]
                v.wait_ge(S["vohm"], NT)
                v.tensor_tensor_scan(out=counts[:], data0=ohm[:], data1=zed[:],
                                     initial=0.0, op0=AluOpType.add, op1=AluOpType.add)
                v.signal(S["vscan"])
                v.tensor_copy(out=tot[:], in_=counts[:, T - 1:T])
                v.signal(S["vscan"])
            @blk.gpsimd
            def _(g0):
                g = SER["g"]
                g.wait_ge(S["vscan"], 2)
                g.dma_start(out=cin[:], in_=tot[:]).then_inc(S["cc"], 16)
                g.wait_ge(S["cc"], 16)
                g.collective_compute(
                    "AllGather", mybir.AluOpType.bypass,
                    replica_groups=[list(range(rg_cores))],
                    ins=[cin.opt()], outs=[cout[0:8 * rg_cores, :].opt()],
                ).then_inc(S["ccc"], 1)
                g.wait_ge(S["ccc"], 1)
                g.memset(cm[:], 0.0)
                g.signal(S["lnb"])
                g.wait_ge(S["lnb"], 2)
                g.dma_start(out=cm[:, 0:rg_cores],
                            in_=cout[0:8 * rg_cores, :].rearrange("(c e) one -> e (c one)", e=8),
                            allow_slow_non_contiguous=True).then_inc(S["cc"], 16)
                g.wait_ge(S["cc"], 32)
                g.wait_ge(S["wldc"], 32)
                g.tensor_tensor(out=cmm[:], in0=cm[:], in1=masksb[:], op=AluOpType.mult)
                g.tensor_tensor(out=car1[:], in0=cmm[:, 0:4], in1=cmm[:, 4:8],
                                op=AluOpType.add)
                g.tensor_tensor(out=car2[:], in0=car1[:, 0:2], in1=car1[:, 2:4],
                                op=AluOpType.add)
                g.tensor_tensor(out=carry[:], in0=car2[:, 0:1], in1=car2[:, 1:2],
                                op=AluOpType.add)
                _ts(g, tmpc[:], counts[:], carry[:], AluOpType.add, 100.0,
                    AluOpType.is_le)
                g.tensor_tensor(out=km[:], in0=tmpc[:], in1=ohm[:], op=AluOpType.mult)
                g.signal(S["gkm"])
            @blk.tensor
            def _(te):
                te.wait_ge(S["gkm"], 1)
                mm = None
                for t in range(NT):
                    mm = te.matmul(out=p_k[:, t:t + 1], lhsT=km[:, 128 * t:128 * (t + 1)],
                                   rhs=ones8[:], start=True, stop=True)
                mm.then_inc(S["pek"], 1)
            @blk.vector
            def _(v0):
                v = SER["v"]
                v.wait_ge(S["pek"], 1)
                v.wait_ge(S["sgin"], NT)
                v.scalar_tensor_tensor(out=chalf[:], in0=p_k[:], scalar=0.5,
                                       in1=gv[:], op0=AluOpType.mult,
                                       op1=AluOpType.mult)
                v.signal(S["vc"])

        with nc.Block() as blk:
            @blk.tensor
            def _(te):
                te.wait_ge(S["vc"], 1)
                te.wait_ge(S["peb"], 2)
                for t in range(NT):
                    if t >= 2:
                        te.wait_ge(S["ev"], 2 * (t - 1))
                    for nblk in range(2):
                        j = 2 * (t % 2) + nblk
                        for k in range(8):
                            mm = te.matmul(out=p_y[j][:],
                                           lhsT=xhf[k][:, 128 * t:128 * (t + 1)],
                                           rhs=wexp[k][:, 512 * nblk:512 * (nblk + 1)],
                                           start=(k == 0), stop=(k == 7))
                        mm.then_inc(S["pey"], 1)
            @blk.vector
            def _(v0):
                v = SER["v"]
                for t in range(NT):
                    e = t % 2
                    if t >= 2:
                        v.wait_ge(S["osc"], t - 1)
                    for nblk in range(2):
                        j = 2 * e + nblk
                        v.wait_ge(S["pey"], 2 * t + nblk + 1)
                        v.tensor_tensor(out=uev[e][:], in0=p_y[j][:],
                                        in1=bb[:, 512 * nblk:512 * (nblk + 1)],
                                        op=AluOpType.add)
                        v.scalar_tensor_tensor(
                            out=xot[e][:, 512 * nblk:512 * (nblk + 1)],
                            in0=uev[e][:], scalar=chalf[:, t:t + 1],
                            in1=xt[t][:, 512 * nblk:512 * (nblk + 1)],
                            op0=AluOpType.mult, op1=AluOpType.add)
                        v.signal(S["ev"])
            @blk.scalar
            def _(s0):
                if xh_out is None:
                    return
                s = SER["s"]
                for t in range(NT):
                    s.wait_ge(S["ev"], 2 * t + 2)
                    if t >= 2:
                        s.wait_ge(S["sthx"], 16 * (t - 1))
                    emit_ln(s, xot[t % 2][:], lnsums, sc2, xhb2m[t % 2][:],
                            junk[:])
                    s.signal(S["sxh2"])
            @blk.gpsimd
            def _(g0):
                g = SER["g"]
                xh3o = (xh_out.rearrange("(t p) d -> t p d", p=128)
                        if xh_out is not None else None)
                for t in range(NT):
                    g.wait_ge(S["ev"], 2 * t + 2)
                    g.dma_start(out=xo3[t], in_=xot[t % 2][:]).then_inc(S["osem"], 16)
                    if xh_out is not None:
                        g.wait_ge(S["sxh2"], t + 1)
                        g.dma_start(out=xh3o[t], in_=xhb2m[t % 2][:]) \
                            .then_inc(S["sthx"], 16)
                        g.wait_ge(S["sthx"], 16 * (t + 1))
                    g.wait_ge(S["osem"], 16 * (t + 1))
                    g.sem_inc(S["osc"], 1)
        if xh_out is not None:
            with nc.Block() as blk:
                @blk.gpsimd
                def _(g0):
                    g0.collective_compute(
                        "AllGather", mybir.AluOpType.bypass,
                        replica_groups=[list(range(NCORES))],
                        ins=[xh_out.opt()], outs=[xhg.opt()],
                    ).then_inc(S["ccc2"])
                    g0.wait_ge(S["ccc2"], 1)


def run_moe(nc_cache, x_flat, w_gate, w_expert, b_expert):
    if "moe" not in nc_cache:
        nc_cache["moe"] = build_moe()
    nc = nc_cache["moe"]
    in_maps = []
    for c in range(NCORES):
        mask = np.tile((np.arange(8) < c).astype(np.float32)[None, :], (8, 1))
        in_maps.append({"x": np.ascontiguousarray(x_flat[c * T:(c + 1) * T]),
                        "wgate": w_gate, "wexp": w_expert,
                        "bexp": b_expert.reshape(1, D), "coremask": mask})
    res = run_bass_kernel_spmd(nc, in_maps, core_ids=list(range(NCORES)))
    return np.concatenate([np.asarray(res.results[c]["x_out"]) for c in range(NCORES)], axis=0)

def _emit_ln_stream(nc, stack, S, SER, x3, xh3, NTILES=NT, tag=""):
    lnsums = stack.enter_context(nc.sbuf_tensor(tag + "lnsums2", [128, 2], F32))
    sc2 = stack.enter_context(nc.sbuf_tensor(tag + "sc22", [128, 4], F32))
    junk = stack.enter_context(nc.sbuf_tensor(tag + "junk2", [128, D], BF16))
    xin = [stack.enter_context(nc.sbuf_tensor(f"{tag}xin{e}", [128, D], F32))
           for e in range(2)]
    xhb = [stack.enter_context(nc.sbuf_tensor(f"{tag}xhb{e}", [128, D], BF16))
           for e in range(2)]
    with nc.Block() as blk:
        @blk.sync
        def _(sync):
            for i in range(NTILES):
                if i >= 2:
                    sync.wait_ge(S["sxh"], i - 1)
                sync.dma_start(out=xin[i % 2][:], in_=x3[i]).then_inc(
                    S["ldv" if i % 2 == 0 else "ldg"], 16)
        @blk.gpsimd
        def _(g0):
            g = SER["g"]
            for i in range(NTILES):
                g.wait_ge(S["sxh"], i + 1)
                if i % 2 == 0:
                    g.dma_start(out=xh3[i], in_=xhb[0][:]).then_inc(S["stv"], 16)
                else:
                    g.dma_start(out=xh3[i], in_=xhb[1][:]).then_inc(S["stg"], 16)
        @blk.scalar
        def _(s0):
            s = SER["s"]
            for i in range(NTILES):
                s.wait_ge(S["ldv" if i % 2 == 0 else "ldg"], 16 * (i // 2 + 1))
                if i >= 2:
                    if i % 2 == 0:
                        s.wait_ge(S["stv"], 16 * (i // 2))
                    else:
                        s.wait_ge(S["stg"], 16 * ((i - 1) // 2))
                emit_ln(s, xin[i % 2][:], lnsums, sc2, xhb[i % 2][:], junk[:])
                s.signal(S["sxh"])


def _emit_ln_to_dram(nc, stack, S, SER, xt, x3, xh3, NTILES=NT):
    lnsums = stack.enter_context(nc.sbuf_tensor("lnsums", [128, 2], F32))
    sc2 = stack.enter_context(nc.sbuf_tensor("sc2", [128, 4], F32))
    junk = stack.enter_context(nc.sbuf_tensor("junk", [128, D], BF16))
    xhb = [stack.enter_context(nc.sbuf_tensor(f"xhb{e}", [128, D], BF16))
           for e in range(2)]
    with nc.Block() as blk:
        @blk.scalar
        def _(s0):
            s = SER["s"]
            s.wait_ge(S["ld"], 16 * NTILES)
            for i in range(NTILES):
                if i >= 2:
                    if i % 2 == 0:
                        s.wait_ge(S["stv"], 16 * (i // 2))
                    else:
                        s.wait_ge(S["stg"], 16 * ((i - 1) // 2))
                emit_ln(s, xt[i][:], lnsums, sc2, xhb[i % 2][:], junk[:])
                s.signal(S["sxh"])
        @blk.sync
        def _(sync):
            for i in range(NTILES):
                sync.wait_ge(S["sxh"], i + 1)
                if i % 2 == 0:
                    sync.dma_start(out=xh3[i], in_=xhb[0][:]).then_inc(S["stv"], 16)
                else:
                    sync.dma_start(out=xh3[i], in_=xhb[1][:]).then_inc(S["stg"], 16)


def build_att1_a():
    nc = bass.Bass("TRN2", target_bir_lowering=False)
    x_in = nc.dram_tensor("x", [T, D], F32, kind="ExternalInput").ap()
    wqT_d = nc.dram_tensor("wqT", [D, D], F32, kind="ExternalInput").ap()
    wkT_d = nc.dram_tensor("wkT", [D, D], F32, kind="ExternalInput").ap()
    wvT_d = nc.dram_tensor("wvT", [D, D], F32, kind="ExternalInput").ap()
    bq_d = nc.dram_tensor("bq", [1, D], F32, kind="ExternalInput").ap()
    bk_d = nc.dram_tensor("bk", [1, D], F32, kind="ExternalInput").ap()
    bv_d = nc.dram_tensor("bv", [1, D], F32, kind="ExternalInput").ap()
    poff_d = nc.dram_tensor("poff", [1, 1], F32, kind="ExternalInput").ap()
    q_out = nc.dram_tensor("q_out", [T, D], BF16, kind="ExternalOutput").ap()
    k_out = nc.dram_tensor("k_out", [T, D], BF16, kind="ExternalOutput").ap()
    v_out = nc.dram_tensor("v_out", [T, D], BF16, kind="ExternalOutput").ap()
    xh_dram = nc.dram_tensor("xh_dram", [T, D], BF16, kind="Internal").ap()

    x3 = x_in.rearrange("(t p) d -> t p d", p=128)
    xh3 = xh_dram.rearrange("(t p) d -> t p d", p=128)
    q3 = q_out.rearrange("(t p) d -> t p d", p=128)
    k3 = k_out.rearrange("(t p) d -> t p d", p=128)
    v3 = v_out.rearrange("(t p) d -> t p d", p=128)

    with ExitStack() as stack:
        def sb(name, shape, dt=F32):
            return stack.enter_context(nc.sbuf_tensor(name, shape, dt))
        def psum(name, shape, dt=F32):
            return stack.enter_context(nc.psum_tensor(name, shape, dt))
        xhf = [sb(f"xhf{k}", [128, T], BF16) for k in range(8)]
        wq_s = [sb(f"wq_s{k}", [128, D], BF16) for k in range(8)]
        wk_s = [sb(f"wk_s{k}", [128, D], BF16) for k in range(8)]
        wv_s = [sb(f"wv_s{k}", [128, D], BF16) for k in range(8)]
        bqs = sb("bqs", [1, D]); bks = sb("bks", [1, D]); bvs = sb("bvs", [1, D])
        bb = {0: sb("bbq", [128, D]), 1: sb("bbk", [128, D]), 2: sb("bbv", [128, D])}
        onesr = sb("onesr", [1, 128])
        poffs = sb("poffs", [1, 1])
        iotar = sb("iotar", [1, 128])
        iotaj = sb("iotaj", [1, 32])
        invf = sb("invf", [1, 32]); invf64 = sb("invf64", [1, 64])
        posr = sb("posr", [1, 128])
        angs = sb("angs", [128, 64])
        rr = [sb(f"rr{e}", [128, 64]) for e in range(2)]
        sint = sb("sint", [128, 64 * NT]); cost = sb("cost", [128, 64 * NT])
        prj = sb("prj", [128, D])
        rot = sb("rot", [128, D]); tmpr = sb("tmpr", [128, D])
        outq = [sb(f"outq{e}", [128, D], BF16) for e in range(2)]
        outk = [sb(f"outk{e}", [128, D], BF16) for e in range(2)]
        outv = [sb(f"outv{e}", [128, D], BF16) for e in range(2)]
        p_q = [psum(f"p_q{j}", [128, 512]) for j in range(2)]
        p_k = [psum(f"p_k{j}", [128, 512]) for j in range(2)]
        p_v = [psum(f"p_v{j}", [128, 512]) for j in range(2)]
        p_ang = psum("p_ang", [128, 64])

        sem_names = ["oos", "oov", "oog", "ldv", "ldg", "wld", "cld", "sxh",
                     "stv", "stg", "trx", "peq", "pek", "pev", "pang", "pbias",
                     "vcb", "vq", "vk", "vv", "ostq0", "ostq1", "ostk0", "ostk1",
                     "ostv0", "ostv1", "vps", "ssin", "gcon"]
        S = {n: stack.enter_context(nc.semaphore(n)) for n in sem_names}
        SER = {"s": SerEngine(nc.scalar, S["oos"]),
               "v": SerEngine(nc.vector, S["oov"]),
               "g": SerEngine(nc.gpsimd, S["oog"])}

        with nc.Block() as blk:
            @blk.sync
            def _(sync):
                sync.dma_start(out=bqs[:], in_=bq_d[:]).then_inc(S["cld"], 16)
                sync.dma_start(out=bks[:], in_=bk_d[:]).then_inc(S["cld"], 16)
                sync.dma_start(out=bvs[:], in_=bv_d[:]).then_inc(S["cld"], 16)
                sync.dma_start(out=poffs[:], in_=poff_d[:]).then_inc(S["cld"], 16)
            @blk.gpsimd
            def _(g0):
                g = SER["g"]
                for k in range(8):
                    g.dma_start(out=wq_s[k][:], in_=wqT_d[128 * k:128 * (k + 1), :]) \
                        .then_inc(S["wld"], 16)
                    g.dma_start(out=wk_s[k][:], in_=wkT_d[128 * k:128 * (k + 1), :]) \
                        .then_inc(S["wld"], 16)
                    g.dma_start(out=wv_s[k][:], in_=wvT_d[128 * k:128 * (k + 1), :]) \
                        .then_inc(S["wld"], 16)
                g.iota(iotar[:], [[1, 128]], base=0, channel_multiplier=0,
                       allow_small_or_imprecise_dtypes=True)
                g.iota(iotaj[:], [[1, 32]], base=0, channel_multiplier=0,
                       allow_small_or_imprecise_dtypes=True)
                g.signal(S["gcon"])
            @blk.vector
            def _(v0):
                v = SER["v"]
                v.memset(onesr[:], 1.0)
                v.signal(S["vcb"])
            @blk.scalar
            def _(s0):
                s = SER["s"]
                s.wait_ge(S["gcon"], 1)
                s.activation(out=invf[:], in_=iotaj[:], func=AF.Exp,
                             scale=-float(np.log(10000.0)) / 32.0)
                iv2 = invf64[:].rearrange("one (k two) -> one k two", two=2)
                s.activation(out=iv2[:, :, 0:1], in_=invf[:].unsqueeze(2), func=AF.Copy)
                s.activation(out=iv2[:, :, 1:2], in_=invf[:].unsqueeze(2), func=AF.Copy)
                s.signal(S["ssin"])

        with nc.Block() as blk:
            @blk.tensor
            def _(te):
                te.wait_ge(S["cld"], 16 * 4)
                te.wait_ge(S["vcb"], 1)
                for nm, src in enumerate((bqs, bks, bvs)):
                    for half in range(2):
                        te.matmul(out=p_v[half][:], lhsT=onesr[:],
                                  rhs=src[:, 512 * half:512 * (half + 1)],
                                  start=True, stop=True).then_inc(S["pbias"], 1)
                    te.wait_ge(S["vcb"], 1 + nm + 1)
                for t in range(NT):
                    te.wait_ge(S["vps"], 4 * t + 1)
                    te.matmul(out=p_ang[:], lhsT=posr[:], rhs=invf64[:],
                              start=True, stop=True).then_inc(S["pang"], 1)
                    te.wait_ge(S["vps"], 4 * t + 2)
            @blk.vector
            def _(v0):
                v = SER["v"]
                for nm in range(3):
                    v.wait_ge(S["pbias"], 2 * nm + 2)
                    v.tensor_copy(out=bb[nm][:, 0:512], in_=p_v[0][:])
                    v.tensor_copy(out=bb[nm][:, 512:1024], in_=p_v[1][:])
                    v.signal(S["vcb"])
                for t in range(NT):
                    _ts(v, posr[:], iotar[:], poffs[0:1, 0:1], AluOpType.add,
                        float(128 * t), AluOpType.add)
                    v.signal(S["vps"])
                    v.wait_ge(S["pang"], t + 1)
                    v.tensor_copy(out=angs[:], in_=p_ang[:])
                    v.signal(S["vps"])
                    _ts(v, rr[0][:], angs[:], 1.0 / (2 * np.pi), AluOpType.mult)
                    _ts(v, rr[0][:], rr[0][:], MAGIC, AluOpType.add)
                    _ts(v, rr[0][:], rr[0][:], MAGIC, AluOpType.subtract)
                    _ts(v, rr[0][:], rr[0][:], -2 * np.pi, AluOpType.mult)
                    v.tensor_tensor(out=rr[0][:], in0=angs[:], in1=rr[0][:],
                                    op=AluOpType.add)
                    _ts(v, rr[0][:], rr[0][:], -float(np.pi), AluOpType.max,
                        float(np.pi), AluOpType.min)
                    v.signal(S["vps"])
                    _ts(v, rr[1][:], angs[:], np.pi / 2, AluOpType.add)
                    _ts(v, tmpr[:, 0:64], rr[1][:], 1.0 / (2 * np.pi), AluOpType.mult)
                    _ts(v, tmpr[:, 0:64], tmpr[:, 0:64], MAGIC, AluOpType.add)
                    _ts(v, tmpr[:, 0:64], tmpr[:, 0:64], MAGIC, AluOpType.subtract)
                    _ts(v, tmpr[:, 0:64], tmpr[:, 0:64], -2 * np.pi, AluOpType.mult)
                    v.tensor_tensor(out=rr[1][:], in0=rr[1][:], in1=tmpr[:, 0:64],
                                    op=AluOpType.add)
                    _ts(v, rr[1][:], rr[1][:], -float(np.pi), AluOpType.max,
                        float(np.pi), AluOpType.min)
                    v.signal(S["vps"])
                    v.wait_ge(S["ssin"], 2 * t + 3)
            @blk.scalar
            def _(s0):
                s = SER["s"]
                for t in range(NT):
                    s.wait_ge(S["vps"], 4 * t + 3)
                    s.activation(out=sint[:, 64 * t:64 * (t + 1)], in_=rr[0][:],
                                 func=AF.Sin)
                    s.signal(S["ssin"])
                    s.wait_ge(S["vps"], 4 * t + 4)
                    s.activation(out=cost[:, 64 * t:64 * (t + 1)], in_=rr[1][:],
                                 func=AF.Sin)
                    s.signal(S["ssin"])

        _emit_ln_stream(nc, stack, S, SER, x3, xh3)

        with nc.Block() as blk:
            @blk.sync
            def _(sync):
                sync.wait_ge(S["stv"], 16 * 8)
                sync.wait_ge(S["stg"], 16 * 8)
                for k in range(8):
                    sync.dma_start(out=xhf[k][:], in_=xh_dram[:, 128 * k:128 * (k + 1)],
                                   transpose=True).then_inc(S["trx"], 16)

        with nc.Block() as blk:
            @blk.tensor
            def _(te):
                te.wait_ge(S["trx"], 16 * 8)
                te.wait_ge(S["wld"], 16 * 24)
                for t in range(NT):
                    for (ps, w_s, sem, vsem) in ((p_q, wq_s, "peq", "vq"),
                                                 (p_k, wk_s, "pek", "vk"),
                                                 (p_v, wv_s, "pev", "vv")):
                        if t >= 1:
                            te.wait_ge(S[vsem], t)
                        for half in range(2):
                            for k in range(8):
                                mm = te.matmul(out=ps[half][:],
                                               lhsT=xhf[k][:, 128 * t:128 * (t + 1)],
                                               rhs=w_s[k][:, 512 * half:512 * (half + 1)],
                                               start=(k == 0), stop=(k == 7))
                        mm.then_inc(S[sem], 1)
            @blk.vector
            def _(v0):
                v = SER["v"]
                v.wait_ge(S["ssin"], 1 + 2 * NT)
                for t in range(NT):
                    e = t % 2
                    cr = cost[:, 64 * t:64 * (t + 1)].unsqueeze(1) \
                        .broadcast_to((128, 16, 64))
                    sr = sint[:, 64 * t:64 * (t + 1)].unsqueeze(1) \
                        .broadcast_to((128, 16, 64))
                    for (ps, pesem, stsem, outx, rope) in (
                            (p_q, "peq", "ostq", outq, True),
                            (p_k, "pek", "ostk", outk, True),
                            (p_v, "pev", "ostv", outv, False)):
                        v.wait_ge(S[pesem], t + 1)
                        if t >= 2:
                            v.wait_ge(S[stsem + str(e)], 16 * (t // 2))
                        bbx = bb[0] if ps is p_q else (bb[1] if ps is p_k else bb[2])
                        for half in range(2):
                            v.tensor_tensor(out=prj[:, 512 * half:512 * (half + 1)],
                                            in0=ps[half][:],
                                            in1=bbx[:, 512 * half:512 * (half + 1)],
                                            op=AluOpType.add)
                        if rope:
                            pr2 = prj[:].rearrange("p (h k two) -> p h k two",
                                                   h=16, two=2)
                            ro2 = rot[:].rearrange("p (h k two) -> p h k two",
                                                   h=16, two=2)
                            _ts(v, ro2[:, :, :, 0:1], pr2[:, :, :, 1:2], -1.0,
                                AluOpType.mult)
                            v.tensor_copy(out=ro2[:, :, :, 1:2], in_=pr2[:, :, :, 0:1])
                            v.tensor_tensor(
                                out=tmpr[:].rearrange("p (h k) -> p h k", h=16),
                                in0=prj[:].rearrange("p (h k) -> p h k", h=16),
                                in1=cr, op=AluOpType.mult)
                            v.tensor_tensor(
                                out=rot[:].rearrange("p (h k) -> p h k", h=16),
                                in0=rot[:].rearrange("p (h k) -> p h k", h=16),
                                in1=sr, op=AluOpType.mult)
                            v.tensor_tensor(out=outx[e][:], in0=tmpr[:], in1=rot[:],
                                            op=AluOpType.add)
                        else:
                            v.tensor_copy(out=outx[e][:], in_=prj[:])
                        vs = "vq" if ps is p_q else ("vk" if ps is p_k else "vv")
                        v.signal(S[vs])
            @blk.sync
            def _(sync):
                for t in range(NT):
                    e = str(t % 2)
                    sync.wait_ge(S["vq"], t + 1)
                    sync.dma_start(out=q3[t], in_=outq[t % 2][:]) \
                        .then_inc(S["ostq" + e], 16)
                    sync.wait_ge(S["vk"], t + 1)
                    sync.dma_start(out=k3[t], in_=outk[t % 2][:]) \
                        .then_inc(S["ostk" + e], 16)
                    sync.wait_ge(S["vv"], t + 1)
                    sync.dma_start(out=v3[t], in_=outv[t % 2][:]) \
                        .then_inc(S["ostv" + e], 16)
                for e in "01":
                    sync.wait_ge(S["ostq" + e], 16 * (NT // 2))
                    sync.wait_ge(S["ostk" + e], 16 * (NT // 2))
                    sync.wait_ge(S["ostv" + e], 16 * (NT // 2))
    return nc



SQ = 8192

def _mk_sched():
    sched = []
    for qb in range(SQ // 512):
        nkt = 4 * (qb + 1)
        for kt in range(nkt):
            sched.append((qb, kt, kt == 0, kt == nkt - 1, kt >= 4 * qb))
    return sched

_SCHED_B = _mk_sched()
NMASK_UPTO = []
_nm = 0
for _u in range(2):
    for (_qb, _kt, _f, _l, _d) in _SCHED_B:
        if _d:
            _nm += 1
        NMASK_UPTO.append(_nm)

def build_att1_b():
    nc = bass.Bass("TRN2", target_bir_lowering=False)
    TEN = {}
    for n in ("qsl", "ksl", "vsl"):
        TEN[n] = nc.dram_tensor(n, [2, SQ, 128], BF16, kind="ExternalInput").ap()
    TEN["oT_out"] = nc.dram_tensor("oT_out", [2, 2, 65, SQ], BF16,
                                   kind="ExternalOutput").ap()
    emit_att1b(nc, TEN, fm=False)
    return nc


def emit_att1b(nc, TEN, tag="", fm=False):
    if fm:
        qfm_loc = TEN["qfm_loc"]; kfm_loc = TEN["kfm_loc"]
        vfm_loc = TEN["vfm_loc"]
    else:
        qsl = TEN["qsl"]; ksl = TEN["ksl"]; vsl = TEN["vsl"]
    oT_out = TEN["oT_out"]

    NQB = SQ // 512
    sched = []
    for qb in range(NQB):
        nkt = 4 * (qb + 1)
        for kt in range(nkt):
            sched.append((qb, kt, kt == 0, kt == nkt - 1, kt >= 4 * qb))
    NSTEP = len(sched)

    with ExitStack() as stack:
        def sb(name, shape, dt=F32):
            return stack.enter_context(nc.sbuf_tensor(tag + name, shape, dt))
        def psum(name, shape, dt=F32):
            return stack.enter_context(nc.psum_tensor(tag + name, shape, dt))
        qfm = sb("qfm", [128, SQ], BF16)
        kfm = sb("kfm", [128, SQ], BF16)
        VS = 128 if fm else 65
        vv = [sb(f"vv{e}", [128, 64 * VS], BF16) for e in range(2)]
        a2 = [sb(f"a2_{c}", [128, 1024], BF16) for c in range(3)]
        masks = [sb(f"mask{j}", [128, 512], BF16) for j in range(4)]
        coli = sb("coli", [128, 512]); rowi = sb("rowi", [128, 512])
        dmi = sb("dmi", [128, 512])
        b15 = sb("b15", [128, 1])
        oev = [sb(f"oev{e}", [128, 512], BF16) for e in range(2)]
        p_s = [psum(f"p_s{c}", [128, 1024]) for c in range(3)]
        p_o = [psum(f"p_o{e}", [128, 512]) for e in range(2)]

        sem_names = ["oos", "oov", "oog", "qld", "kld", "vld", "pes", "sexp",
                     "vmsk", "pav", "vev", "ostA", "ostB", "gcon", "vcon"]
        S = {n: nc.alloc_semaphore(name=tag + n) for n in sem_names}
        SER = {"s": SerEngine(nc.scalar, S["oos"]),
               "v": SerEngine(nc.vector, S["oov"]),
               "g": SerEngine(nc.gpsimd, S["oog"])}
        ost = [S["ostA"], S["ostB"]]

        with nc.Block() as blk:
            @blk.gpsimd
            def _(g0):
                g = SER["g"]
                g.iota(coli[:], [[1, 512]], base=0, channel_multiplier=0,
                       allow_small_or_imprecise_dtypes=True)
                g.iota(rowi[:], [[0, 512]], base=0, channel_multiplier=1,
                       allow_small_or_imprecise_dtypes=True)
                g.signal(S["gcon"])
            @blk.vector
            def _(v0):
                v = SER["v"]
                v.memset(b15[:], -15.0)
                for e in range(2):
                    v.memset(vv[e][:].rearrange("p (n c) -> p n c", c=VS)[:, :, 64:65],
                             1.0)
                v.wait_ge(S["gcon"], 1)
                v.tensor_tensor(out=dmi[:], in0=coli[:], in1=rowi[:],
                                op=AluOpType.subtract)
                for j in range(4):
                    _ts(v, masks[j][:], dmi[:], float(128 * j), AluOpType.is_ge)
                v.signal(S["vcon"])

        cnt = {"pes": 0, "sexp": 0, "vmsk": 0, "pav": 0, "vev": 0,
               "ost": [0, 0], "qld": 0, "kld": 0, "vld": 0}
        nmask_upto = []
        for u in range(2):
            with nc.Block() as blk:
                @blk.sync
                def _(sync, u=u):
                    if u == 1:
                        sync.wait_ge(S["pav"], NSTEP)
                    if fm:
                        sync.dma_start(out=qfm[:], in_=qfm_loc[u]) \
                            .then_inc(S["qld"], 16)
                        sync.dma_start(out=kfm[:], in_=kfm_loc[u]) \
                            .then_inc(S["kld"], 16)
                        for e in range(2):
                            for kt in range(SQ // 128):
                                sync.dma_start(
                                    out=vv[e][:, VS * kt:VS * kt + 64],
                                    in_=vfm_loc[u][64 * e:64 * (e + 1),
                                                   128 * kt:128 * (kt + 1)],
                                    transpose=True).then_inc(S["vld"], 16)
                    else:
                        sync.dma_start(out=qfm[:], in_=qsl[u], transpose=True) \
                            .then_inc(S["qld"], 16)
                        sync.dma_start(out=kfm[:], in_=ksl[u], transpose=True) \
                            .then_inc(S["kld"], 16)
                @blk.gpsimd
                def _(g0, u=u):
                    if fm:
                        return
                    g = SER["g"]
                    if u == 1:
                        g.wait_ge(S["pav"], NSTEP)
                    vin = vsl[u].rearrange("(n p) h2 -> p n h2", p=128)
                    for e in range(2):
                        g.dma_start(
                            out=vv[e][:].rearrange("p (n c) -> p n c", c=VS)[:, :, 0:64],
                            in_=vin[:, :, 64 * e:64 * (e + 1)]).then_inc(S["vld"], 16)

            with nc.Block() as blk:
                base = u * NSTEP
                qb_base = u * NQB

                @blk.tensor
                def _(te, u=u):
                    te.wait_ge(S["qld"], 16 * (u + 1))
                    te.wait_ge(S["kld"], 16 * (u + 1))
                    te.wait_ge(S["vld"], (16 * 128 if fm else 32) * (u + 1))
                    te.wait_ge(S["vcon"], 1)

                    def issue_score(i):
                        qb, kt, first, last, diag = sched[i]
                        gi = base + i
                        cur = gi % 3
                        if gi >= 3:
                            te.wait_ge(S["sexp"], gi - 2)
                        mm = None
                        for e in range(2):
                            mm = te.matmul(
                                out=p_s[cur][:, 512 * e:512 * (e + 1)],
                                lhsT=kfm[64 * e:64 * (e + 1), 128 * kt:128 * (kt + 1)],
                                rhs=qfm[64 * e:64 * (e + 1), 512 * qb:512 * (qb + 1)],
                                start=True, stop=True)
                        mm.then_inc(S["pes"], 1)

                    issue_score(0)
                    for i, (qb, kt, first, last, diag) in enumerate(sched):
                        gi = base + i
                        cur = gi % 3
                        if i + 1 < NSTEP:
                            issue_score(i + 1)
                        if diag:
                            te.wait_ge(S["vmsk"], NMASK_UPTO[gi])
                        else:
                            te.wait_ge(S["sexp"], gi + 1)
                        if first:
                            te.wait_ge(S["vev"], 2 * (qb_base + qb))
                        mm = None
                        for e in range(2):
                            mm = te.matmul(out=p_o[e][0:65, :],
                                           lhsT=vv[e][:, VS * kt:VS * kt + 65],
                                           rhs=a2[cur][:, 512 * e:512 * (e + 1)],
                                           start=first, stop=last)
                        mm.then_inc(S["pav"], 1)

                @blk.scalar
                def _(s0, u=u):
                    s = SER["s"]
                    for i, (qb, kt, first, last, diag) in enumerate(sched):
                        gi = base + i
                        cur = gi % 3
                        s.wait_ge(S["pes"], gi + 1)
                        if gi >= 3:
                            s.wait_ge(S["pav"], gi - 2)
                        s.activation(out=a2[cur][:], in_=p_s[cur][:], func=AF.Exp,
                                     scale=0.125, bias=b15[:])
                        s.signal(S["sexp"])

                @blk.vector
                def _(v0, u=u):
                    v = SER["v"]
                    ev_i = 0
                    for i, (qb, kt, first, last, diag) in enumerate(sched):
                        gi = base + i
                        cur = gi % 3
                        if diag:
                            v.wait_ge(S["sexp"], gi + 1)
                            j = kt - 4 * qb
                            mrep = masks[j][:].unsqueeze(1).broadcast_to((128, 2, 512))
                            v.tensor_tensor(
                                out=a2[cur][:].rearrange("p (e q) -> p e q", e=2),
                                in0=a2[cur][:].rearrange("p (e q) -> p e q", e=2),
                                in1=mrep, op=AluOpType.mult)
                            v.signal(S["vmsk"])
                        if last:
                            v.wait_ge(S["pav"], gi + 1)
                            for e in range(2):
                                gqb = qb_base + qb
                                if gqb >= 1:
                                    v.wait_ge(ost[e], 16 * gqb)
                                v.tensor_copy(out=oev[e][0:65, :], in_=p_o[e][0:65, :])
                                v.signal(S["vev"])

                @blk.sync
                def _(sync, u=u):
                    for qb in range(NQB):
                        gqb = qb_base + qb
                        for e in range(2):
                            sync.wait_ge(S["vev"], 2 * gqb + e + 1)
                            sync.dma_start(out=oT_out[u, e, :, 512 * qb:512 * (qb + 1)],
                                           in_=oev[e][0:65, :]).then_inc(ost[e], 16)

        with nc.Block() as blk:
            @blk.sync
            def _(sync):
                sync.wait_ge(S["ostA"], 16 * 2 * NQB)
                sync.wait_ge(S["ostB"], 16 * 2 * NQB)
    return nc

def build_att1_c(normalize=True):
    nc = bass.Bass("TRN2", target_bir_lowering=False)
    x_in = nc.dram_tensor("x", [T, D], F32, kind="ExternalInput").ap()
    onum_d = nc.dram_tensor("onum", [T, D], BF16, kind="ExternalInput").ap()
    oden_d = nc.dram_tensor("oden", [T, 16], F32, kind="ExternalInput").ap()
    woT_d = nc.dram_tensor("woT", [D, D], F32, kind="ExternalInput").ap()
    bo_d = nc.dram_tensor("bo", [1, D], F32, kind="ExternalInput").ap()
    x_out = nc.dram_tensor("x_out", [T, D], F32, kind="ExternalOutput").ap()
    on_dram = nc.dram_tensor("on_dram", [T, D], BF16, kind="Internal").ap()

    x3 = x_in.rearrange("(t p) d -> t p d", p=128)
    xo3 = x_out.rearrange("(t p) d -> t p d", p=128)
    onum3 = onum_d.rearrange("(t p) d -> t p d", p=128)
    oden3 = oden_d.rearrange("(t p) d -> t p d", p=128)
    on3 = on_dram.rearrange("(t p) d -> t p d", p=128)

    with ExitStack() as stack:
        def sb(name, shape, dt=F32):
            return stack.enter_context(nc.sbuf_tensor(name, shape, dt))
        def psum(name, shape, dt=F32):
            return stack.enter_context(nc.psum_tensor(name, shape, dt))
        xt = [sb(f"xt{i}", [128, D]) for i in range(NT)]
        onb = [sb(f"onb{e}", [128, D], BF16) for e in range(2)]
        odn = sb("odn", [128, 16])
        rdn = sb("rdn", [128, 16])
        onn = [sb(f"onn{e}", [128, D], BF16) for e in range(2)]
        onf = [sb(f"onf{k}", [128, T], BF16) for k in range(8)]
        wo_s = [sb(f"wo_s{k}", [128, D], BF16) for k in range(8)]
        bos = sb("bos", [1, D]); bbo = sb("bbo", [128, D])
        onesr = sb("onesr", [1, 128])
        uev = sb("uev", [128, 512])
        xot = [sb(f"xot{e}", [128, D]) for e in range(2)]
        p_y = [psum(f"p_y{j}", [128, 512]) for j in range(4)]

        sem_names = ["oos", "oov", "oog", "ld", "old", "dld", "wld", "cld",
                     "snrm", "vnrm", "stv", "stg", "trx", "pbias", "vcb",
                     "pey", "vev", "osem", "osc"]
        S = {n: stack.enter_context(nc.semaphore(n)) for n in sem_names}
        SER = {"s": SerEngine(nc.scalar, S["oos"]),
               "v": SerEngine(nc.vector, S["oov"]),
               "g": SerEngine(nc.gpsimd, S["oog"])}

        with nc.Block() as blk:
            @blk.sync
            def _(sync):
                for i in range(NT):
                    sync.dma_start(out=xt[i][:], in_=x3[i]).then_inc(S["ld"], 16)
                sync.dma_start(out=bos[:], in_=bo_d[:]).then_inc(S["cld"], 16)
            @blk.gpsimd
            def _(g0):
                g = SER["g"]
                for k in range(8):
                    g.dma_start(out=wo_s[k][:], in_=woT_d[128 * k:128 * (k + 1), :]) \
                        .then_inc(S["wld"], 16)
            @blk.vector
            def _(v0):
                v = SER["v"]
                v.memset(onesr[:], 1.0)
                v.signal(S["vcb"])
            @blk.tensor
            def _(te):
                te.wait_ge(S["cld"], 16)
                te.wait_ge(S["vcb"], 1)
                for half in range(2):
                    te.matmul(out=p_y[half][:], lhsT=onesr[:],
                              rhs=bos[:, 512 * half:512 * (half + 1)],
                              start=True, stop=True).then_inc(S["pbias"], 1)
        with nc.Block() as blk:
            @blk.vector
            def _(v0):
                v = SER["v"]
                v.wait_ge(S["pbias"], 2)
                v.tensor_copy(out=bbo[:, 0:512], in_=p_y[0][:])
                v.tensor_copy(out=bbo[:, 512:1024], in_=p_y[1][:])
                v.signal(S["vcb"])

        with nc.Block() as blk:
            @blk.sync
            def _(sync):
                for i in range(NT):
                    if i >= 2:
                        sync.wait_ge(S["vnrm"], i - 1)
                    sync.dma_start(out=onb[i % 2][:], in_=onum3[i]).then_inc(
                        S["old" if i % 2 == 0 else "dld"], 16)
            @blk.scalar
            def _(s0):
                s = SER["s"]
                s.dma_start(out=odn[:], in_=oden3[0]).then_inc(S["cld"], 16)
                for i in range(NT):
                    s.wait_ge(S["cld"], 16 * (i + 2))
                    if i >= 1:
                        s.wait_ge(S["vnrm"], i)
                    s.activation(out=rdn[:], in_=odn[:], func=AF.Ln)
                    s.activation(out=rdn[:], in_=rdn[:], func=AF.Exp, scale=-1.0)
                    s.signal(S["snrm"])
                    if i + 1 < NT:
                        s.wait_ge(S["snrm"], i + 1)
                        s.dma_start(out=odn[:], in_=oden3[i + 1]).then_inc(S["cld"], 16)
            @blk.vector
            def _(v0):
                v = SER["v"]
                for i in range(NT):
                    e = i % 2
                    v.wait_ge(S["old" if e == 0 else "dld"], 16 * (i // 2 + 1))
                    v.wait_ge(S["snrm"], i + 1)
                    if i >= 2:
                        v.wait_ge(S["stv" if e == 0 else "stg"], 16 * (i // 2))
                    if normalize:
                        rrep = rdn[:].unsqueeze(2).broadcast_to((128, 16, 64))
                        v.tensor_tensor(out=onn[e][:].rearrange("p (h k) -> p h k", h=16),
                                        in0=onb[e][:].rearrange("p (h k) -> p h k", h=16),
                                        in1=rrep, op=AluOpType.mult)
                    else:
                        v.tensor_copy(out=onn[e][:], in_=onb[e][:])
                    v.signal(S["vnrm"])
            @blk.gpsimd
            def _(g0):
                g = SER["g"]
                for i in range(NT):
                    g.wait_ge(S["vnrm"], i + 1)
                    g.dma_start(out=on3[i], in_=onn[i % 2][:]).then_inc(
                        S["stv" if i % 2 == 0 else "stg"], 16)

        with nc.Block() as blk:
            @blk.sync
            def _(sync):
                sync.wait_ge(S["stv"], 16 * 8)
                sync.wait_ge(S["stg"], 16 * 8)
                for k in range(8):
                    sync.dma_start(out=onf[k][:], in_=on_dram[:, 128 * k:128 * (k + 1)],
                                   transpose=True).then_inc(S["trx"], 16)

        with nc.Block() as blk:
            @blk.tensor
            def _(te):
                te.wait_ge(S["trx"], 16 * 8)
                te.wait_ge(S["wld"], 16 * 8)
                te.wait_ge(S["vcb"], 2)
                for t in range(NT):
                    for half in range(2):
                        j = 2 * (t % 2) + half
                        if t >= 2:
                            te.wait_ge(S["vev"], 2 * (t - 1))
                        mm = None
                        for k in range(8):
                            mm = te.matmul(out=p_y[j][:],
                                           lhsT=onf[k][:, 128 * t:128 * (t + 1)],
                                           rhs=wo_s[k][:, 512 * half:512 * (half + 1)],
                                           start=(k == 0), stop=(k == 7))
                        mm.then_inc(S["pey"], 1)
            @blk.vector
            def _(v0):
                v = SER["v"]
                v.wait_ge(S["ld"], 16 * NT)
                for t in range(NT):
                    e = t % 2
                    if t >= 2:
                        v.wait_ge(S["osc"], t - 1)
                    for half in range(2):
                        j = 2 * e + half
                        v.wait_ge(S["pey"], 2 * t + half + 1)
                        v.tensor_tensor(out=uev[:], in0=p_y[j][:],
                                        in1=bbo[:, 512 * half:512 * (half + 1)],
                                        op=AluOpType.add)
                        v.tensor_tensor(out=xot[e][:, 512 * half:512 * (half + 1)],
                                        in0=uev[:],
                                        in1=xt[t][:, 512 * half:512 * (half + 1)],
                                        op=AluOpType.add)
                        v.signal(S["vev"])
            @blk.gpsimd
            def _(g0):
                g = SER["g"]
                for t in range(NT):
                    g.wait_ge(S["vev"], 2 * t + 2)
                    g.dma_start(out=xo3[t], in_=xot[t % 2][:]).then_inc(S["osem"], 16)
                    g.wait_ge(S["osem"], 16 * (t + 1))
                    g.sem_inc(S["osc"], 1)
    return nc


def run_att1(nc_cache, x_flat, Wq, bq, Wk, bk, Wv, bv, Wo, bo):
    if "att1a" not in nc_cache:
        nc_cache["att1a"] = build_att1_a()
        nc_cache["att1b"] = build_att1_b()
        nc_cache["att1c"] = build_att1_c()
    wqT = np.ascontiguousarray(Wq.T)
    wkT = np.ascontiguousarray(Wk.T)
    wvT = np.ascontiguousarray(Wv.T)
    woT = np.ascontiguousarray(Wo.T)
    in_maps = []
    for c in range(NCORES):
        poff = np.array([[float((c % 4) * T)]], np.float32)
        in_maps.append({"x": np.ascontiguousarray(x_flat[c * T:(c + 1) * T]),
                        "wqT": wqT, "wkT": wkT, "wvT": wvT,
                        "bq": bq.reshape(1, D), "bk": bk.reshape(1, D),
                        "bv": bv.reshape(1, D), "poff": poff})
    res = run_bass_kernel_spmd(nc_cache["att1a"], in_maps, core_ids=list(range(NCORES)))
    qf = np.concatenate([np.asarray(res.results[c]["q_out"]) for c in range(NCORES)], 0)
    kf = np.concatenate([np.asarray(res.results[c]["k_out"]) for c in range(NCORES)], 0)
    vf = np.concatenate([np.asarray(res.results[c]["v_out"]) for c in range(NCORES)], 0)
    in_maps = []
    for c in range(NCORES):
        cols = slice(128 * c, 128 * (c + 1))
        in_maps.append({
            "qsl": np.ascontiguousarray(qf.reshape(2, SQ, D)[:, :, cols]),
            "ksl": np.ascontiguousarray(kf.reshape(2, SQ, D)[:, :, cols]),
            "vsl": np.ascontiguousarray(vf.reshape(2, SQ, D)[:, :, cols])})
    res = run_bass_kernel_spmd(nc_cache["att1b"], in_maps, core_ids=list(range(NCORES)))
    onum = np.zeros((2, SQ, D), qf.dtype)
    oden = np.zeros((2, SQ, 16), np.float32)
    for c in range(NCORES):
        oT = np.asarray(res.results[c]["oT_out"])
        for u in range(2):
            for e in range(2):
                h = 2 * c + e
                onum[u, :, 64 * h:64 * (h + 1)] = oT[u, e, 0:64, :].T
                oden[u, :, h] = oT[u, e, 64, :].astype(np.float32)
    onum = onum.reshape(TOK, D)
    oden = oden.reshape(TOK, 16)
    in_maps = []
    for c in range(NCORES):
        rows = slice(c * T, (c + 1) * T)
        in_maps.append({"x": np.ascontiguousarray(x_flat[rows]),
                        "onum": np.ascontiguousarray(onum[rows]),
                        "oden": np.ascontiguousarray(oden[rows]),
                        "woT": woT, "bo": bo.reshape(1, D)})
    res = run_bass_kernel_spmd(nc_cache["att1c"], in_maps, core_ids=list(range(NCORES)))
    return np.concatenate([np.asarray(res.results[c]["x_out"]) for c in range(NCORES)], 0)

def build_gla_a():
    nc = bass.Bass("TRN2", target_bir_lowering=False)
    x_in = nc.dram_tensor("x", [T, D], F32, kind="ExternalInput").ap()
    wq_dd = nc.dram_tensor("wq", [D, D], F32, kind="ExternalInput").ap()
    wk1_dd = nc.dram_tensor("wk1", [D, D], F32, kind="ExternalInput").ap()
    wk2_dd = nc.dram_tensor("wk2", [D, D], F32, kind="ExternalInput").ap()
    bq_d = nc.dram_tensor("bq", [1, D], F32, kind="ExternalInput").ap()
    bk1_d = nc.dram_tensor("bk1", [1, D], F32, kind="ExternalInput").ap()
    bk2_d = nc.dram_tensor("bk2", [1, D], F32, kind="ExternalInput").ap()
    wgh_d = nc.dram_tensor("wgh", [D, 16], F32, kind="ExternalInput").ap()
    bg_d = nc.dram_tensor("bg", [1, 1], F32, kind="ExternalInput").ap()
    xh_out = nc.dram_tensor("xh_out", [T, D], BF16, kind="ExternalOutput").ap()
    q_out = nc.dram_tensor("q_out", [T, D], BF16, kind="ExternalOutput").ap()
    k1_out = nc.dram_tensor("k1_out", [T, D], BF16, kind="ExternalOutput").ap()
    k2_out = nc.dram_tensor("k2_out", [T, D], BF16, kind="ExternalOutput").ap()
    q2_out = nc.dram_tensor("q2_out", [T, D], BF16, kind="ExternalOutput").ap()
    lg_out = nc.dram_tensor("lg_out", [16, T], F32, kind="ExternalOutput").ap()

    x3 = x_in.rearrange("(t p) d -> t p d", p=128)
    xh3 = xh_out.rearrange("(t p) d -> t p d", p=128)
    q3 = q_out.rearrange("(t p) d -> t p d", p=128)
    k13 = k1_out.rearrange("(t p) d -> t p d", p=128)
    k23 = k2_out.rearrange("(t p) d -> t p d", p=128)
    q23 = q2_out.rearrange("(t p) d -> t p d", p=128)

    K2SC = 0.02 / 32.0

    with ExitStack() as stack:
        def sb(name, shape, dt=F32):
            return stack.enter_context(nc.sbuf_tensor(name, shape, dt))
        def psum(name, shape, dt=F32):
            return stack.enter_context(nc.psum_tensor(name, shape, dt))
        xhf = [sb(f"xhf{k}", [128, T], BF16) for k in range(8)]
        wq_s = [sb(f"wq_s{k}", [128, D], BF16) for k in range(8)]
        wk1_s = [sb(f"wk1_s{k}", [128, D], BF16) for k in range(8)]
        wk2_s = [sb(f"wk2_s{k}", [128, D], BF16) for k in range(8)]
        wgh_s = [sb(f"wgh_s{k}", [128, 16], BF16) for k in range(8)]
        bqb = sb("bqb", [1, D], BF16); bk1b = sb("bk1b", [1, D], BF16)
        bk2b = sb("bk2b", [1, D], BF16)
        bgc = sb("bgc", [16, 1])
        ones1 = sb("ones1", [1, 128])
        ones16c = sb("ones16c", [1, 16])
        bgs = sb("bgs", [1, 1])
        prj = sb("prj", [128, D])
        qn1 = sb("qn1", [128, D]); qab = sb("qab", [128, D])
        ones1b = sb("ones1b", [1, 128], BF16)
        outq = [sb(f"outq{e}", [128, D], BF16) for e in range(2)]
        outk1 = [sb(f"outk1{e}", [128, D], BF16) for e in range(2)]
        outk2 = [sb(f"outk2{e}", [128, D], BF16) for e in range(2)]
        outq2 = [sb(f"outq2{e}", [128, D], BF16) for e in range(2)]
        glg = sb("glg", [16, T])
        p_q = [psum(f"p_q{j}", [128, 512]) for j in range(2)]
        p_k1 = [psum(f"p_k1{j}", [128, 512]) for j in range(2)]
        p_k2 = [psum(f"p_k2{j}", [128, 512]) for j in range(2)]
        p_g = psum("p_g", [16, T // 4])
        p_bg = psum("p_bg", [16, 1])

        sem_names = ["oos", "oov", "oog", "ldv", "ldg", "wld", "cld", "bld",
                     "sxh",
                     "stv", "stg", "trx", "peq", "pek1", "pek2", "peg", "pbg",
                     "vq", "vk1", "sk2", "vq2", "sg", "sga", "sgb", "vcb",
                     "ostq0", "ostq1", "ostk10", "ostk11", "ostk20", "ostk21",
                     "ostq20", "ostq21", "ostlg"]
        S = {n: stack.enter_context(nc.semaphore(n)) for n in sem_names}
        SER = {"s": SerEngine(nc.scalar, S["oos"]),
               "v": SerEngine(nc.vector, S["oov"]),
               "g": SerEngine(nc.gpsimd, S["oog"])}

        with nc.Block() as blk:
            @blk.sync
            def _(sync):
                sync.dma_start(out=bgs[:], in_=bg_d[:]).then_inc(S["cld"], 16)
            @blk.gpsimd
            def _(g0):
                g = SER["g"]
                g.dma_start(out=bqb[:], in_=bq_d[:]).then_inc(S["bld"], 16)
                g.dma_start(out=bk1b[:], in_=bk1_d[:]).then_inc(S["bld"], 16)
                g.dma_start(out=bk2b[:], in_=bk2_d[:]).then_inc(S["bld"], 16)
                for k in range(8):
                    g.dma_start(out=wq_s[k][:], in_=wq_dd[128 * k:128 * (k + 1), :]) \
                        .then_inc(S["wld"], 16)
                    g.dma_start(out=wk1_s[k][:], in_=wk1_dd[128 * k:128 * (k + 1), :]) \
                        .then_inc(S["wld"], 16)
                    g.dma_start(out=wk2_s[k][:], in_=wk2_dd[128 * k:128 * (k + 1), :]) \
                        .then_inc(S["wld"], 16)
                    g.dma_start(out=wgh_s[k][:], in_=wgh_d[128 * k:128 * (k + 1), :]) \
                        .then_inc(S["wld"], 16)
            @blk.vector
            def _(v0):
                v = SER["v"]
                v.memset(ones1[:], 1.0)
                v.memset(ones1b[:], 1.0)
                v.memset(ones16c[:], 1.0)
                v.signal(S["vcb"])
            @blk.tensor
            def _(te):
                te.wait_ge(S["cld"], 16)
                te.wait_ge(S["bld"], 16 * 3)
                te.wait_ge(S["vcb"], 1)
                te.matmul(out=p_bg[:], lhsT=ones16c[:], rhs=bgs[:],
                          start=True, stop=True).then_inc(S["pbg"], 1)
        with nc.Block() as blk:
            @blk.vector
            def _(v0):
                v = SER["v"]
                v.wait_ge(S["pbg"], 1)
                v.tensor_copy(out=bgc[:], in_=p_bg[:])
                v.signal(S["vcb"])

        _emit_ln_stream(nc, stack, S, SER, x3, xh3)
        with nc.Block() as blk:
            @blk.sync
            def _(sync):
                sync.wait_ge(S["stv"], 16 * 8)
                sync.wait_ge(S["stg"], 16 * 8)
                for k in range(8):
                    sync.dma_start(out=xhf[k][:], in_=xh_out[:, 128 * k:128 * (k + 1)],
                                   transpose=True).then_inc(S["trx"], 16)

        with nc.Block() as blk:
            @blk.tensor
            def _(te):
                te.wait_ge(S["trx"], 16 * 8)
                te.wait_ge(S["wld"], 16 * 32)
                for cb in range(4):
                    for k in range(8):
                        mm = te.matmul(out=p_g[:, 0:512],
                                       lhsT=wgh_s[k][:],
                                       rhs=xhf[k][:, 512 * cb:512 * (cb + 1)],
                                       start=(k == 0), stop=(k == 7))
                    mm.then_inc(S["peg"], 1)
                    te.wait_ge(S["sga"], cb + 1)
            @blk.scalar
            def _(s0):
                s = SER["s"]
                s.wait_ge(S["vcb"], 2)
                for cb in range(4):
                    s.wait_ge(S["peg"], cb + 1)
                    s.activation(out=glg[:, 512 * cb:512 * (cb + 1)],
                                 in_=p_g[:, 0:512], func=AF.Sigmoid, bias=bgc[:])
                    s.signal(S["sga"])
            @blk.vector
            def _(v0):
                v = SER["v"]
                for cb in range(4):
                    v.wait_ge(S["sga"], cb + 1)
                    _ts(v, glg[:, 512 * cb:512 * (cb + 1)],
                        glg[:, 512 * cb:512 * (cb + 1)], 1e-6, AluOpType.max)
                    v.signal(S["sgb"])
        with nc.Block() as blk:
            @blk.scalar
            def _(s0):
                s = SER["s"]
                s.wait_ge(S["sgb"], 4)
                s.activation(out=glg[:], in_=glg[:], func=AF.Ln)
                s.signal(S["sga"])
            @blk.gpsimd
            def _(g0):
                g = SER["g"]
                g.wait_ge(S["sga"], 5)
                g.dma_start(out=lg_out[:], in_=glg[:]).then_inc(S["ostlg"], 16)

        with nc.Block() as blk:
            @blk.tensor
            def _(te):
                for t in range(NT):
                    for (ps, w_s, bsrc, pesem, vsem) in (
                            (p_q, wq_s, bqb, "peq", "vq"),
                            (p_k1, wk1_s, bk1b, "pek1", "vk1"),
                            (p_k2, wk2_s, bk2b, "pek2", "sk2")):
                        if t >= 1:
                            te.wait_ge(S[vsem], t)
                        mm = None
                        for half in range(2):
                            for k in range(8):
                                te.matmul(out=ps[half][:],
                                          lhsT=xhf[k][:, 128 * t:128 * (t + 1)],
                                          rhs=w_s[k][:, 512 * half:512 * (half + 1)],
                                          start=(k == 0), stop=False)
                            mm = te.matmul(out=ps[half][:], lhsT=ones1b[:],
                                           rhs=bsrc[:, 512 * half:512 * (half + 1)],
                                           start=False, stop=True)
                        mm.then_inc(S[pesem], 1)

            @blk.vector
            def _(v0):
                v = SER["v"]
                for t in range(NT):
                    e = str(t % 2)
                    ei = t % 2
                    v.wait_ge(S["peq"], t + 1)
                    if t >= 2:
                        v.wait_ge(S["ostq" + e], 16 * (t // 2))
                    for half in range(2):
                        v.tensor_copy(out=prj[:, 512 * half:512 * (half + 1)],
                                      in_=p_q[half][:])
                        v.tensor_copy(out=outq[ei][:, 512 * half:512 * (half + 1)],
                                      in_=p_q[half][:])
                    v.signal(S["vq"])
                    if t >= 2:
                        v.wait_ge(S["ostq2" + e], 16 * (t // 2))
                    _ts(v, qn1[:], prj[:], -1.0, AluOpType.mult)
                    v.tensor_tensor(out=qab[:], in0=prj[:], in1=qn1[:], op=AluOpType.max)
                    _ts(v, qn1[:], prj[:], 0.06375, AluOpType.mult)
                    v.scalar_tensor_tensor(out=outq2[ei][:], in0=qab[:],
                                           scalar=-0.06125, in1=qn1[:],
                                           op0=AluOpType.mult, op1=AluOpType.add)
                    v.signal(S["vq2"])
                    v.wait_ge(S["pek1"], t + 1)
                    if t >= 2:
                        v.wait_ge(S["ostk1" + e], 16 * (t // 2))
                    for half in range(2):
                        v.tensor_copy(out=outk1[ei][:, 512 * half:512 * (half + 1)],
                                      in_=p_k1[half][:])
                    v.signal(S["vk1"])

            @blk.scalar
            def _(s0):
                s = SER["s"]
                for t in range(NT):
                    e = str(t % 2)
                    s.wait_ge(S["pek2"], t + 1)
                    if t >= 2:
                        s.wait_ge(S["ostk2" + e], 16 * (t // 2))
                    for half in range(2):
                        s.activation(out=outk2[t % 2][:, 512 * half:512 * (half + 1)],
                                     in_=p_k2[half][:], func=AF.Sigmoid, scale=K2SC)
                    s.signal(S["sk2"])

            @blk.sync
            def _(sync):
                for t in range(NT):
                    e = str(t % 2)
                    sync.wait_ge(S["vq"], t + 1)
                    sync.dma_start(out=q3[t], in_=outq[t % 2][:]) \
                        .then_inc(S["ostq" + e], 16)
                    sync.wait_ge(S["vq2"], t + 1)
                    sync.dma_start(out=q23[t], in_=outq2[t % 2][:]) \
                        .then_inc(S["ostq2" + e], 16)
                    sync.wait_ge(S["vk1"], t + 1)
                    sync.dma_start(out=k13[t], in_=outk1[t % 2][:]) \
                        .then_inc(S["ostk1" + e], 16)
                    sync.wait_ge(S["sk2"], t + 1)
                    sync.dma_start(out=k23[t], in_=outk2[t % 2][:]) \
                        .then_inc(S["ostk2" + e], 16)
                for e in "01":
                    for nm in ("ostq", "ostq2", "ostk1", "ostk2"):
                        sync.wait_ge(S[nm + e], 16 * (NT // 2))
                sync.wait_ge(S["ostlg"], 16)
    return nc

def build_gla_b(upto=4, ld_lvl=4, p1_lvl=4):
    nc = bass.Bass("TRN2", target_bir_lowering=False)
    TEN = {}
    for n in ("qsl", "k1sl", "k2sl", "q2sl", "xhsl", "k2ssl"):
        TEN[n] = nc.dram_tensor(n, [2, SQ, 128], BF16, kind="ExternalInput").ap()
    TEN["lgsl"] = nc.dram_tensor("lgsl", [2, 2, SQ], F32, kind="ExternalInput").ap()
    TEN["ws"] = nc.dram_tensor("ws", [64, 1], F32, kind="ExternalInput").ap()
    TEN["bs"] = nc.dram_tensor("bs", [1, 1], F32, kind="ExternalInput").ap()
    TEN["o_slab"] = nc.dram_tensor("o_slab", [2, SQ, 128], BF16,
                                   kind="ExternalOutput").ap()
    TEN["rowd"] = nc.dram_tensor("rowd", [SQ], F32, kind="Internal").ap()
    emit_glab(nc, TEN, upto=upto, ld_lvl=ld_lvl, p1_lvl=p1_lvl, fm=False)
    return nc


def emit_glab(nc, TEN, upto=4, ld_lvl=4, p1_lvl=4, tag="", fm=False):
    if fm:
        qfm_loc = TEN["qfm_loc"]; k1fm_loc = TEN["k1fm_loc"]
        k2fm_loc = TEN["k2fm_loc"]; q2fm_loc = TEN["q2fm_loc"]
        xhfm_loc = TEN["xhfm_loc"]; k2sfm_loc = TEN["k2sfm_loc"]
        rrow_d = TEN["rrow_d"]
    else:
        qsl = TEN["qsl"]; k1sl = TEN["k1sl"]; k2sl = TEN["k2sl"]
        q2sl = TEN["q2sl"]; xhsl = TEN["xhsl"]; k2ssl = TEN["k2ssl"]
    lgsl = TEN["lgsl"]; ws_d = TEN["ws"]; bs_d = TEN["bs"]
    o_slab = TEN["o_slab"]; rowd = TEN["rowd"]
    rowd4 = nc.dram_tensor(tag + "rowd4", [4, SQ], F32, kind="Internal").ap()

    NCH = SQ // 128
    HSQ = SQ // 2

    with ExitStack() as stack:
        def sb(name, shape, dt=F32):
            return stack.enter_context(nc.sbuf_tensor(tag + name, shape, dt))
        def psum(name, shape, dt=F32):
            return stack.enter_context(nc.psum_tensor(tag + name, shape, dt))
        qfm = sb("qfm", [128, SQ], BF16)
        k1fm = sb("k1fm", [128, SQ], BF16)
        k2fm = sb("k2fm", [128, SQ], BF16)
        q2fm = sb("q2fm", [128, SQ], BF16)
        vtok = sb("vtok", [128, SQ], BF16)
        k1tok = sb("k1tok", [128, SQ], BF16)
        k2stok = sb("k2stok", [128, SQ], BF16)
        k2tok = sb("k2tok", [128, SQ], BF16)
        k2s0 = sb("k2s0", [128, 128], BF16)
        q2s0 = sb("q2s0", [128, 128], BF16)
        o1b = [sb(f"o1b{e}", [128, 64 * NCH], BF16) for e in range(2)]
        if not fm:
            rowA = sb("rowA", [1, HSQ])
        carry4 = sb("carry4", [1, 1]); zero1 = sb("zero1", [1, 1])
        row4A = sb("row4A", [4, HSQ]); row4B = sb("row4B", [4, HSQ])
        carry4b = sb("carry4b", [4, 1]); zero4 = sb("zero4", [4, 1])
        ws_e = [sb(f"ws_e{e}", [128, 1], BF16) for e in range(2)]
        bs_s = sb("bs_s", [1, 1])
        aT = [[sb(f"aT{u}{e}", [128, NCH]) for e in range(2)] for u in range(2)]
        bT = [[sb(f"bT{u}{e}", [128, NCH]) for e in range(2)] for u in range(2)]
        tsc = sb("tsc", [128, NCH]); tsc2 = sb("tsc2", [128, NCH])
        fold = sb("fold", [64, 128])
        colidx = sb("colidx", [128, 128]); rowix = sb("rowix", [128, 128])
        identf = sb("identf", [128, 128])
        triuI = sb("triuI", [128, 128], BF16)
        triuSN = sb("triuSN", [128, 128], BF16)
        mb = [sb(f"mb{e}", [128, 128], BF16) for e in range(2)]
        at_ = [sb(f"at_{e}", [128, 128], BF16) for e in range(2)]
        a2v = [sb(f"a2v{e}", [128, 128], BF16) for e in range(2)]
        a2o = [sb(f"a2o{e}", [128, 128], BF16) for e in range(2)]
        kt_ = [sb(f"kt_{e}", [128, 64], BF16) for e in range(2)]
        k2n = [sb(f"k2n{e}", [128, 64], BF16) for e in range(2)]
        sbE = [sb(f"sbE{e}", [128, 64], BF16) for e in range(2)]
        s2bE = [sb(f"s2bE{e}", [128, 64], BF16) for e in range(2)]
        sacc = [sb(f"sacc{e}", [64, 64]) for e in range(2)]
        s2acc = [sb(f"s2acc{e}", [64, 64]) for e in range(2)]
        obuf = [sb(f"obuf{e2}", [128, 128], BF16) for e2 in range(2)]
        p_eh = [psum(f"p_eh{e}", [128, 512]) for e in range(2)]
        p_p = [[p_eh[e][:, 0:128], p_eh[e][:, 256:384]] for e in range(2)]
        p_p2 = [[p_eh[e][:, 128:256], p_eh[e][:, 384:512]] for e in range(2)]
        p_o = psum("p_o", [128, 128])
        p_sB = psum("p_sB", [64, 128])
        p_s = [p_sB[0:64, 64 * e:64 * (e + 1)] for e in range(2)]
        p_tr = psum("p_tr", [128, 128])
        p_r = psum("p_r", [1, 512])

        sem_names = ["oos", "oov", "oog", "seqc", "dA", "dB", "wsld", "tld", "pld",
                     "pldg", "pes", "peo", "pesd", "vS", "vA", "vK", "vO1", "vmm",
                     "ost0", "ost1", "vcon", "gcon"]
        S = {n: nc.alloc_semaphore(name=tag + n) for n in sem_names}
        SER = {"s": SerEngine(nc.scalar, S["oos"]),
               "v": SerEngine(nc.vector, S["oov"]),
               "g": SerEngine(nc.gpsimd, S["oog"])}

        with nc.Block() as blk:
            @blk.gpsimd
            def _(g0):
                g = SER["g"]
                g.iota(colidx[:], [[1, 128]], base=0, channel_multiplier=0,
                       allow_small_or_imprecise_dtypes=True)
                g.iota(rowix[:], [[0, 128]], base=0, channel_multiplier=1,
                       allow_small_or_imprecise_dtypes=True)
                g.memset(ws_e[0][64:128, :], 0.0)
                g.memset(ws_e[1][0:64, :], 0.0)
                g.dma_start(out=ws_e[0][0:64, :], in_=ws_d[:]).then_inc(S["wsld"], 16)
                g.dma_start(out=ws_e[1][64:128, :], in_=ws_d[:]).then_inc(S["wsld"], 16)
                g.signal(S["gcon"])
            @blk.sync
            def _(sync):
                sync.dma_start(out=bs_s[:], in_=bs_d[:]).then_inc(S["dA"], 16)
            @blk.vector
            def _(v0):
                v = SER["v"]
                v.memset(zero1[:], 0.0)
                v.memset(zero4[:], 0.0)
                for e in range(2):
                    v.memset(sbE[e][:], 0.0)
                    v.memset(s2bE[e][:], 0.0)
                v.wait_ge(S["gcon"], 1)
                v.tensor_tensor(out=identf[:], in0=colidx[:], in1=rowix[:],
                                op=AluOpType.is_equal)
                v.tensor_tensor(out=triuI[:], in0=colidx[:], in1=rowix[:],
                                op=AluOpType.is_ge)
                v.tensor_tensor(out=triuSN[:], in0=rowix[:], in1=colidx[:],
                                op=AluOpType.is_lt)
                _ts(v, triuSN[:], triuSN[:], -1.0, AluOpType.mult)
                v.signal(S["vcon"])

        seq_n = [0]
        dcnt = {"dA": 16, "dB": 0, "tld": 0}

        def step_wait(eng):
            if seq_n[0] > 0:
                eng.wait_ge(S["seqc"], seq_n[0])

        def step_done(eng_raw):
            seq_n[0] += 1
            eng_raw.sem_inc(S["seqc"], 1)

        _pro_scope = nc.enter_named_scope(tag + "pro", False)
        with nc.Block() as blk:
            steps = []

            def emit_prologue():
                lgflat = lgsl.rearrange("u e q -> (u e) q")
                for hf in range(2):
                    def ld_lg4(sync, hf=hf):
                        dcnt["dA"] += 16
                        sync.dma_start(out=row4A[:],
                                       in_=lgflat[:, HSQ * hf:HSQ * (hf + 1)]) \
                            .then_inc(S["dA"], 16)
                        sync.wait_ge(S["dA"], dcnt["dA"])
                    steps.append(("sync", ld_lg4))

                    def scan4(v, hf=hf):
                        ini = zero4[0:4, 0:1] if hf == 0 else carry4b[0:4, 0:1]
                        v.tensor_tensor_scan(
                            out=row4B[:], data0=row4A[:],
                            data1=zero4[:].broadcast_to((4, HSQ)),
                            initial=ini, op0=AluOpType.add, op1=AluOpType.add)
                    steps.append(("v", scan4))
                    if hf == 0:
                        def save4(g):
                            g.tensor_copy(out=carry4b[:], in_=row4B[:, HSQ - 1:HSQ])
                        steps.append(("g", save4))

                    def clip4(g):
                        _ts(g, row4B[:], row4B[:], -30.0, AluOpType.max,
                            30.0, AluOpType.min)
                    steps.append(("g", clip4))

                    def st4(sync, hf=hf):
                        dcnt["dB"] += 16
                        sync.dma_start(out=rowd4[:, HSQ * hf:HSQ * (hf + 1)],
                                       in_=row4B[:]).then_inc(S["dB"], 16)
                        sync.wait_ge(S["dB"], dcnt["dB"])
                    steps.append(("sync", st4))
                for u in range(2):
                    def ld_k1(sync, u=u):
                        if u == 0:
                            sync.wait_ge(S["wsld"], 32)
                        dcnt["tld"] += 16
                        if fm:
                            sync.dma_start(out=k1fm[:], in_=k1fm_loc[u]) \
                                .then_inc(S["tld"], 16)
                        else:
                            sync.dma_start(out=k1fm[:], in_=k1sl[u], transpose=True) \
                                .then_inc(S["tld"], 16)
                        sync.wait_ge(S["tld"], dcnt["tld"])
                    steps.append(("sync", ld_k1))
                    for e in range(2):
                        def ld_fold(sync, u=u, e=e):
                            dcnt["dA"] += 16
                            sync.dma_start(out=fold[:],
                                           in_=rowd4[2 * u + e, :].rearrange(
                                               "(c p) -> c p", p=128)) \
                                .then_inc(S["dA"], 16)
                            sync.wait_ge(S["dA"], dcnt["dA"])
                        steps.append(("sync", ld_fold))

                        def tr_fold(te):
                            return te.transpose(out=p_tr[:, 0:64], in_=fold[:],
                                                identity=identf[0:64, 0:64])
                        steps.append(("te", tr_fold))

                        def mk_aT(s, u=u, e=e):
                            s.activation(out=aT[u][e][:], in_=p_tr[0:128, 0:64],
                                         func=AF.Exp)
                        steps.append(("s", mk_aT))

                        def add_eps(v, u=u, e=e):
                            _ts(v, aT[u][e][:], aT[u][e][:], 1e-6, AluOpType.add)
                        steps.append(("v", add_eps))

                        def mk_tinv(s, u=u, e=e):
                            s.activation(out=tsc[:], in_=aT[u][e][:], func=AF.Ln)
                            s.activation(out=tsc[:], in_=tsc[:], func=AF.Exp,
                                         scale=-1.0)
                        steps.append(("s", mk_tinv))
                        if not fm:
                            for cq in range(16):
                                def mm_r(te, e=e, cq=cq):
                                    return te.matmul(out=p_r[:],
                                                     lhsT=ws_e[e][:],
                                                     rhs=k1fm[:, 512 * cq:512 * (cq + 1)],
                                                     start=True, stop=True)
                                steps.append(("te", mm_r))

                                def silu_r(s, cq=cq):
                                    off = (512 * cq) % HSQ
                                    s.activation(out=rowA[:, off:off + 512],
                                                 in_=p_r[:], func=AF.Silu,
                                                 bias=bs_s[0:1, 0:1])
                                steps.append(("s", silu_r))
                                if cq == 7 or cq == 15:
                                    def st_r(sync, cq=cq):
                                        hf = 0 if cq == 7 else 1
                                        dcnt["dB"] += 16
                                        sync.dma_start(out=rowd[HSQ * hf:HSQ * (hf + 1)]
                                                       .unsqueeze(0), in_=rowA[:]) \
                                            .then_inc(S["dB"], 16)
                                        sync.wait_ge(S["dB"], dcnt["dB"])
                                    steps.append(("sync", st_r))

                        def ld_fold2(sync, u=u, e=e):
                            dcnt["dA"] += 16
                            if fm:
                                sync.dma_start(
                                    out=fold[:],
                                    in_=rrow_d[u, e, :].rearrange(
                                        "(c p) -> c p", p=128)) \
                                    .then_inc(S["dA"], 16)
                            else:
                                sync.dma_start(out=fold[:],
                                               in_=rowd.rearrange("(c p) -> c p", p=128)) \
                                    .then_inc(S["dA"], 16)
                            sync.wait_ge(S["dA"], dcnt["dA"])
                        steps.append(("sync", ld_fold2))

                        def tr_fold2(te):
                            return te.transpose(out=p_tr[:, 0:64], in_=fold[:],
                                                identity=identf[0:64, 0:64])
                        steps.append(("te", tr_fold2))

                        def mk_bT(v, u=u, e=e):
                            v.tensor_copy(out=tsc2[:], in_=p_tr[0:128, 0:64])
                            v.tensor_tensor(out=bT[u][e][:], in0=tsc2[:], in1=tsc[:],
                                            op=AluOpType.mult)
                        steps.append(("v", mk_bT))

            emit_prologue()

            @blk.sync
            def _(sync):
                for i, (eng, fn) in enumerate(steps):
                    if eng == "sync":
                        if i > 0:
                            sync.wait_ge(S["seqc"], i)
                        fn(sync)
                        sync.sem_inc(S["seqc"], 1)
            @blk.vector
            def _(v0):
                v = SER["v"]
                for i, (eng, fn) in enumerate(steps):
                    if eng == "v":
                        if i > 0:
                            v.wait_ge(S["seqc"], i)
                        fn(v)
                        v.signal(S["seqc"])
            @blk.gpsimd
            def _(g0):
                g = SER["g"]
                for i, (eng, fn) in enumerate(steps):
                    if eng == "g":
                        if i > 0:
                            g.wait_ge(S["seqc"], i)
                        fn(g)
                        g.signal(S["seqc"])
            @blk.scalar
            def _(s0):
                s = SER["s"]
                for i, (eng, fn) in enumerate(steps):
                    if eng == "s":
                        if i > 0:
                            s.wait_ge(S["seqc"], i)
                        fn(s)
                        s.signal(S["seqc"])
            @blk.tensor
            def _(te):
                te.wait_ge(S["vcon"], 1)
                for i, (eng, fn) in enumerate(steps):
                    if eng == "te":
                        if i > 0:
                            te.wait_ge(S["seqc"], i)
                        inst = fn(te)
                        inst.then_inc(S["seqc"], 1)

        nc.leave_named_scope(tag + "pro", _pro_scope[0], False)
        _mn_scope = nc.enter_named_scope(tag + "mn", False)
        cnt = {"pes": 0, "peo": 0, "pesd": 0, "vS": 0, "vA": 0, "vK": 0,
               "vO1": 0, "vmm": 0, "ost": [0, 0], "tld": dcnt["tld"], "pld": 0,
               "pldg": 0}
        n_units = 2 if upto >= 4 else (1 if upto >= 1 else 0)
        for u in range(n_units):
            _u_scope = nc.enter_named_scope(tag + f"u{u}", False)
            with nc.Block() as blk:
                @blk.sync
                def _(sync, u=u):
                    if u == 1 and not fm:
                        sync.wait_ge(S["ost0"], 16 * (NCH // 2))
                        sync.wait_ge(S["ost1"], 16 * (NCH // 2))

                    def tldloads():
                        if ld_lvl < 1:
                            return
                        if fm:
                            for srct, dst in ((qfm_loc, qfm), (k1fm_loc, k1fm),
                                              (k2fm_loc, k2fm), (q2fm_loc, q2fm)):
                                cnt["tld"] += 16
                                sync.dma_start(out=dst[:], in_=srct[u]) \
                                    .then_inc(S["tld"], 16)
                        else:
                            for srct, dst in ((qsl, qfm), (k1sl, k1fm),
                                              (k2sl, k2fm), (q2sl, q2fm)):
                                cnt["tld"] += 16
                                sync.dma_start(out=dst[:], in_=srct[u],
                                               transpose=True) \
                                    .then_inc(S["tld"], 16)
                    if u == 0 or not fm:
                        tldloads()
                    def tp(srct, dst, j, sem):
                        cnt[sem] += 16
                        sync.dma_start(out=dst[:, 128 * j:128 * (j + 1)],
                                       in_=srct[u][:, 128 * j:128 * (j + 1)],
                                       transpose=True) \
                            .then_inc(S[sem], 16)
                    if fm and ld_lvl >= 2 and u == 0:
                        for j in range(NCH):
                            for srct, dst in ((xhfm_loc, vtok), (k1fm_loc, k1tok)):
                                tp(srct, dst, j, "pld")
                        for j in range(NCH):
                            for srct, dst in ((k2fm_loc, k2tok),
                                              (k2sfm_loc, k2stok)):
                                tp(srct, dst, j, "pldg")
                    if fm and ld_lvl >= 2 and u == 1:
                        for j in range(NCH):
                            m = min(j + 1, NCH - 1)
                            if m % 2 == 0:
                                sync.wait_ge(S["ost0"], 16 * (m // 2 + 1))
                            else:
                                sync.wait_ge(S["ost1"], 16 * ((m + 1) // 2))
                            if j == NCH - 1:
                                sync.wait_ge(S["pesd"], cnt["pesd"])
                            for srct, dst in ((xhfm_loc, vtok), (k1fm_loc, k1tok)):
                                tp(srct, dst, j, "pld")
                            for srct, dst in ((k2fm_loc, k2tok),
                                              (k2sfm_loc, k2stok)):
                                tp(srct, dst, j, "pldg")
                    if fm and u == 1:
                        sync.wait_ge(S["peo"], cnt["peo"])
                        tldloads()
                @blk.gpsimd
                def _(g0, u=u):
                    g = SER["g"]
                    if u == 1 and not fm:
                        g.wait_ge(S["ost0"], 16 * (NCH // 2))
                        g.wait_ge(S["ost1"], 16 * (NCH // 2))
                    if ld_lvl >= 2 and not fm:
                        for srct, dst in ((xhsl, vtok), (k1sl, k1tok), (k2sl, k2tok),
                                         (k2ssl, k2stok)):
                            cnt["pld"] += 16
                            g.dma_start(out=dst[:].rearrange("p (n h) -> p n h", h=128),
                                        in_=srct[u].rearrange("(n p) h -> p n h", p=128)) \
                                .then_inc(S["pld"], 16)
                    if ld_lvl >= 2:
                        g0.sem_inc(S["gcon"], 1)

            with nc.Block() as blk:
                @blk.vector
                def _(v0, u=u):
                    v = SER["v"]
                    if fm and u == 1:
                        v.wait_ge(S["ost0"], 16 * (NCH // 2))
                        v.wait_ge(S["ost1"], 16 * (NCH // 2))
                    v.wait_ge(S["tld"], 16 * cnt["tld"] // 16)
                    v.memset(k2s0[:, 0:1], 0.0)
                    v.memset(q2s0[:, 0:1], 0.0)
                    v.tensor_copy(out=k2s0[:, 1:128], in_=k2fm[:, 0:127])
                    v.tensor_copy(out=q2s0[:, 1:128], in_=q2fm[:, 0:127])
                    for e in range(2):
                        v.memset(sbE[e][:], 0.0)
                        v.memset(s2bE[e][:], 0.0)
                        v.memset(sacc[e][:], 0.0)
                        v.memset(s2acc[e][:], 0.0)
                    v.signal(S["vcon"])

            _p1_scope = nc.enter_named_scope(tag + f"p1u{u}", False)
            with nc.Block() as blk:
                @blk.tensor
                def _(te, u=u):
                    if upto < 2:
                        return
                    te.wait_ge(S["vcon"], 2 + u)
                    pld_base = cnt["pld"] - 16 * 2 * NCH

                    def p1_score(j):
                        if fm:
                            te.wait_ge(S["pld"], pld_base + 16 * 2 * (j + 1))
                        elif j == 0:
                            te.wait_ge(S["pld"], cnt["pld"])
                        if j >= 2:
                            te.wait_ge(S["vmm"], cnt["vmm"] + 2 * (j - 1))
                        mm = None
                        for e in range(2):
                            mm = te.matmul(out=p_p[e][j % 2],
                                           lhsT=k1fm[64 * e:64 * (e + 1),
                                                     128 * j:128 * (j + 1)],
                                           rhs=qfm[64 * e:64 * (e + 1),
                                                   128 * j:128 * (j + 1)],
                                           start=True, stop=True)
                        mm.then_inc(S["pes"], 1)

                    p1_score(0)
                    for j in range(NCH):
                        if j + 1 < NCH:
                            p1_score(j + 1)
                        if p1_lvl >= 3:
                            te.wait_ge(S["vA"], cnt["vA"] + 2 * (j + 1))
                            if j >= 1 and p1_lvl >= 4:
                                te.wait_ge(S["vS"], cnt["vS"] + j)
                            if j >= 1:
                                te.wait_ge(S["vO1"], cnt["vO1"] + 2 * j)
                            mm = None
                            for e in range(2):
                                te.matmul(out=p_o[:, 64 * e:64 * (e + 1)],
                                          lhsT=at_[e][:],
                                          rhs=vtok[:, 128 * j + 64 * e:
                                                   128 * j + 64 * (e + 1)],
                                          start=True, stop=False)
                                mm = te.matmul(out=p_o[:, 64 * e:64 * (e + 1)],
                                               lhsT=qfm[:, 128 * j:128 * (j + 1)],
                                               rhs=sbE[e][:],
                                               start=False, stop=True)
                            mm.then_inc(S["peo"], 1)
                        if p1_lvl >= 4:
                            te.wait_ge(S["vK"], cnt["vK"] + 2 * (j + 1))
                            mm = None
                            for e in range(2):
                                mm = te.matmul(out=p_s[e], lhsT=kt_[e][:],
                                               rhs=vtok[:, 128 * j + 64 * e:
                                                        128 * j + 64 * (e + 1)],
                                               start=True, stop=True)
                            mm.then_inc(S["pesd"], 1)

                @blk.vector
                def _(v0, u=u):
                    if upto < 2:
                        return
                    v = SER["v"]
                    for j in range(NCH):
                        if j >= 1 and p1_lvl >= 4:
                            v.wait_ge(S["pesd"], cnt["pesd"] + j)
                            for e in range(2):
                                v.tensor_tensor(out=sacc[e][:], in0=sacc[e][:],
                                                in1=p_s[e], op=AluOpType.add)
                                v.tensor_copy(out=sbE[e][64 * e:64 * (e + 1), :],
                                              in_=sacc[e][:])
                            v.signal(S["vS"])
                        v.wait_ge(S["pes"], cnt["pes"] + j + 1)
                        if j >= 1:
                            v.wait_ge(S["vA"], cnt["vA"] + 2 * j)
                        for e in range(2):
                            v.tensor_tensor(out=mb[e][:], in0=p_p[e][j % 2],
                                            in1=triuI[:], op=AluOpType.mult)
                            v.signal(S["vmm"])

                @blk.scalar
                def _(s0, u=u):
                    if upto < 2 or p1_lvl < 2:
                        return
                    s = SER["s"]
                    pld_base_s = cnt["pld"] - 16 * 2 * NCH
                    for j in range(NCH):
                        if fm:
                            s.wait_ge(S["pld"], pld_base_s + 16 * 2 * (j + 1))
                        elif j == 0:
                            s.wait_ge(S["pld"], cnt["pld"])
                        for e in range(2):
                            s.wait_ge(S["vmm"], cnt["vmm"] + 2 * j + e + 1)
                            s.activation(out=at_[e][:], in_=mb[e][:],
                                         func=AF.Identity,
                                         scale=bT[u][e][:, j:j + 1])
                            s.signal(S["vA"])
                        for e in range(2):
                            if j >= 1 and e == 0 and p1_lvl >= 4:
                                s.wait_ge(S["pesd"], cnt["pesd"] + j)
                            s.activation(
                                out=kt_[e][:],
                                in_=k1tok[:, 128 * j + 64 * e:128 * j + 64 * (e + 1)],
                                func=AF.Identity, scale=bT[u][e][:, j:j + 1])
                            s.signal(S["vK"])
                        if p1_lvl >= 3:
                            s.wait_ge(S["peo"], cnt["peo"] + j + 1)
                            for e in range(2):
                                s.activation(out=o1b[e][:, 64 * j:64 * (j + 1)],
                                             in_=p_o[:, 64 * e:64 * (e + 1)],
                                             func=AF.Identity,
                                             scale=aT[u][e][:, j:j + 1])
                                s.signal(S["vO1"])
                cnt["pes"] += NCH
                cnt["peo"] += NCH
                cnt["pesd"] += NCH
                cnt["vS"] += NCH - 1
                cnt["vA"] += 2 * NCH
                cnt["vK"] += 2 * NCH
                cnt["vO1"] += 2 * NCH
                cnt["vmm"] = cnt.get("vmm", 0) + 2 * NCH

            nc.leave_named_scope(tag + f"p1u{u}", _p1_scope[0], False)
            _p2_scope = nc.enter_named_scope(tag + f"p2u{u}", False)
            with nc.Block() as blk:
                @blk.tensor
                def _(te, u=u):
                    if upto < 3:
                        return
                    te.wait_ge(S["gcon"], 2 + u)

                    def p2_score(j):
                        k2sf = k2s0 if j == 0 else k2fm[:, 128 * j - 1:128 * (j + 1) - 1]
                        q2sf = q2s0 if j == 0 else q2fm[:, 128 * j - 1:128 * (j + 1) - 1]
                        if j >= 2:
                            te.wait_ge(S["vA"], cnt["vA"] + 4 * (j - 1))
                        mm = None
                        for e in range(2):
                            te.matmul(out=p_p[e][j % 2],
                                      lhsT=k2sf[64 * e:64 * (e + 1), :],
                                      rhs=q2sf[64 * e:64 * (e + 1), :],
                                      start=True, stop=True)
                            mm = te.matmul(out=p_p2[e][j % 2],
                                           lhsT=k2fm[64 * e:64 * (e + 1),
                                                     128 * j:128 * (j + 1)],
                                           rhs=q2sf[64 * e:64 * (e + 1), :],
                                           start=True, stop=True)
                        mm.then_inc(S["pes"], 1)

                    p2_score(0)
                    for j in range(NCH):
                        q2sf = q2s0 if j == 0 else q2fm[:, 128 * j - 1:128 * (j + 1) - 1]
                        if j + 1 < NCH:
                            p2_score(j + 1)
                        te.wait_ge(S["vA"], cnt["vA"] + 4 * (j + 1))
                        if j >= 1:
                            te.wait_ge(S["vS"], cnt["vS"] + j)
                            te.wait_ge(S["vO1"], cnt["vO1"] + j)
                        mm = None
                        for e in range(2):
                            te.matmul(out=p_o[:, 64 * e:64 * (e + 1)], lhsT=a2v[e][:],
                                      rhs=vtok[:, 128 * j + 64 * e:
                                               128 * j + 64 * (e + 1)],
                                      start=True, stop=False)
                            te.matmul(out=p_o[:, 64 * e:64 * (e + 1)], lhsT=a2o[e][:],
                                      rhs=o1b[e][:, 64 * j:64 * (j + 1)],
                                      start=False, stop=False)
                            mm = te.matmul(out=p_o[:, 64 * e:64 * (e + 1)],
                                           lhsT=q2sf[:],
                                           rhs=s2bE[e][:],
                                           start=False, stop=True)
                        mm.then_inc(S["peo"], 1)
                        te.wait_ge(S["vK"], cnt["vK"] + 2 * (j + 1))
                        mm = None
                        for e in range(2):
                            te.matmul(out=p_s[e],
                                      lhsT=k2stok[:, 128 * j + 64 * e:
                                                  128 * j + 64 * (e + 1)],
                                      rhs=vtok[:, 128 * j + 64 * e:
                                               128 * j + 64 * (e + 1)],
                                      start=True, stop=False)
                            mm = te.matmul(out=p_s[e], lhsT=k2n[e][:],
                                           rhs=o1b[e][:, 64 * j:64 * (j + 1)],
                                           start=False, stop=True)
                        mm.then_inc(S["pesd"], 1)

                @blk.vector
                def _(v0, u=u):
                    if upto < 3:
                        return
                    v = SER["v"]
                    for j in range(NCH):
                        if j >= 1:
                            v.wait_ge(S["pesd"], cnt["pesd"] + j)
                            for e in range(2):
                                v.tensor_tensor(out=s2acc[e][:], in0=s2acc[e][:],
                                                in1=p_s[e], op=AluOpType.add)
                                v.tensor_copy(out=s2bE[e][64 * e:64 * (e + 1), :],
                                              in_=s2acc[e][:])
                            v.signal(S["vS"])
                        v.wait_ge(S["pes"], cnt["pes"] + j + 1)
                        for e in range(2):
                            v.tensor_tensor(out=a2v[e][:],
                                            in0=p_p[e][j % 2],
                                            in1=triuI[:], op=AluOpType.mult)
                            v.signal(S["vA"])
                            v.tensor_tensor(out=a2o[e][:],
                                            in0=p_p2[e][j % 2],
                                            in1=triuSN[:], op=AluOpType.mult)
                            v.signal(S["vA"])
                        v.wait_ge(S["peo"], cnt["peo"] + j + 1)
                        e2 = j % 2
                        if j >= 2:
                            v.wait_ge(S["ost" + str(e2)], 16 * cnt["ost"][e2] + 16 * (j // 2))
                        for e in range(2):
                            v.tensor_tensor(out=obuf[e2][:, 64 * e:64 * (e + 1)],
                                            in0=p_o[:, 64 * e:64 * (e + 1)],
                                            in1=o1b[e][:, 64 * j:64 * (j + 1)],
                                            op=AluOpType.add)
                        v.signal(S["vO1"])

                @blk.scalar
                def _(s0, u=u):
                    if upto < 3:
                        return
                    s = SER["s"]
                    s.wait_ge(S["pld"], cnt["pld"])
                    if fm:
                        s.wait_ge(S["pldg"], cnt["pldg"])
                    for j in range(NCH):
                        for e in range(2):
                            if j >= 1 and e == 0:
                                s.wait_ge(S["pesd"], cnt["pesd"] + j)
                            s.mul(out=k2n[e][:],
                                  in_=k2tok[:, 128 * j + 64 * e:
                                            128 * j + 64 * (e + 1)],
                                  mul=-1.0)
                            s.signal(S["vK"])

                @blk.sync
                def _(sync, u=u):
                    if upto < 3:
                        return
                    for j in range(NCH):
                        e2 = j % 2
                        sync.wait_ge(S["vO1"], cnt["vO1"] + j + 1)
                        sync.dma_start(out=o_slab[u, 128 * j:128 * (j + 1), :],
                                       in_=obuf[e2][:]).then_inc(S["ost" + str(e2)], 16)
                cnt["pes"] += NCH
                cnt["peo"] += NCH
                cnt["pesd"] += NCH
                cnt["vS"] += NCH - 1
                cnt["vA"] += 4 * NCH
                cnt["vK"] += 2 * NCH
                cnt["vO1"] += NCH
                cnt["ost"][0] += NCH // 2
                cnt["ost"][1] += NCH // 2
            nc.leave_named_scope(tag + f"p2u{u}", _p2_scope[0], False)
            nc.leave_named_scope(tag + f"u{u}", _u_scope[0], False)

        nc.leave_named_scope(tag + "mn", _mn_scope[0], False)
        with nc.Block() as blk:
            @blk.sync
            def _(sync):
                if upto >= 3:
                    sync.wait_ge(S["ost0"], 16 * (NCH // 2) * n_units)
                    sync.wait_ge(S["ost1"], 16 * (NCH // 2) * n_units)
                elif upto >= 1:
                    sync.wait_ge(S["tld"], cnt["tld"])
                    sync.wait_ge(S["pld"], cnt["pld"])
                    if cnt["pldg"]:
                        sync.wait_ge(S["pldg"], cnt["pldg"])
    return nc

def run_gla(nc_cache, x_flat, Wq, bq, Wk1, bk1, Wk2, bk2, Wg, bg, Ws, bs, Wp, bp):
    if "glaa" not in nc_cache:
        nc_cache["glaa"] = build_gla_a()
        nc_cache["glab"] = build_gla_b()
        nc_cache["glac"] = build_att1_c(normalize=False)
    wgh = np.zeros((D, 16), np.float32)
    for h in range(16):
        wgh[64 * h:64 * (h + 1), h] = Wg[:, 0]
    in_maps = []
    for c in range(NCORES):
        in_maps.append({"x": np.ascontiguousarray(x_flat[c * T:(c + 1) * T]),
                        "wq": Wq, "wk1": Wk1, "wk2": Wk2,
                        "bq": bq.reshape(1, D), "bk1": bk1.reshape(1, D),
                        "bk2": bk2.reshape(1, D), "wgh": wgh,
                        "bg": bg.reshape(1, 1)})
    res = run_bass_kernel_spmd(nc_cache["glaa"], in_maps, core_ids=list(range(NCORES)))
    xh = np.concatenate([np.asarray(res.results[c]["xh_out"]) for c in range(NCORES)], 0)
    qf = np.concatenate([np.asarray(res.results[c]["q_out"]) for c in range(NCORES)], 0)
    k1 = np.concatenate([np.asarray(res.results[c]["k1_out"]) for c in range(NCORES)], 0)
    k2 = np.concatenate([np.asarray(res.results[c]["k2_out"]) for c in range(NCORES)], 0)
    q2 = np.concatenate([np.asarray(res.results[c]["q2_out"]) for c in range(NCORES)], 0)
    lg = np.concatenate([np.asarray(res.results[c]["lg_out"]) for c in range(NCORES)], 1)
    k2r = k2.reshape(2, SQ, D)
    k2s = np.zeros_like(k2r)
    k2s[:, 1:] = k2r[:, :-1]
    in_maps = []
    for c in range(NCORES):
        cols = slice(128 * c, 128 * (c + 1))
        hh = slice(2 * c, 2 * c + 2)
        in_maps.append({
            "qsl": np.ascontiguousarray(qf.reshape(2, SQ, D)[:, :, cols]),
            "k1sl": np.ascontiguousarray(k1.reshape(2, SQ, D)[:, :, cols]),
            "k2sl": np.ascontiguousarray(k2r[:, :, cols]),
            "q2sl": np.ascontiguousarray(q2.reshape(2, SQ, D)[:, :, cols]),
            "xhsl": np.ascontiguousarray(xh.reshape(2, SQ, D)[:, :, cols]),
            "k2ssl": np.ascontiguousarray(k2s[:, :, cols]),
            "lgsl": np.ascontiguousarray(
                lg.reshape(16, 2, SQ).transpose(1, 0, 2)[:, hh, :]),
            "ws": Ws, "bs": bs.reshape(1, 1)})
    res = run_bass_kernel_spmd(nc_cache["glab"], in_maps, core_ids=list(range(NCORES)))
    o_full = np.zeros((2, SQ, D), qf.dtype)
    for c in range(NCORES):
        o_full[:, :, 128 * c:128 * (c + 1)] = np.asarray(res.results[c]["o_slab"])
    o_full = o_full.reshape(TOK, D)
    oden_dummy = np.ones((T, 16), np.float32)
    in_maps = []
    for c in range(NCORES):
        rows = slice(c * T, (c + 1) * T)
        in_maps.append({"x": np.ascontiguousarray(x_flat[rows]),
                        "onum": np.ascontiguousarray(o_full[rows]),
                        "oden": oden_dummy,
                        "woT": Wp, "bo": bp.reshape(1, D)})
    res = run_bass_kernel_spmd(nc_cache["glac"], in_maps, core_ids=list(range(NCORES)))
    return np.concatenate([np.asarray(res.results[c]["x_out"]) for c in range(NCORES)], 0)



def kernel_multi(**inputs):
    g = lambda n: np.ascontiguousarray(np.asarray(inputs[n], np.float32))
    x = g("L0_attn_x").reshape(TOK, D)
    nc_cache = {}
    wgT, wuT, wdT = run_wprep(g("L0_ffn_w_gate"), g("L0_ffn_w_up"),
                              g("L0_ffn_w_down"))
    x = run_ffn0(nc_cache, x, wgT, wuT, wdT)
    x = run_gla(nc_cache, x, g("L0_attn_Wq"), g("L0_attn_bq"),
                g("L0_attn_Wk1"), g("L0_attn_bk1"),
                g("L0_attn_Wk2"), g("L0_attn_bk2"),
                g("L0_attn_Wg"), g("L0_attn_bg"),
                g("L0_attn_Ws"), g("L0_attn_bs"),
                g("L0_attn_Wp"), g("L0_attn_bp"))
    x = run_ffn0(nc_cache, x, wgT, wuT, wdT)
    x = run_moe(nc_cache, x, g("L1_ffn_w_gate"), g("L1_ffn_w_expert"),
                g("L1_ffn_b_expert"))
    x = run_att1(nc_cache, x, g("L1_attn_Wq"), g("L1_attn_bq"),
                 g("L1_attn_Wk"), g("L1_attn_bk"),
                 g("L1_attn_Wv"), g("L1_attn_bv"),
                 g("L1_attn_Wo"), g("L1_attn_bo"))
    x = run_moe(nc_cache, x, g("L1_ffn_w_gate"), g("L1_ffn_w_expert"),
                g("L1_ffn_b_expert"))
    return np.ascontiguousarray(x.reshape(B, S, D).astype(np.float32))



def emit_lnag(nc, x_in, xh_loc, xhg, tag=""):
    with ExitStack() as stack:
        names = ("oos", "oov", "oog", "ldv", "ldg", "sxh", "stv", "stg", "ccc")
        S = {n: nc.alloc_semaphore(name=tag + n) for n in names}
        SER = {"s": SerEngine(nc.scalar, S["oos"]),
               "v": SerEngine(nc.vector, S["oov"]),
               "g": SerEngine(nc.gpsimd, S["oog"])}
        x3 = x_in.rearrange("(t p) d -> t p d", p=128)
        xh3 = xh_loc.rearrange("(t p) d -> t p d", p=128)
        _emit_ln_stream(nc, stack, S, SER, x3, xh3, tag=tag)
        with nc.Block() as blk:
            @blk.gpsimd
            def _(g0):
                g0.wait_ge(S["stv"], 16 * (NT // 2))
                g0.wait_ge(S["stg"], 16 * (NT // 2))
                g0.collective_compute(
                    "AllGather", mybir.AluOpType.bypass,
                    replica_groups=[list(range(NCORES))],
                    ins=[xh_loc.opt()], outs=[xhg.opt()],
                ).then_inc(S["ccc"])
                g0.wait_ge(S["ccc"], 1)


def _emit_rope_args(v, angs, rr, tmpr):
    _ts(v, rr[0][:], angs[:], 1.0 / (2 * np.pi), AluOpType.mult)
    _ts(v, rr[0][:], rr[0][:], MAGIC, AluOpType.add)
    _ts(v, rr[0][:], rr[0][:], MAGIC, AluOpType.subtract)
    _ts(v, rr[0][:], rr[0][:], -2 * np.pi, AluOpType.mult)
    v.tensor_tensor(out=rr[0][:], in0=angs[:], in1=rr[0][:], op=AluOpType.add)
    _ts(v, rr[0][:], rr[0][:], -float(np.pi), AluOpType.max,
        float(np.pi), AluOpType.min)
    _ts(v, rr[1][:], angs[:], np.pi / 2, AluOpType.add)
    _ts(v, tmpr[:], rr[1][:], 1.0 / (2 * np.pi), AluOpType.mult)
    _ts(v, tmpr[:], tmpr[:], MAGIC, AluOpType.add)
    _ts(v, tmpr[:], tmpr[:], MAGIC, AluOpType.subtract)
    _ts(v, tmpr[:], tmpr[:], -2 * np.pi, AluOpType.mult)
    v.tensor_tensor(out=rr[1][:], in0=rr[1][:], in1=tmpr[:], op=AluOpType.add)
    _ts(v, rr[1][:], rr[1][:], -float(np.pi), AluOpType.max,
        float(np.pi), AluOpType.min)


def emit_att1proj(nc, xhg, wq_c, wk_c, wv_c, bq_cc, bk_cc, bv_cc, prot_d,
                  invf2_d, qfm_loc, kfm_loc, vfm_loc, tag=""):
    NTB = TOK // 512
    with ExitStack() as stack:
        def sb(name, shape, dt=F32):
            return stack.enter_context(nc.sbuf_tensor(tag + name, shape, dt))
        def psum(name, shape, dt=F32):
            return stack.enter_context(nc.psum_tensor(tag + name, shape, dt))
        names = ("oos", "oov", "oog", "wld", "cld", "xld0", "xld1", "pe",
                 "per", "sev", "vro", "ost", "vcb", "scs", "gcon", "spos")
        S = {n: nc.alloc_semaphore(name=tag + n) for n in names}
        SER = {"s": SerEngine(nc.scalar, S["oos"]),
               "v": SerEngine(nc.vector, S["oov"]),
               "g": SerEngine(nc.gpsimd, S["oog"])}
        w_s = {nm: [sb(f"w{nm}{k}", [128, 128], BF16) for k in range(8)]
               for nm in ("q", "k", "v")}
        prot_b = sb("prot_b", [128, 128], BF16)
        invf2 = sb("invf2", [128, 1])
        bcol = {nm: sb(f"bcol{nm}", [128, 1]) for nm in ("q", "k", "v")}
        xck = [sb(f"xck{e}", [128, 512], BF16) for e in range(2)]
        qsb = sb("qsb", [128, 512], BF16)
        ksb = sb("ksb", [128, 512], BF16)
        vsb = sb("vsb", [128, 512], BF16)
        qro = sb("qro", [128, 512], BF16)
        kro = sb("kro", [128, 512], BF16)
        angs = sb("angs", [128, 512])
        rr = [sb(f"rr{e}", [128, 512]) for e in range(2)]
        tmpr = sb("tmpr", [128, 512])
        sint = sb("sint", [128, 512])
        cost = sb("cost", [128, 512])
        tq = sb("tq", [128, 512])
        tq2 = sb("tq2", [128, 512])
        iotaf = sb("iotaf", [128, 512])
        ang0 = sb("ang0", [128, 512])
        poscol = sb("poscol", [128, 1])
        p_q = psum("p_q", [128, 512])
        p_k = psum("p_k", [128, 512])
        p_v = psum("p_v", [128, 512])
        p_rot = psum("p_rot", [128, 512])

        with nc.Block() as blk:
            @blk.gpsimd
            def _(g0):
                g = SER["g"]
                g.iota(iotaf[:], [[1, 512]], base=0, channel_multiplier=0,
                       allow_small_or_imprecise_dtypes=True)
                g.signal(S["gcon"])
                for nm, wd in (("q", wq_c), ("k", wk_c), ("v", wv_c)):
                    for k in range(8):
                        g.dma_start(out=w_s[nm][k][:],
                                    in_=wd[128 * k:128 * (k + 1), :]) \
                            .then_inc(S["wld"], 16)
                g.dma_start(out=prot_b[:], in_=prot_d[:]).then_inc(S["wld"], 16)
            @blk.sync
            def _(sync):
                sync.dma_start(out=invf2[:], in_=invf2_d[:]).then_inc(S["cld"], 16)
                for nm, bd in (("q", bq_cc), ("k", bk_cc), ("v", bv_cc)):
                    sync.dma_start(out=bcol[nm][:], in_=bd[:]).then_inc(S["cld"], 16)

        with nc.Block() as blk:
            @blk.sync
            def _(sync):
                for tb in range(NTB):
                    for k in range(8):
                        m = tb * 8 + k
                        if m >= 2:
                            sync.wait_ge(S["pe"], 3 * (m - 1))
                        sync.dma_start(
                            out=xck[m % 2][:],
                            in_=xhg[512 * tb:512 * (tb + 1), 128 * k:128 * (k + 1)],
                            transpose=True).then_inc(S["xld" + str(m % 2)], 16)
            @blk.tensor
            def _(te):
                te.wait_ge(S["wld"], 16 * 25)
                for tb in range(NTB):
                    for k in range(8):
                        m = tb * 8 + k
                        te.wait_ge(S["xld" + str(m % 2)], 16 * (m // 2 + 1))
                        if k == 0 and tb >= 1:
                            te.wait_ge(S["sev"], 3 * tb)
                        mm = None
                        for nm, ps in (("q", p_q), ("k", p_k), ("v", p_v)):
                            mm = te.matmul(out=ps[:], lhsT=w_s[nm][k][:],
                                           rhs=xck[m % 2][:],
                                           start=(k == 0), stop=(k == 7))
                            mm.then_inc(S["pe"], 1)
                    te.wait_ge(S["sev"], 3 * tb + 2)
                    te.matmul(out=p_rot[:], lhsT=prot_b[:], rhs=qsb[:],
                              start=True, stop=True).then_inc(S["per"], 1)
                    te.wait_ge(S["vro"], 2 * tb + 1)
                    te.matmul(out=p_rot[:], lhsT=prot_b[:], rhs=ksb[:],
                              start=True, stop=True).then_inc(S["per"], 1)
            @blk.scalar
            def _(s0):
                s = SER["s"]
                s.wait_ge(S["cld"], 16 * 4)
                for tb in range(NTB):
                    if tb >= 1:
                        s.wait_ge(S["vcb"], tb)
                    s.mul(out=poscol[:], in_=invf2[:],
                          mul=float((512 * tb) % SQ))
                    s.signal(S["spos"])
                    if tb >= 1:
                        s.wait_ge(S["vro"], 2 * tb)
                    s.wait_ge(S["pe"], 24 * tb + 22)
                    s.activation(out=qsb[:], in_=p_q[:], func=AF.Identity,
                                 bias=bcol["q"][:])
                    s.signal(S["sev"])
                    s.wait_ge(S["pe"], 24 * tb + 23)
                    s.activation(out=ksb[:], in_=p_k[:], func=AF.Identity,
                                 bias=bcol["k"][:])
                    s.signal(S["sev"])
                    s.wait_ge(S["pe"], 24 * tb + 24)
                    if tb >= 1:
                        s.wait_ge(S["ost"], 16 * 3 * tb)
                    s.activation(out=vsb[:], in_=p_v[:], func=AF.Identity,
                                 bias=bcol["v"][:])
                    s.signal(S["sev"])
                    s.wait_ge(S["vcb"], tb + 1)
                    s.activation(out=sint[:], in_=rr[0][:], func=AF.Sin)
                    s.signal(S["scs"])
                    s.activation(out=cost[:], in_=rr[1][:], func=AF.Sin)
                    s.signal(S["scs"])
            @blk.vector
            def _(v0):
                v = SER["v"]
                v.wait_ge(S["cld"], 16)
                v.wait_ge(S["gcon"], 1)
                _ts(v, ang0[:], iotaf[:], invf2[:, 0:1], AluOpType.mult)
                for tb in range(NTB):
                    if tb >= 1:
                        v.wait_ge(S["scs"], 2 * tb)
                    v.wait_ge(S["spos"], tb + 1)
                    _ts(v, angs[:], ang0[:], poscol[:, 0:1], AluOpType.add)
                    _emit_rope_args(v, angs, rr, tmpr)
                    v.signal(S["vcb"])
                    v.wait_ge(S["per"], 2 * tb + 1)
                    v.wait_ge(S["scs"], 2 * tb + 2)
                    if tb >= 1:
                        v.wait_ge(S["ost"], 16 * 3 * tb)
                    v.tensor_tensor(out=tq[:], in0=qsb[:], in1=cost[:],
                                    op=AluOpType.mult)
                    v.tensor_tensor(out=tq2[:], in0=p_rot[:], in1=sint[:],
                                    op=AluOpType.mult)
                    v.tensor_tensor(out=qro[:], in0=tq[:], in1=tq2[:],
                                    op=AluOpType.add)
                    v.signal(S["vro"])
                    v.wait_ge(S["per"], 2 * tb + 2)
                    v.tensor_tensor(out=tq[:], in0=ksb[:], in1=cost[:],
                                    op=AluOpType.mult)
                    v.tensor_tensor(out=tq2[:], in0=p_rot[:], in1=sint[:],
                                    op=AluOpType.mult)
                    v.tensor_tensor(out=kro[:], in0=tq[:], in1=tq2[:],
                                    op=AluOpType.add)
                    v.signal(S["vro"])
            @blk.gpsimd
            def _(g0):
                g = SER["g"]
                for tb in range(NTB):
                    u = (512 * tb) // SQ
                    c0 = (512 * tb) % SQ
                    g.wait_ge(S["vro"], 2 * tb + 1)
                    g.dma_start(out=qfm_loc[u][:, c0:c0 + 512], in_=qro[:]) \
                        .then_inc(S["ost"], 16)
                    g.wait_ge(S["vro"], 2 * tb + 2)
                    g.dma_start(out=kfm_loc[u][:, c0:c0 + 512], in_=kro[:]) \
                        .then_inc(S["ost"], 16)
                    g.wait_ge(S["sev"], 3 * tb + 3)
                    g.dma_start(out=vfm_loc[u][:, c0:c0 + 512], in_=vsb[:]) \
                        .then_inc(S["ost"], 16)
        with nc.Block() as blk:
            @blk.sync
            def _(sync):
                sync.wait_ge(S["ost"], 16 * 3 * NTB)


def emit_dproj(nc, kind, o_src, wo_c, bo_d, x_in, part_loc, rs_loc, x_out,
               rdn_d=None, tag=""):
    with ExitStack() as stack:
        def sb(name, shape, dt=F32):
            return stack.enter_context(nc.sbuf_tensor(tag + name, shape, dt))
        def psum(name, shape, dt=F32):
            return stack.enter_context(nc.psum_tensor(tag + name, shape, dt))
        names = ("oos", "oov", "oog", "wld", "old", "dld", "snm", "rst", "rtl",
                 "pe", "vev", "pst", "ccc", "rld", "xld", "vres", "ost",
                 "pbb", "vbb", "sev2")
        S = {n: nc.alloc_semaphore(name=tag + n) for n in names}
        SER = {"s": SerEngine(nc.scalar, S["oos"]),
               "v": SerEngine(nc.vector, S["oov"]),
               "g": SerEngine(nc.gpsimd, S["oog"])}
        wo_s = [sb(f"wo_s{nb}", [128, 512], BF16) for nb in range(2)]
        o2h = sb("o2h", [128, SQ], BF16)
        att = kind == "att1"
        if att:
            denT = sb("denT", [128, SQ // 128], BF16)
            rdnT = [sb(f"rdnT{e}", [128, SQ // 128]) for e in range(2)]
            ta0s = [[sb(f"ta0s{s2}_{nb}", [128, 512]) for nb in range(2)]
                    for s2 in range(2)]
            ta1 = sb("ta1", [128, 512])
        pout = [sb(f"pout{e}", [128, D], BF16) for e in range(2)]
        p_pt = [psum(f"p_pt{j}", [128, 512]) for j in range(8 if att else 4)]
        part3 = part_loc.rearrange("(t p) d -> t p d", p=128)
        NCK = SQ // 128

        with nc.Block() as blk:
            @blk.gpsimd
            def _(g0):
                g = SER["g"]
                for nb in range(2):
                    g.dma_start(out=wo_s[nb][:],
                                in_=wo_c[:, 512 * nb:512 * (nb + 1)]) \
                        .then_inc(S["wld"], 16)

        for u in range(2):
            if att:
                with nc.Block() as blk:
                    @blk.sync
                    def _(sync, u=u):
                        if u == 1:
                            sync.wait_ge(S["pst"], 16 * NCK)
                            sync.wait_ge(S["vev"], NCK)
                        for e in range(2):
                            sync.dma_start(out=o2h[64 * e:64 * (e + 1), :],
                                           in_=o_src[u, e, 0:64, :]) \
                                .then_inc(S["old"], 16)
                        for e in range(2):
                            n = 2 * u + e
                            if n >= 1:
                                sync.wait_ge(S["snm"], 2 * n)
                            sync.dma_start(
                                out=denT[:],
                                in_=o_src[u, e, 64, :].rearrange(
                                    "(c p) -> c p", p=128),
                                transpose=True).then_inc(S["dld"], 16)
                            sync.wait_ge(S["dld"], 16 * (n + 1))
                    @blk.scalar
                    def _(s0, u=u):
                        s = SER["s"]
                        for e in range(2):
                            n = 2 * u + e
                            s.wait_ge(S["dld"], 16 * (n + 1))
                            s.activation(out=rdnT[e][:], in_=denT[:], func=AF.Ln)
                            s.signal(S["snm"])
                            s.activation(out=rdnT[e][:], in_=rdnT[e][:],
                                         func=AF.Exp, scale=-1.0)
                            s.signal(S["snm"])
            else:
                with nc.Block() as blk:
                    @blk.sync
                    def _(sync, u=u):
                        if u == 1:
                            sync.wait_ge(S["pst"], 16 * NCK)
                        for j in range(NCK):
                            sync.dma_start(out=o2h[:, 128 * j:128 * (j + 1)],
                                           in_=o_src[u, 128 * j:128 * (j + 1), :],
                                           transpose=True).then_inc(S["old"], 16)

            with nc.Block() as blk:
                @blk.tensor
                def _(te, u=u):
                    te.wait_ge(S["wld"], 16 * 2)
                    te.wait_ge(S["old"], 16 * (2 if att else NCK) * (u + 1))
                    for tc in range(NCK):
                        mm = None
                        if att:
                            if tc >= 2:
                                te.wait_ge(S["vev"], u * NCK + tc - 1)
                            for nb in range(2):
                                for e in range(2):
                                    mm = te.matmul(
                                        out=p_pt[4 * (tc % 2) + 2 * nb + e][:],
                                        lhsT=o2h[64 * e:64 * (e + 1),
                                                 128 * tc:128 * (tc + 1)],
                                        rhs=wo_s[nb][64 * e:64 * (e + 1), :],
                                        start=True, stop=True)
                        else:
                            if tc >= 2:
                                te.wait_ge(S["vev"], u * NCK + tc - 1)
                            for nb in range(2):
                                mm = te.matmul(out=p_pt[2 * (tc % 2) + nb][:],
                                               lhsT=o2h[:, 128 * tc:128 * (tc + 1)],
                                               rhs=wo_s[nb][:],
                                               start=True, stop=True)
                        mm.then_inc(S["pe"], 1)
                @blk.vector
                def _(v0, u=u):
                    v = SER["v"]
                    if att:
                        v.wait_ge(S["snm"], 4 * u + 4)
                    for tc in range(NCK):
                        v.wait_ge(S["pe"], u * NCK + tc + 1)
                        if tc >= 2:
                            v.wait_ge(S["pst"], 16 * (u * NCK + tc - 1))
                        for nb in range(2):
                            if att:
                                _ts(v, ta0s[0][0][:],
                                    p_pt[4 * (tc % 2) + 2 * nb][:],
                                    rdnT[0][:, tc:tc + 1], AluOpType.mult)
                                _ts(v, ta1[:], p_pt[4 * (tc % 2) + 2 * nb + 1][:],
                                    rdnT[1][:, tc:tc + 1], AluOpType.mult)
                                v.tensor_tensor(
                                    out=pout[tc % 2][:, 512 * nb:512 * (nb + 1)],
                                    in0=ta0s[0][0][:], in1=ta1[:], op=AluOpType.add)
                            else:
                                v.tensor_copy(
                                    out=pout[tc % 2][:, 512 * nb:512 * (nb + 1)],
                                    in_=p_pt[2 * (tc % 2) + nb][:])
                        v.signal(S["vev"])
                @blk.gpsimd
                def _(g0, u=u):
                    g = SER["g"]
                    for tc in range(NCK):
                        g.wait_ge(S["vev"], u * NCK + tc + 1)
                        g.dma_start(out=part3[u * NCK + tc], in_=pout[tc % 2][:]) \
                            .then_inc(S["pst"], 16)

        with nc.Block() as blk:
            @blk.gpsimd
            def _(g0):
                g0.wait_ge(S["pst"], 16 * 2 * NCK)
                g0.collective_compute(
                    "ReduceScatter", mybir.AluOpType.add,
                    replica_groups=[list(range(NCORES))],
                    ins=[part_loc.opt()], outs=[rs_loc.opt()],
                ).then_inc(S["ccc"])
                g0.wait_ge(S["ccc"], 1)
        with ExitStack() as stk2:
            def sb2(name, shape, dt=F32):
                return stk2.enter_context(nc.sbuf_tensor(tag + name, shape, dt))
            xt2 = [sb2(f"xt2{e}", [128, D]) for e in range(2)]
            rt2 = [sb2(f"rt2{e}", [128, D], BF16) for e in range(2)]
            ot2 = [sb2(f"ot2{e}", [128, D]) for e in range(2)]
            bosb = sb2("bosb", [1, D], BF16)
            onescol = sb2("onescol", [1, 128], BF16)
            bb = sb2("bb", [128, D])
            p_bb = p_pt[0]
            x3 = x_in.rearrange("(t p) d -> t p d", p=128)
            rs3 = rs_loc.rearrange("(t p) d -> t p d", p=128)
            xo3 = x_out.rearrange("(t p) d -> t p d", p=128)
            with nc.Block() as blk:
                @blk.gpsimd
                def _(g0):
                    g = SER["g"]
                    g.memset(onescol[:], 1.0)
                    g.dma_start(out=bosb[:], in_=bo_d[:]).then_inc(S["rld"], 16)
                @blk.sync
                def _(sync):
                    for i in range(NT):
                        if i >= 2:
                            sync.wait_ge(S["vres"], i - 1)
                        sync.dma_start(out=xt2[i % 2][:], in_=x3[i]) \
                            .then_inc(S["xld"], 16)
                        sync.dma_start(out=rt2[i % 2][:], in_=rs3[i]) \
                            .then_inc(S["xld"], 16)
                @blk.tensor
                def _(te):
                    te.wait_ge(S["rld"], 16)
                    for nb in range(2):
                        te.matmul(out=p_bb[:], lhsT=onescol[:],
                                  rhs=bosb[:, 512 * nb:512 * (nb + 1)],
                                  start=True, stop=True).then_inc(S["pbb"], 1)
                        te.wait_ge(S["vbb"], nb + 1)
                @blk.vector
                def _(v0):
                    v = SER["v"]
                    for nb in range(2):
                        v.wait_ge(S["pbb"], nb + 1)
                        v.tensor_copy(out=bb[:, 512 * nb:512 * (nb + 1)],
                                      in_=p_bb[:])
                        v.signal(S["vbb"])
                    for i in range(NT):
                        v.wait_ge(S["xld"], 16 * 2 * (i + 1))
                        if i >= 2:
                            v.wait_ge(S["ost"], 16 * (i - 1))
                        v.tensor_tensor(out=ot2[i % 2][:], in0=xt2[i % 2][:],
                                        in1=rt2[i % 2][:], op=AluOpType.add)
                        v.tensor_tensor(out=ot2[i % 2][:], in0=ot2[i % 2][:],
                                        in1=bb[:], op=AluOpType.add)
                        v.signal(S["vres"])
                @blk.gpsimd
                def _(g0):
                    g = SER["g"]
                    for i in range(NT):
                        g.wait_ge(S["vres"], i + 1)
                        g.dma_start(out=xo3[i], in_=ot2[i % 2][:]) \
                            .then_inc(S["ost"], 16)
                    g.wait_ge(S["ost"], 16 * NT)


def emit_glaproj(nc, xhg, wq_c, wk1_c, wk2_c, sel_c, wgh2_c, bq_cc, bk1_cc,
                 bk2_cc, bg2_d, qfm_loc, k1fm_loc, k2fm_loc, k2sfm_loc,
                 q2fm_loc, xhfm_loc, lgsl_loc, ws2_d=None, bs2_d=None,
                 rrow_d=None, tag=""):
    NTB = TOK // 512
    with ExitStack() as stack:
        def sb(name, shape, dt=F32):
            return stack.enter_context(nc.sbuf_tensor(tag + name, shape, dt))
        def psum(name, shape, dt=F32):
            return stack.enter_context(nc.psum_tensor(tag + name, shape, dt))
        names = ("oos", "oov", "oog", "wld", "cld", "xld0", "xld1", "pe",
                 "sev", "vq2", "vxh", "vsg", "slg", "stq", "stk1", "stk2",
                 "stq2", "stxh", "stlg", "zst", "vkb", "per2", "srr", "strr")
        S = {n: nc.alloc_semaphore(name=tag + n) for n in names}
        SER = {"s": SerEngine(nc.scalar, S["oos"]),
               "v": SerEngine(nc.vector, S["oov"]),
               "g": SerEngine(nc.gpsimd, S["oog"])}
        w_s = {nm: [sb(f"w{nm}{k}", [128, 128], BF16) for k in range(8)]
               for nm in ("q", "k1", "k2", "sl")}
        wg_s = [sb(f"wgs{k}", [128, 2], BF16) for k in range(8)]
        bqc = sb("bqc", [128, 1]); bk1c = sb("bk1c", [128, 1])
        bk2c = sb("bk2c", [128, 1]); bk2sc = sb("bk2sc", [128, 1])
        bg2c = sb("bg2c", [2, 1])
        zcol = sb("zcol", [128, 1], BF16)
        xck = [sb(f"xck{e}", [128, 512], BF16) for e in range(2)]
        qf = sb("qf", [128, 512])
        k1sb = sb("k1sb", [128, 512])
        k1b = sb("k1b", [128, 512], BF16)
        ws2_b = sb("ws2_b", [128, 2], BF16)
        bs2c = sb("bs2c", [2, 1])
        rrowb = sb("rrowb", [2, 512])
        k2sb = sb("k2sb", [128, 512], BF16)
        q2b = sb("q2b", [128, 512], BF16)
        xsb = sb("xsb", [128, 512], BF16)
        qn1 = sb("qn1", [128, 512]); qab = sb("qab", [128, 512])
        glg = sb("glg", [2, 512])
        p_q = psum("p_q", [128, 512])
        p_k1 = psum("p_k1", [128, 512])
        p_k2 = psum("p_k2", [128, 512])
        p_xh = psum("p_xh", [128, 512])
        p_g = psum("p_g", [2, 512])
        p_r2 = psum("p_r2", [2, 512])

        with nc.Block() as blk:
            @blk.gpsimd
            def _(g0):
                g = SER["g"]
                g.memset(zcol[:], 0.0)
                g.signal(S["zst"])
                for nm, wd in (("q", wq_c), ("k1", wk1_c), ("k2", wk2_c),
                               ("sl", sel_c)):
                    for k in range(8):
                        g.dma_start(out=w_s[nm][k][:],
                                    in_=wd[128 * k:128 * (k + 1), :]) \
                            .then_inc(S["wld"], 16)
                for k in range(8):
                    g.dma_start(out=wg_s[k][:],
                                in_=wgh2_c[128 * k:128 * (k + 1), :]) \
                        .then_inc(S["wld"], 16)
                if rrow_d is not None:
                    g.dma_start(out=ws2_b[:], in_=ws2_d[:]).then_inc(S["wld"], 16)
            @blk.sync
            def _(sync):
                pairs = [(bqc, bq_cc), (bk1c, bk1_cc), (bk2c, bk2_cc),
                         (bg2c, bg2_d)]
                if rrow_d is not None:
                    pairs.append((bs2c, bs2_d))
                for dst, srcd in pairs:
                    sync.dma_start(out=dst[:], in_=srcd[:]).then_inc(S["cld"], 16)
            @blk.scalar
            def _(s0):
                s = SER["s"]
                s.wait_ge(S["cld"], 16 * 4)
                s.mul(out=bk2sc[:], in_=bk2c[:], mul=K2SC)
                s.signal(S["slg"])
        with nc.Block() as blk:
            @blk.gpsimd
            def _(g0):
                g = SER["g"]
                g.wait_ge(S["zst"], 1)
                for u in range(2):
                    g.dma_start(out=k2sfm_loc[u][:, 0:1], in_=zcol[:],
                                allow_slow_non_contiguous=True) \
                        .then_inc(S["stk2"], 16)

        with nc.Block() as blk:
            @blk.sync
            def _(sync):
                for tb in range(NTB):
                    u = (512 * tb) // SQ
                    c0 = (512 * tb) % SQ
                    for k in range(8):
                        m = tb * 8 + k
                        if m >= 2:
                            sync.wait_ge(S["pe"], m - 1)
                        sync.dma_start(
                            out=xck[m % 2][:],
                            in_=xhg[512 * tb:512 * (tb + 1), 128 * k:128 * (k + 1)],
                            transpose=True).then_inc(S["xld" + str(m % 2)], 16)
                    sync.wait_ge(S["slg"], tb + 2)
                    sync.dma_start(out=lgsl_loc[u, :, c0:c0 + 512], in_=glg[:]) \
                        .then_inc(S["stlg"], 16)
                    if rrow_d is not None:
                        sync.wait_ge(S["srr"], tb + 1)
                        sync.dma_start(out=rrow_d[u, :, c0:c0 + 512], in_=rrowb[:]) \
                            .then_inc(S["strr"], 16)
            @blk.tensor
            def _(te):
                te.wait_ge(S["wld"], 16 * (41 if rrow_d is not None else 40))
                for tb in range(NTB):
                    for k in range(8):
                        m = tb * 8 + k
                        te.wait_ge(S["xld" + str(m % 2)], 16 * (m // 2 + 1))
                        if k == 0 and tb >= 1:
                            te.wait_ge(S["sev"], 4 * tb)
                            te.wait_ge(S["vxh"], tb)
                        mm = None
                        for nm, ps in (("q", p_q), ("k1", p_k1), ("k2", p_k2),
                                       ("sl", p_xh)):
                            te.matmul(out=ps[:], lhsT=w_s[nm][k][:],
                                      rhs=xck[m % 2][:],
                                      start=(k == 0), stop=(k == 7))
                        mm = te.matmul(out=p_g[:], lhsT=wg_s[k][:],
                                       rhs=xck[m % 2][:],
                                       start=(k == 0), stop=(k == 7))
                        mm.then_inc(S["pe"], 1)
                    if rrow_d is not None:
                        te.wait_ge(S["vkb"], tb + 1)
                        if tb >= 1:
                            te.wait_ge(S["srr"], tb)
                        te.matmul(out=p_r2[:], lhsT=ws2_b[:], rhs=k1b[:],
                                  start=True, stop=True).then_inc(S["per2"], 1)
            @blk.scalar
            def _(s0):
                s = SER["s"]
                for tb in range(NTB):
                    s.wait_ge(S["pe"], 8 * tb + 8)
                    if tb >= 1:
                        s.wait_ge(S["vq2"], tb)
                        s.wait_ge(S["stq"], 16 * tb)
                    s.activation(out=qf[:], in_=p_q[:], func=AF.Identity,
                                 bias=bqc[:])
                    s.signal(S["sev"])
                    if tb >= 1:
                        s.wait_ge(S["stk1"], 16 * tb)
                    s.activation(out=k1sb[:], in_=p_k1[:], func=AF.Identity,
                                 bias=bk1c[:])
                    s.signal(S["sev"])
                    if tb >= 1:
                        s.wait_ge(S["stk2"], 32 + 32 * tb)
                    s.activation(out=k2sb[:], in_=p_k2[:], func=AF.Sigmoid,
                                 scale=K2SC, bias=bk2sc[:])
                    s.signal(S["sev"])
                    if tb >= 1:
                        s.wait_ge(S["stlg"], 16 * tb)
                    s.activation(out=glg[:], in_=p_g[:], func=AF.Sigmoid,
                                 bias=bg2c[:])
                    s.signal(S["sev"])
                    s.wait_ge(S["vsg"], tb + 1)
                    s.activation(out=glg[:], in_=glg[:], func=AF.Ln)
                    s.signal(S["slg"])
                    if rrow_d is not None:
                        s.wait_ge(S["per2"], tb + 1)
                        if tb >= 1:
                            s.wait_ge(S["strr"], 16 * tb)
                        s.activation(out=rrowb[:], in_=p_r2[:], func=AF.Silu,
                                     bias=bs2c[:])
                        s.signal(S["srr"])
            @blk.vector
            def _(v0):
                v = SER["v"]
                for tb in range(NTB):
                    v.wait_ge(S["sev"], 4 * tb + 1)
                    if tb >= 1:
                        v.wait_ge(S["stq2"], 16 * tb)
                    _ts(v, qn1[:], qf[:], -1.0, AluOpType.mult)
                    v.tensor_tensor(out=qab[:], in0=qf[:], in1=qn1[:],
                                    op=AluOpType.max)
                    _ts(v, qn1[:], qf[:], 0.06375, AluOpType.mult)
                    v.scalar_tensor_tensor(out=q2b[:], in0=qab[:],
                                           scalar=-0.06125, in1=qn1[:],
                                           op0=AluOpType.mult, op1=AluOpType.add)
                    v.signal(S["vq2"])
                    v.wait_ge(S["pe"], 8 * tb + 8)
                    if tb >= 1:
                        v.wait_ge(S["stxh"], 16 * tb)
                    v.tensor_copy(out=xsb[:], in_=p_xh[:])
                    v.signal(S["vxh"])
                    if rrow_d is not None:
                        v.wait_ge(S["sev"], 4 * tb + 2)
                        if tb >= 1:
                            v.wait_ge(S["per2"], tb)
                        v.tensor_copy(out=k1b[:], in_=k1sb[:])
                        v.signal(S["vkb"])
                    v.wait_ge(S["sev"], 4 * tb + 4)
                    _ts(v, glg[:], glg[:], 1e-6, AluOpType.max)
                    v.signal(S["vsg"])
            @blk.gpsimd
            def _(g0):
                g = SER["g"]
                for tb in range(NTB):
                    u = (512 * tb) // SQ
                    c0 = (512 * tb) % SQ
                    g.wait_ge(S["sev"], 4 * tb + 1)
                    g.dma_start(out=qfm_loc[u][:, c0:c0 + 512], in_=qf[:]) \
                        .then_inc(S["stq"], 16)
                    g.wait_ge(S["sev"], 4 * tb + 2)
                    g.dma_start(out=k1fm_loc[u][:, c0:c0 + 512], in_=k1sb[:]) \
                        .then_inc(S["stk1"], 16)
                    g.wait_ge(S["sev"], 4 * tb + 3)
                    g.dma_start(out=k2fm_loc[u][:, c0:c0 + 512], in_=k2sb[:]) \
                        .then_inc(S["stk2"], 16)
                    if c0 + 512 < SQ:
                        g.dma_start(out=k2sfm_loc[u][:, c0 + 1:c0 + 513],
                                    in_=k2sb[:]).then_inc(S["stk2"], 16)
                    else:
                        g.dma_start(out=k2sfm_loc[u][:, c0 + 1:SQ],
                                    in_=k2sb[:, 0:511]).then_inc(S["stk2"], 16)
                    g.wait_ge(S["vq2"], tb + 1)
                    g.dma_start(out=q2fm_loc[u][:, c0:c0 + 512], in_=q2b[:]) \
                        .then_inc(S["stq2"], 16)
                    g.wait_ge(S["vxh"], tb + 1)
                    g.dma_start(out=xhfm_loc[u][:, c0:c0 + 512], in_=xsb[:]) \
                        .then_inc(S["stxh"], 16)
        with nc.Block() as blk:
            @blk.sync
            def _(sync):
                sync.wait_ge(S["stq"], 16 * NTB)
                sync.wait_ge(S["stk1"], 16 * NTB)
                sync.wait_ge(S["stk2"], 32 + 32 * NTB)
                sync.wait_ge(S["stq2"], 16 * NTB)
                sync.wait_ge(S["stxh"], 16 * NTB)
                sync.wait_ge(S["stlg"], 16 * NTB)
                if rrow_d is not None:
                    sync.wait_ge(S["strr"], 16 * NTB)


def _fused_att1_tensors(nc):
    TEN = {}
    TEN["x"] = nc.dram_tensor("x", [T, D], F32, kind="ExternalInput").ap()
    for n in ("waq_c", "wak_c", "wav_c"):
        TEN[n] = nc.dram_tensor(n, [D, 128], F32, kind="ExternalInput").ap()
    for n in ("baq_cc", "bak_cc", "bav_cc"):
        TEN[n] = nc.dram_tensor(n, [128, 1], F32, kind="ExternalInput").ap()
    TEN["prot"] = nc.dram_tensor("prot", [128, 128], F32, kind="ExternalInput").ap()
    TEN["invf2"] = nc.dram_tensor("invf2", [128, 1], F32, kind="ExternalInput").ap()
    TEN["wo_c"] = nc.dram_tensor("wo_c", [128, D], F32, kind="ExternalInput").ap()
    TEN["bo"] = nc.dram_tensor("bo", [1, D], F32, kind="ExternalInput").ap()
    TEN["x_out"] = nc.dram_tensor("x_out", [T, D], F32, kind="ExternalOutput").ap()
    return TEN


def _fused_att1_scratch(nc, TEN, pfx=""):
    TEN["xh_loc"] = nc.dram_tensor(pfx + "xh_loc", [T, D], BF16).ap()
    TEN["xhg"] = nc.dram_tensor(pfx + "xhg", [TOK, D], BF16,
                                addr_space="Shared").ap()
    for n in ("qfm_loc", "kfm_loc", "vfm_loc"):
        TEN[n] = nc.dram_tensor(pfx + n, [2, 128, SQ], BF16).ap()
    TEN["oT_loc"] = nc.dram_tensor(pfx + "oT_loc", [2, 2, 65, SQ], BF16).ap()
    TEN["part_loc"] = nc.dram_tensor(pfx + "part_loc", [TOK, D], BF16).ap()
    TEN["rs_loc"] = nc.dram_tensor(pfx + "rs_loc", [T, D], BF16).ap()
    TEN["rdn_d"] = nc.dram_tensor(pfx + "rdn_d", [2, SQ], F32).ap()


def emit_att1_fused(nc, TEN, x_in, x_out, pfx="", pre_ln=False):
    with nc.cleanup_on_exit():
        if not pre_ln:
            with nc.named_scope(pfx + "a_lnag"):
                emit_lnag(nc, x_in, TEN["xh_loc"], TEN["xhg"], tag=pfx + "la_")
        with nc.named_scope(pfx + "a_proj"):
            emit_att1proj(nc, TEN["xhg"], TEN["waq_c"], TEN["wak_c"],
                          TEN["wav_c"], TEN["baq_cc"], TEN["bak_cc"],
                          TEN["bav_cc"], TEN["prot"], TEN["invf2"],
                          TEN["qfm_loc"], TEN["kfm_loc"], TEN["vfm_loc"],
                          tag=pfx + "pj_")
        with nc.named_scope(pfx + "a_flash"):
            emit_att1b(nc, {"qfm_loc": TEN["qfm_loc"],
                            "kfm_loc": TEN["kfm_loc"],
                            "vfm_loc": TEN["vfm_loc"],
                            "oT_out": TEN["oT_loc"]},
                       tag=pfx + "fa_", fm=True)
        with nc.named_scope(pfx + "a_dproj"):
            emit_dproj(nc, "att1", TEN["oT_loc"], TEN["wo_c"], TEN["bo"], x_in,
                       TEN["part_loc"], TEN["rs_loc"], x_out,
                       rdn_d=TEN["rdn_d"], tag=pfx + "dp_")


def build_att1_fused():
    nc = bass.Bass("TRN2", target_bir_lowering=False)
    TEN = _fused_att1_tensors(nc)
    _fused_att1_scratch(nc, TEN)
    emit_att1_fused(nc, TEN, TEN["x"], TEN["x_out"])
    return nc


def _att1_fused_inmaps(x_flat, Wq, bq, Wk, bk, Wv, bv, Wo, bo):
    invf2 = (1.0 / 10000.0 ** ((np.arange(128) % 64 // 2).astype(np.float32)
                               / 32.0)).astype(np.float32).reshape(128, 1)
    prot = np.zeros((128, 128), np.float32)
    for i in range(64):
        prot[2 * i + 1, 2 * i] = -1.0
        prot[2 * i, 2 * i + 1] = 1.0
    wqT = Wq.T; wkT = Wk.T; wvT = Wv.T; woT = Wo.T
    in_maps = []
    for c in range(NCORES):
        cols = slice(128 * c, 128 * (c + 1))
        in_maps.append({
            "x": np.ascontiguousarray(x_flat[c * T:(c + 1) * T]),
            "waq_c": np.ascontiguousarray(wqT[:, cols]),
            "wak_c": np.ascontiguousarray(wkT[:, cols]),
            "wav_c": np.ascontiguousarray(wvT[:, cols]),
            "baq_cc": np.ascontiguousarray(bq[cols].reshape(128, 1)),
            "bak_cc": np.ascontiguousarray(bk[cols].reshape(128, 1)),
            "bav_cc": np.ascontiguousarray(bv[cols].reshape(128, 1)),
            "prot": prot, "invf2": invf2,
            "wo_c": np.ascontiguousarray(woT[cols, :]),
            "bo": bo.reshape(1, D)})
    return in_maps


def run_att1_fused(nc_cache, x_flat, Wq, bq, Wk, bk, Wv, bv, Wo, bo):
    if "att1f" not in nc_cache:
        nc_cache["att1f"] = build_att1_fused()
    in_maps = _att1_fused_inmaps(x_flat, Wq, bq, Wk, bk, Wv, bv, Wo, bo)
    res = run_bass_kernel_spmd(nc_cache["att1f"], in_maps,
                               core_ids=list(range(NCORES)))
    return np.concatenate([np.asarray(res.results[c]["x_out"])
                           for c in range(NCORES)], 0)


def build_att1_fused_dbg(upto=4):
    nc = bass.Bass("TRN2", target_bir_lowering=False)
    TEN = _fused_att1_tensors(nc)
    _fused_att1_scratch(nc, TEN)
    dumps = []
    if upto >= 1:
        with nc.cleanup_on_exit():
            emit_lnag(nc, TEN["x"], TEN["xh_loc"], TEN["xhg"], tag="la_")
        dumps.append(("xhg_dbg", TEN["xhg"], [TOK, D], BF16))
    if upto >= 2:
        with nc.cleanup_on_exit():
            emit_att1proj(nc, TEN["xhg"], TEN["waq_c"], TEN["wak_c"],
                          TEN["wav_c"], TEN["baq_cc"], TEN["bak_cc"],
                          TEN["bav_cc"], TEN["prot"], TEN["invf2"],
                          TEN["qfm_loc"], TEN["kfm_loc"], TEN["vfm_loc"],
                          tag="pj_")
        for n in ("qfm_loc", "kfm_loc", "vfm_loc"):
            dumps.append((n + "_dbg", TEN[n], [2, 128, SQ], BF16))
    if upto >= 3:
        with nc.cleanup_on_exit():
            emit_att1b(nc, {"qfm_loc": TEN["qfm_loc"], "kfm_loc": TEN["kfm_loc"],
                            "vfm_loc": TEN["vfm_loc"], "oT_out": TEN["oT_loc"]},
                       tag="fa_", fm=True)
        dumps.append(("oT_dbg", TEN["oT_loc"], [2, 2, 65, SQ], BF16))
    if upto >= 4:
        with nc.cleanup_on_exit():
            emit_dproj(nc, "att1", TEN["oT_loc"], TEN["wo_c"], TEN["bo"],
                       TEN["x"], TEN["part_loc"], TEN["rs_loc"], TEN["x_out"],
                       rdn_d=TEN["rdn_d"], tag="dp_")
        dumps.append(("part_dbg", TEN["part_loc"], [TOK, D], F32))
        dumps.append(("rs_dbg", TEN["rs_loc"], [T, D], F32))
        dumps.append(("rdn_dbg", TEN["rdn_d"], [2, SQ], F32))
    with ExitStack() as stack:
        dsem = nc.alloc_semaphore(name="dbg_sem")
        with nc.Block() as blk:
            @blk.gpsimd
            def _(g0):
                ninc = 0
                for nm, src, shape, dt in dumps:
                    dst = nc.dram_tensor(nm, shape, dt, kind="ExternalOutput").ap()
                    g0.dma_start(out=dst, in_=src).then_inc(dsem, 16)
                    ninc += 16
                if ninc:
                    g0.wait_ge(dsem, ninc)
    return nc


def _fused_gla_tensors(nc):
    TEN = {}
    TEN["x"] = nc.dram_tensor("x", [T, D], F32, kind="ExternalInput").ap()
    for n in ("wq_c", "wk1_c", "wk2_c", "sel_c"):
        TEN[n] = nc.dram_tensor(n, [D, 128], F32, kind="ExternalInput").ap()
    TEN["wgh2_c"] = nc.dram_tensor("wgh2_c", [D, 2], F32, kind="ExternalInput").ap()
    for n in ("bq_cc", "bk1_cc", "bk2_cc"):
        TEN[n] = nc.dram_tensor(n, [128, 1], F32, kind="ExternalInput").ap()
    TEN["bg2"] = nc.dram_tensor("bg2", [2, 1], F32, kind="ExternalInput").ap()
    TEN["ws"] = nc.dram_tensor("ws", [64, 1], F32, kind="ExternalInput").ap()
    TEN["bs"] = nc.dram_tensor("bs", [1, 1], F32, kind="ExternalInput").ap()
    TEN["ws2_c"] = nc.dram_tensor("ws2_c", [128, 2], F32,
                                  kind="ExternalInput").ap()
    TEN["bs2"] = nc.dram_tensor("bs2", [2, 1], F32, kind="ExternalInput").ap()
    TEN["wp_c"] = nc.dram_tensor("wp_c", [128, D], F32, kind="ExternalInput").ap()
    TEN["bp"] = nc.dram_tensor("bp", [1, D], F32, kind="ExternalInput").ap()
    TEN["x_out"] = nc.dram_tensor("x_out", [T, D], F32, kind="ExternalOutput").ap()
    return TEN


def _fused_gla_scratch(nc, TEN, pfx=""):
    TEN["xh_loc"] = nc.dram_tensor(pfx + "gxh_loc", [T, D], BF16).ap()
    TEN["xhg"] = nc.dram_tensor(pfx + "gxhg", [TOK, D], BF16,
                                addr_space="Shared").ap()
    for n in ("qfm_loc", "k1fm_loc", "k2fm_loc", "k2sfm_loc", "q2fm_loc",
              "xhfm_loc"):
        TEN[n] = nc.dram_tensor(pfx + "g" + n, [2, 128, SQ], BF16).ap()
    TEN["lgsl"] = nc.dram_tensor(pfx + "glgsl", [2, 2, SQ], F32).ap()
    TEN["rrow_d"] = nc.dram_tensor(pfx + "grrow", [2, 2, SQ], F32).ap()
    TEN["o_slab"] = nc.dram_tensor(pfx + "go_slab", [2, SQ, 128], BF16).ap()
    TEN["rowd"] = nc.dram_tensor(pfx + "growd", [SQ], F32, kind="Internal").ap()
    TEN["part_loc"] = nc.dram_tensor(pfx + "gpart_loc", [TOK, D], BF16).ap()
    TEN["rs_loc"] = nc.dram_tensor(pfx + "grs_loc", [T, D], BF16).ap()


def emit_gla_fused(nc, TEN, x_in, x_out, pfx="", pre_ln=False):
    if not pre_ln:
        with nc.named_scope(pfx + "g_lnag"), nc.cleanup_on_exit():
            emit_lnag(nc, x_in, TEN["xh_loc"], TEN["xhg"], tag=pfx + "gla_")
    with nc.cleanup_on_exit():
      with nc.named_scope(pfx + "g_proj"):
        emit_glaproj(nc, TEN["xhg"], TEN["wq_c"], TEN["wk1_c"], TEN["wk2_c"],
                     TEN["sel_c"], TEN["wgh2_c"], TEN["bq_cc"], TEN["bk1_cc"],
                     TEN["bk2_cc"], TEN["bg2"], TEN["qfm_loc"], TEN["k1fm_loc"],
                     TEN["k2fm_loc"], TEN["k2sfm_loc"], TEN["q2fm_loc"],
                     TEN["xhfm_loc"], TEN["lgsl"], ws2_d=TEN["ws2_c"],
                     bs2_d=TEN["bs2"], rrow_d=TEN["rrow_d"], tag=pfx + "gpj_")
      with nc.named_scope(pfx + "g_scan"):
        emit_glab(nc, {"qfm_loc": TEN["qfm_loc"], "k1fm_loc": TEN["k1fm_loc"],
                       "k2fm_loc": TEN["k2fm_loc"],
                       "k2sfm_loc": TEN["k2sfm_loc"],
                       "q2fm_loc": TEN["q2fm_loc"],
                       "xhfm_loc": TEN["xhfm_loc"], "lgsl": TEN["lgsl"],
                       "ws": TEN["ws"], "bs": TEN["bs"],
                       "rrow_d": TEN["rrow_d"],
                       "o_slab": TEN["o_slab"], "rowd": TEN["rowd"]},
                  tag=pfx + "gsc_", fm=True)
      with nc.named_scope(pfx + "g_dproj"):
        emit_dproj(nc, "gla", TEN["o_slab"], TEN["wp_c"], TEN["bp"], x_in,
                   TEN["part_loc"], TEN["rs_loc"], x_out, tag=pfx + "gdp_")


def build_gla_fused():
    nc = bass.Bass("TRN2", target_bir_lowering=False)
    TEN = _fused_gla_tensors(nc)
    _fused_gla_scratch(nc, TEN)
    emit_gla_fused(nc, TEN, TEN["x"], TEN["x_out"])
    return nc


def _gla_fused_inmaps(x_flat, Wq, bq, Wk1, bk1, Wk2, bk2, Wg, bg, Ws, bs,
                      Wp, bp):
    in_maps = []
    for c in range(NCORES):
        cols = slice(128 * c, 128 * (c + 1))
        sel = np.zeros((D, 128), np.float32)
        sel[np.arange(128 * c, 128 * (c + 1)), np.arange(128)] = 1.0
        wgh2 = np.zeros((D, 2), np.float32)
        ws2 = np.zeros((128, 2), np.float32)
        for e in range(2):
            h = 2 * c + e
            wgh2[64 * h:64 * (h + 1), e] = Wg[:, 0]
            ws2[64 * e:64 * (e + 1), e] = Ws[:, 0]
        in_maps.append({
            "x": np.ascontiguousarray(x_flat[c * T:(c + 1) * T]),
            "wq_c": np.ascontiguousarray(Wq[:, cols]),
            "wk1_c": np.ascontiguousarray(Wk1[:, cols]),
            "wk2_c": np.ascontiguousarray(Wk2[:, cols]),
            "sel_c": sel, "wgh2_c": wgh2,
            "bq_cc": np.ascontiguousarray(bq[cols].reshape(128, 1)),
            "bk1_cc": np.ascontiguousarray(bk1[cols].reshape(128, 1)),
            "bk2_cc": np.ascontiguousarray(bk2[cols].reshape(128, 1)),
            "bg2": np.full((2, 1), bg[0], np.float32),
            "ws": Ws, "bs": bs.reshape(1, 1),
            "ws2_c": ws2, "bs2": np.full((2, 1), bs[0], np.float32),
            "wp_c": np.ascontiguousarray(Wp[cols, :]),
            "bp": bp.reshape(1, D)})
    return in_maps


def run_gla_fused(nc_cache, x_flat, Wq, bq, Wk1, bk1, Wk2, bk2, Wg, bg, Ws,
                  bs, Wp, bp):
    if "glaf" not in nc_cache:
        nc_cache["glaf"] = build_gla_fused()
    in_maps = _gla_fused_inmaps(x_flat, Wq, bq, Wk1, bk1, Wk2, bk2, Wg, bg,
                                Ws, bs, Wp, bp)
    res = run_bass_kernel_spmd(nc_cache["glaf"], in_maps,
                               core_ids=list(range(NCORES)))
    return np.concatenate([np.asarray(res.results[c]["x_out"])
                           for c in range(NCORES)], 0)


def emit_wb(nc, wgu_q, wd_q, wgu_qg, wd_qg, wgT_dram, wuT_dram, wdT_dram,
            tag=""):
    with ExitStack() as stack:
        def sb(name, shape, dt=F32):
            return stack.enter_context(nc.sbuf_tensor(tag + name, shape, dt))
        names = ("ccc", "tld", "tst")
        S = {n: nc.alloc_semaphore(name=tag + n) for n in names}
        stg = [sb(f"stg{e}", [128, 2816], BF16) for e in range(2)]
        with nc.Block() as blk:
            @blk.gpsimd
            def _(g0):
                g0.collective_compute(
                    "AllGather", mybir.AluOpType.bypass,
                    replica_groups=[list(range(NCORES))],
                    ins=[wgu_q.opt()], outs=[wgu_qg.opt()],
                ).then_inc(S["ccc"])
                g0.collective_compute(
                    "AllGather", mybir.AluOpType.bypass,
                    replica_groups=[list(range(NCORES))],
                    ins=[wd_q.opt()], outs=[wd_qg.opt()],
                ).then_inc(S["ccc"])
                g0.wait_ge(S["ccc"], 2)
        jobs = []
        for k in range(8):
            jobs.append((wgu_qg[0:2816, 128 * k:128 * (k + 1)],
                         wgT_dram[128 * k:128 * (k + 1), :], 2816))
            jobs.append((wgu_qg[2816:5632, 128 * k:128 * (k + 1)],
                         wuT_dram[128 * k:128 * (k + 1), :], 2816))
        for m in range(22):
            jobs.append((wd_qg[0:1024, 128 * m:128 * (m + 1)],
                         wdT_dram[128 * m:128 * (m + 1), :], 1024))
        with nc.Block() as blk:
            @blk.sync
            def _(sync):
                for i, (src, dst, w) in enumerate(jobs):
                    if i >= 2:
                        sync.wait_ge(S["tst"], 16 * (i - 1))
                    sync.dma_start(out=stg[i % 2][:, 0:w], in_=src,
                                   transpose=True).then_inc(S["tld"], 16)
            @blk.gpsimd
            def _(g0):
                for i, (src, dst, w) in enumerate(jobs):
                    g0.wait_ge(S["tld"], 16 * (i + 1))
                    g0.dma_start(out=dst, in_=stg[i % 2][:, 0:w]) \
                        .then_inc(S["tst"], 16)
                g0.wait_ge(S["tst"], 16 * len(jobs))


def build_mega():
    nc = bass.Bass("TRN2", target_bir_lowering=False)
    TEN = {}
    TEN["x"] = nc.dram_tensor("x", [T, D], F32, kind="ExternalInput").ap()
    TEN["wgu"] = nc.dram_tensor("wgu", [WGU_TPC * 128, 1024], F32,
                                kind="ExternalInput").ap()
    TEN["wd"] = nc.dram_tensor("wd", [128, 2816], F32,
                               kind="ExternalInput").ap()
    for n in ("wq_c", "wk1_c", "wk2_c", "sel_c"):
        TEN[n] = nc.dram_tensor(n, [D, 128], F32, kind="ExternalInput").ap()
    TEN["wgh2_c"] = nc.dram_tensor("wgh2_c", [D, 2], F32,
                                   kind="ExternalInput").ap()
    for n in ("bq_cc", "bk1_cc", "bk2_cc"):
        TEN[n] = nc.dram_tensor(n, [128, 1], F32, kind="ExternalInput").ap()
    TEN["bg2"] = nc.dram_tensor("bg2", [2, 1], F32, kind="ExternalInput").ap()
    TEN["ws"] = nc.dram_tensor("ws", [64, 1], F32, kind="ExternalInput").ap()
    TEN["bs"] = nc.dram_tensor("bs", [1, 1], F32, kind="ExternalInput").ap()
    TEN["ws2_c"] = nc.dram_tensor("ws2_c", [128, 2], F32,
                                  kind="ExternalInput").ap()
    TEN["bs2"] = nc.dram_tensor("bs2", [2, 1], F32, kind="ExternalInput").ap()
    TEN["wp_c"] = nc.dram_tensor("wp_c", [128, D], F32,
                                 kind="ExternalInput").ap()
    TEN["bp"] = nc.dram_tensor("bp", [1, D], F32, kind="ExternalInput").ap()
    for n in ("waq_c", "wak_c", "wav_c"):
        TEN[n] = nc.dram_tensor(n, [D, 128], F32, kind="ExternalInput").ap()
    for n in ("baq_cc", "bak_cc", "bav_cc"):
        TEN[n] = nc.dram_tensor(n, [128, 1], F32, kind="ExternalInput").ap()
    TEN["prot"] = nc.dram_tensor("prot", [128, 128], F32,
                                 kind="ExternalInput").ap()
    TEN["invf2"] = nc.dram_tensor("invf2", [128, 1], F32,
                                  kind="ExternalInput").ap()
    TEN["wo_c"] = nc.dram_tensor("wo_c", [128, D], F32,
                                 kind="ExternalInput").ap()
    TEN["bo"] = nc.dram_tensor("bo", [1, D], F32, kind="ExternalInput").ap()
    TEN["wgate"] = nc.dram_tensor("wgate", [D, 8], F32,
                                  kind="ExternalInput").ap()
    TEN["wexp"] = nc.dram_tensor("wexp", [D, D], F32,
                                 kind="ExternalInput").ap()
    TEN["bexp"] = nc.dram_tensor("bexp", [1, D], F32,
                                 kind="ExternalInput").ap()
    TEN["coremask"] = nc.dram_tensor("coremask", [8, 8], F32,
                                     kind="ExternalInput").ap()
    TEN["x_out"] = nc.dram_tensor("x_out", [T, D], F32,
                                  kind="ExternalOutput").ap()
    wgu_q = nc.dram_tensor("wgu_q", [WGU_TPC * 128, 1024], BF16).ap()
    wd_q = nc.dram_tensor("wd_q", [128, 2816], BF16).ap()
    wgu_qg = nc.dram_tensor("wgu_qg", [WGU_TPC * 128 * 8, 1024], BF16,
                            addr_space="Shared").ap()
    wd_qg = nc.dram_tensor("wd_qg", [1024, 2816], BF16,
                           addr_space="Shared").ap()
    wgT_dram = nc.dram_tensor("wgT_dram", [D, I_FFN], BF16, kind="Internal").ap()
    wuT_dram = nc.dram_tensor("wuT_dram", [D, I_FFN], BF16, kind="Internal").ap()
    wdT_dram = nc.dram_tensor("wdT_dram", [I_FFN, D], BF16, kind="Internal").ap()
    xq_dram = nc.dram_tensor("xq_dram", [T, D], BF16, kind="Internal").ap()
    hq_dram = nc.dram_tensor("hq_dram", [T, I_FFN], BF16, kind="Internal").ap()
    xh_moe = nc.dram_tensor("xh_moe", [T, D], BF16, kind="Internal").ap()
    cin = nc.dram_tensor("cin", [8, 1], F32).ap()
    cout = nc.dram_tensor("cout", [64, 1], F32, addr_space="Shared").ap()
    xs = [nc.dram_tensor(f"xstage{i}", [T, D], F32, kind="Internal").ap()
          for i in range(5)]
    _fused_gla_scratch(nc, TEN)
    _fused_att1_scratch(nc, TEN)

    with nc.named_scope("p_wprep"), nc.cleanup_on_exit():
        emit_wprep(nc, TEN["wgu"], TEN["wd"], wgu_q, wd_q, tag="wp_")
    with nc.named_scope("p_wb"), nc.cleanup_on_exit():
        emit_wb(nc, wgu_q, wd_q, wgu_qg, wd_qg, wgT_dram, wuT_dram, wdT_dram,
                tag="wb_")
    with nc.named_scope("p_ffn1"), nc.cleanup_on_exit():
        emit_ffn0(nc, TEN["x"], xs[0], wgT_dram, wuT_dram, wdT_dram,
                  xq_dram, hq_dram, tag="f1_", xh_out=TEN["xh_loc"],
                  xhg=TEN["xhg"])
    emit_gla_fused(nc, TEN, xs[0], xs[1], pfx="", pre_ln=True)
    with nc.named_scope("p_ffn2"), nc.cleanup_on_exit():
        emit_ffn0(nc, xs[1], xs[2], wgT_dram, wuT_dram, wdT_dram,
                  xq_dram, hq_dram, tag="f2_")
    with nc.named_scope("p_moe1"), nc.cleanup_on_exit():
        emit_moe(nc, xs[2], xs[3], TEN["wgate"], TEN["wexp"], TEN["bexp"],
                 TEN["coremask"], xh_moe, cin, cout, tag="m1_",
                 xh_out=TEN["xh_loc"], xhg=TEN["xhg"])
    emit_att1_fused(nc, TEN, xs[3], xs[4], pfx="", pre_ln=True)
    with nc.named_scope("p_moe2"), nc.cleanup_on_exit():
        emit_moe(nc, xs[4], TEN["x_out"], TEN["wgate"], TEN["wexp"],
                 TEN["bexp"], TEN["coremask"], xh_moe, cin, cout, tag="m2_")
    return nc


def kernel(**inputs):
    g = lambda n: np.ascontiguousarray(np.asarray(inputs[n], np.float32))
    x = g("L0_attn_x").reshape(TOK, D)
    wgu = np.concatenate([g("L0_ffn_w_gate"), g("L0_ffn_w_up"),
                          np.zeros((512, 1024), np.float32)], axis=0)
    per = WGU_TPC * 128
    w_down = g("L0_ffn_w_down")
    gla_maps = _gla_fused_inmaps(
        x, g("L0_attn_Wq"), g("L0_attn_bq"), g("L0_attn_Wk1"), g("L0_attn_bk1"),
        g("L0_attn_Wk2"), g("L0_attn_bk2"), g("L0_attn_Wg"), g("L0_attn_bg"),
        g("L0_attn_Ws"), g("L0_attn_bs"), g("L0_attn_Wp"), g("L0_attn_bp"))
    att_maps = _att1_fused_inmaps(
        x, g("L1_attn_Wq"), g("L1_attn_bq"), g("L1_attn_Wk"), g("L1_attn_bk"),
        g("L1_attn_Wv"), g("L1_attn_bv"), g("L1_attn_Wo"), g("L1_attn_bo"))
    nc = build_mega()
    in_maps = []
    for c in range(NCORES):
        m = {"x": np.ascontiguousarray(x[c * T:(c + 1) * T]),
             "wgu": np.ascontiguousarray(wgu[c * per:(c + 1) * per]),
             "wd": np.ascontiguousarray(w_down[c * 128:(c + 1) * 128]),
             "wgate": g("L1_ffn_w_gate"),
             "wexp": g("L1_ffn_w_expert"),
             "bexp": g("L1_ffn_b_expert").reshape(1, D),
             "coremask": np.tile((np.arange(8) < c).astype(np.float32)[None, :],
                                 (8, 1))}
        for k, v in gla_maps[c].items():
            if k != "x":
                m[k] = v
        for k, v in att_maps[c].items():
            if k != "x":
                m[k] = v
        in_maps.append(m)
    res = run_bass_kernel_spmd(nc, in_maps, core_ids=list(range(NCORES)))
    out = np.concatenate([np.asarray(res.results[c]["x_out"])
                          for c in range(NCORES)], 0)
    return np.ascontiguousarray(out.reshape(B, S, D).astype(np.float32))
